# version 05
# baseline (speedup 1.0000x reference)
import numpy as np
from contextlib import ExitStack
import concourse.bass as bass
import concourse.mybir as mybir

F32 = mybir.dt.float32
F32R = mybir.dt.float32r
BF16 = mybir.dt.bfloat16
I32 = mybir.dt.int32
AL = mybir.AluOpType
AF = mybir.ActivationFunctionType


class Prog:
    ENG = {"v": "vector", "s": "scalar", "t": "tensor", "g": "gpsimd", "y": "sync"}

    def __init__(self, nc, ctx):
        self.nc = nc
        self.ctx = ctx
        self.sems = {}
        self.items = {e: [] for e in self.ENG}
        self.cnt = {}
        for q in ["v", "s", "t", "gc"]:
            self._newsem(q)

    def _newsem(self, name):
        self.sems[name] = self.ctx.enter_context(self.nc.semaphore("sem_" + name))
        self.cnt[name] = 0

    def _emit(self, eng, sem, step, fn, deps):
        deps = [d for d in (deps or []) if d is not None]
        self.cnt[sem] += step
        tok = (sem, self.cnt[sem])
        self.items[eng].append((fn, tuple(deps), sem, step))
        return tok

    def op(self, eng, fn, deps=None):
        assert eng in ("v", "s", "t", "g")
        return self._emit(eng, "gc" if eng == "g" else eng, 1, fn, deps)

    def dma(self, q, out, in_, deps=None, key=None, slow=False):
        eng = {"dS": "y", "dA": "s", "dG": "g"}[q]
        sem = f"{q}_{key if key is not None else 'misc'}"
        if sem not in self.sems:
            self._newsem(sem)
        if slow:
            return self._emit(eng, sem, 16,
                              lambda e: e.dma_start(out=out, in_=in_, allow_slow_non_contiguous=True),
                              deps)
        return self._emit(eng, sem, 16, lambda e: e.dma_start(out=out, in_=in_), deps)

    def fence(self):
        return [(s, c) for s, c in self.cnt.items() if c > 0]

    def build(self, block):
        sems = self.sems
        for eng, engname in self.ENG.items():
            items = self.items[eng]
            if not items and engname != "sync":
                continue

            def make(items, eng):
                def body(e):
                    water = {}
                    dedup = True
                    for fn, deps, sem, step in items:
                        for (ds, dv) in deps:
                            if dedup and water.get(ds, 0) >= dv:
                                continue
                            e.wait_ge(sems[ds], dv)
                            water[ds] = dv
                        fn(e).then_inc(sems[sem], step)
                    if eng == "y":
                        for s2, c in self.cnt.items():
                            if c > 0 and water.get(s2, 0) < c:
                                e.wait_ge(sems[s2], c)
                return body
            getattr(block, engname)(make(items, eng))


_BUFN = [0]


class Buf:
    __slots__ = ("ap", "w", "r", "key", "sticky", "sr")

    def __init__(self, ap, sticky=False):
        self.ap = ap
        self.w = None
        self.r = []
        self.sticky = sticky
        self.sr = {}
        _BUFN[0] += 1
        self.key = _BUFN[0]

    def wdep(self):
        base = self.r + ([self.w] if self.w else [])
        if self.sticky:
            base = base + [(s, v) for s, v in self.sr.items()]
        return base

    def rdep(self):
        return [self.w] if self.w else []

    def ww(self, tok):
        if self.sticky:
            for (s, v) in self.r:
                if self.sr.get(s, -1) < v:
                    self.sr[s] = v
            if self.w is not None:
                s, v = self.w
                if self.sr.get(s, -1) < v:
                    self.sr[s] = v
        self.w = tok
        self.r = []

    def rr(self, tok):
        self.r.append(tok)
        if len(self.r) > 24:
            self.r = self.r[-24:]


def aslist(d):
    return d if isinstance(d, list) else ([d] if d else [])


class MK:
    NS = 16
    DT = 64

    def __init__(self, T, enabled=(1, 2, 3, 4, 5, 6)):
        self.T = T
        self.DM = 1024
        self.DI = 2048
        self.DH = 4096
        self.enabled = tuple(enabled)
        self.nc = bass.Bass("TRN2", target_bir_lowering=False)
        self.ctx = ExitStack()
        self._uid = 0

    def uid(self, p="u"):
        self._uid += 1
        return f"{p}_{self._uid}"

    def dram(self, name, shape, dt=F32, kind=None):
        t = self.nc.dram_tensor(name, shape, dt, kind=kind) if kind else self.nc.dram_tensor(name, shape, dt)
        return t.ap()

    def sbuf(self, name, shape, dt=F32):
        return Buf(self.ctx.enter_context(self.nc.sbuf_tensor(name, shape, dt)))

    def pool(self, tag):
        return self._pool[tag]

    def psumb(self, name, shape, dt=F32):
        return Buf(self.ctx.enter_context(self.nc.psum_tensor(name, shape, dt)), sticky=True)

    def build(self):
        nc, T, DM, DI, DH = self.nc, self.T, self.DM, self.DI, self.DH
        ein = lambda n, s, dt=F32: self.dram(n, s, dt, kind="ExternalInput")
        self.x_in = ein("x_in", [T, DM])
        self.out = self.dram("out", [T, DM], F32, kind="ExternalOutput")
        w = {}
        w["in_projT"] = ein("in_projT", [DM, 2 * DI], BF16)
        w["x_projT"] = ein("x_projT", [DI, 96], BF16)
        w["dt_projT"] = ein("dt_projT", [self.DT, DI], BF16)
        w["out_projT"] = ein("out_projT", [DI, DM], BF16)
        w["w_qkv"] = ein("w_qkv", [DM, 3 * DM], BF16)
        w["w_out"] = ein("w_out", [DM, DM], BF16)
        w["w1T"] = ein("w1T", [DM, DH])
        w["w2T"] = ein("w2T", [DH, DM])
        w["wfr"] = ein("wfr", [DM, DM], F32R)
        w["wfi"] = ein("wfi", [DM, DM], F32R)
        w["C_d"] = ein("C_d", [DM, DM], F32R)
        w["S_d"] = ein("S_d", [DM, DM], F32R)
        w["gb"] = ein("gb", [128, 4 * DM], BF16)
        w["ident"] = ein("ident", [128, 128], BF16)
        w["maskb"] = ein("maskb", [128, 512], BF16)
        w["smalls"] = ein("smalls", [128, 504])
        self.w = w
        self.xres = self.dram("xres", [T, DM])
        self.xnp_d = self.dram("xnp_d", [T, DM])
        self.xn_fm16 = self.dram("xn_fm16", [DM, T], BF16)
        self.xn_fmR = self.dram("xn_fmR", [DM, T], F32R)
        self.u2_fm = self.dram("u2_fm", [DI, T], BF16)
        self.sz_fm = self.dram("sz_fm", [DI, T], BF16)
        self.du_fm = self.dram("du_fm", [DI, T], BF16)
        self.delta_fm = self.dram("delta_fm", [DI, T])
        self.dt_d = self.dram("dt_d", [self.DT, T], BF16)
        self.BC_d = self.dram("BC_d", [32, T])
        self.yg_fm = self.dram("yg_fm", [DI, T], BF16)
        self.w1q_d = self.dram("w1q_d", [DM, DH], BF16)
        self.w2q_d = self.dram("w2q_d", [DH, DM], BF16)
        self.wsc_d = self.dram("wsc_d", [1, 2])
        self.xq_fm = self.dram("xq_fm", [DM, T], BF16)
        self.hq_fm = self.dram("hq_fm", [DH, T], BF16)
        self.qk_fm = self.dram("qk_fm", [2 * DM, T], BF16)
        self.v_tm = self.dram("v_tm", [T, DM], BF16)
        self.ao_fm = self.dram("ao_fm", [DM, T], BF16)
        self.Q1_d = self.dram("Q1_d", [DM, DM], F32R)
        self.Q2_d = self.dram("Q2_d", [DM, DM], F32R)
        self.M_d = self.dram("M_d", [DM, DM], F32R)

        self.ident = self.sbuf("sb_ident", [128, 128], BF16)
        self.identR = self.sbuf("sb_identR", [128, 128], F32R)
        self.maskb = self.sbuf("sb_maskb", [128, 512], BF16)
        self.gb = self.sbuf("sb_gb", [128, 4 * DM], BF16)
        self.smalls = self.sbuf("sb_smalls", [128, 504], F32)
        self.A_neg = self.sbuf("sb_Aneg", [128, 256], F32)
        self.invws = self.sbuf("sb_invws", [128, 4], F32)
        self.states = self.sbuf("sb_states", [128, 256], F32)
        self.tails = self.sbuf("sb_tails", [128, 128], F32)
        self.stv = self.sbuf("sb_stv", [128, 16], F32)
        self.mx = self.sbuf("sb_mx", [128, 4], F32)
        self.sms = self.sbuf("sb_sms", [128, 4], F32)
        self.sc2 = self.sbuf("sb_sc2", [128, 2], F32)
        self.wres = self.sbuf("sb_wres", [128, 16384], BF16)
        self.wresR = self.sbuf("sb_wresR", [128, 8192], F32R)
        self.rhs16 = [self.sbuf("sb_rhsA", [128, 4096], BF16), self.sbuf("sb_rhsB", [128, 4096], BF16)]
        self.rhsR = [self.sbuf("sb_rhsRA", [128, 2048], F32R), self.sbuf("sb_rhsRB", [128, 2048], F32R)]
        self._pool = {}
        for tag, cols, dt in [
            ("F32A", 4224, F32), ("F32B", 2048, F32), ("F32C", 2048, F32),
            ("F32D", 512, F32), ("F32E", 512, F32), ("I32A", 2048, I32),
            ("B16A", 2048, BF16), ("B16B", 2048, BF16), ("B16C", 2048, BF16),
            ("B16D", 1024, BF16), ("B16E", 1024, BF16), ("B16F", 1024, BF16),
            ("B16G", 1024, BF16),
            ("BIGB", 4096, BF16), ("BIGC", 4096, BF16),
            ("R32A", 1024, F32R), ("R32B", 1024, F32R),
        ]:
            self._pool[tag] = Buf(self.ctx.enter_context(self.nc.sbuf_tensor("pool_" + tag, [128, cols], dt)))
        self.ps = {}
        for i in range(4):
            self.ps[f"mm{i}"] = self.psumb(f"ps_mm{i}", [128, 512], F32)
        self.ps["tr"] = self.psumb("ps_tr", [128, 1024], BF16)
        self.ps["trR"] = self.psumb("ps_trR", [128, 512], F32R)
        self.ps["sc"] = self.psumb("ps_sc", [128, 512], F32)
        self.ps["av"] = self.psumb("ps_av", [128, 512], F32)

        self.P = Prog(nc, self.ctx)
        self.prelude()
        subs = {1: self.sub_l0ffn, 2: self.sub_mamba, 3: self.sub_l0ffn,
                4: self.sub_l1ffn, 5: self.sub_attn, 6: self.sub_l1ffn}
        for i in self.enabled:
            subs[i]()
        self.final_out()
        with nc.Block() as block:
            self.P.build(block)
        self.ctx.close()
        return nc

    def mm_stream(self, lhsT_d, K, M, rhs_d, N, evac, rhs_dt=BF16, rhs_dep=None,
                  lhs_dep=None, mt_list=None, mt_off=0):
        P = self.P
        nK = (K + 127) // 128
        KG = 4 if rhs_dt == F32R else 8
        ngrp = (nK + KG - 1) // KG
        wbuf = self.wresR if rhs_dt == F32R else self.wres
        wcap = 8192 if rhs_dt == F32R else 16384
        resident = nK * M <= wcap
        ldep = aslist(lhs_dep)
        full_chunks = (K % 128 == 0)
        wtok = None
        if resident:
            deps = wbuf.wdep() + ldep
            if full_chunks:
                sview = lhsT_d.rearrange("(k p) m -> p k m", p=128)[:, :, :M]
                dview = wbuf.ap[:, :nK * M].rearrange("p (k m) -> p k m", m=M)
                wtok = P.dma("dS", dview, sview, deps, key=wbuf.key)
            else:
                tt = None
                for k in range(nK):
                    pk = min(128, K - k * 128)
                    tt = P.dma("dS", wbuf.ap[:pk, k * M:(k + 1) * M], lhsT_d[k * 128:k * 128 + pk, :M],
                               deps if k == 0 else [tt], key=wbuf.key)
                wtok = tt
            wbuf.ww(wtok)
        rbufs = self.rhsR if rhs_dt == F32R else self.rhs16
        rdeps = aslist(rhs_dep)
        mts = mt_list if mt_list is not None else list(range((M + 127) // 128))
        nt_cols = 512
        mt_groups = [mts[i:i + 4] for i in range(0, len(mts), 4)]
        nNT = N // nt_cols

        def load_rhs(b, k0, k1, ntile, extra):
            deps = b.wdep() + rdeps + extra
            if full_chunks:
                sview = rhs_d.rearrange("(k p) n -> p k n", p=128)[:, k0:k1,
                                                                   ntile * nt_cols:(ntile + 1) * nt_cols]
                dview = b.ap[:, :(k1 - k0) * nt_cols].rearrange("p (k n) -> p k n", n=nt_cols)
                return P.dma("dA", dview, sview, deps, key=b.key)
            tt = None
            for k in range(k0, k1):
                pk = min(128, K - k * 128)
                tt = P.dma("dA", b.ap[:pk, (k - k0) * nt_cols:(k - k0 + 1) * nt_cols],
                           rhs_d[k * 128:k * 128 + pk, ntile * nt_cols:(ntile + 1) * nt_cols],
                           deps if k == k0 else [tt], key=b.key)
            return tt

        if ngrp == 1:
            rtoks = {}
            rtoks[0] = load_rhs(rbufs[0], 0, nK, 0, [])
            rbufs[0].ww(rtoks[0])
            for ntile in range(nNT):
                b = rbufs[ntile % 2]
                tt = rtoks[ntile]
                if ntile + 1 < nNT:
                    nb = rbufs[(ntile + 1) % 2]
                    rtoks[ntile + 1] = load_rhs(nb, 0, nK, ntile + 1, [])
                    nb.ww(rtoks[ntile + 1])
                for mts_g in mt_groups:
                    mm_last = {}
                    for j, mt in enumerate(mts_g):
                        ms = min(128, M - mt * 128)
                        psb = self.ps[f"mm{j % 4}"]
                        mm_tok = None
                        for k in range(nK):
                            pk = min(128, K - k * 128)
                            wap = wbuf.ap[:pk, k * M + mt * 128:k * M + mt * 128 + ms]
                            d = [wtok, tt] + (psb.wdep() if k == 0 else [mm_tok])
                            mm_tok = P.op("t", lambda e, wap=wap, psb=psb, b=b, ms=ms, pk=pk, k=k,
                                          st=(k == 0), sp=(k == nK - 1):
                                          e.matmul(psb.ap[:ms, :nt_cols], wap,
                                                   b.ap[:pk, k * nt_cols:(k + 1) * nt_cols],
                                                   start=st, stop=sp), d)
                            psb.ww(mm_tok)
                            b.rr(mm_tok)
                            wbuf.rr(mm_tok)
                        mm_last[mt] = mm_tok
                    for j, mt in enumerate(mts_g):
                        psb = self.ps[f"mm{j % 4}"]
                        et = evac(mt + mt_off, ntile, psb, [mm_last[mt]])
                        psb.rr(et)
                        b.rr(et)
        else:
            for ntile in range(nNT):
                for mts_g in mt_groups:
                    mm_tok = {mt: None for mt in mts_g}
                    for kg in range(ngrp):
                        b = rbufs[kg % 2]
                        k0 = kg * KG
                        k1 = min(nK, k0 + KG)
                        tt = load_rhs(b, k0, k1, ntile, [])
                        b.ww(tt)
                        if not resident:
                            base = (kg % 2) * 8192
                            dw = self.wres.wdep() + ldep
                            if full_chunks:
                                sview = lhsT_d.rearrange("(k p) m -> p k m", p=128)[:, k0:k1, :M]
                                dview = self.wres.ap[:, base:base + (k1 - k0) * M].rearrange(
                                    "p (k m) -> p k m", m=M)
                                tw = P.dma("dS", dview, sview, dw, key=self.wres.key)
                            else:
                                tw = None
                                for k in range(k0, k1):
                                    pk = min(128, K - k * 128)
                                    tw = P.dma("dS", self.wres.ap[:pk, base + (k - k0) * M:
                                                                  base + (k - k0 + 1) * M],
                                               lhsT_d[k * 128:k * 128 + pk, :M], dw if k == k0 else [tw],
                                               key=self.wres.key)
                            self.wres.ww(tw)
                            wtok = tw
                        for j, mt in enumerate(mts_g):
                            ms = min(128, M - mt * 128)
                            psb = self.ps[f"mm{j % 4}"]
                            for k in range(k0, k1):
                                pk = min(128, K - k * 128)
                                if resident:
                                    wap = wbuf.ap[:pk, k * M + mt * 128:k * M + mt * 128 + ms]
                                else:
                                    base = (kg % 2) * 8192
                                    wap = self.wres.ap[:pk, base + (k - k0) * M + mt * 128:
                                                       base + (k - k0) * M + mt * 128 + ms]
                                d = [wtok, tt]
                                if k == 0:
                                    d += psb.wdep()
                                else:
                                    d.append(mm_tok[mt])
                                mm_tok[mt] = P.op("t", lambda e, wap=wap, psb=psb, b=b, ms=ms, pk=pk,
                                                  kk=(k - k0), st=(k == 0), sp=(k == nK - 1):
                                                  e.matmul(psb.ap[:ms, :nt_cols], wap,
                                                           b.ap[:pk, kk * nt_cols:(kk + 1) * nt_cols],
                                                           start=st, stop=sp), d)
                                psb.ww(mm_tok[mt])
                                b.rr(mm_tok[mt])
                                (wbuf if resident else self.wres).rr(mm_tok[mt])
                    for j, mt in enumerate(mts_g):
                        psb = self.ps[f"mm{j % 4}"]
                        et = evac(mt + mt_off, ntile, psb, [mm_tok[mt]])
                        psb.rr(et)

    def prelude(self):
        P, w = self.P, self.w
        t = P.dma("dS", self.ident.ap[:], w["ident"][:, :], key=self.ident.key); self.ident.ww(t)
        t = P.dma("dS", self.maskb.ap[:], w["maskb"][:, :], key=self.maskb.key); self.maskb.ww(t)
        t = P.dma("dS", self.gb.ap[:], w["gb"][:, :], key=self.gb.key); self.gb.ww(t)
        ts = P.dma("dS", self.smalls.ap[:], w["smalls"][:, :], key=self.smalls.key); self.smalls.ww(ts)
        ti = P.op("v", lambda e: e.tensor_copy(self.identR.ap[:], self.ident.ap[:]),
                  self.ident.rdep() + self.identR.wdep())
        self.identR.ww(ti); self.ident.rr(ti)
        self.csts = self.sbuf("sb_csts", [128, 2], F32)
        tc1 = P.op("v", lambda e: e.memset(self.csts.ap[:, 0:1], 1e-5), self.csts.wdep())
        tc2 = P.op("v", lambda e: e.memset(self.csts.ap[:, 1:2], 1.0), [tc1])
        self.csts.ww(tc2)
        self.nc.const_aps.register(F32, 1e-5, self.csts.ap[:, 0:1])
        self.nc.const_aps.register(F32, 1.0, self.csts.ap[:, 1:2])
        self.t_x0 = P.dma("dS", self.xres[:, :], self.x_in[:, :])
        self.off = {"cw0": 0, "cb0": 64, "dsk": 80, "dtb": 96, "A": 112, "cw1": 368, "cb1": 464, "fbias": 496}
        ta = P.op("s", lambda e: e.activation(out=self.A_neg.ap[:], in_=self.smalls.ap[:, 112:368],
                                              func=AF.Exp), [ts] + self.A_neg.wdep())
        self.smalls.rr(ta)
        ta = P.op("v", lambda e: e.tensor_scalar(self.A_neg.ap[:], self.A_neg.ap[:], -1.0, None, AL.mult), [ta])
        self.A_neg.ww(ta)
        tz = P.op("v", lambda e: e.memset(self.states.ap[:], 0.0), self.states.wdep())
        self.states.ww(tz)
        tz = P.op("v", lambda e: e.memset(self.tails.ap[:], 0.0), self.tails.wdep())
        self.tails.ww(tz)
        self.quant_weights()
        self.build_M()

    def smalls_pp(self, key, idx):
        return self.smalls.ap[:, self.off[key] + idx:self.off[key] + idx + 1]

    def quant_weights(self):
        P = self.P
        HB = 2048
        buf = self.pool("F32A")
        g = self.pool("F32B")
        qb = self.pool("F32C")
        qi = self.pool("I32A")
        qo = self.pool("B16A")
        acc = self.sbuf("q_acc", [128, 2], F32)
        part = self.sbuf("q_part", [128, 64], F32)
        sval = self.sbuf("q_sval", [1, 1], F32)
        onec = self.sbuf("q_onec", [128, 1], F32)
        totb = self.sbuf("q_totb", [128, 1], F32)
        rws = self.sbuf("q_rws", [128, 2], F32)
        toc = P.op("v", lambda e: e.memset(onec.ap[:], 1.0), onec.wdep())
        onec.ww(toc)
        self.t_wq = None
        for (wT_d, dst, K, N, idx) in [(self.w["w1T"], self.w1q_d, self.DM, self.DH, 0),
                                       (self.w["w2T"], self.w2q_d, self.DH, self.DM, 1)]:
            nH = (N + HB - 1) // HB
            tprev = None
            ncols = 0
            for kk in range(K // 128):
                for h in range(nH):
                    hw = min(HB, N - h * HB)
                    tl = P.dma("dS", buf.ap[:, :hw], wT_d[kk * 128:(kk + 1) * 128, h * HB:h * HB + hw],
                               buf.wdep() + [tprev], key=buf.key)
                    buf.ww(tl)
                    tv = P.op("s", lambda e, hw=hw: e.activation(out=g.ap[:, :hw], in_=buf.ap[:, :hw],
                                                                 func=AF.Abs, accum_out=acc.ap[:, 0:1]),
                              [tl] + g.wdep() + acc.wdep())
                    g.ww(tv); acc.ww(tv); buf.rr(tv)
                    tc = P.op("v", lambda e, c=ncols: e.tensor_copy(part.ap[:, c:c + 1], acc.ap[:, 0:1]),
                              [tv] + (part.wdep() if ncols == 0 else []))
                    part.ww(tc); acc.rr(tc)
                    tprev = tc
                    ncols += 1
            t1 = P.op("v", lambda e, nc_=ncols: e.tensor_reduce(totb.ap[:], part.ap[:, :nc_],
                                                                mybir.AxisListType.X, AL.add),
                      [tprev] + totb.wdep())
            totb.ww(t1); part.rr(t1)
            psb = self.ps["sc"]
            tmm = P.op("t", lambda e: e.matmul(psb.ap[:1, :1], totb.ap[:], onec.ap[:],
                                               start=True, stop=True),
                       [t1, toc] + psb.wdep())
            psb.ww(tmm); totb.rr(tmm); onec.rr(tmm)
            tsv = P.op("v", lambda e, d=1.0 / (K * N): e.tensor_scalar(sval.ap[:], psb.ap[0:1, 0:1], d, 1e-5,
                                                                      AL.mult, AL.max),
                       [tmm] + sval.wdep())
            sval.ww(tsv); psb.rr(tsv)
            tst = P.dma("dS", self.wsc_d[0:1, idx:idx + 1], sval.ap[:], [tsv], key=sval.key)
            sval.rr(tst)
            tbc = P.dma("dG", self.invws.ap[:, idx:idx + 1], self.wsc_d[0, idx:idx + 1].partition_broadcast(128),
                        [tst] + self.invws.wdep(), key=self.invws.key)
            self.invws.ww(tbc)
            trw = P.op("v", lambda e, idx=idx: e.reciprocal(rws.ap[:, idx:idx + 1],
                                                            self.invws.ap[:, idx:idx + 1]),
                       [tbc] + (rws.wdep() if idx == 0 else []))
            rws.ww(trw); self.invws.rr(trw)
            tpr = None
            for kk in range(K // 128):
                for h in range(nH):
                    hw = min(HB, N - h * HB)
                    tl = P.dma("dS", buf.ap[:, :hw], wT_d[kk * 128:(kk + 1) * 128, h * HB:h * HB + hw],
                               buf.wdep() + [tpr, trw], key=buf.key)
                    buf.ww(tl)
                    tg = P.op("v", lambda e, idx=idx, hw=hw: e.tensor_scalar(g.ap[:, :hw], buf.ap[:, :hw],
                                                                             rws.ap[:, idx:idx + 1], None,
                                                                             AL.mult), [tl, trw] + g.wdep())
                    g.ww(tg); buf.rr(tg); rws.rr(tg)
                    tg = P.op("v", lambda e, hw=hw: e.tensor_scalar(g.ap[:, :hw], g.ap[:, :hw], -1.5, 1.5,
                                                                    AL.max, AL.min), [tg])
                    g.ww(tg)
                    tq = P.op("v", lambda e, hw=hw: e.tensor_scalar(qi.ap[:, :hw], g.ap[:, :hw], 128.0, None,
                                                                    AL.add), [tg] + qi.wdep())
                    qi.ww(tq)
                    tq2 = P.op("v", lambda e, hw=hw: e.tensor_copy(qb.ap[:, :hw], qi.ap[:, :hw]),
                               [tq] + qb.wdep())
                    qb.ww(tq2); qi.rr(tq2)
                    tg2 = P.op("v", lambda e, hw=hw: e.scalar_tensor_tensor(g.ap[:, :hw], g.ap[:, :hw], 128.0,
                                                                            qb.ap[:, :hw], AL.add,
                                                                            AL.subtract), [tq2, tg])
                    g.ww(tg2); qb.rr(tg2)
                    tq3 = P.op("v", lambda e, hw=hw: e.scalar_tensor_tensor(qb.ap[:, :hw], g.ap[:, :hw], 0.5,
                                                                            qb.ap[:, :hw], AL.is_ge,
                                                                            AL.add), [tg2])
                    qb.ww(tq3); g.rr(tq3)
                    tq4 = P.op("v", lambda e, hw=hw: e.tensor_scalar(qb.ap[:, :hw], qb.ap[:, :hw], -128.0,
                                                                     None, AL.add), [tq3])
                    qb.ww(tq4)
                    tq5 = P.op("v", lambda e, hw=hw: e.tensor_scalar(qo.ap[:, :hw], qb.ap[:, :hw], -1.0, 1.0,
                                                                     AL.max, AL.min), [tq4] + qo.wdep())
                    qo.ww(tq5); qb.rr(tq5)
                    tpr = P.dma("dG", dst[kk * 128:(kk + 1) * 128, h * HB:h * HB + hw], qo.ap[:, :hw],
                                [tq5], key=qo.key)
                    qo.rr(tpr)
            self.t_wq = tpr if self.t_wq is None else tpr

    def build_M(self):
        P, w, DM = self.P, self.w, self.DM
        ev = self.pool("R32A")
        l1 = self.sbuf("ml1", [128, 128], F32R)
        l2 = self.sbuf("ml2", [128, 128], F32R)
        l2n = self.sbuf("ml2n", [128, 128], F32R)
        r1 = Buf(self.rhsR[0].ap[:, 0:512])
        r2 = Buf(self.rhsR[1].ap[:, 0:512])
        r1.w = self.rhsR[0].w; r2.w = self.rhsR[1].w

        def pair(A1, B1, A2, B2, neg2, out_d, scale):
            td = None
            for mt in range(DM // 128):
                for ntile in range(DM // 512):
                    psb = self.ps["mm0"]
                    mm_tok = None
                    for kk in range(DM // 128):
                        tl1 = P.dma("dS", l1.ap[:], A1[kk * 128:(kk + 1) * 128, mt * 128:(mt + 1) * 128],
                                    l1.wdep(), key=l1.key)
                        l1.ww(tl1)
                        tr1 = P.dma("dA", r1.ap[:], B1[kk * 128:(kk + 1) * 128, ntile * 512:(ntile + 1) * 512],
                                    r1.wdep(), key=r1.key)
                        r1.ww(tr1)
                        tl2 = P.dma("dS", l2.ap[:], A2[kk * 128:(kk + 1) * 128, mt * 128:(mt + 1) * 128],
                                    l2.wdep(), key=l2.key)
                        l2.ww(tl2)
                        tr2 = P.dma("dA", r2.ap[:], B2[kk * 128:(kk + 1) * 128, ntile * 512:(ntile + 1) * 512],
                                    r2.wdep(), key=r2.key)
                        r2.ww(tr2)
                        d = [tl1, tr1] + (psb.wdep() if kk == 0 else [mm_tok])
                        mm_tok = P.op("t", lambda e, f=(kk == 0): e.matmul(psb.ap[:, :], l1.ap[:], r1.ap[:],
                                                                          start=f, stop=False), d)
                        psb.ww(mm_tok); l1.rr(mm_tok); r1.rr(mm_tok)
                        lu, dl = l2, [tl2]
                        if neg2:
                            tn = P.op("v", lambda e: e.tensor_scalar(l2n.ap[:], l2.ap[:], -1.0, None,
                                                                    AL.mult), [tl2] + l2n.wdep())
                            l2n.ww(tn); l2.rr(tn)
                            lu, dl = l2n, [tn]
                        mm_tok = P.op("t", lambda e, lu=lu, lk=(kk == DM // 128 - 1): e.matmul(
                            psb.ap[:, :], lu.ap[:], r2.ap[:], start=False, stop=lk), dl + [tr2, mm_tok])
                        psb.ww(mm_tok); lu.rr(mm_tok); r2.rr(mm_tok)
                    te = P.op("s", lambda e, sc=scale: e.activation(out=ev.ap[:, :512], in_=psb.ap[:, :],
                                                                   func=AF.Identity, scale=sc),
                              [mm_tok] + ev.wdep())
                    ev.ww(te); psb.rr(te)
                    td = P.dma("dS", out_d[mt * 128:(mt + 1) * 128, ntile * 512:(ntile + 1) * 512],
                               ev.ap[:, :512], [te], key=ev.key)
                    ev.rr(td)
            return td

        pair(w["wfr"], w["C_d"], w["wfi"], w["S_d"], False, self.Q1_d, 1.0)
        pair(w["wfi"], w["C_d"], w["wfr"], w["S_d"], True, self.Q2_d, 1.0)
        self.t_M = pair(self.Q1_d, w["C_d"], self.Q2_d, w["S_d"], True, self.M_d, 1.0 / DM)

    def ln(self, src_d, dst_tm=None, dst_fm=None, fm_dt=BF16, gbi=None, dep=None):
        dep = aslist(dep)
        P, T, DM = self.P, self.T, self.DM
        bxt, xts = self.slots("F32A", 1024, 2)
        sq = self.pool("F32B")
        bxo, xos = self.slots("F32C", 1024, 2)
        bxb, xbs = self.slots("B16A", 1024, 2)
        xb2 = self.pool("B16B")
        xr = self.pool("R32A")
        evb = self.pool("B16C")
        evr = self.pool("R32B")
        stv0 = self.stv
        stvs = [Buf(stv0.ap[:, 0:8]), Buf(stv0.ap[:, 8:16])]
        for b in stvs:
            b.w = stv0.w; b.r = list(stv0.r)
        last = None
        for it in range(T // 128):
            sl = it % 2
            xt, xo, xb, stv = xts[sl], xos[sl], xbs[sl], stvs[sl]
            tl = P.dma("dS", xt.ap[:], src_d[it * 128:(it + 1) * 128, :], xt.wdep() + dep + [last],
                       key=xt.key)
            xt.ww(tl)
            t1 = P.op("s", lambda e, xt=xt, stv=stv: e.activation(out=sq.ap[:, :1024], in_=xt.ap[:],
                                                                  func=AF.Identity,
                                                                  accum_out=stv.ap[:, 0:1]),
                      [tl] + sq.wdep() + stv.wdep())
            t2 = P.op("s", lambda e, xt=xt, stv=stv: e.activation(out=sq.ap[:, :1024], in_=xt.ap[:],
                                                                  func=AF.Square,
                                                                  accum_out=stv.ap[:, 1:2]), [t1])
            sq.ww(t2); stv.ww(t2); xt.rr(t2)
            t3 = P.op("v", lambda e, d=1.0 / DM, stv=stv: e.tensor_scalar(stv.ap[:, 2:3], stv.ap[:, 0:1],
                                                                         d, None, AL.mult), [t2])
            t4 = P.op("v", lambda e, d=1.0 / DM, stv=stv: e.tensor_scalar(stv.ap[:, 3:4], stv.ap[:, 1:2],
                                                                         d, None, AL.mult), [t2])
            t5 = P.op("v", lambda e, stv=stv: e.tensor_tensor(stv.ap[:, 6:7], stv.ap[:, 2:3],
                                                              stv.ap[:, 2:3], AL.mult), [t3])
            t6 = P.op("v", lambda e, stv=stv: e.tensor_tensor(stv.ap[:, 3:4], stv.ap[:, 3:4],
                                                              stv.ap[:, 6:7], AL.subtract), [t4, t5])
            t7 = P.op("s", lambda e, stv=stv: e.activation(out=stv.ap[:, 6:7], in_=stv.ap[:, 3:4],
                                                           func=AF.Ln, bias=1e-5), [t6])
            t8 = P.op("s", lambda e, stv=stv: e.activation(out=stv.ap[:, 4:5], in_=stv.ap[:, 6:7],
                                                           func=AF.Exp, scale=-0.5), [t7])
            t9 = P.op("v", lambda e, stv=stv: e.tensor_tensor(stv.ap[:, 5:6], stv.ap[:, 2:3],
                                                              stv.ap[:, 4:5], AL.mult), [t8, t3])
            t9 = P.op("v", lambda e, stv=stv: e.tensor_scalar(stv.ap[:, 5:6], stv.ap[:, 5:6], -1.0, None,
                                                              AL.mult), [t9])
            stv.ww(t9)
            if dst_tm is not None:
                tn = P.op("s", lambda e, xt=xt, xo=xo, stv=stv: e.activation(
                    out=xo.ap[:], in_=xt.ap[:], func=AF.Identity, bias=stv.ap[:, 5:6],
                    scale=stv.ap[:, 4:5]), [t9] + xo.wdep())
                xo.ww(tn); xt.rr(tn); stv.rr(tn)
                last = P.dma("dG", dst_tm[it * 128:(it + 1) * 128, :], xo.ap[:], [tn], key=xo.key)
                xo.rr(last)
            if dst_fm is not None:
                tgt = xr if fm_dt == F32R else xb
                tn = P.op("s", lambda e, gg=tgt, xt=xt, stv=stv: e.activation(
                    out=gg.ap[:, :1024], in_=xt.ap[:], func=AF.Identity, bias=stv.ap[:, 5:6],
                    scale=stv.ap[:, 4:5]), [t9] + tgt.wdep())
                tgt.ww(tn); xt.rr(tn); stv.rr(tn)
                if gbi is not None:
                    o = gbi * 2 * DM
                    tn = P.op("v", lambda e, o=o, xb=xb: e.tensor_tensor(xb2.ap[:, :1024], xb.ap[:, :1024],
                                                                        self.gb.ap[:, o:o + DM], AL.mult),
                              [tn] + xb2.wdep() + self.gb.rdep())
                    xb2.ww(tn); xb.rr(tn); self.gb.rr(tn)
                    tn = P.op("v", lambda e, o=o, xb=xb: e.tensor_tensor(xb.ap[:, :1024], xb2.ap[:, :1024],
                                                                        self.gb.ap[:, o + DM:o + 2 * DM],
                                                                        AL.add), [tn] + xb.wdep())
                    xb.ww(tn); xb2.rr(tn); self.gb.rr(tn)
                    tgt = xb
                last = self.transpose_out(tgt, DM, dst_fm, it, fm_dt, evb if fm_dt == BF16 else evr, [tn])
        self.slots_done(bxt, xts)
        self.slots_done(bxo, xos)
        self.slots_done(bxb, xbs)
        self.slots_done(stv0, stvs)
        return last

    def transpose_out(self, srcbuf, D, dst_fm, it, dt, evbuf, deps, src_off=0, dst_row=0):
        P = self.P
        pst = self.ps["tr"] if dt == BF16 else self.ps["trR"]
        idn = self.ident if dt == BF16 else self.identR
        grp_sz = 8 if dt == BF16 else 4
        ts = None
        for grp in range((D // 128 + grp_sz - 1) // grp_sz):
            toks = []
            nb = min(grp_sz, D // 128 - grp * grp_sz)
            for b2 in range(nb):
                blk = grp * grp_sz + b2
                d = deps + idn.rdep() + (pst.wdep() if b2 == 0 else [toks[-1]])
                tt = P.op("t", lambda e, blk=blk, b2=b2: e.transpose(
                    pst.ap[:, b2 * 128:(b2 + 1) * 128],
                    srcbuf.ap[:, src_off + blk * 128:src_off + (blk + 1) * 128],
                    idn.ap[:]), d)
                toks.append(tt)
                srcbuf.rr(tt); idn.rr(tt)
            pst.ww(toks[-1])
            te = P.op("v", lambda e, nb=nb: e.tensor_copy(
                evbuf.ap[:, :nb * 128], pst.ap[:, :nb * 128]),
                [toks[-1]] + evbuf.wdep())
            evbuf.ww(te); pst.rr(te)
            r0 = dst_row + grp * grp_sz * 128
            dview = dst_fm[r0:r0 + nb * 128, :].rearrange(
                "(a p) t -> p a t", p=128)[:, :, it * 128:(it + 1) * 128]
            sview = evbuf.ap[:, :nb * 128].rearrange("p (a t) -> p a t", t=128)
            ts = P.dma("dG", dview, sview, [te], key=evbuf.key)
            evbuf.rr(ts)
        return ts

    def add_contrib_fm(self, contrib_fm_d, extra_tm_d=None, dep=None):
        dep = aslist(dep)
        P, T, DM = self.P, self.T, self.DM
        cb = self.pool("R32A")
        ct = self.pool("F32C")
        bxt, xts = self.slots("F32A", 1024, 2)
        et = self.pool("F32B")
        pst = self.ps["trR"]
        last = None
        for it in range(T // 128):
            xt = xts[it % 2]
            sview = contrib_fm_d.rearrange("(a p) t -> p a t", p=128)[:, :, it * 128:(it + 1) * 128]
            dview = cb.ap[:, :DM].rearrange("p (a t) -> p a t", t=128)
            tl = P.dma("dS", dview, sview, cb.wdep() + dep + [last], key=cb.key)
            cb.ww(tl)
            te = None
            for grp in range(2):
                toks = []
                for b2 in range(4):
                    blk = grp * 4 + b2
                    d = [tl] + self.identR.rdep() + (pst.wdep() if b2 == 0 else [toks[-1]])
                    tt = P.op("t", lambda e, blk=blk, b2=b2: e.transpose(
                        pst.ap[:, b2 * 128:(b2 + 1) * 128], cb.ap[:, blk * 128:(blk + 1) * 128],
                        self.identR.ap[:]), d)
                    toks.append(tt); cb.rr(tt); self.identR.rr(tt)
                pst.ww(toks[-1])
                te = P.op("v", lambda e, grp=grp: e.tensor_copy(ct.ap[:, grp * 512:(grp + 1) * 512],
                                                               pst.ap[:, :512]),
                          [toks[-1]] + (ct.wdep() if grp == 0 else []))
                ct.ww(te); pst.rr(te)
            tx = P.dma("dA", xt.ap[:], self.xres[it * 128:(it + 1) * 128, :],
                       xt.wdep() + dep + [self.t_x0], key=xt.key)
            xt.ww(tx)
            d2 = [te, tx]
            if extra_tm_d is not None:
                tex = P.dma("dA", et.ap[:, :DM], extra_tm_d[it * 128:(it + 1) * 128, :],
                            et.wdep() + dep, key=et.key)
                et.ww(tex)
                ta = P.op("v", lambda e, xt=xt: e.tensor_tensor(xt.ap[:], xt.ap[:], et.ap[:, 0:1024],
                                                               AL.add), [tex] + d2)
                xt.ww(ta); et.rr(ta)
                d2 = [ta]
            ta = P.op("v", lambda e, xt=xt: e.tensor_tensor(xt.ap[:], xt.ap[:], ct.ap[:, 0:1024],
                                                           AL.add), d2)
            xt.ww(ta); ct.rr(ta)
            last = P.dma("dG", self.xres[it * 128:(it + 1) * 128, :], xt.ap[:], [ta], key=xt.key)
            xt.rr(last)
        self.slots_done(bxt, xts)
        self.t_x0 = last
        return last

    def sub_l0ffn(self):
        P, T, DM = self.P, self.T, self.DM
        fence = P.fence()
        tln = self.ln(self.xres, dst_fm=self.xn_fmR, fm_dt=F32R, dep=fence)
        contrib = self.dram(self.uid("c0"), [DM, T], F32R)
        ev = self.pool("R32B")

        def evac(mt, ntile, psb, deps):
            te = P.op("s", lambda e, mt=mt: e.activation(out=ev.ap[:, :512], in_=psb.ap[:, :],
                                                         func=AF.Identity, scale=0.5,
                                                         bias=self.smalls_pp("fbias", mt)),
                      deps + ev.wdep() + self.smalls.rdep())
            ev.ww(te); self.smalls.rr(te); psb.rr(te)
            ts = P.dma("dG", contrib[mt * 128:(mt + 1) * 128, ntile * 512:(ntile + 1) * 512],
                       ev.ap[:, :512], [te], key=ev.key)
            ev.rr(ts)
            return ts

        self.mm_stream(self.M_d, DM, DM, self.xn_fmR, T, evac, rhs_dt=F32R,
                       rhs_dep=[tln], lhs_dep=[self.t_M])
        self.add_contrib_fm(contrib, dep=P.fence())

    def sub_mamba(self):
        P, T, DM, DI = self.P, self.T, self.DM, self.DI
        fence = P.fence()
        t1 = self.ln(self.xres, dst_tm=self.xnp_d, dep=fence)
        t2 = self.ln(self.xnp_d, dst_fm=self.xn_fm16, fm_dt=BF16, gbi=0, dep=[t1])
        ub = self.pool("F32A")
        acc = self.pool("F32C")
        u2b = self.pool("B16D")
        szb = self.pool("B16E")
        nDIt = DI // 128
        tz = P.op("v", lambda e: e.memset(self.tails.ap[:], 0.0), self.tails.wdep() + fence)
        self.tails.ww(tz)

        def evac_u(mt, ntile, psb, deps):
            te = P.op("s", lambda e: e.activation(out=ub.ap[:, 3:515], in_=psb.ap[:, :], func=AF.Identity),
                      deps + ub.wdep())
            ub.ww(te); psb.rr(te)
            tc = P.op("v", lambda e, mt=mt: e.tensor_copy(ub.ap[:, 0:3], self.tails.ap[:, 3 * mt:3 * mt + 3]),
                      [te] + self.tails.rdep())
            ub.ww(tc); self.tails.rr(tc)
            tt = P.op("v", lambda e, mt=mt: e.tensor_copy(self.tails.ap[:, 3 * mt:3 * mt + 3],
                                                          ub.ap[:, 512:515]), [tc] + self.tails.wdep())
            self.tails.ww(tt)
            t0 = P.op("v", lambda e, mt=mt: e.tensor_scalar(acc.ap[:, :512], ub.ap[:, 0:512],
                                                            self.smalls_pp("cw0", 4 * mt), None, AL.mult),
                      [tc] + acc.wdep() + self.smalls.rdep())
            acc.ww(t0); self.smalls.rr(t0)
            for k2 in (1, 2, 3):
                t0 = P.op("v", lambda e, mt=mt, k2=k2: e.scalar_tensor_tensor(
                    acc.ap[:, :512], ub.ap[:, k2:512 + k2], self.smalls_pp("cw0", 4 * mt + k2),
                    acc.ap[:, :512], AL.mult, AL.add), [t0])
                acc.ww(t0)
            ub.rr(t0)
            ts = P.op("s", lambda e, mt=mt: e.activation(out=u2b.ap[:, :512], in_=acc.ap[:, :512],
                                                         func=AF.Silu, bias=self.smalls_pp("cb0", mt)),
                      [t0] + u2b.wdep() + self.smalls.rdep())
            u2b.ww(ts); acc.rr(ts); self.smalls.rr(ts)
            td = P.dma("dG", self.u2_fm[mt * 128:(mt + 1) * 128, ntile * 512:(ntile + 1) * 512],
                       u2b.ap[:, :512], [ts], key=u2b.key)
            u2b.rr(td)
            return td

        def evac_z(mt, ntile, psb, deps):
            mz = mt - nDIt
            ts = P.op("s", lambda e: e.activation(out=szb.ap[:, :512], in_=psb.ap[:, :], func=AF.Silu),
                      deps + szb.wdep())
            szb.ww(ts); psb.rr(ts)
            td = P.dma("dG", self.sz_fm[mz * 128:(mz + 1) * 128, ntile * 512:(ntile + 1) * 512],
                       szb.ap[:, :512], [ts], key=szb.key)
            szb.rr(td)
            return td

        self.mm_stream(self.w["in_projT"][:, 0:DI], DM, DI, self.xn_fm16, T, evac_u, rhs_dep=[t2, tz])
        self.mm_stream(self.w["in_projT"][:, DI:2 * DI], DM, DI, self.xn_fm16, T, evac_z,
                       rhs_dep=[t2], mt_off=nDIt)
        f2 = P.fence()
        xdb = self.pool("B16D")
        xdf = self.pool("F32D")

        def evac_xproj(mt, ntile, psb, deps):
            te = P.op("v", lambda e: e.tensor_copy(xdb.ap[:96, 0:512], psb.ap[:96, :]), deps + xdb.wdep())
            xdb.ww(te); psb.rr(te)
            tf = P.op("v", lambda e: e.tensor_copy(xdf.ap[:32, 0:512], psb.ap[64:96, :]),
                      deps + [te] + xdf.wdep())
            xdf.ww(tf); psb.rr(tf)
            ta = P.dma("dG", self.dt_d[:, ntile * 512:(ntile + 1) * 512], xdb.ap[0:64, 0:512], [te], key=xdb.key)
            xdb.rr(ta)
            tb = P.dma("dG", self.BC_d[:, ntile * 512:(ntile + 1) * 512], xdf.ap[:32, 0:512], [tf], key=xdf.key)
            xdf.rr(tb)
            return tb

        self.mm_stream(self.w["x_projT"], DI, 96, self.u2_fm, T, evac_xproj, rhs_dep=f2,
                       lhs_dep=f2, mt_list=[0])
        f3 = P.fence()
        ab = self.pool("F32B")
        eb = self.pool("F32C")
        rb = self.pool("F32D")
        db = self.pool("F32E")
        dub = self.pool("B16D")
        u2l = self.pool("B16E")

        def evac_dt(mt, ntile, psb, deps):
            ta = P.op("s", lambda e, mt=mt: e.activation(out=ab.ap[:, :512], in_=psb.ap[:, :], func=AF.Abs,
                                                         bias=self.smalls_pp("dtb", mt)),
                      deps + ab.wdep() + self.smalls.rdep())
            ab.ww(ta); self.smalls.rr(ta); psb.rr(ta)
            tb = P.op("s", lambda e: e.activation(out=eb.ap[:, :512], in_=ab.ap[:, :512], func=AF.Exp,
                                                  scale=-1.0), [ta] + eb.wdep())
            eb.ww(tb); ab.rr(tb)
            tc = P.op("s", lambda e: e.activation(out=eb.ap[:, :512], in_=eb.ap[:, :512], func=AF.Ln,
                                                  bias=1.0), [tb])
            eb.ww(tc)
            tr = P.op("s", lambda e, mt=mt: e.activation(out=rb.ap[:, :512], in_=psb.ap[:, :], func=AF.Relu,
                                                         bias=self.smalls_pp("dtb", mt)),
                      deps + [tc] + rb.wdep() + self.smalls.rdep())
            rb.ww(tr); self.smalls.rr(tr); psb.rr(tr)
            td = P.op("v", lambda e: e.tensor_tensor(db.ap[:, :512], rb.ap[:, :512], eb.ap[:, :512],
                                                     AL.add), [tr, tc] + db.wdep())
            db.ww(td); rb.rr(td); eb.rr(td)
            tw = P.dma("dG", self.delta_fm[mt * 128:(mt + 1) * 128, ntile * 512:(ntile + 1) * 512],
                       db.ap[:, :512], [td], key=db.key)
            db.rr(tw)
            tu = P.dma("dA", u2l.ap[:, :512],
                       self.u2_fm[mt * 128:(mt + 1) * 128, ntile * 512:(ntile + 1) * 512], u2l.wdep(),
                       key=u2l.key)
            u2l.ww(tu)
            tm = P.op("v", lambda e: e.tensor_tensor(dub.ap[:, :512], db.ap[:, :512], u2l.ap[:, :512],
                                                     AL.mult), [td, tu] + dub.wdep())
            dub.ww(tm); db.rr(tm); u2l.rr(tm)
            t2w = P.dma("dG", self.du_fm[mt * 128:(mt + 1) * 128, ntile * 512:(ntile + 1) * 512],
                        dub.ap[:, :512], [tm], key=dub.key)
            dub.rr(t2w)
            return t2w

        self.mm_stream(self.w["dt_projT"], 64, DI, self.dt_d, T, evac_dt, rhs_dep=f3, lhs_dep=f3)
        self.scan_phase(P.fence())
        f4 = P.fence()
        contrib = self.dram(self.uid("cm"), [DM, T], F32R)
        ev4 = self.pool("R32B")

        def evac_out(mt, ntile, psb, deps):
            te = P.op("v", lambda e: e.tensor_copy(ev4.ap[:, :512], psb.ap[:, :]), deps + ev4.wdep())
            ev4.ww(te); psb.rr(te)
            ts = P.dma("dG", contrib[mt * 128:(mt + 1) * 128, ntile * 512:(ntile + 1) * 512],
                       ev4.ap[:, :512], [te], key=ev4.key)
            ev4.rr(ts)
            return ts

        self.mm_stream(self.w["out_projT"], DI, DM, self.yg_fm, T, evac_out, rhs_dep=f4, lhs_dep=f4)
        self.add_contrib_fm(contrib, extra_tm_d=self.xnp_d, dep=P.fence())

    def slots(self, tag, ncols, n):
        base = self.pool(tag)
        out = []
        for i in range(n):
            b = Buf(base.ap[:, i * ncols:(i + 1) * ncols])
            b.w = base.w
            b.r = list(base.r)
            out.append(b)
        return base, out

    def slots_done(self, base, sl):
        for b in sl:
            if b.w is not None:
                base.rr(b.w)
            for t in b.r:
                base.rr(t)

    def scan_phase(self, fence):
        P, T, DI = self.P, self.T, self.DI
        Tc = min(512, T)
        nCh = DI // 128
        Brep = self.pool("BIGB")
        Crep = self.pool("BIGC")
        bdlt, dlts = self.slots("F32A", Tc, 2)
        bdul, duls = self.slots("B16A", Tc, 2)
        bu2l, u2ls = self.slots("B16B", Tc, 2)
        bszl, szls = self.slots("B16C", Tc, 2)
        bdA, dAs = self.slots("F32B", Tc, 2)
        bbb, bbs = self.slots("B16D", Tc, 2)
        bhn, hns = self.slots("B16E", Tc, 2)
        bhc, hcs = self.slots("B16F", Tc, 2)
        yb = self.pool("F32C")
        ygb = self.pool("B16G")
        nb = (Tc + 511) // 512
        nTch = T // Tc

        def load_ct(tch, ct):
            s = ct % 2
            t1 = P.dma("dS", dlts[s].ap[:], self.delta_fm[ct * 128:(ct + 1) * 128, tch * Tc:(tch + 1) * Tc],
                       dlts[s].wdep() + fence, key=dlts[s].key)
            dlts[s].ww(t1)
            t2 = P.dma("dS", duls[s].ap[:], self.du_fm[ct * 128:(ct + 1) * 128, tch * Tc:(tch + 1) * Tc],
                       duls[s].wdep() + fence, key=duls[s].key)
            duls[s].ww(t2)
            t3 = P.dma("dA", u2ls[s].ap[:], self.u2_fm[ct * 128:(ct + 1) * 128, tch * Tc:(tch + 1) * Tc],
                       u2ls[s].wdep() + fence, key=u2ls[s].key)
            u2ls[s].ww(t3)
            t4 = P.dma("dA", szls[s].ap[:], self.sz_fm[ct * 128:(ct + 1) * 128, tch * Tc:(tch + 1) * Tc],
                       szls[s].wdep() + fence, key=szls[s].key)
            szls[s].ww(t4)

        nPH = max(1, 4096 // Tc)
        nPH = min(nPH, 16)
        for tch in range(nTch):
            if tch == 0:
                load_ct(0, 0)
            tB = Brep.w
            tC = Crep.w
            for ct in range(nCh):
                s = ct % 2
                dlt, dul, u2l, szl = dlts[s], duls[s], u2ls[s], szls[s]
                tl1, tl2, tl3, tl4 = dlt.w, dul.w, u2l.w, szl.w
                if ct + 1 < nCh:
                    load_ct(tch, ct + 1)
                elif tch + 1 < nTch:
                    load_ct(tch + 1, 0)
                banks = [self.ps["mm0"], self.ps["mm1"]][:nb]
                ytoks = []
                for n in range(16):
                    nh, nr = divmod(n, nPH)
                    if nr == 0 and (ct == 0 or nPH < 16):
                        sB = self.BC_d[nh * nPH:(nh + 1) * nPH,
                                       tch * Tc:(tch + 1) * Tc].partition_broadcast(128)
                        dB = Brep.ap[:, :nPH * Tc].rearrange("p (n t) -> p n t", t=Tc)
                        tB = P.dma("dG", dB, sB, Brep.wdep() + fence, key=Brep.key)
                        Brep.ww(tB)
                        sC = self.BC_d[16 + nh * nPH:16 + (nh + 1) * nPH,
                                       tch * Tc:(tch + 1) * Tc].partition_broadcast(128)
                        dC = Crep.ap[:, :nPH * Tc].rearrange("p (n t) -> p n t", t=Tc)
                        tC = P.dma("dG", dC, sC, Crep.wdep() + fence, key=Crep.key)
                        Crep.ww(tC)
                    sn = n % 2
                    dA, bb, hn, hc = dAs[sn], bbs[sn], hns[sn], hcs[sn]
                    ta = P.op("s", lambda e, ct=ct, n=n, dA=dA, dlt=dlt: e.activation(
                        out=dA.ap[:], in_=dlt.ap[:], func=AF.Exp,
                        scale=self.A_neg.ap[:, ct * 16 + n:ct * 16 + n + 1]),
                        [tl1] + dA.wdep() + self.A_neg.rdep())
                    dA.ww(ta); dlt.rr(ta); self.A_neg.rr(ta)
                    tb = P.op("v", lambda e, nr=nr, bb=bb, dul=dul: e.tensor_tensor(
                        bb.ap[:], dul.ap[:], Brep.ap[:, nr * Tc:(nr + 1) * Tc], AL.mult),
                        [tl2, tB] + bb.wdep())
                    bb.ww(tb); dul.rr(tb); Brep.rr(tb)
                    tsc = P.op("v", lambda e, ct=ct, n=n, hn=hn, dA=dA, bb=bb: e.tensor_tensor_scan(
                        hn.ap[:], dA.ap[:], bb.ap[:],
                        self.states.ap[:, ct * 16 + n:ct * 16 + n + 1], AL.mult, AL.add),
                        [ta, tb] + hn.wdep() + self.states.rdep() + self.states.wdep())
                    hn.ww(tsc); dA.rr(tsc); bb.rr(tsc); self.states.rr(tsc)
                    tst = P.op("v", lambda e, ct=ct, n=n, hn=hn: e.tensor_copy(
                        self.states.ap[:, ct * 16 + n:ct * 16 + n + 1], hn.ap[:, Tc - 1:Tc]), [tsc])
                    self.states.ww(tst); hn.rr(tst)
                    tm = P.op("v", lambda e, nr=nr, hc=hc, hn=hn: e.tensor_tensor(
                        hc.ap[:], hn.ap[:], Crep.ap[:, nr * Tc:(nr + 1) * Tc], AL.mult),
                        [tsc, tC] + hc.wdep())
                    hc.ww(tm); hn.rr(tm); Crep.rr(tm)
                    for sb2 in range(nb):
                        w2 = min(512, Tc - sb2 * 512)
                        psb = banks[sb2]
                        d = [tm] + self.ident.rdep() + (psb.wdep() if n == 0 else [ytoks[-1]])
                        ty = P.op("t", lambda e, sb2=sb2, psb=psb, n=n, w2=w2, hc=hc: e.matmul(
                            psb.ap[:, :w2], self.ident.ap[:], hc.ap[:, sb2 * 512:sb2 * 512 + w2],
                            start=(n == 0), stop=(n == 15)), d)
                        ytoks.append(ty)
                        if n == 0:
                            psb.ww(ty)
                        hc.rr(ty); self.ident.rr(ty)
                for sb2 in range(nb):
                    banks[sb2].ww(ytoks[-(nb - sb2)])
                t1s = []
                for sb2 in range(nb):
                    w2 = min(512, Tc - sb2 * 512)
                    psb = banks[sb2]
                    t1 = P.op("v", lambda e, ct=ct, sb2=sb2, psb=psb, w2=w2, u2l=u2l: e.scalar_tensor_tensor(
                        yb.ap[:, sb2 * 512:sb2 * 512 + w2], u2l.ap[:, sb2 * 512:sb2 * 512 + w2],
                        self.smalls_pp("dsk", ct), psb.ap[:, :w2], AL.mult, AL.add),
                        [tl3, ytoks[-(nb - sb2)]] + (yb.wdep() if sb2 == 0 else []) + self.smalls.rdep())
                    yb.ww(t1); u2l.rr(t1); psb.rr(t1); self.smalls.rr(t1)
                    t1s.append(t1)
                t2 = P.op("v", lambda e, szl=szl: e.tensor_tensor(ygb.ap[:, :Tc], yb.ap[:, :Tc],
                                                                 szl.ap[:, :Tc], AL.mult),
                          t1s + [tl4] + ygb.wdep())
                ygb.ww(t2); yb.rr(t2); szl.rr(t2)
                ts = P.dma("dG", self.yg_fm[ct * 128:(ct + 1) * 128, tch * Tc:(tch + 1) * Tc],
                           ygb.ap[:, :Tc], [t2], key=ygb.key)
                ygb.rr(ts)
        for base, sl in [(bdlt, dlts), (bdul, duls), (bu2l, u2ls), (bszl, szls),
                         (bdA, dAs), (bbb, bbs), (bhn, hns), (bhc, hcs)]:
            self.slots_done(base, sl)

    def sub_l1ffn(self):
        P, T, DM, DH = self.P, self.T, self.DM, self.DH
        fence = P.fence()
        t1 = self.ln(self.xres, dst_tm=self.xnp_d, dep=fence)
        tq = self.quant_tm(self.xnp_d, self.xq_fm, T, DM, dep=[t1])
        hs_fm = self.dram(self.uid("hs"), [DH, T], BF16)
        ub = self.pool("F32A")
        acc = self.pool("F32C")
        hsb = self.pool("B16D")
        nT = T // 512
        nDHt = DH // 128

        def evac_mm1(mt, ntile, psb, deps):
            te = P.op("s", lambda e: e.activation(out=ub.ap[:, 2:514], in_=psb.ap[:, :], func=AF.Identity,
                                                  scale=self.invws.ap[:, 0:1]),
                      deps + ub.wdep() + self.invws.rdep())
            ub.ww(te); self.invws.rr(te); psb.rr(te)
            tc = P.op("v", lambda e, mt=mt: e.tensor_copy(ub.ap[:, 0:2],
                                                          self.tails.ap[:, 2 * mt:2 * mt + 2]),
                      [te] + self.tails.rdep())
            ub.ww(tc); self.tails.rr(tc)
            tt = P.op("v", lambda e, mt=mt: e.tensor_copy(self.tails.ap[:, 2 * mt:2 * mt + 2],
                                                          ub.ap[:, 512:514]), [tc] + self.tails.wdep())
            self.tails.ww(tt)
            t0 = P.op("v", lambda e, mt=mt: e.tensor_scalar(acc.ap[:, :512], ub.ap[:, 0:512],
                                                            self.smalls_pp("cw1", 3 * mt), None, AL.mult),
                      [tc] + acc.wdep() + self.smalls.rdep())
            acc.ww(t0); self.smalls.rr(t0)
            for k2 in (1, 2):
                t0 = P.op("v", lambda e, mt=mt, k2=k2: e.scalar_tensor_tensor(
                    acc.ap[:, :512], ub.ap[:, k2:512 + k2], self.smalls_pp("cw1", 3 * mt + k2),
                    acc.ap[:, :512], AL.mult, AL.add), [t0])
                acc.ww(t0)
            ub.rr(t0)
            ts = P.op("s", lambda e, mt=mt: e.activation(out=hsb.ap[:, :512], in_=acc.ap[:, :512],
                                                         func=AF.Silu, bias=self.smalls_pp("cb1", mt)),
                      [t0] + hsb.wdep() + self.smalls.rdep())
            hsb.ww(ts); acc.rr(ts); self.smalls.rr(ts)
            lo = ntile * 512 - 1
            if lo < 0:
                td = P.dma("dG", hs_fm[mt * 128:(mt + 1) * 128, 0:511], hsb.ap[:, 1:512], [ts], key=hsb.key)
            else:
                td = P.dma("dG", hs_fm[mt * 128:(mt + 1) * 128, lo:lo + 512], hsb.ap[:, :512], [ts], key=hsb.key)
            hsb.rr(td)
            if ntile == nT - 1:
                tf = P.op("v", lambda e, mt=mt: e.tensor_scalar(acc.ap[:, 0:1],
                                                                self.tails.ap[:, 2 * mt:2 * mt + 1],
                                                                self.smalls_pp("cw1", 3 * mt), None,
                                                                AL.mult),
                          [tt, td] + acc.wdep() + self.smalls.rdep())
                acc.ww(tf); self.tails.rr(tf); self.smalls.rr(tf)
                tf = P.op("v", lambda e, mt=mt: e.scalar_tensor_tensor(
                    acc.ap[:, 0:1], self.tails.ap[:, 2 * mt + 1:2 * mt + 2],
                    self.smalls_pp("cw1", 3 * mt + 1), acc.ap[:, 0:1], AL.mult, AL.add), [tf])
                acc.ww(tf); self.tails.rr(tf)
                tf2 = P.op("s", lambda e, mt=mt: e.activation(out=hsb.ap[:, 0:1], in_=acc.ap[:, 0:1],
                                                              func=AF.Silu, bias=self.smalls_pp("cb1", mt)),
                           [tf] + hsb.wdep())
                hsb.ww(tf2); acc.rr(tf2)
                td = P.dma("dG", hs_fm[mt * 128:(mt + 1) * 128, T - 1:T], hsb.ap[:, 0:1], [tf2, td],
                           key=hsb.key, slow=True)
                hsb.rr(td)
            return td

        tz = P.op("v", lambda e: e.memset(self.tails.ap[:], 0.0), self.tails.wdep() + fence)
        self.tails.ww(tz)
        self.mm_stream(self.w1q_d[:, 0:DH // 2], DM, DH // 2, self.xq_fm, T, evac_mm1,
                       rhs_dep=[tq, tz], lhs_dep=[self.t_wq])
        self.mm_stream(self.w1q_d[:, DH // 2:DH], DM, DH // 2, self.xq_fm, T, evac_mm1,
                       rhs_dep=[tq, tz], lhs_dep=[self.t_wq], mt_off=nDHt // 2)
        tq2 = self.quant_fm_roundtrip(hs_fm, self.hq_fm, T, DH, dep=P.fence())
        contrib = self.dram(self.uid("c1"), [DM, T], F32R)
        part_d = self.dram(self.uid("p1"), [DM, T], F32)
        ev = self.pool("R32B")
        evf = self.pool("F32D")
        evp = self.pool("F32E")
        th = P.op("v", lambda e: e.tensor_scalar(self.sc2.ap[:, 0:1], self.invws.ap[:, 1:2], 0.5, None,
                                                 AL.mult), self.invws.rdep() + self.sc2.wdep())
        self.sc2.ww(th); self.invws.rr(th)

        def evac_mm2a(mt, ntile, psb, deps):
            te = P.op("v", lambda e: e.tensor_copy(evf.ap[:, :512], psb.ap[:, :]), deps + evf.wdep())
            evf.ww(te); psb.rr(te)
            ts = P.dma("dG", part_d[mt * 128:(mt + 1) * 128, ntile * 512:(ntile + 1) * 512],
                       evf.ap[:, :512], [te], key=evf.key)
            evf.rr(ts)
            return ts

        def evac_mm2b(mt, ntile, psb, deps):
            tl = P.dma("dS", evp.ap[:, :512],
                       part_d[mt * 128:(mt + 1) * 128, ntile * 512:(ntile + 1) * 512],
                       evp.wdep(), key=evp.key)
            evp.ww(tl)
            ta = P.op("v", lambda e: e.tensor_tensor(evp.ap[:, :512], evp.ap[:, :512], psb.ap[:, :],
                                                     AL.add), deps + [tl])
            evp.ww(ta); psb.rr(ta)
            te = P.op("s", lambda e: e.activation(out=ev.ap[:, :512], in_=evp.ap[:, :512],
                                                  func=AF.Identity, scale=self.sc2.ap[:, 0:1]),
                      [ta, th] + ev.wdep())
            ev.ww(te); self.sc2.rr(te); evp.rr(te)
            ts = P.dma("dG", contrib[mt * 128:(mt + 1) * 128, ntile * 512:(ntile + 1) * 512],
                       ev.ap[:, :512], [te], key=ev.key)
            ev.rr(ts)
            return ts

        self.mm_stream(self.w2q_d[0:DH // 2, :], DH // 2, DM, self.hq_fm[0:DH // 2, :], T,
                       evac_mm2a, rhs_dep=[tq2], lhs_dep=[tq2])
        f6 = P.fence()
        self.mm_stream(self.w2q_d[DH // 2:DH, :], DH // 2, DM, self.hq_fm[DH // 2:DH, :], T,
                       evac_mm2b, rhs_dep=f6, lhs_dep=f6)
        self.add_contrib_fm(contrib, dep=P.fence())

    def quant_tm(self, src_tm_d, dst_fm_d, T, D, dep=None):
        dep = aslist(dep)
        P = self.P
        xt = self.pool("F32A")
        g = self.pool("F32B")
        qb = self.pool("F32C")
        qi = self.pool("I32A")
        xq = self.pool("B16B")
        ev = self.pool("B16C")
        mx = self.mx
        last = None
        for it in range(T // 128):
            tl = P.dma("dS", xt.ap[:, :D], src_tm_d[it * 128:(it + 1) * 128, :], xt.wdep() + dep + [last],
                       key=xt.key)
            xt.ww(tl)
            tm = P.op("v", lambda e, D=D: e.tensor_reduce(mx.ap[:, 0:1], xt.ap[:, :D], mybir.AxisListType.X,
                                                          AL.max, apply_absolute_value=True),
                      [tl] + mx.wdep())
            tm = P.op("v", lambda e: e.tensor_scalar(mx.ap[:, 1:2], mx.ap[:, 0:1], 1e-5, None, AL.max), [tm])
            tm = P.op("v", lambda e: e.reciprocal(mx.ap[:, 2:3], mx.ap[:, 1:2]), [tm])
            tm = P.op("v", lambda e: e.tensor_scalar(mx.ap[:, 2:3], mx.ap[:, 2:3], 127.0, None, AL.mult),
                      [tm])
            tm = P.op("v", lambda e: e.tensor_scalar(mx.ap[:, 3:4], mx.ap[:, 1:2], 1.0 / 127.0, None,
                                                     AL.mult), [tm])
            mx.ww(tm)
            tqq = self._quant_block(xt, 0, D, mx, g, qb, qi, xq, [tm, tl])
            last = self.transpose_out(xq, D, dst_fm_d, it, BF16, ev, [tqq])
        return last

    def _quant_block(self, srcbuf, off, width, mx, g, qb, qi, xq, deps, out_off0=False):
        P = self.P
        HB = 2048
        tq = None
        for o in range(off, off + width, HB):
            wd = min(HB, off + width - o)
            xo = 0 if out_off0 else o
            t1 = P.op("v", lambda e, o=o, wd=wd: e.tensor_scalar(qi.ap[:, :wd], srcbuf.ap[:, o:o + wd],
                                                                 mx.ap[:, 2:3], 128.0, AL.mult, AL.add),
                      deps + qi.wdep() + ([tq] if tq else []))
            qi.ww(t1); srcbuf.rr(t1); mx.rr(t1)
            t2 = P.op("v", lambda e, wd=wd: e.tensor_copy(qb.ap[:, :wd], qi.ap[:, :wd]), [t1] + qb.wdep())
            qb.ww(t2); qi.rr(t2)
            t3 = P.op("v", lambda e, o=o, wd=wd: e.scalar_tensor_tensor(
                g.ap[:, :wd], srcbuf.ap[:, o:o + wd], mx.ap[:, 2:3], qb.ap[:, :wd], AL.mult, AL.subtract),
                [t2] + deps + g.wdep())
            g.ww(t3); srcbuf.rr(t3); qb.rr(t3); mx.rr(t3)
            t4 = P.op("v", lambda e, wd=wd: e.scalar_tensor_tensor(qb.ap[:, :wd], g.ap[:, :wd], -127.5,
                                                                  qb.ap[:, :wd], AL.is_ge, AL.add), [t3])
            qb.ww(t4); g.rr(t4)
            tq = P.op("v", lambda e, xo=xo, wd=wd: e.tensor_scalar(xq.ap[:, xo:xo + wd], qb.ap[:, :wd],
                                                                   -128.0, mx.ap[:, 3:4], AL.add, AL.mult),
                      [t4] + xq.wdep())
            xq.ww(tq); qb.rr(tq); mx.rr(tq)
        return tq

    def quant_fm_roundtrip(self, src_fm_d, dst_fm_d, T, D, dep=None):
        dep = aslist(dep)
        P = self.P
        tmv = self.pool("F32A")
        g = self.pool("F32B")
        qb = self.pool("F32C")
        qi = self.pool("I32A")
        cb = self.pool("B16A")
        xq = self.pool("B16B")
        ev = self.pool("B16C")
        mx = self.mx
        pst = self.ps["tr"]
        last = None
        for it in range(T // 128):
            te = None
            for half in range(D // 2048):
                sview = src_fm_d.rearrange("(a p) t -> p a t", p=128)[:, half * 16:(half + 1) * 16,
                                                                       it * 128:(it + 1) * 128]
                dview = cb.ap[:, :2048].rearrange("p (a t) -> p a t", t=128)
                tl = P.dma("dS", dview, sview, cb.wdep() + dep + [last, te], key=cb.key)
                cb.ww(tl)
                for grp in range(2):
                    toks = []
                    for b2 in range(8):
                        blk = grp * 8 + b2
                        d = [tl] + self.ident.rdep() + (pst.wdep() if b2 == 0 else [toks[-1]])
                        tt = P.op("t", lambda e, blk=blk, b2=b2: e.transpose(
                            pst.ap[:, b2 * 128:(b2 + 1) * 128], cb.ap[:, blk * 128:(blk + 1) * 128],
                            self.ident.ap[:]), d)
                        toks.append(tt); cb.rr(tt); self.ident.rr(tt)
                    pst.ww(toks[-1])
                    te = P.op("v", lambda e, half=half, grp=grp: e.tensor_copy(
                        tmv.ap[:, half * 2048 + grp * 1024:half * 2048 + (grp + 1) * 1024], pst.ap[:]),
                        [toks[-1]] + (tmv.wdep() if half == 0 and grp == 0 else []))
                    tmv.ww(te); pst.rr(te)
            tm = P.op("v", lambda e, D=D: e.tensor_reduce(mx.ap[:, 0:1], tmv.ap[:, :D],
                                                          mybir.AxisListType.X, AL.max,
                                                          apply_absolute_value=True), [te] + mx.wdep())
            tm = P.op("v", lambda e: e.tensor_scalar(mx.ap[:, 1:2], mx.ap[:, 0:1], 1e-5, None, AL.max), [tm])
            tm = P.op("v", lambda e: e.reciprocal(mx.ap[:, 2:3], mx.ap[:, 1:2]), [tm])
            tm = P.op("v", lambda e: e.tensor_scalar(mx.ap[:, 2:3], mx.ap[:, 2:3], 127.0, None, AL.mult),
                      [tm])
            tm = P.op("v", lambda e: e.tensor_scalar(mx.ap[:, 3:4], mx.ap[:, 1:2], 1.0 / 127.0, None,
                                                     AL.mult), [tm])
            mx.ww(tm)
            for half in range(D // 2048):
                tqq = self._quant_block(tmv, half * 2048, 2048, mx, g, qb, qi, xq, [tm], out_off0=True)
                last = self.transpose_out(xq, 2048, dst_fm_d, it, BF16, ev, [tqq],
                                          dst_row=half * 2048)
        return last

    def sub_attn(self):
        P, T, DM = self.P, self.T, self.DM
        fence = P.fence()
        t1 = self.ln(self.xres, dst_tm=self.xnp_d, dep=fence)
        t2 = self.ln(self.xnp_d, dst_fm=self.xn_fm16, fm_dt=BF16, gbi=1, dep=[t1])
        evq = self.pool("B16D")
        vtb = self.pool("B16E")
        pst = self.ps["tr"]

        def evac_qk(mt, ntile, psb, deps):
            te = P.op("v", lambda e: e.tensor_copy(evq.ap[:, :512], psb.ap[:, :]), deps + evq.wdep())
            evq.ww(te); psb.rr(te)
            ts = P.dma("dG", self.qk_fm[mt * 128:(mt + 1) * 128, ntile * 512:(ntile + 1) * 512],
                       evq.ap[:, :512], [te], key=evq.key)
            evq.rr(ts)
            return ts

        def evac_v(mt, ntile, psb, deps):
            te = P.op("v", lambda e: e.tensor_copy(vtb.ap[:, :512], psb.ap[:, :]), deps + vtb.wdep())
            vtb.ww(te); psb.rr(te)
            toks = []
            for b2 in range(4):
                d = [te] + self.ident.rdep() + (pst.wdep() if b2 == 0 else [toks[-1]])
                tt = P.op("t", lambda e, b2=b2: e.transpose(pst.ap[:, b2 * 128:(b2 + 1) * 128],
                                                            vtb.ap[:, b2 * 128:(b2 + 1) * 128],
                                                            self.ident.ap[:]), d)
                toks.append(tt); vtb.rr(tt); self.ident.rr(tt)
            pst.ww(toks[-1])
            te2 = P.op("v", lambda e: e.tensor_copy(evq.ap[:, :512], pst.ap[:, 0:512]),
                       [toks[-1]] + evq.wdep())
            evq.ww(te2); pst.rr(te2)
            dview = self.v_tm[ntile * 512:(ntile + 1) * 512, mt * 128:(mt + 1) * 128].rearrange(
                "(a p) c -> p a c", p=128)
            sview = evq.ap[:, :512].rearrange("p (a c) -> p a c", c=128)
            ts = P.dma("dG", dview, sview, [te2], key=evq.key)
            evq.rr(ts)
            return ts

        self.mm_stream(self.w["w_qkv"][:, 0:2 * DM], DM, 2 * DM, self.xn_fm16, T, evac_qk, rhs_dep=[t2])
        self.mm_stream(self.w["w_qkv"][:, 2 * DM:3 * DM], DM, DM, self.xn_fm16, T, evac_v, rhs_dep=[t2])
        fatt = P.fence()
        qsb = self.pool("B16A")
        ksb = self.pool("B16B")
        vsb = self.pool("B16C")
        ea = self.pool("B16D")
        eaT = self.pool("B16E")
        aost = self.pool("B16F")
        sms = self.sms
        W = 128
        nw = T // W
        pssc, psav = self.ps["sc"], self.ps["av"]
        last = None
        for wdx in range(nw):
            qv = self.qk_fm.rearrange("(a p) t -> p a t", p=128)[:, 0:8, wdx * W:(wdx + 1) * W]
            tq = P.dma("dS", qsb.ap[:, :8 * W].rearrange("p (a t) -> p a t", t=W), qv,
                       qsb.wdep() + fatt + [last], key=qsb.key)
            qsb.ww(tq)
            lo = max(wdx - 1, 0) * W
            kv = self.qk_fm.rearrange("(a p) t -> p a t", p=128)[:, 8:16, lo:lo + 2 * W]
            tk = P.dma("dS", ksb.ap[:, :8 * 2 * W].rearrange("p (a t) -> p a t", t=2 * W), kv,
                       ksb.wdep() + fatt, key=ksb.key)
            ksb.ww(tk)
            vv = self.v_tm[lo:lo + 2 * W, :].rearrange("(a p) c -> p a c", p=128)
            tv = P.dma("dA", vsb.ap[:, :2 * DM].rearrange("p (a c) -> p a c", c=DM), vv,
                       vsb.wdep() + fatt, key=vsb.key)
            vsb.ww(tv)
            tae = None
            for h in range(16):
                pb = 64 * (h % 2)
                tile = h // 2
                d0 = [tq, tk] + pssc.wdep()
                tmm = P.op("t", lambda e, pb=pb, tile=tile: e.matmul(
                    pssc.ap[:, 0:256], qsb.ap[pb:pb + 64, tile * W:(tile + 1) * W],
                    ksb.ap[pb:pb + 64, tile * 2 * W:(tile + 1) * 2 * W], start=True, stop=False), d0)
                qsb.rr(tmm); ksb.rr(tmm)
                moff = 0 if wdx > 0 else 256
                tmm = P.op("t", lambda e, moff=moff: e.matmul(pssc.ap[:, 0:256], self.ident.ap[:],
                                                              self.maskb.ap[:, moff:moff + 256],
                                                              start=False, stop=True),
                           [tmm] + self.maskb.rdep() + self.ident.rdep())
                pssc.ww(tmm); self.maskb.rr(tmm); self.ident.rr(tmm)
                tr = P.op("v", lambda e: e.tensor_reduce(sms.ap[:, 0:1], pssc.ap[:, 0:256],
                                                         mybir.AxisListType.X, AL.max),
                          [tmm] + sms.wdep())
                tr = P.op("v", lambda e: e.tensor_scalar(sms.ap[:, 1:2], sms.ap[:, 0:1], -0.125, None,
                                                         AL.mult), [tr])
                sms.ww(tr)
                texp = P.op("s", lambda e: e.activation(out=ea.ap[:, :256], in_=pssc.ap[:, 0:256],
                                                        func=AF.Exp, scale=0.125, bias=sms.ap[:, 1:2],
                                                        accum_out=sms.ap[:, 2:3]), [tr] + ea.wdep())
                ea.ww(texp); pssc.rr(texp); sms.ww(texp)
                trc = P.op("v", lambda e: e.reciprocal(sms.ap[:, 3:4], sms.ap[:, 2:3]), [texp])
                sms.ww(trc)
                tnm = P.op("v", lambda e: e.tensor_scalar(ea.ap[:, :256], ea.ap[:, :256], sms.ap[:, 3:4],
                                                          None, AL.mult), [trc])
                ea.ww(tnm); sms.rr(tnm)
                toks = []
                for b2 in range(2):
                    d = [tnm] + self.ident.rdep() + (pst.wdep() if b2 == 0 else [toks[-1]])
                    tt = P.op("t", lambda e, b2=b2: e.transpose(pst.ap[:, b2 * 128:(b2 + 1) * 128],
                                                                ea.ap[:, b2 * 128:(b2 + 1) * 128],
                                                                self.ident.ap[:]), d)
                    toks.append(tt); ea.rr(tt); self.ident.rr(tt)
                pst.ww(toks[-1])
                tev = P.op("v", lambda e: e.tensor_copy(eaT.ap[:, :256], pst.ap[:, 0:256]),
                           [toks[-1]] + eaT.wdep())
                eaT.ww(tev); pst.rr(tev)
                tavm = None
                for k2 in range(2):
                    d = [tev, tv] + (psav.wdep() if k2 == 0 else [tavm])
                    tavm = P.op("t", lambda e, k2=k2, h=h: e.matmul(
                        psav.ap[:64, 0:128], vsb.ap[:, k2 * DM + h * 64:k2 * DM + (h + 1) * 64],
                        eaT.ap[:, k2 * 128:(k2 + 1) * 128], start=(k2 == 0), stop=(k2 == 1)), d)
                    eaT.rr(tavm); vsb.rr(tavm)
                psav.ww(tavm)
                tae = P.op("v", lambda e, pb=pb, tile=tile: e.tensor_copy(
                    aost.ap[pb:pb + 64, tile * W:(tile + 1) * W], psav.ap[:64, 0:128]),
                    [tavm] + (aost.wdep() if h == 0 else [tae]))
                aost.ww(tae); psav.rr(tae)
            dview = self.ao_fm.rearrange("(a p) t -> p a t", p=128)[:, :, wdx * W:(wdx + 1) * W]
            sview = aost.ap[:, :8 * W].rearrange("p (a t) -> p a t", t=W)
            last = P.dma("dG", dview, sview, [tae], key=aost.key)
            aost.rr(last)
        fout = P.fence()
        contrib = self.dram(self.uid("ca"), [DM, T], F32R)
        ev2 = self.pool("R32B")

        def evac_out(mt, ntile, psb, deps):
            te = P.op("v", lambda e: e.tensor_copy(ev2.ap[:, :512], psb.ap[:, :]), deps + ev2.wdep())
            ev2.ww(te); psb.rr(te)
            ts = P.dma("dG", contrib[mt * 128:(mt + 1) * 128, ntile * 512:(ntile + 1) * 512],
                       ev2.ap[:, :512], [te], key=ev2.key)
            ev2.rr(ts)
            return ts

        self.mm_stream(self.w["w_out"], DM, DM, self.ao_fm, T, evac_out, rhs_dep=fout)
        self.add_contrib_fm(contrib, dep=P.fence())

    def final_out(self):
        self.P.dma("dS", self.out[:, :], self.xres[:, :], [self.t_x0])


def make_inputs(ins, x_seq, T):
    DM, DI, DH = 1024, 2048, 4096
    f32 = np.float32
    import ml_dtypes
    bf16 = ml_dtypes.bfloat16
    g = lambda k: np.asarray(ins[k], dtype=f32)
    d = {}
    d["x_in"] = np.ascontiguousarray(x_seq, dtype=f32)
    d["in_projT"] = np.ascontiguousarray(g("L0_attn_in_proj_w").T).astype(bf16)
    d["x_projT"] = np.ascontiguousarray(g("L0_attn_x_proj_w").T).astype(bf16)
    d["dt_projT"] = np.ascontiguousarray(g("L0_attn_dt_proj_w").T).astype(bf16)
    d["out_projT"] = np.ascontiguousarray(g("L0_attn_out_proj_w").T).astype(bf16)
    d["w_qkv"] = np.ascontiguousarray(g("L1_attn_w_qkv")).astype(bf16)
    d["w_out"] = np.ascontiguousarray(g("L1_attn_w_out")).astype(bf16)
    d["w1T"] = np.ascontiguousarray(g("L1_ffn_w1").T, dtype=f32)
    d["w2T"] = np.ascontiguousarray(g("L1_ffn_w2").T, dtype=f32)
    d["wfr"] = np.ascontiguousarray(g("L0_ffn_w_freq_real"), dtype=f32)
    d["wfi"] = np.ascontiguousarray(g("L0_ffn_w_freq_imag"), dtype=f32)
    k = np.arange(DM)
    ang = 2.0 * np.pi * ((k[:, None] * k[None, :]) % DM) / DM
    d["C_d"] = np.cos(ang).astype(f32)
    d["S_d"] = np.sin(ang).astype(f32)
    gb = np.zeros((128, 4 * DM), f32)
    gb[:, 0:DM] = g("L0_attn_ln_gamma")[None, :]
    gb[:, DM:2 * DM] = g("L0_attn_ln_beta")[None, :]
    gb[:, 2 * DM:3 * DM] = g("L1_attn_ln_gamma")[None, :]
    gb[:, 3 * DM:4 * DM] = g("L1_attn_ln_beta")[None, :]
    d["gb"] = gb.astype(bf16)
    d["ident"] = np.eye(128, dtype=bf16)
    qi = np.arange(128)
    maskp = qi[:, None] > qi[None, :]
    masko = qi[:, None] < qi[None, :]
    mb = np.zeros((128, 512), f32)
    mb[:, 0:256][np.concatenate([maskp, masko], axis=1)] = -1e38
    mb[:, 256:512][np.concatenate([masko, np.ones((128, 128), bool)], axis=1)] = -1e38
    d["maskb"] = mb.astype(bf16)
    sm = np.zeros((128, 504), f32)
    cw = g("L0_attn_conv_w")[:, 0, :].reshape(DI // 128, 128, 4)
    for c in range(DI // 128):
        sm[:, 4 * c:4 * c + 4] = cw[c]
    sm[:, 64:80] = g("L0_attn_conv_b").reshape(DI // 128, 128).T
    sm[:, 80:96] = g("L0_attn_D_skip").reshape(DI // 128, 128).T
    sm[:, 96:112] = g("L0_attn_dt_proj_b").reshape(DI // 128, 128).T
    al = g("L0_attn_A_log").reshape(DI // 128, 128, 16)
    for c in range(DI // 128):
        sm[:, 112 + 16 * c:112 + 16 * (c + 1)] = al[c]
    cw1 = g("L1_ffn_conv_w")[:, 0, :].reshape(DH // 128, 128, 3)
    for c in range(DH // 128):
        sm[:, 368 + 3 * c:368 + 3 * c + 3] = cw1[c]
    sm[:, 464:496] = g("L1_ffn_conv_b").reshape(DH // 128, 128).T
    sm[:, 496:504] = 0.5 * g("L0_ffn_bias").reshape(DM // 128, 128).T
    d["smalls"] = sm
    return d


def kernel(**inputs):
    import numpy as np
    from concourse.bass_utils import run_bass_kernel_spmd
    T = 8192
    x = np.asarray(inputs["L0_attn_x"], dtype=np.float32)
    B = x.shape[0]
    mk = MK(T)
    nc = mk.build()
    base = make_inputs(inputs, x[0], T)
    in_maps = []
    for core in range(8):
        m = dict(base)
        m["x_in"] = np.ascontiguousarray(x[core % B], dtype=np.float32)
        in_maps.append(m)
    res = run_bass_kernel_spmd(nc, in_maps, core_ids=list(range(8)))
    out = np.stack([np.asarray(res.results[b]["out"]) for b in range(B)], axis=0)
    return out.astype(np.float32)


# version 06
# speedup vs baseline: 1.2223x; 1.2223x over previous
import numpy as np
from contextlib import ExitStack
import concourse.bass as bass
import concourse.mybir as mybir

F32 = mybir.dt.float32
F32R = mybir.dt.float32r
BF16 = mybir.dt.bfloat16
I32 = mybir.dt.int32
AL = mybir.AluOpType
AF = mybir.ActivationFunctionType


class Prog:
    ENG = {"v": "vector", "s": "scalar", "t": "tensor", "g": "gpsimd", "y": "sync"}

    def __init__(self, nc, ctx):
        self.nc = nc
        self.ctx = ctx
        self.sems = {}
        self.items = {e: [] for e in self.ENG}
        self.cnt = {}
        for q in ["v", "s", "t", "gc"]:
            self._newsem(q)

    def _newsem(self, name):
        self.sems[name] = self.ctx.enter_context(self.nc.semaphore("sem_" + name))
        self.cnt[name] = 0

    def _emit(self, eng, sem, step, fn, deps):
        deps = [d for d in (deps or []) if d is not None]
        self.cnt[sem] += step
        tok = (sem, self.cnt[sem])
        self.items[eng].append((fn, tuple(deps), sem, step))
        return tok

    def op(self, eng, fn, deps=None):
        assert eng in ("v", "s", "t", "g")
        return self._emit(eng, "gc" if eng == "g" else eng, 1, fn, deps)

    def dma(self, q, out, in_, deps=None, key=None, slow=False):
        eng = {"dS": "y", "dA": "s", "dG": "g"}[q]
        sem = f"{q}_{key if key is not None else 'misc'}"
        if sem not in self.sems:
            self._newsem(sem)
        if slow:
            return self._emit(eng, sem, 16,
                              lambda e: e.dma_start(out=out, in_=in_, allow_slow_non_contiguous=True),
                              deps)
        return self._emit(eng, sem, 16, lambda e: e.dma_start(out=out, in_=in_), deps)

    def fence(self):
        return [(s, c) for s, c in self.cnt.items() if c > 0]

    def build(self, block):
        sems = self.sems
        for eng, engname in self.ENG.items():
            items = self.items[eng]
            if not items and engname != "sync":
                continue

            def make(items, eng):
                def body(e):
                    water = {}
                    dedup = True
                    for fn, deps, sem, step in items:
                        for (ds, dv) in deps:
                            if dedup and water.get(ds, 0) >= dv:
                                continue
                            e.wait_ge(sems[ds], dv)
                            water[ds] = dv
                        fn(e).then_inc(sems[sem], step)
                    if eng == "y":
                        for s2, c in self.cnt.items():
                            if c > 0 and water.get(s2, 0) < c:
                                e.wait_ge(sems[s2], c)
                return body
            getattr(block, engname)(make(items, eng))


_BUFN = [0]


class Buf:
    __slots__ = ("ap", "w", "r", "key", "sticky", "sr")

    def __init__(self, ap, sticky=False):
        self.ap = ap
        self.w = None
        self.r = []
        self.sticky = sticky
        self.sr = {}
        _BUFN[0] += 1
        self.key = _BUFN[0]

    def wdep(self):
        base = self.r + ([self.w] if self.w else [])
        if self.sticky:
            base = base + [(s, v) for s, v in self.sr.items()]
        return base

    def rdep(self):
        return [self.w] if self.w else []

    def ww(self, tok):
        if self.sticky:
            for (s, v) in self.r:
                if self.sr.get(s, -1) < v:
                    self.sr[s] = v
            if self.w is not None:
                s, v = self.w
                if self.sr.get(s, -1) < v:
                    self.sr[s] = v
        self.w = tok
        self.r = []

    def rr(self, tok):
        self.r.append(tok)
        if len(self.r) > 24:
            self.r = self.r[-24:]


def aslist(d):
    return d if isinstance(d, list) else ([d] if d else [])


class MK:
    NS = 16
    DT = 64

    def __init__(self, T, enabled=(1, 2, 3, 4, 5, 6)):
        self.T = T
        self.DM = 1024
        self.DI = 2048
        self.DH = 4096
        self.enabled = tuple(enabled)
        self.nc = bass.Bass("TRN2", target_bir_lowering=False)
        self.ctx = ExitStack()
        self._uid = 0

    def uid(self, p="u"):
        self._uid += 1
        return f"{p}_{self._uid}"

    def dram(self, name, shape, dt=F32, kind=None):
        t = self.nc.dram_tensor(name, shape, dt, kind=kind) if kind else self.nc.dram_tensor(name, shape, dt)
        return t.ap()

    def sbuf(self, name, shape, dt=F32):
        return Buf(self.ctx.enter_context(self.nc.sbuf_tensor(name, shape, dt)))

    def pool(self, tag):
        return self._pool[tag]

    def tfm(self, name, C, T, tw, dt=F32):
        return self.dram(name, [C // 128, T // tw, 128, tw], dt)

    @staticmethod
    def tfm_tile(ap4, c, n):
        return ap4[c, n]

    @staticmethod
    def tfm_rhs(ap4, k0, k1, ntile):
        return ap4[k0:k1, ntile].rearrange("k p t -> p k t")

    def psumb(self, name, shape, dt=F32):
        return Buf(self.ctx.enter_context(self.nc.psum_tensor(name, shape, dt)), sticky=True)

    def build(self):
        nc, T, DM, DI, DH = self.nc, self.T, self.DM, self.DI, self.DH
        ein = lambda n, s, dt=F32: self.dram(n, s, dt, kind="ExternalInput")
        self.x_in = ein("x_in", [T, DM])
        self.out = self.dram("out", [T, DM], F32, kind="ExternalOutput")
        w = {}
        w["in_projT"] = ein("in_projT", [DM, 2 * DI], BF16)
        w["x_projT"] = ein("x_projT", [DI, 96], BF16)
        w["dt_projT"] = ein("dt_projT", [self.DT, DI], BF16)
        w["out_projT"] = ein("out_projT", [DI, DM], BF16)
        w["w_qkv"] = ein("w_qkv", [DM, 3 * DM], BF16)
        w["w_out"] = ein("w_out", [DM, DM], BF16)
        w["w1T"] = ein("w1T", [DM, DH])
        w["w2T"] = ein("w2T", [DH, DM])
        w["wfr"] = ein("wfr", [DM, DM], F32R)
        w["wfi"] = ein("wfi", [DM, DM], F32R)
        w["C_d"] = ein("C_d", [DM, DM], F32R)
        w["S_d"] = ein("S_d", [DM, DM], F32R)
        w["gb"] = ein("gb", [128, 4 * DM], BF16)
        w["ident"] = ein("ident", [128, 128], BF16)
        w["maskb"] = ein("maskb", [128, 512], BF16)
        w["smalls"] = ein("smalls", [128, 504])
        self.w = w
        self.xres = self.dram("xres", [T, DM])
        self.xnp_d = self.dram("xnp_d", [T, DM])
        self.xn_fm16 = self.dram("xn_fm16", [DM, T], BF16)
        self.xn_fmR = self.dram("xn_fmR", [DM, T], F32R)
        self.u2_fm = self.dram("u2_fm", [DI, T], BF16)
        self.sz_fm = self.dram("sz_fm", [DI, T], BF16)
        self.du_fm = self.dram("du_fm", [DI, T], BF16)
        self.delta_fm = self.dram("delta_fm", [DI, T])
        self.dt_d = self.dram("dt_d", [self.DT, T], BF16)
        self.BC_d = self.dram("BC_d", [32, T])
        self.yg_fm = self.dram("yg_fm", [DI, T], BF16)
        self.w1q_d = self.dram("w1q_d", [DM, DH], BF16)
        self.w2q_d = self.dram("w2q_d", [DH, DM], BF16)
        self.wsc_d = self.dram("wsc_d", [1, 2])
        self.xq_fm = self.dram("xq_fm", [DM, T], BF16)
        self.hq_fm = self.dram("hq_fm", [DH, T], BF16)
        self.qk_fm = self.dram("qk_fm", [2 * DM, T], BF16)
        self.v_tm = self.dram("v_tm", [T, DM], BF16)
        self.ao_fm = self.dram("ao_fm", [DM, T], BF16)
        self.Q1_d = self.dram("Q1_d", [DM, DM], F32R)
        self.Q2_d = self.dram("Q2_d", [DM, DM], F32R)
        self.M_d = self.dram("M_d", [DM, DM], F32R)

        self.ident = self.sbuf("sb_ident", [128, 128], BF16)
        self.identR = self.sbuf("sb_identR", [128, 128], F32R)
        self.maskb = self.sbuf("sb_maskb", [128, 512], BF16)
        self.gb = self.sbuf("sb_gb", [128, 4 * DM], BF16)
        self.smalls = self.sbuf("sb_smalls", [128, 504], F32)
        self.A_neg = self.sbuf("sb_Aneg", [128, 256], F32)
        self.invws = self.sbuf("sb_invws", [128, 4], F32)
        self.states = self.sbuf("sb_states", [128, 256], F32)
        self.tails = self.sbuf("sb_tails", [128, 128], F32)
        self.stv = self.sbuf("sb_stv", [128, 16], F32)
        self.mx = self.sbuf("sb_mx", [128, 4], F32)
        self.sms = self.sbuf("sb_sms", [128, 4], F32)
        self.sc2 = self.sbuf("sb_sc2", [128, 2], F32)
        self.ones1 = self.sbuf("sb_ones1", [1, 128], BF16)
        self.wres = self.sbuf("sb_wres", [128, 16384], BF16)
        self.wresR = self.sbuf("sb_wresR", [128, 8192], F32R)
        self.rhs16 = [self.sbuf("sb_rhsA", [128, 4096], BF16), self.sbuf("sb_rhsB", [128, 4096], BF16)]
        self.rhsR = [self.sbuf("sb_rhsRA", [128, 2048], F32R), self.sbuf("sb_rhsRB", [128, 2048], F32R)]
        self._pool = {}
        for tag, cols, dt in [
            ("F32A", 4224, F32), ("F32B", 2048, F32), ("F32C", 2048, F32),
            ("F32D", 512, F32), ("F32E", 512, F32), ("I32A", 2048, I32),
            ("B16A", 2048, BF16), ("B16B", 2048, BF16), ("B16C", 2048, BF16),
            ("B16D", 1024, BF16), ("B16E", 1024, BF16), ("B16F", 1024, BF16),
            ("B16G", 1024, BF16),
            ("BIGB", 4096, BF16), ("BIGC", 4096, BF16),
            ("R32A", 1024, F32R), ("R32B", 1024, F32R),
        ]:
            self._pool[tag] = Buf(self.ctx.enter_context(self.nc.sbuf_tensor("pool_" + tag, [128, cols], dt)))
        self.ps = {}
        for i in range(4):
            self.ps[f"mm{i}"] = self.psumb(f"ps_mm{i}", [128, 512], F32)
        self.ps["tr"] = self.psumb("ps_tr", [128, 1024], BF16)
        self.ps["trR"] = self.psumb("ps_trR", [128, 512], F32R)
        self.ps["sc"] = self.psumb("ps_sc", [128, 512], F32)
        self.ps["av"] = self.psumb("ps_av", [128, 512], F32)

        self.P = Prog(nc, self.ctx)
        self.prelude()
        subs = {1: self.sub_l0ffn, 2: self.sub_mamba, 3: self.sub_l0ffn,
                4: self.sub_l1ffn, 5: self.sub_attn, 6: self.sub_l1ffn}
        for i in self.enabled:
            subs[i]()
        self.final_out()
        with nc.Block() as block:
            self.P.build(block)
        self.ctx.close()
        return nc

    def mm_stream(self, lhsT_d, K, M, rhs_d, N, evac, rhs_dt=BF16, rhs_dep=None,
                  lhs_dep=None, mt_list=None, mt_off=0):
        P = self.P
        nK = (K + 127) // 128
        KG = 4 if rhs_dt == F32R else 8
        ngrp = (nK + KG - 1) // KG
        wbuf = self.wresR if rhs_dt == F32R else self.wres
        wcap = 8192 if rhs_dt == F32R else 16384
        resident = nK * M <= wcap
        ldep = aslist(lhs_dep)
        full_chunks = (K % 128 == 0)
        wtok = None
        if resident:
            deps = wbuf.wdep() + ldep
            if full_chunks:
                sview = lhsT_d.rearrange("(k p) m -> p k m", p=128)[:, :, :M]
                dview = wbuf.ap[:, :nK * M].rearrange("p (k m) -> p k m", m=M)
                wtok = P.dma("dS", dview, sview, deps, key=wbuf.key)
            else:
                tt = None
                for k in range(nK):
                    pk = min(128, K - k * 128)
                    tt = P.dma("dS", wbuf.ap[:pk, k * M:(k + 1) * M], lhsT_d[k * 128:k * 128 + pk, :M],
                               deps if k == 0 else [tt], key=wbuf.key)
                wtok = tt
            wbuf.ww(wtok)
        rbufs = self.rhsR if rhs_dt == F32R else self.rhs16
        rdeps = aslist(rhs_dep)
        mts = mt_list if mt_list is not None else list(range((M + 127) // 128))
        nt_cols = 512
        mt_groups = [mts[i:i + 4] for i in range(0, len(mts), 4)]
        nNT = N // nt_cols

        def load_rhs(b, k0, k1, ntile, extra):
            deps = b.wdep() + rdeps + extra
            if full_chunks:
                sview = rhs_d.rearrange("(k p) n -> p k n", p=128)[:, k0:k1,
                                                                   ntile * nt_cols:(ntile + 1) * nt_cols]
                dview = b.ap[:, :(k1 - k0) * nt_cols].rearrange("p (k n) -> p k n", n=nt_cols)
                return P.dma("dA", dview, sview, deps, key=b.key)
            tt = None
            for k in range(k0, k1):
                pk = min(128, K - k * 128)
                tt = P.dma("dA", b.ap[:pk, (k - k0) * nt_cols:(k - k0 + 1) * nt_cols],
                           rhs_d[k * 128:k * 128 + pk, ntile * nt_cols:(ntile + 1) * nt_cols],
                           deps if k == k0 else [tt], key=b.key)
            return tt

        if ngrp == 1:
            rtoks = {}
            rtoks[0] = load_rhs(rbufs[0], 0, nK, 0, [])
            rbufs[0].ww(rtoks[0])
            for ntile in range(nNT):
                b = rbufs[ntile % 2]
                tt = rtoks[ntile]
                if ntile + 1 < nNT:
                    nb = rbufs[(ntile + 1) % 2]
                    rtoks[ntile + 1] = load_rhs(nb, 0, nK, ntile + 1, [])
                    nb.ww(rtoks[ntile + 1])
                for mts_g in mt_groups:
                    mm_last = {}
                    for j, mt in enumerate(mts_g):
                        ms = min(128, M - mt * 128)
                        psb = self.ps[f"mm{j % 4}"]
                        mm_tok = None
                        for k in range(nK):
                            pk = min(128, K - k * 128)
                            wap = wbuf.ap[:pk, k * M + mt * 128:k * M + mt * 128 + ms]
                            d = [wtok, tt] + (psb.wdep() if k == 0 else [mm_tok])
                            mm_tok = P.op("t", lambda e, wap=wap, psb=psb, b=b, ms=ms, pk=pk, k=k,
                                          st=(k == 0), sp=(k == nK - 1):
                                          e.matmul(psb.ap[:ms, :nt_cols], wap,
                                                   b.ap[:pk, k * nt_cols:(k + 1) * nt_cols],
                                                   start=st, stop=sp), d)
                            psb.ww(mm_tok)
                            b.rr(mm_tok)
                            wbuf.rr(mm_tok)
                        mm_last[mt] = mm_tok
                    for j, mt in enumerate(mts_g):
                        psb = self.ps[f"mm{j % 4}"]
                        et = evac(mt + mt_off, ntile, psb, [mm_last[mt]])
                        psb.rr(et)
                        b.rr(et)
        else:
            for ntile in range(nNT):
                for mts_g in mt_groups:
                    mm_tok = {mt: None for mt in mts_g}
                    for kg in range(ngrp):
                        b = rbufs[kg % 2]
                        k0 = kg * KG
                        k1 = min(nK, k0 + KG)
                        tt = load_rhs(b, k0, k1, ntile, [])
                        b.ww(tt)
                        if not resident:
                            base = (kg % 2) * 8192
                            dw = self.wres.wdep() + ldep
                            if full_chunks:
                                sview = lhsT_d.rearrange("(k p) m -> p k m", p=128)[:, k0:k1, :M]
                                dview = self.wres.ap[:, base:base + (k1 - k0) * M].rearrange(
                                    "p (k m) -> p k m", m=M)
                                tw = P.dma("dS", dview, sview, dw, key=self.wres.key)
                            else:
                                tw = None
                                for k in range(k0, k1):
                                    pk = min(128, K - k * 128)
                                    tw = P.dma("dS", self.wres.ap[:pk, base + (k - k0) * M:
                                                                  base + (k - k0 + 1) * M],
                                               lhsT_d[k * 128:k * 128 + pk, :M], dw if k == k0 else [tw],
                                               key=self.wres.key)
                            self.wres.ww(tw)
                            wtok = tw
                        for j, mt in enumerate(mts_g):
                            ms = min(128, M - mt * 128)
                            psb = self.ps[f"mm{j % 4}"]
                            for k in range(k0, k1):
                                pk = min(128, K - k * 128)
                                if resident:
                                    wap = wbuf.ap[:pk, k * M + mt * 128:k * M + mt * 128 + ms]
                                else:
                                    base = (kg % 2) * 8192
                                    wap = self.wres.ap[:pk, base + (k - k0) * M + mt * 128:
                                                       base + (k - k0) * M + mt * 128 + ms]
                                d = [wtok, tt]
                                if k == 0:
                                    d += psb.wdep()
                                else:
                                    d.append(mm_tok[mt])
                                mm_tok[mt] = P.op("t", lambda e, wap=wap, psb=psb, b=b, ms=ms, pk=pk,
                                                  kk=(k - k0), st=(k == 0), sp=(k == nK - 1):
                                                  e.matmul(psb.ap[:ms, :nt_cols], wap,
                                                           b.ap[:pk, kk * nt_cols:(kk + 1) * nt_cols],
                                                           start=st, stop=sp), d)
                                psb.ww(mm_tok[mt])
                                b.rr(mm_tok[mt])
                                (wbuf if resident else self.wres).rr(mm_tok[mt])
                    for j, mt in enumerate(mts_g):
                        psb = self.ps[f"mm{j % 4}"]
                        et = evac(mt + mt_off, ntile, psb, [mm_tok[mt]])
                        psb.rr(et)

    def prelude(self):
        P, w = self.P, self.w
        t = P.dma("dS", self.ident.ap[:], w["ident"][:, :], key=self.ident.key); self.ident.ww(t)
        t = P.dma("dS", self.maskb.ap[:], w["maskb"][:, :], key=self.maskb.key); self.maskb.ww(t)
        t = P.dma("dS", self.gb.ap[:], w["gb"][:, :], key=self.gb.key); self.gb.ww(t)
        ts = P.dma("dS", self.smalls.ap[:], w["smalls"][:, :], key=self.smalls.key); self.smalls.ww(ts)
        ti = P.op("v", lambda e: e.tensor_copy(self.identR.ap[:], self.ident.ap[:]),
                  self.ident.rdep() + self.identR.wdep())
        self.identR.ww(ti); self.ident.rr(ti)
        self.csts = self.sbuf("sb_csts", [128, 2], F32)
        tc1 = P.op("v", lambda e: e.memset(self.csts.ap[:, 0:1], 1e-5), self.csts.wdep())
        tc2 = P.op("v", lambda e: e.memset(self.csts.ap[:, 1:2], 1.0), [tc1])
        self.csts.ww(tc2)
        self.nc.const_aps.register(F32, 1e-5, self.csts.ap[:, 0:1])
        self.nc.const_aps.register(F32, 1.0, self.csts.ap[:, 1:2])
        self.t_x0 = P.dma("dS", self.xres[:, :], self.x_in[:, :])
        self.off = {"cw0": 0, "cb0": 64, "dsk": 80, "dtb": 96, "A": 112, "cw1": 368, "cb1": 464, "fbias": 496}
        ta = P.op("s", lambda e: e.activation(out=self.A_neg.ap[:], in_=self.smalls.ap[:, 112:368],
                                              func=AF.Exp), [ts] + self.A_neg.wdep())
        self.smalls.rr(ta)
        ta = P.op("v", lambda e: e.tensor_scalar(self.A_neg.ap[:], self.A_neg.ap[:], -1.0, None, AL.mult), [ta])
        self.A_neg.ww(ta)
        tz = P.op("v", lambda e: e.memset(self.states.ap[:], 0.0), self.states.wdep())
        self.states.ww(tz)
        tz = P.op("v", lambda e: e.memset(self.tails.ap[:], 0.0), self.tails.wdep())
        self.tails.ww(tz)
        to1 = P.op("v", lambda e: e.memset(self.ones1.ap[:], 1.0), self.ones1.wdep())
        self.ones1.ww(to1)
        self.quant_weights()
        self.build_M()

    def smalls_pp(self, key, idx):
        return self.smalls.ap[:, self.off[key] + idx:self.off[key] + idx + 1]

    def quant_weights(self):
        P = self.P
        HB = 2048
        buf = self.pool("F32A")
        g = self.pool("F32B")
        qb = self.pool("F32C")
        qi = self.pool("I32A")
        qo = self.pool("B16A")
        acc = self.sbuf("q_acc", [128, 2], F32)
        part = self.sbuf("q_part", [128, 64], F32)
        sval = self.sbuf("q_sval", [1, 1], F32)
        onec = self.sbuf("q_onec", [128, 1], F32)
        totb = self.sbuf("q_totb", [128, 1], F32)
        rws = self.sbuf("q_rws", [128, 2], F32)
        toc = P.op("v", lambda e: e.memset(onec.ap[:], 1.0), onec.wdep())
        onec.ww(toc)
        self.t_wq = None
        for (wT_d, dst, K, N, idx) in [(self.w["w1T"], self.w1q_d, self.DM, self.DH, 0),
                                       (self.w["w2T"], self.w2q_d, self.DH, self.DM, 1)]:
            nH = (N + HB - 1) // HB
            tprev = None
            ncols = 0
            for kk in range(K // 128):
                for h in range(nH):
                    hw = min(HB, N - h * HB)
                    tl = P.dma("dS", buf.ap[:, :hw], wT_d[kk * 128:(kk + 1) * 128, h * HB:h * HB + hw],
                               buf.wdep() + [tprev], key=buf.key)
                    buf.ww(tl)
                    tv = P.op("s", lambda e, hw=hw: e.activation(out=g.ap[:, :hw], in_=buf.ap[:, :hw],
                                                                 func=AF.Abs, accum_out=acc.ap[:, 0:1]),
                              [tl] + g.wdep() + acc.wdep())
                    g.ww(tv); acc.ww(tv); buf.rr(tv)
                    tc = P.op("v", lambda e, c=ncols: e.tensor_copy(part.ap[:, c:c + 1], acc.ap[:, 0:1]),
                              [tv] + (part.wdep() if ncols == 0 else []))
                    part.ww(tc); acc.rr(tc)
                    tprev = tc
                    ncols += 1
            t1 = P.op("v", lambda e, nc_=ncols: e.tensor_reduce(totb.ap[:], part.ap[:, :nc_],
                                                                mybir.AxisListType.X, AL.add),
                      [tprev] + totb.wdep())
            totb.ww(t1); part.rr(t1)
            psb = self.ps["sc"]
            tmm = P.op("t", lambda e: e.matmul(psb.ap[:1, :1], totb.ap[:], onec.ap[:],
                                               start=True, stop=True),
                       [t1, toc] + psb.wdep())
            psb.ww(tmm); totb.rr(tmm); onec.rr(tmm)
            tsv = P.op("v", lambda e, d=1.0 / (K * N): e.tensor_scalar(sval.ap[:], psb.ap[0:1, 0:1], d, 1e-5,
                                                                      AL.mult, AL.max),
                       [tmm] + sval.wdep())
            sval.ww(tsv); psb.rr(tsv)
            tst = P.dma("dS", self.wsc_d[0:1, idx:idx + 1], sval.ap[:], [tsv], key=sval.key)
            sval.rr(tst)
            tbc = P.dma("dG", self.invws.ap[:, idx:idx + 1], self.wsc_d[0, idx:idx + 1].partition_broadcast(128),
                        [tst] + self.invws.wdep(), key=self.invws.key)
            self.invws.ww(tbc)
            trw = P.op("v", lambda e, idx=idx: e.reciprocal(rws.ap[:, idx:idx + 1],
                                                            self.invws.ap[:, idx:idx + 1]),
                       [tbc] + (rws.wdep() if idx == 0 else []))
            rws.ww(trw); self.invws.rr(trw)
            tpr = None
            for kk in range(K // 128):
                for h in range(nH):
                    hw = min(HB, N - h * HB)
                    tl = P.dma("dS", buf.ap[:, :hw], wT_d[kk * 128:(kk + 1) * 128, h * HB:h * HB + hw],
                               buf.wdep() + [tpr, trw], key=buf.key)
                    buf.ww(tl)
                    tg = P.op("v", lambda e, idx=idx, hw=hw: e.tensor_scalar(g.ap[:, :hw], buf.ap[:, :hw],
                                                                             rws.ap[:, idx:idx + 1], None,
                                                                             AL.mult), [tl, trw] + g.wdep())
                    g.ww(tg); buf.rr(tg); rws.rr(tg)
                    tg = P.op("v", lambda e, hw=hw: e.tensor_scalar(g.ap[:, :hw], g.ap[:, :hw], -1.5, 1.5,
                                                                    AL.max, AL.min), [tg])
                    g.ww(tg)
                    tq = P.op("v", lambda e, hw=hw: e.tensor_scalar(qi.ap[:, :hw], g.ap[:, :hw], 128.0, None,
                                                                    AL.add), [tg] + qi.wdep())
                    qi.ww(tq)
                    tq2 = P.op("v", lambda e, hw=hw: e.tensor_copy(qb.ap[:, :hw], qi.ap[:, :hw]),
                               [tq] + qb.wdep())
                    qb.ww(tq2); qi.rr(tq2)
                    tg2 = P.op("v", lambda e, hw=hw: e.scalar_tensor_tensor(g.ap[:, :hw], g.ap[:, :hw], 128.0,
                                                                            qb.ap[:, :hw], AL.add,
                                                                            AL.subtract), [tq2, tg])
                    g.ww(tg2); qb.rr(tg2)
                    tq3 = P.op("v", lambda e, hw=hw: e.scalar_tensor_tensor(qb.ap[:, :hw], g.ap[:, :hw], 0.5,
                                                                            qb.ap[:, :hw], AL.is_ge,
                                                                            AL.add), [tg2])
                    qb.ww(tq3); g.rr(tq3)
                    tq4 = P.op("v", lambda e, hw=hw: e.tensor_scalar(qb.ap[:, :hw], qb.ap[:, :hw], -128.0,
                                                                     None, AL.add), [tq3])
                    qb.ww(tq4)
                    tq5 = P.op("v", lambda e, hw=hw: e.tensor_scalar(qo.ap[:, :hw], qb.ap[:, :hw], -1.0, 1.0,
                                                                     AL.max, AL.min), [tq4] + qo.wdep())
                    qo.ww(tq5); qb.rr(tq5)
                    tpr = P.dma("dG", dst[kk * 128:(kk + 1) * 128, h * HB:h * HB + hw], qo.ap[:, :hw],
                                [tq5], key=qo.key)
                    qo.rr(tpr)
            self.t_wq = tpr if self.t_wq is None else tpr

    def build_M(self):
        P, w, DM = self.P, self.w, self.DM
        ev = self.pool("R32A")
        l1 = self.sbuf("ml1", [128, 128], F32R)
        l2 = self.sbuf("ml2", [128, 128], F32R)
        l2n = self.sbuf("ml2n", [128, 128], F32R)
        r1 = Buf(self.rhsR[0].ap[:, 0:512])
        r2 = Buf(self.rhsR[1].ap[:, 0:512])
        r1.w = self.rhsR[0].w; r2.w = self.rhsR[1].w

        def pair(A1, B1, A2, B2, neg2, out_d, scale):
            td = None
            for mt in range(DM // 128):
                for ntile in range(DM // 512):
                    psb = self.ps["mm0"]
                    mm_tok = None
                    for kk in range(DM // 128):
                        tl1 = P.dma("dS", l1.ap[:], A1[kk * 128:(kk + 1) * 128, mt * 128:(mt + 1) * 128],
                                    l1.wdep(), key=l1.key)
                        l1.ww(tl1)
                        tr1 = P.dma("dA", r1.ap[:], B1[kk * 128:(kk + 1) * 128, ntile * 512:(ntile + 1) * 512],
                                    r1.wdep(), key=r1.key)
                        r1.ww(tr1)
                        tl2 = P.dma("dS", l2.ap[:], A2[kk * 128:(kk + 1) * 128, mt * 128:(mt + 1) * 128],
                                    l2.wdep(), key=l2.key)
                        l2.ww(tl2)
                        tr2 = P.dma("dA", r2.ap[:], B2[kk * 128:(kk + 1) * 128, ntile * 512:(ntile + 1) * 512],
                                    r2.wdep(), key=r2.key)
                        r2.ww(tr2)
                        d = [tl1, tr1] + (psb.wdep() if kk == 0 else [mm_tok])
                        mm_tok = P.op("t", lambda e, f=(kk == 0): e.matmul(psb.ap[:, :], l1.ap[:], r1.ap[:],
                                                                          start=f, stop=False), d)
                        psb.ww(mm_tok); l1.rr(mm_tok); r1.rr(mm_tok)
                        lu, dl = l2, [tl2]
                        if neg2:
                            tn = P.op("v", lambda e: e.tensor_scalar(l2n.ap[:], l2.ap[:], -1.0, None,
                                                                    AL.mult), [tl2] + l2n.wdep())
                            l2n.ww(tn); l2.rr(tn)
                            lu, dl = l2n, [tn]
                        mm_tok = P.op("t", lambda e, lu=lu, lk=(kk == DM // 128 - 1): e.matmul(
                            psb.ap[:, :], lu.ap[:], r2.ap[:], start=False, stop=lk), dl + [tr2, mm_tok])
                        psb.ww(mm_tok); lu.rr(mm_tok); r2.rr(mm_tok)
                    te = P.op("s", lambda e, sc=scale: e.activation(out=ev.ap[:, :512], in_=psb.ap[:, :],
                                                                   func=AF.Identity, scale=sc),
                              [mm_tok] + ev.wdep())
                    ev.ww(te); psb.rr(te)
                    td = P.dma("dS", out_d[mt * 128:(mt + 1) * 128, ntile * 512:(ntile + 1) * 512],
                               ev.ap[:, :512], [te], key=ev.key)
                    ev.rr(td)
            return td

        pair(w["wfr"], w["C_d"], w["wfi"], w["S_d"], False, self.Q1_d, 1.0)
        pair(w["wfi"], w["C_d"], w["wfr"], w["S_d"], True, self.Q2_d, 1.0)
        self.t_M = pair(self.Q1_d, w["C_d"], self.Q2_d, w["S_d"], True, self.M_d, 1.0 / DM)

    def ln(self, src_d, dst_tm=None, dst_fm=None, fm_dt=BF16, gbi=None, dep=None):
        dep = aslist(dep)
        P, T, DM = self.P, self.T, self.DM
        bxt, xts = self.slots("F32A", 1024, 2)
        sq = self.pool("F32B")
        bxo, xos = self.slots("F32C", 1024, 2)
        bxb, xbs = self.slots("B16A", 1024, 2)
        xb2 = self.pool("B16B")
        xr = self.pool("R32A")
        evb = self.pool("B16C")
        evr = self.pool("R32B")
        stv0 = self.stv
        stvs = [Buf(stv0.ap[:, 0:8]), Buf(stv0.ap[:, 8:16])]
        for b in stvs:
            b.w = stv0.w; b.r = list(stv0.r)
        last = None
        for it in range(T // 128):
            sl = it % 2
            xt, xo, xb, stv = xts[sl], xos[sl], xbs[sl], stvs[sl]
            tl = P.dma("dS", xt.ap[:], src_d[it * 128:(it + 1) * 128, :], xt.wdep() + dep + [last],
                       key=xt.key)
            xt.ww(tl)
            t1 = P.op("s", lambda e, xt=xt, stv=stv: e.activation(out=sq.ap[:, :1024], in_=xt.ap[:],
                                                                  func=AF.Identity,
                                                                  accum_out=stv.ap[:, 0:1]),
                      [tl] + sq.wdep() + stv.wdep())
            t2 = P.op("s", lambda e, xt=xt, stv=stv: e.activation(out=sq.ap[:, :1024], in_=xt.ap[:],
                                                                  func=AF.Square,
                                                                  accum_out=stv.ap[:, 1:2]), [t1])
            sq.ww(t2); stv.ww(t2); xt.rr(t2)
            t3 = P.op("v", lambda e, d=1.0 / DM, stv=stv: e.tensor_scalar(stv.ap[:, 2:3], stv.ap[:, 0:1],
                                                                         d, None, AL.mult), [t2])
            t4 = P.op("v", lambda e, d=1.0 / DM, stv=stv: e.tensor_scalar(stv.ap[:, 3:4], stv.ap[:, 1:2],
                                                                         d, None, AL.mult), [t2])
            t5 = P.op("v", lambda e, stv=stv: e.tensor_tensor(stv.ap[:, 6:7], stv.ap[:, 2:3],
                                                              stv.ap[:, 2:3], AL.mult), [t3])
            t6 = P.op("v", lambda e, stv=stv: e.tensor_tensor(stv.ap[:, 3:4], stv.ap[:, 3:4],
                                                              stv.ap[:, 6:7], AL.subtract), [t4, t5])
            t7 = P.op("s", lambda e, stv=stv: e.activation(out=stv.ap[:, 6:7], in_=stv.ap[:, 3:4],
                                                           func=AF.Ln, bias=1e-5), [t6])
            t8 = P.op("s", lambda e, stv=stv: e.activation(out=stv.ap[:, 4:5], in_=stv.ap[:, 6:7],
                                                           func=AF.Exp, scale=-0.5), [t7])
            t9 = P.op("v", lambda e, stv=stv: e.tensor_tensor(stv.ap[:, 5:6], stv.ap[:, 2:3],
                                                              stv.ap[:, 4:5], AL.mult), [t8, t3])
            t9 = P.op("v", lambda e, stv=stv: e.tensor_scalar(stv.ap[:, 5:6], stv.ap[:, 5:6], -1.0, None,
                                                              AL.mult), [t9])
            stv.ww(t9)
            if dst_tm is not None:
                tn = P.op("s", lambda e, xt=xt, xo=xo, stv=stv: e.activation(
                    out=xo.ap[:], in_=xt.ap[:], func=AF.Identity, bias=stv.ap[:, 5:6],
                    scale=stv.ap[:, 4:5]), [t9] + xo.wdep())
                xo.ww(tn); xt.rr(tn); stv.rr(tn)
                last = P.dma("dG", dst_tm[it * 128:(it + 1) * 128, :], xo.ap[:], [tn], key=xo.key)
                xo.rr(last)
            if dst_fm is not None:
                tgt = xr if fm_dt == F32R else xb
                tn = P.op("s", lambda e, gg=tgt, xt=xt, stv=stv: e.activation(
                    out=gg.ap[:, :1024], in_=xt.ap[:], func=AF.Identity, bias=stv.ap[:, 5:6],
                    scale=stv.ap[:, 4:5]), [t9] + tgt.wdep())
                tgt.ww(tn); xt.rr(tn); stv.rr(tn)
                if gbi is not None:
                    o = gbi * 2 * DM
                    tn = P.op("v", lambda e, o=o, xb=xb: e.tensor_tensor(xb2.ap[:, :1024], xb.ap[:, :1024],
                                                                        self.gb.ap[:, o:o + DM], AL.mult),
                              [tn] + xb2.wdep() + self.gb.rdep())
                    xb2.ww(tn); xb.rr(tn); self.gb.rr(tn)
                    tn = P.op("v", lambda e, o=o, xb=xb: e.tensor_tensor(xb.ap[:, :1024], xb2.ap[:, :1024],
                                                                        self.gb.ap[:, o + DM:o + 2 * DM],
                                                                        AL.add), [tn] + xb.wdep())
                    xb.ww(tn); xb2.rr(tn); self.gb.rr(tn)
                    tgt = xb
                last = self.transpose_out(tgt, DM, dst_fm, it, fm_dt, evb if fm_dt == BF16 else evr, [tn])
        self.slots_done(bxt, xts)
        self.slots_done(bxo, xos)
        self.slots_done(bxb, xbs)
        self.slots_done(stv0, stvs)
        return last

    def transpose_out(self, srcbuf, D, dst_fm, it, dt, evbuf, deps, src_off=0, dst_row=0):
        P = self.P
        pst = self.ps["tr"] if dt == BF16 else self.ps["trR"]
        idn = self.ident if dt == BF16 else self.identR
        grp_sz = 8 if dt == BF16 else 4
        ts = None
        for grp in range((D // 128 + grp_sz - 1) // grp_sz):
            toks = []
            nb = min(grp_sz, D // 128 - grp * grp_sz)
            for b2 in range(nb):
                blk = grp * grp_sz + b2
                d = deps + idn.rdep() + (pst.wdep() if b2 == 0 else [toks[-1]])
                tt = P.op("t", lambda e, blk=blk, b2=b2: e.transpose(
                    pst.ap[:, b2 * 128:(b2 + 1) * 128],
                    srcbuf.ap[:, src_off + blk * 128:src_off + (blk + 1) * 128],
                    idn.ap[:]), d)
                toks.append(tt)
                srcbuf.rr(tt); idn.rr(tt)
            pst.ww(toks[-1])
            te = P.op("v", lambda e, nb=nb: e.tensor_copy(
                evbuf.ap[:, :nb * 128], pst.ap[:, :nb * 128]),
                [toks[-1]] + evbuf.wdep())
            evbuf.ww(te); pst.rr(te)
            r0 = dst_row + grp * grp_sz * 128
            dview = dst_fm[r0:r0 + nb * 128, :].rearrange(
                "(a p) t -> p a t", p=128)[:, :, it * 128:(it + 1) * 128]
            sview = evbuf.ap[:, :nb * 128].rearrange("p (a t) -> p a t", t=128)
            ts = P.dma("dG", dview, sview, [te], key=evbuf.key)
            evbuf.rr(ts)
        return ts

    def add_contrib_fm(self, contrib_fm_d, extra_tm_d=None, dep=None):
        dep = aslist(dep)
        P, T, DM = self.P, self.T, self.DM
        cb = self.pool("R32A")
        ct = self.pool("F32C")
        bxt, xts = self.slots("F32A", 1024, 2)
        et = self.pool("F32B")
        pst = self.ps["trR"]
        last = None
        for it in range(T // 128):
            xt = xts[it % 2]
            sview = contrib_fm_d.rearrange("(a p) t -> p a t", p=128)[:, :, it * 128:(it + 1) * 128]
            dview = cb.ap[:, :DM].rearrange("p (a t) -> p a t", t=128)
            tl = P.dma("dS", dview, sview, cb.wdep() + dep + [last], key=cb.key)
            cb.ww(tl)
            te = None
            for grp in range(2):
                toks = []
                for b2 in range(4):
                    blk = grp * 4 + b2
                    d = [tl] + self.identR.rdep() + (pst.wdep() if b2 == 0 else [toks[-1]])
                    tt = P.op("t", lambda e, blk=blk, b2=b2: e.transpose(
                        pst.ap[:, b2 * 128:(b2 + 1) * 128], cb.ap[:, blk * 128:(blk + 1) * 128],
                        self.identR.ap[:]), d)
                    toks.append(tt); cb.rr(tt); self.identR.rr(tt)
                pst.ww(toks[-1])
                te = P.op("v", lambda e, grp=grp: e.tensor_copy(ct.ap[:, grp * 512:(grp + 1) * 512],
                                                               pst.ap[:, :512]),
                          [toks[-1]] + (ct.wdep() if grp == 0 else []))
                ct.ww(te); pst.rr(te)
            tx = P.dma("dA", xt.ap[:], self.xres[it * 128:(it + 1) * 128, :],
                       xt.wdep() + dep + [self.t_x0], key=xt.key)
            xt.ww(tx)
            d2 = [te, tx]
            if extra_tm_d is not None:
                tex = P.dma("dA", et.ap[:, :DM], extra_tm_d[it * 128:(it + 1) * 128, :],
                            et.wdep() + dep, key=et.key)
                et.ww(tex)
                ta = P.op("v", lambda e, xt=xt: e.tensor_tensor(xt.ap[:], xt.ap[:], et.ap[:, 0:1024],
                                                               AL.add), [tex] + d2)
                xt.ww(ta); et.rr(ta)
                d2 = [ta]
            ta = P.op("v", lambda e, xt=xt: e.tensor_tensor(xt.ap[:], xt.ap[:], ct.ap[:, 0:1024],
                                                           AL.add), d2)
            xt.ww(ta); ct.rr(ta)
            last = P.dma("dG", self.xres[it * 128:(it + 1) * 128, :], xt.ap[:], [ta], key=xt.key)
            xt.rr(last)
        self.slots_done(bxt, xts)
        self.t_x0 = last
        return last

    def sub_l0ffn(self):
        P, T, DM = self.P, self.T, self.DM
        fence = P.fence()
        tln = self.ln(self.xres, dst_fm=self.xn_fmR, fm_dt=F32R, dep=fence)
        contrib = self.dram(self.uid("c0"), [DM, T], F32R)
        ev = self.pool("R32B")

        def evac(mt, ntile, psb, deps):
            te = P.op("s", lambda e, mt=mt: e.activation(out=ev.ap[:, :512], in_=psb.ap[:, :],
                                                         func=AF.Identity, scale=0.5,
                                                         bias=self.smalls_pp("fbias", mt)),
                      deps + ev.wdep() + self.smalls.rdep())
            ev.ww(te); self.smalls.rr(te); psb.rr(te)
            ts = P.dma("dG", contrib[mt * 128:(mt + 1) * 128, ntile * 512:(ntile + 1) * 512],
                       ev.ap[:, :512], [te], key=ev.key)
            ev.rr(ts)
            return ts

        self.mm_stream(self.M_d, DM, DM, self.xn_fmR, T, evac, rhs_dt=F32R,
                       rhs_dep=[tln], lhs_dep=[self.t_M])
        self.add_contrib_fm(contrib, dep=P.fence())

    def sub_mamba(self):
        P, T, DM, DI = self.P, self.T, self.DM, self.DI
        fence = P.fence()
        t1 = self.ln(self.xres, dst_tm=self.xnp_d, dep=fence)
        t2 = self.ln(self.xnp_d, dst_fm=self.xn_fm16, fm_dt=BF16, gbi=0, dep=[t1])
        ub = self.pool("F32A")
        acc = self.pool("F32C")
        u2b = self.pool("B16D")
        szb = self.pool("B16E")
        nDIt = DI // 128
        tz = P.op("v", lambda e: e.memset(self.tails.ap[:], 0.0), self.tails.wdep() + fence)
        self.tails.ww(tz)

        def evac_u(mt, ntile, psb, deps):
            te = P.op("s", lambda e: e.activation(out=ub.ap[:, 3:515], in_=psb.ap[:, :], func=AF.Identity),
                      deps + ub.wdep())
            ub.ww(te); psb.rr(te)
            tc = P.op("v", lambda e, mt=mt: e.tensor_copy(ub.ap[:, 0:3], self.tails.ap[:, 3 * mt:3 * mt + 3]),
                      [te] + self.tails.rdep())
            ub.ww(tc); self.tails.rr(tc)
            tt = P.op("v", lambda e, mt=mt: e.tensor_copy(self.tails.ap[:, 3 * mt:3 * mt + 3],
                                                          ub.ap[:, 512:515]), [tc] + self.tails.wdep())
            self.tails.ww(tt)
            t0 = P.op("v", lambda e, mt=mt: e.tensor_scalar(acc.ap[:, :512], ub.ap[:, 0:512],
                                                            self.smalls_pp("cw0", 4 * mt), None, AL.mult),
                      [tc] + acc.wdep() + self.smalls.rdep())
            acc.ww(t0); self.smalls.rr(t0)
            for k2 in (1, 2, 3):
                t0 = P.op("v", lambda e, mt=mt, k2=k2: e.scalar_tensor_tensor(
                    acc.ap[:, :512], ub.ap[:, k2:512 + k2], self.smalls_pp("cw0", 4 * mt + k2),
                    acc.ap[:, :512], AL.mult, AL.add), [t0])
                acc.ww(t0)
            ub.rr(t0)
            ts = P.op("s", lambda e, mt=mt: e.activation(out=u2b.ap[:, :512], in_=acc.ap[:, :512],
                                                         func=AF.Silu, bias=self.smalls_pp("cb0", mt)),
                      [t0] + u2b.wdep() + self.smalls.rdep())
            u2b.ww(ts); acc.rr(ts); self.smalls.rr(ts)
            td = P.dma("dG", self.u2_fm[mt * 128:(mt + 1) * 128, ntile * 512:(ntile + 1) * 512],
                       u2b.ap[:, :512], [ts], key=u2b.key)
            u2b.rr(td)
            return td

        def evac_z(mt, ntile, psb, deps):
            mz = mt - nDIt
            ts = P.op("s", lambda e: e.activation(out=szb.ap[:, :512], in_=psb.ap[:, :], func=AF.Silu),
                      deps + szb.wdep())
            szb.ww(ts); psb.rr(ts)
            td = P.dma("dG", self.sz_fm[mz * 128:(mz + 1) * 128, ntile * 512:(ntile + 1) * 512],
                       szb.ap[:, :512], [ts], key=szb.key)
            szb.rr(td)
            return td

        self.mm_stream(self.w["in_projT"][:, 0:DI], DM, DI, self.xn_fm16, T, evac_u, rhs_dep=[t2, tz])
        self.mm_stream(self.w["in_projT"][:, DI:2 * DI], DM, DI, self.xn_fm16, T, evac_z,
                       rhs_dep=[t2], mt_off=nDIt)
        f2 = P.fence()
        xdb = self.pool("B16D")
        xdf = self.pool("F32D")

        def evac_xproj(mt, ntile, psb, deps):
            te = P.op("v", lambda e: e.tensor_copy(xdb.ap[:96, 0:512], psb.ap[:96, :]), deps + xdb.wdep())
            xdb.ww(te); psb.rr(te)
            tf = P.op("v", lambda e: e.tensor_copy(xdf.ap[:32, 0:512], psb.ap[64:96, :]),
                      deps + [te] + xdf.wdep())
            xdf.ww(tf); psb.rr(tf)
            ta = P.dma("dG", self.dt_d[:, ntile * 512:(ntile + 1) * 512], xdb.ap[0:64, 0:512], [te], key=xdb.key)
            xdb.rr(ta)
            tb = P.dma("dG", self.BC_d[:, ntile * 512:(ntile + 1) * 512], xdf.ap[:32, 0:512], [tf], key=xdf.key)
            xdf.rr(tb)
            return tb

        self.mm_stream(self.w["x_projT"], DI, 96, self.u2_fm, T, evac_xproj, rhs_dep=f2,
                       lhs_dep=f2, mt_list=[0])
        f3 = P.fence()
        ab = self.pool("F32B")
        eb = self.pool("F32C")
        rb = self.pool("F32D")
        db = self.pool("F32E")
        dub = self.pool("B16D")
        u2l = self.pool("B16E")

        def evac_dt(mt, ntile, psb, deps):
            ta = P.op("s", lambda e, mt=mt: e.activation(out=ab.ap[:, :512], in_=psb.ap[:, :], func=AF.Abs,
                                                         bias=self.smalls_pp("dtb", mt)),
                      deps + ab.wdep() + self.smalls.rdep())
            ab.ww(ta); self.smalls.rr(ta); psb.rr(ta)
            tb = P.op("s", lambda e: e.activation(out=eb.ap[:, :512], in_=ab.ap[:, :512], func=AF.Exp,
                                                  scale=-1.0), [ta] + eb.wdep())
            eb.ww(tb); ab.rr(tb)
            tc = P.op("s", lambda e: e.activation(out=eb.ap[:, :512], in_=eb.ap[:, :512], func=AF.Ln,
                                                  bias=1.0), [tb])
            eb.ww(tc)
            tr = P.op("s", lambda e, mt=mt: e.activation(out=rb.ap[:, :512], in_=psb.ap[:, :], func=AF.Relu,
                                                         bias=self.smalls_pp("dtb", mt)),
                      deps + [tc] + rb.wdep() + self.smalls.rdep())
            rb.ww(tr); self.smalls.rr(tr); psb.rr(tr)
            td = P.op("v", lambda e: e.tensor_tensor(db.ap[:, :512], rb.ap[:, :512], eb.ap[:, :512],
                                                     AL.add), [tr, tc] + db.wdep())
            db.ww(td); rb.rr(td); eb.rr(td)
            tw = P.dma("dG", self.delta_fm[mt * 128:(mt + 1) * 128, ntile * 512:(ntile + 1) * 512],
                       db.ap[:, :512], [td], key=db.key)
            db.rr(tw)
            tu = P.dma("dA", u2l.ap[:, :512],
                       self.u2_fm[mt * 128:(mt + 1) * 128, ntile * 512:(ntile + 1) * 512], u2l.wdep(),
                       key=u2l.key)
            u2l.ww(tu)
            tm = P.op("v", lambda e: e.tensor_tensor(dub.ap[:, :512], db.ap[:, :512], u2l.ap[:, :512],
                                                     AL.mult), [td, tu] + dub.wdep())
            dub.ww(tm); db.rr(tm); u2l.rr(tm)
            t2w = P.dma("dG", self.du_fm[mt * 128:(mt + 1) * 128, ntile * 512:(ntile + 1) * 512],
                        dub.ap[:, :512], [tm], key=dub.key)
            dub.rr(t2w)
            return t2w

        self.mm_stream(self.w["dt_projT"], 64, DI, self.dt_d, T, evac_dt, rhs_dep=f3, lhs_dep=f3)
        self.scan_phase(P.fence())
        f4 = P.fence()
        contrib = self.dram(self.uid("cm"), [DM, T], F32R)
        ev4 = self.pool("R32B")

        def evac_out(mt, ntile, psb, deps):
            te = P.op("v", lambda e: e.tensor_copy(ev4.ap[:, :512], psb.ap[:, :]), deps + ev4.wdep())
            ev4.ww(te); psb.rr(te)
            ts = P.dma("dG", contrib[mt * 128:(mt + 1) * 128, ntile * 512:(ntile + 1) * 512],
                       ev4.ap[:, :512], [te], key=ev4.key)
            ev4.rr(ts)
            return ts

        self.mm_stream(self.w["out_projT"], DI, DM, self.yg_fm, T, evac_out, rhs_dep=f4, lhs_dep=f4)
        self.add_contrib_fm(contrib, extra_tm_d=self.xnp_d, dep=P.fence())

    def slots(self, tag, ncols, n):
        base = self.pool(tag)
        out = []
        for i in range(n):
            b = Buf(base.ap[:, i * ncols:(i + 1) * ncols])
            b.w = base.w
            b.r = list(base.r)
            out.append(b)
        return base, out

    def slots_done(self, base, sl):
        for b in sl:
            if b.w is not None:
                base.rr(b.w)
            for t in b.r:
                base.rr(t)

    def scan_phase(self, fence):
        P, T, DI = self.P, self.T, self.DI
        Tc = min(256, T)
        nCh = DI // 128
        Brep = self.pool("BIGB")
        Crep = self.pool("BIGC")
        bdlt, dlts = self.slots("F32A", Tc, 2)
        bdul, duls = self.slots("B16A", Tc, 2)
        bu2l, u2ls = self.slots("B16B", Tc, 2)
        bszl, szls = self.slots("B16C", Tc, 2)
        bdA, dAs = self.slots("F32B", Tc, 2)
        bbb, bbs = self.slots("B16D", Tc, 2)
        bhn, hns = self.slots("B16E", Tc, 2)
        bhc, hcs = self.slots("B16F", Tc, 2)
        yb = self.pool("F32C")
        ygb = self.pool("B16G")
        nb = (Tc + 511) // 512
        nTch = T // Tc

        def load_ct(tch, ct):
            s = ct % 2
            t1 = P.dma("dS", dlts[s].ap[:], self.delta_fm[ct * 128:(ct + 1) * 128, tch * Tc:(tch + 1) * Tc],
                       dlts[s].wdep() + fence, key=dlts[s].key)
            dlts[s].ww(t1)
            t2 = P.dma("dS", duls[s].ap[:], self.du_fm[ct * 128:(ct + 1) * 128, tch * Tc:(tch + 1) * Tc],
                       duls[s].wdep() + fence, key=duls[s].key)
            duls[s].ww(t2)
            t3 = P.dma("dA", u2ls[s].ap[:], self.u2_fm[ct * 128:(ct + 1) * 128, tch * Tc:(tch + 1) * Tc],
                       u2ls[s].wdep() + fence, key=u2ls[s].key)
            u2ls[s].ww(t3)
            t4 = P.dma("dA", szls[s].ap[:], self.sz_fm[ct * 128:(ct + 1) * 128, tch * Tc:(tch + 1) * Tc],
                       szls[s].wdep() + fence, key=szls[s].key)
            szls[s].ww(t4)

        ones1f = self.sbuf(self.uid("o1f"), [1, 128], F32)
        to1f = P.op("v", lambda e: e.memset(ones1f.ap[:], 1.0), ones1f.wdep() + fence)
        ones1f.ww(to1f)
        bc2s = [self.sbuf(self.uid(f"bc2{i}"), [1, 512], F32) for i in range(2)]
        for tch in range(nTch):
            for rep, base_row in ((Brep, 0), (Crep, 16)):
                for half in range(8):
                    rows = base_row + half * 2
                    bc2 = bc2s[half % 2]
                    psb = self.ps["sc"] if half % 2 == 0 else self.ps["av"]
                    sv = self.BC_d[rows:rows + 2,
                                   tch * Tc:(tch + 1) * Tc].rearrange("(o r) t -> o r t", o=1)
                    dv = bc2.ap[:, :2 * Tc].rearrange("p (r t) -> p r t", t=Tc)
                    tld = P.dma("dA", dv, sv, bc2.wdep() + fence, key=bc2.key)
                    bc2.ww(tld)
                    tmm = P.op("t", lambda e, psb=psb, bc2=bc2: e.matmul(
                        psb.ap[:, :2 * Tc], ones1f.ap[:], bc2.ap[:, :2 * Tc],
                        start=True, stop=True), [tld, to1f] + psb.wdep() + ones1f.rdep())
                    psb.ww(tmm); bc2.rr(tmm); ones1f.rr(tmm)
                    tev = P.op("v", lambda e, rep=rep, half=half, psb=psb: e.tensor_copy(
                        rep.ap[:, half * 2 * Tc:(half + 1) * 2 * Tc], psb.ap[:, :2 * Tc]),
                        [tmm] + rep.wdep())
                    rep.ww(tev); psb.rr(tev)
            tB = Brep.w
            tC = Crep.w
            if tch == 0:
                load_ct(0, 0)
            for ct in range(nCh):
                s = ct % 2
                dlt, dul, u2l, szl = dlts[s], duls[s], u2ls[s], szls[s]
                tl1, tl2, tl3, tl4 = dlt.w, dul.w, u2l.w, szl.w
                if ct + 1 < nCh:
                    load_ct(tch, ct + 1)
                elif tch + 1 < nTch:
                    load_ct(tch + 1, 0)
                banks = [self.ps["mm0"], self.ps["mm1"]][:nb]
                ytoks = []
                for n in range(16):
                    nr = n
                    sn = n % 2
                    dA, bb, hn, hc = dAs[sn], bbs[sn], hns[sn], hcs[sn]
                    ta = P.op("s", lambda e, ct=ct, n=n, dA=dA, dlt=dlt: e.activation(
                        out=dA.ap[:], in_=dlt.ap[:], func=AF.Exp,
                        scale=self.A_neg.ap[:, ct * 16 + n:ct * 16 + n + 1]),
                        [tl1] + dA.wdep() + self.A_neg.rdep())
                    dA.ww(ta); dlt.rr(ta); self.A_neg.rr(ta)
                    tb = P.op("v", lambda e, nr=nr, bb=bb, dul=dul: e.tensor_tensor(
                        bb.ap[:], dul.ap[:], Brep.ap[:, nr * Tc:(nr + 1) * Tc], AL.mult),
                        [tl2, tB] + bb.wdep())
                    bb.ww(tb); dul.rr(tb); Brep.rr(tb)
                    tsc = P.op("v", lambda e, ct=ct, n=n, hn=hn, dA=dA, bb=bb: e.tensor_tensor_scan(
                        hn.ap[:], dA.ap[:], bb.ap[:],
                        self.states.ap[:, ct * 16 + n:ct * 16 + n + 1], AL.mult, AL.add),
                        [ta, tb] + hn.wdep() + self.states.rdep() + self.states.wdep())
                    hn.ww(tsc); dA.rr(tsc); bb.rr(tsc); self.states.rr(tsc)
                    tst = P.op("v", lambda e, ct=ct, n=n, hn=hn: e.tensor_copy(
                        self.states.ap[:, ct * 16 + n:ct * 16 + n + 1], hn.ap[:, Tc - 1:Tc]), [tsc])
                    self.states.ww(tst); hn.rr(tst)
                    tm = P.op("v", lambda e, nr=nr, hc=hc, hn=hn: e.tensor_tensor(
                        hc.ap[:], hn.ap[:], Crep.ap[:, nr * Tc:(nr + 1) * Tc], AL.mult),
                        [tsc, tC] + hc.wdep())
                    hc.ww(tm); hn.rr(tm); Crep.rr(tm)
                    for sb2 in range(nb):
                        w2 = min(512, Tc - sb2 * 512)
                        psb = banks[sb2]
                        d = [tm] + self.ident.rdep() + (psb.wdep() if n == 0 else [ytoks[-1]])
                        ty = P.op("t", lambda e, sb2=sb2, psb=psb, n=n, w2=w2, hc=hc: e.matmul(
                            psb.ap[:, :w2], self.ident.ap[:], hc.ap[:, sb2 * 512:sb2 * 512 + w2],
                            start=(n == 0), stop=(n == 15)), d)
                        ytoks.append(ty)
                        if n == 0:
                            psb.ww(ty)
                        hc.rr(ty); self.ident.rr(ty)
                for sb2 in range(nb):
                    banks[sb2].ww(ytoks[-(nb - sb2)])
                t1s = []
                for sb2 in range(nb):
                    w2 = min(512, Tc - sb2 * 512)
                    psb = banks[sb2]
                    t1 = P.op("v", lambda e, ct=ct, sb2=sb2, psb=psb, w2=w2, u2l=u2l: e.scalar_tensor_tensor(
                        yb.ap[:, sb2 * 512:sb2 * 512 + w2], u2l.ap[:, sb2 * 512:sb2 * 512 + w2],
                        self.smalls_pp("dsk", ct), psb.ap[:, :w2], AL.mult, AL.add),
                        [tl3, ytoks[-(nb - sb2)]] + (yb.wdep() if sb2 == 0 else []) + self.smalls.rdep())
                    yb.ww(t1); u2l.rr(t1); psb.rr(t1); self.smalls.rr(t1)
                    t1s.append(t1)
                t2 = P.op("v", lambda e, szl=szl: e.tensor_tensor(ygb.ap[:, :Tc], yb.ap[:, :Tc],
                                                                 szl.ap[:, :Tc], AL.mult),
                          t1s + [tl4] + ygb.wdep())
                ygb.ww(t2); yb.rr(t2); szl.rr(t2)
                ts = P.dma("dG", self.yg_fm[ct * 128:(ct + 1) * 128, tch * Tc:(tch + 1) * Tc],
                           ygb.ap[:, :Tc], [t2], key=ygb.key)
                ygb.rr(ts)
        for base, sl in [(bdlt, dlts), (bdul, duls), (bu2l, u2ls), (bszl, szls),
                         (bdA, dAs), (bbb, bbs), (bhn, hns), (bhc, hcs)]:
            self.slots_done(base, sl)

    def sub_l1ffn(self):
        P, T, DM, DH = self.P, self.T, self.DM, self.DH
        fence = P.fence()
        t1 = self.ln(self.xres, dst_tm=self.xnp_d, dep=fence)
        tq = self.quant_tm(self.xnp_d, self.xq_fm, T, DM, dep=[t1])
        hs_fm = self.dram(self.uid("hs"), [DH, T], BF16)
        ub = self.pool("F32A")
        acc = self.pool("F32C")
        hsb = self.pool("B16D")
        nT = T // 512
        nDHt = DH // 128

        def evac_mm1(mt, ntile, psb, deps):
            te = P.op("s", lambda e: e.activation(out=ub.ap[:, 2:514], in_=psb.ap[:, :], func=AF.Identity,
                                                  scale=self.invws.ap[:, 0:1]),
                      deps + ub.wdep() + self.invws.rdep())
            ub.ww(te); self.invws.rr(te); psb.rr(te)
            tc = P.op("v", lambda e, mt=mt: e.tensor_copy(ub.ap[:, 0:2],
                                                          self.tails.ap[:, 2 * mt:2 * mt + 2]),
                      [te] + self.tails.rdep())
            ub.ww(tc); self.tails.rr(tc)
            tt = P.op("v", lambda e, mt=mt: e.tensor_copy(self.tails.ap[:, 2 * mt:2 * mt + 2],
                                                          ub.ap[:, 512:514]), [tc] + self.tails.wdep())
            self.tails.ww(tt)
            t0 = P.op("v", lambda e, mt=mt: e.tensor_scalar(acc.ap[:, :512], ub.ap[:, 0:512],
                                                            self.smalls_pp("cw1", 3 * mt), None, AL.mult),
                      [tc] + acc.wdep() + self.smalls.rdep())
            acc.ww(t0); self.smalls.rr(t0)
            for k2 in (1, 2):
                t0 = P.op("v", lambda e, mt=mt, k2=k2: e.scalar_tensor_tensor(
                    acc.ap[:, :512], ub.ap[:, k2:512 + k2], self.smalls_pp("cw1", 3 * mt + k2),
                    acc.ap[:, :512], AL.mult, AL.add), [t0])
                acc.ww(t0)
            ub.rr(t0)
            ts = P.op("s", lambda e, mt=mt: e.activation(out=hsb.ap[:, :512], in_=acc.ap[:, :512],
                                                         func=AF.Silu, bias=self.smalls_pp("cb1", mt)),
                      [t0] + hsb.wdep() + self.smalls.rdep())
            hsb.ww(ts); acc.rr(ts); self.smalls.rr(ts)
            lo = ntile * 512 - 1
            if lo < 0:
                td = P.dma("dG", hs_fm[mt * 128:(mt + 1) * 128, 0:511], hsb.ap[:, 1:512], [ts], key=hsb.key)
            else:
                td = P.dma("dG", hs_fm[mt * 128:(mt + 1) * 128, lo:lo + 512], hsb.ap[:, :512], [ts], key=hsb.key)
            hsb.rr(td)
            if ntile == nT - 1:
                tf = P.op("v", lambda e, mt=mt: e.tensor_scalar(acc.ap[:, 0:1],
                                                                self.tails.ap[:, 2 * mt:2 * mt + 1],
                                                                self.smalls_pp("cw1", 3 * mt), None,
                                                                AL.mult),
                          [tt, td] + acc.wdep() + self.smalls.rdep())
                acc.ww(tf); self.tails.rr(tf); self.smalls.rr(tf)
                tf = P.op("v", lambda e, mt=mt: e.scalar_tensor_tensor(
                    acc.ap[:, 0:1], self.tails.ap[:, 2 * mt + 1:2 * mt + 2],
                    self.smalls_pp("cw1", 3 * mt + 1), acc.ap[:, 0:1], AL.mult, AL.add), [tf])
                acc.ww(tf); self.tails.rr(tf)
                tf2 = P.op("s", lambda e, mt=mt: e.activation(out=hsb.ap[:, 0:1], in_=acc.ap[:, 0:1],
                                                              func=AF.Silu, bias=self.smalls_pp("cb1", mt)),
                           [tf] + hsb.wdep())
                hsb.ww(tf2); acc.rr(tf2)
                td = P.dma("dG", hs_fm[mt * 128:(mt + 1) * 128, T - 1:T], hsb.ap[:, 0:1], [tf2, td],
                           key=hsb.key, slow=True)
                hsb.rr(td)
            return td

        tz = P.op("v", lambda e: e.memset(self.tails.ap[:], 0.0), self.tails.wdep() + fence)
        self.tails.ww(tz)
        self.mm_stream(self.w1q_d[:, 0:DH // 2], DM, DH // 2, self.xq_fm, T, evac_mm1,
                       rhs_dep=[tq, tz], lhs_dep=[self.t_wq])
        self.mm_stream(self.w1q_d[:, DH // 2:DH], DM, DH // 2, self.xq_fm, T, evac_mm1,
                       rhs_dep=[tq, tz], lhs_dep=[self.t_wq], mt_off=nDHt // 2)
        tq2 = self.quant_fm_roundtrip(hs_fm, self.hq_fm, T, DH, dep=P.fence())
        contrib = self.dram(self.uid("c1"), [DM, T], F32R)
        ev = self.pool("R32B")
        th = P.op("v", lambda e: e.tensor_scalar(self.sc2.ap[:, 0:1], self.invws.ap[:, 1:2], 0.5, None,
                                                 AL.mult), self.invws.rdep() + self.sc2.wdep())
        self.sc2.ww(th); self.invws.rr(th)

        def evac_mm2(mt, ntile, psb, deps):
            te = P.op("s", lambda e: e.activation(out=ev.ap[:, :512], in_=psb.ap[:, :],
                                                  func=AF.Identity, scale=self.sc2.ap[:, 0:1]),
                      deps + [th] + ev.wdep())
            ev.ww(te); self.sc2.rr(te); psb.rr(te)
            ts = P.dma("dG", contrib[mt * 128:(mt + 1) * 128, ntile * 512:(ntile + 1) * 512],
                       ev.ap[:, :512], [te], key=ev.key)
            ev.rr(ts)
            return ts

        self.mm_stream(self.w2q_d[:, 0:DM // 2], DH, DM // 2, self.hq_fm, T,
                       evac_mm2, rhs_dep=[tq2], lhs_dep=[tq2])
        self.mm_stream(self.w2q_d[:, DM // 2:DM], DH, DM // 2, self.hq_fm, T,
                       evac_mm2, rhs_dep=[tq2], lhs_dep=[tq2], mt_off=DM // 256)
        self.add_contrib_fm(contrib, dep=P.fence())

    def quant_tm(self, src_tm_d, dst_fm_d, T, D, dep=None):
        dep = aslist(dep)
        P = self.P
        xt = self.pool("F32A")
        g = self.pool("F32B")
        qb = self.pool("F32C")
        qi = self.pool("I32A")
        xq = self.pool("B16B")
        ev = self.pool("B16C")
        mx = self.mx
        last = None
        for it in range(T // 128):
            tl = P.dma("dS", xt.ap[:, :D], src_tm_d[it * 128:(it + 1) * 128, :], xt.wdep() + dep + [last],
                       key=xt.key)
            xt.ww(tl)
            tm = P.op("v", lambda e, D=D: e.tensor_reduce(mx.ap[:, 0:1], xt.ap[:, :D], mybir.AxisListType.X,
                                                          AL.max, apply_absolute_value=True),
                      [tl] + mx.wdep())
            tm = P.op("v", lambda e: e.tensor_scalar(mx.ap[:, 1:2], mx.ap[:, 0:1], 1e-5, None, AL.max), [tm])
            tm = P.op("v", lambda e: e.reciprocal(mx.ap[:, 2:3], mx.ap[:, 1:2]), [tm])
            tm = P.op("v", lambda e: e.tensor_scalar(mx.ap[:, 2:3], mx.ap[:, 2:3], 127.0, None, AL.mult),
                      [tm])
            tm = P.op("v", lambda e: e.tensor_scalar(mx.ap[:, 3:4], mx.ap[:, 1:2], 1.0 / 127.0, None,
                                                     AL.mult), [tm])
            mx.ww(tm)
            tqq = self._quant_block(xt, 0, D, mx, g, qb, qi, xq, [tm, tl])
            last = self.transpose_out(xq, D, dst_fm_d, it, BF16, ev, [tqq])
        return last

    def _quant_block(self, srcbuf, off, width, mx, g, qb, qi, xq, deps, out_off0=False):
        P = self.P
        HB = 2048
        tq = None
        for o in range(off, off + width, HB):
            wd = min(HB, off + width - o)
            xo = 0 if out_off0 else o
            t1 = P.op("v", lambda e, o=o, wd=wd: e.tensor_scalar(qi.ap[:, :wd], srcbuf.ap[:, o:o + wd],
                                                                 mx.ap[:, 2:3], 128.0, AL.mult, AL.add),
                      deps + qi.wdep() + ([tq] if tq else []))
            qi.ww(t1); srcbuf.rr(t1); mx.rr(t1)
            t2 = P.op("v", lambda e, wd=wd: e.tensor_copy(qb.ap[:, :wd], qi.ap[:, :wd]), [t1] + qb.wdep())
            qb.ww(t2); qi.rr(t2)
            t3 = P.op("v", lambda e, o=o, wd=wd: e.scalar_tensor_tensor(
                g.ap[:, :wd], srcbuf.ap[:, o:o + wd], mx.ap[:, 2:3], qb.ap[:, :wd], AL.mult, AL.subtract),
                [t2] + deps + g.wdep())
            g.ww(t3); srcbuf.rr(t3); qb.rr(t3); mx.rr(t3)
            t4 = P.op("v", lambda e, wd=wd: e.scalar_tensor_tensor(qb.ap[:, :wd], g.ap[:, :wd], -127.5,
                                                                  qb.ap[:, :wd], AL.is_ge, AL.add), [t3])
            qb.ww(t4); g.rr(t4)
            tq = P.op("v", lambda e, xo=xo, wd=wd: e.tensor_scalar(xq.ap[:, xo:xo + wd], qb.ap[:, :wd],
                                                                   -128.0, mx.ap[:, 3:4], AL.add, AL.mult),
                      [t4] + xq.wdep())
            xq.ww(tq); qb.rr(tq); mx.rr(tq)
        return tq

    def quant_fm_roundtrip(self, src_fm_d, dst_fm_d, T, D, dep=None):
        dep = aslist(dep)
        P = self.P
        tmv = self.pool("F32A")
        g = self.pool("F32B")
        qb = self.pool("F32C")
        qi = self.pool("I32A")
        cb = self.pool("B16A")
        xq = self.pool("B16B")
        ev = self.pool("B16C")
        mx = self.mx
        pst = self.ps["tr"]
        last = None
        for it in range(T // 128):
            te = None
            for half in range(D // 2048):
                sview = src_fm_d.rearrange("(a p) t -> p a t", p=128)[:, half * 16:(half + 1) * 16,
                                                                       it * 128:(it + 1) * 128]
                dview = cb.ap[:, :2048].rearrange("p (a t) -> p a t", t=128)
                tl = P.dma("dS", dview, sview, cb.wdep() + dep + [last, te], key=cb.key)
                cb.ww(tl)
                for grp in range(2):
                    toks = []
                    for b2 in range(8):
                        blk = grp * 8 + b2
                        d = [tl] + self.ident.rdep() + (pst.wdep() if b2 == 0 else [toks[-1]])
                        tt = P.op("t", lambda e, blk=blk, b2=b2: e.transpose(
                            pst.ap[:, b2 * 128:(b2 + 1) * 128], cb.ap[:, blk * 128:(blk + 1) * 128],
                            self.ident.ap[:]), d)
                        toks.append(tt); cb.rr(tt); self.ident.rr(tt)
                    pst.ww(toks[-1])
                    te = P.op("v", lambda e, half=half, grp=grp: e.tensor_copy(
                        tmv.ap[:, half * 2048 + grp * 1024:half * 2048 + (grp + 1) * 1024], pst.ap[:]),
                        [toks[-1]] + (tmv.wdep() if half == 0 and grp == 0 else []))
                    tmv.ww(te); pst.rr(te)
            tm = P.op("v", lambda e, D=D: e.tensor_reduce(mx.ap[:, 0:1], tmv.ap[:, :D],
                                                          mybir.AxisListType.X, AL.max,
                                                          apply_absolute_value=True), [te] + mx.wdep())
            tm = P.op("v", lambda e: e.tensor_scalar(mx.ap[:, 1:2], mx.ap[:, 0:1], 1e-5, None, AL.max), [tm])
            tm = P.op("v", lambda e: e.reciprocal(mx.ap[:, 2:3], mx.ap[:, 1:2]), [tm])
            tm = P.op("v", lambda e: e.tensor_scalar(mx.ap[:, 2:3], mx.ap[:, 2:3], 127.0, None, AL.mult),
                      [tm])
            tm = P.op("v", lambda e: e.tensor_scalar(mx.ap[:, 3:4], mx.ap[:, 1:2], 1.0 / 127.0, None,
                                                     AL.mult), [tm])
            mx.ww(tm)
            for half in range(D // 2048):
                tqq = self._quant_block(tmv, half * 2048, 2048, mx, g, qb, qi, xq, [tm], out_off0=True)
                last = self.transpose_out(xq, 2048, dst_fm_d, it, BF16, ev, [tqq],
                                          dst_row=half * 2048)
        return last

    def sub_attn(self):
        P, T, DM = self.P, self.T, self.DM
        fence = P.fence()
        t1 = self.ln(self.xres, dst_tm=self.xnp_d, dep=fence)
        t2 = self.ln(self.xnp_d, dst_fm=self.xn_fm16, fm_dt=BF16, gbi=1, dep=[t1])
        evq = self.pool("B16D")
        vtb = self.pool("B16E")
        pst = self.ps["tr"]

        def evac_qk(mt, ntile, psb, deps):
            te = P.op("v", lambda e: e.tensor_copy(evq.ap[:, :512], psb.ap[:, :]), deps + evq.wdep())
            evq.ww(te); psb.rr(te)
            ts = P.dma("dG", self.qk_fm[mt * 128:(mt + 1) * 128, ntile * 512:(ntile + 1) * 512],
                       evq.ap[:, :512], [te], key=evq.key)
            evq.rr(ts)
            return ts

        def evac_v(mt, ntile, psb, deps):
            te = P.op("v", lambda e: e.tensor_copy(vtb.ap[:, :512], psb.ap[:, :]), deps + vtb.wdep())
            vtb.ww(te); psb.rr(te)
            toks = []
            for b2 in range(4):
                d = [te] + self.ident.rdep() + (pst.wdep() if b2 == 0 else [toks[-1]])
                tt = P.op("t", lambda e, b2=b2: e.transpose(pst.ap[:, b2 * 128:(b2 + 1) * 128],
                                                            vtb.ap[:, b2 * 128:(b2 + 1) * 128],
                                                            self.ident.ap[:]), d)
                toks.append(tt); vtb.rr(tt); self.ident.rr(tt)
            pst.ww(toks[-1])
            te2 = P.op("v", lambda e: e.tensor_copy(evq.ap[:, :512], pst.ap[:, 0:512]),
                       [toks[-1]] + evq.wdep())
            evq.ww(te2); pst.rr(te2)
            dview = self.v_tm[ntile * 512:(ntile + 1) * 512, mt * 128:(mt + 1) * 128].rearrange(
                "(a p) c -> p a c", p=128)
            sview = evq.ap[:, :512].rearrange("p (a c) -> p a c", c=128)
            ts = P.dma("dG", dview, sview, [te2], key=evq.key)
            evq.rr(ts)
            return ts

        self.mm_stream(self.w["w_qkv"][:, 0:2 * DM], DM, 2 * DM, self.xn_fm16, T, evac_qk, rhs_dep=[t2])
        self.mm_stream(self.w["w_qkv"][:, 2 * DM:3 * DM], DM, DM, self.xn_fm16, T, evac_v, rhs_dep=[t2])
        fatt = P.fence()
        qsb = self.pool("B16A")
        ksb = self.pool("B16B")
        vsb = self.pool("B16C")
        ea = self.pool("B16D")
        eaT = self.pool("B16E")
        aost = self.pool("B16F")
        sms = self.sms
        W = 128
        nw = T // W
        pssc, psav = self.ps["sc"], self.ps["av"]
        last = None
        for wdx in range(nw):
            qv = self.qk_fm.rearrange("(a p) t -> p a t", p=128)[:, 0:8, wdx * W:(wdx + 1) * W]
            tq = P.dma("dS", qsb.ap[:, :8 * W].rearrange("p (a t) -> p a t", t=W), qv,
                       qsb.wdep() + fatt + [last], key=qsb.key)
            qsb.ww(tq)
            lo = max(wdx - 1, 0) * W
            kv = self.qk_fm.rearrange("(a p) t -> p a t", p=128)[:, 8:16, lo:lo + 2 * W]
            tk = P.dma("dS", ksb.ap[:, :8 * 2 * W].rearrange("p (a t) -> p a t", t=2 * W), kv,
                       ksb.wdep() + fatt, key=ksb.key)
            ksb.ww(tk)
            vv = self.v_tm[lo:lo + 2 * W, :].rearrange("(a p) c -> p a c", p=128)
            tv = P.dma("dA", vsb.ap[:, :2 * DM].rearrange("p (a c) -> p a c", c=DM), vv,
                       vsb.wdep() + fatt, key=vsb.key)
            vsb.ww(tv)
            tae = None
            for h in range(16):
                pb = 64 * (h % 2)
                tile = h // 2
                d0 = [tq, tk] + pssc.wdep()
                tmm = P.op("t", lambda e, pb=pb, tile=tile: e.matmul(
                    pssc.ap[:, 0:256], qsb.ap[pb:pb + 64, tile * W:(tile + 1) * W],
                    ksb.ap[pb:pb + 64, tile * 2 * W:(tile + 1) * 2 * W], start=True, stop=False), d0)
                qsb.rr(tmm); ksb.rr(tmm)
                moff = 0 if wdx > 0 else 256
                tmm = P.op("t", lambda e, moff=moff: e.matmul(pssc.ap[:, 0:256], self.ident.ap[:],
                                                              self.maskb.ap[:, moff:moff + 256],
                                                              start=False, stop=True),
                           [tmm] + self.maskb.rdep() + self.ident.rdep())
                pssc.ww(tmm); self.maskb.rr(tmm); self.ident.rr(tmm)
                tr = P.op("v", lambda e: e.tensor_reduce(sms.ap[:, 0:1], pssc.ap[:, 0:256],
                                                         mybir.AxisListType.X, AL.max),
                          [tmm] + sms.wdep())
                tr = P.op("v", lambda e: e.tensor_scalar(sms.ap[:, 1:2], sms.ap[:, 0:1], -0.125, None,
                                                         AL.mult), [tr])
                sms.ww(tr)
                texp = P.op("s", lambda e: e.activation(out=ea.ap[:, :256], in_=pssc.ap[:, 0:256],
                                                        func=AF.Exp, scale=0.125, bias=sms.ap[:, 1:2],
                                                        accum_out=sms.ap[:, 2:3]), [tr] + ea.wdep())
                ea.ww(texp); pssc.rr(texp); sms.ww(texp)
                trc = P.op("v", lambda e: e.reciprocal(sms.ap[:, 3:4], sms.ap[:, 2:3]), [texp])
                sms.ww(trc)
                tnm = P.op("v", lambda e: e.tensor_scalar(ea.ap[:, :256], ea.ap[:, :256], sms.ap[:, 3:4],
                                                          None, AL.mult), [trc])
                ea.ww(tnm); sms.rr(tnm)
                toks = []
                for b2 in range(2):
                    d = [tnm] + self.ident.rdep() + (pst.wdep() if b2 == 0 else [toks[-1]])
                    tt = P.op("t", lambda e, b2=b2: e.transpose(pst.ap[:, b2 * 128:(b2 + 1) * 128],
                                                                ea.ap[:, b2 * 128:(b2 + 1) * 128],
                                                                self.ident.ap[:]), d)
                    toks.append(tt); ea.rr(tt); self.ident.rr(tt)
                pst.ww(toks[-1])
                tev = P.op("v", lambda e: e.tensor_copy(eaT.ap[:, :256], pst.ap[:, 0:256]),
                           [toks[-1]] + eaT.wdep())
                eaT.ww(tev); pst.rr(tev)
                tavm = None
                for k2 in range(2):
                    d = [tev, tv] + (psav.wdep() if k2 == 0 else [tavm])
                    tavm = P.op("t", lambda e, k2=k2, h=h: e.matmul(
                        psav.ap[:64, 0:128], vsb.ap[:, k2 * DM + h * 64:k2 * DM + (h + 1) * 64],
                        eaT.ap[:, k2 * 128:(k2 + 1) * 128], start=(k2 == 0), stop=(k2 == 1)), d)
                    eaT.rr(tavm); vsb.rr(tavm)
                psav.ww(tavm)
                tae = P.op("v", lambda e, pb=pb, tile=tile: e.tensor_copy(
                    aost.ap[pb:pb + 64, tile * W:(tile + 1) * W], psav.ap[:64, 0:128]),
                    [tavm] + (aost.wdep() if h == 0 else [tae]))
                aost.ww(tae); psav.rr(tae)
            dview = self.ao_fm.rearrange("(a p) t -> p a t", p=128)[:, :, wdx * W:(wdx + 1) * W]
            sview = aost.ap[:, :8 * W].rearrange("p (a t) -> p a t", t=W)
            last = P.dma("dG", dview, sview, [tae], key=aost.key)
            aost.rr(last)
        fout = P.fence()
        contrib = self.dram(self.uid("ca"), [DM, T], F32R)
        ev2 = self.pool("R32B")

        def evac_out(mt, ntile, psb, deps):
            te = P.op("v", lambda e: e.tensor_copy(ev2.ap[:, :512], psb.ap[:, :]), deps + ev2.wdep())
            ev2.ww(te); psb.rr(te)
            ts = P.dma("dG", contrib[mt * 128:(mt + 1) * 128, ntile * 512:(ntile + 1) * 512],
                       ev2.ap[:, :512], [te], key=ev2.key)
            ev2.rr(ts)
            return ts

        self.mm_stream(self.w["w_out"], DM, DM, self.ao_fm, T, evac_out, rhs_dep=fout)
        self.add_contrib_fm(contrib, dep=P.fence())

    def final_out(self):
        self.P.dma("dS", self.out[:, :], self.xres[:, :], [self.t_x0])


def make_inputs(ins, x_seq, T):
    DM, DI, DH = 1024, 2048, 4096
    f32 = np.float32
    import ml_dtypes
    bf16 = ml_dtypes.bfloat16
    g = lambda k: np.asarray(ins[k], dtype=f32)
    d = {}
    d["x_in"] = np.ascontiguousarray(x_seq, dtype=f32)
    d["in_projT"] = np.ascontiguousarray(g("L0_attn_in_proj_w").T).astype(bf16)
    d["x_projT"] = np.ascontiguousarray(g("L0_attn_x_proj_w").T).astype(bf16)
    d["dt_projT"] = np.ascontiguousarray(g("L0_attn_dt_proj_w").T).astype(bf16)
    d["out_projT"] = np.ascontiguousarray(g("L0_attn_out_proj_w").T).astype(bf16)
    d["w_qkv"] = np.ascontiguousarray(g("L1_attn_w_qkv")).astype(bf16)
    d["w_out"] = np.ascontiguousarray(g("L1_attn_w_out")).astype(bf16)
    d["w1T"] = np.ascontiguousarray(g("L1_ffn_w1").T, dtype=f32)
    d["w2T"] = np.ascontiguousarray(g("L1_ffn_w2").T, dtype=f32)
    d["wfr"] = np.ascontiguousarray(g("L0_ffn_w_freq_real"), dtype=f32)
    d["wfi"] = np.ascontiguousarray(g("L0_ffn_w_freq_imag"), dtype=f32)
    k = np.arange(DM)
    ang = 2.0 * np.pi * ((k[:, None] * k[None, :]) % DM) / DM
    d["C_d"] = np.cos(ang).astype(f32)
    d["S_d"] = np.sin(ang).astype(f32)
    gb = np.zeros((128, 4 * DM), f32)
    gb[:, 0:DM] = g("L0_attn_ln_gamma")[None, :]
    gb[:, DM:2 * DM] = g("L0_attn_ln_beta")[None, :]
    gb[:, 2 * DM:3 * DM] = g("L1_attn_ln_gamma")[None, :]
    gb[:, 3 * DM:4 * DM] = g("L1_attn_ln_beta")[None, :]
    d["gb"] = gb.astype(bf16)
    d["ident"] = np.eye(128, dtype=bf16)
    qi = np.arange(128)
    maskp = qi[:, None] > qi[None, :]
    masko = qi[:, None] < qi[None, :]
    mb = np.zeros((128, 512), f32)
    mb[:, 0:256][np.concatenate([maskp, masko], axis=1)] = -1e38
    mb[:, 256:512][np.concatenate([masko, np.ones((128, 128), bool)], axis=1)] = -1e38
    d["maskb"] = mb.astype(bf16)
    sm = np.zeros((128, 504), f32)
    cw = g("L0_attn_conv_w")[:, 0, :].reshape(DI // 128, 128, 4)
    for c in range(DI // 128):
        sm[:, 4 * c:4 * c + 4] = cw[c]
    sm[:, 64:80] = g("L0_attn_conv_b").reshape(DI // 128, 128).T
    sm[:, 80:96] = g("L0_attn_D_skip").reshape(DI // 128, 128).T
    sm[:, 96:112] = g("L0_attn_dt_proj_b").reshape(DI // 128, 128).T
    al = g("L0_attn_A_log").reshape(DI // 128, 128, 16)
    for c in range(DI // 128):
        sm[:, 112 + 16 * c:112 + 16 * (c + 1)] = al[c]
    cw1 = g("L1_ffn_conv_w")[:, 0, :].reshape(DH // 128, 128, 3)
    for c in range(DH // 128):
        sm[:, 368 + 3 * c:368 + 3 * c + 3] = cw1[c]
    sm[:, 464:496] = g("L1_ffn_conv_b").reshape(DH // 128, 128).T
    sm[:, 496:504] = 0.5 * g("L0_ffn_bias").reshape(DM // 128, 128).T
    d["smalls"] = sm
    return d


def kernel(**inputs):
    import numpy as np
    from concourse.bass_utils import run_bass_kernel_spmd
    T = 8192
    x = np.asarray(inputs["L0_attn_x"], dtype=np.float32)
    B = x.shape[0]
    mk = MK(T)
    nc = mk.build()
    base = make_inputs(inputs, x[0], T)
    in_maps = []
    for core in range(8):
        m = dict(base)
        m["x_in"] = np.ascontiguousarray(x[core % B], dtype=np.float32)
        in_maps.append(m)
    res = run_bass_kernel_spmd(nc, in_maps, core_ids=list(range(8)))
    out = np.stack([np.asarray(res.results[b]["out"]) for b in range(B)], axis=0)
    return out.astype(np.float32)


# version 07
# speedup vs baseline: 1.2880x; 1.0537x over previous
import numpy as np
from contextlib import ExitStack
import concourse.bass as bass
import concourse.mybir as mybir

F32 = mybir.dt.float32
F32R = mybir.dt.float32r
BF16 = mybir.dt.bfloat16
I32 = mybir.dt.int32
AL = mybir.AluOpType
AF = mybir.ActivationFunctionType


class Prog:
    ENG = {"v": "vector", "s": "scalar", "t": "tensor", "g": "gpsimd", "y": "sync"}

    def __init__(self, nc, ctx):
        self.nc = nc
        self.ctx = ctx
        self.sems = {}
        self.items = {e: [] for e in self.ENG}
        self.cnt = {}
        for q in ["v", "s", "t", "gc"]:
            self._newsem(q)

    def _newsem(self, name):
        self.sems[name] = self.ctx.enter_context(self.nc.semaphore("sem_" + name))
        self.cnt[name] = 0

    def _emit(self, eng, sem, step, fn, deps):
        deps = [d for d in (deps or []) if d is not None]
        self.cnt[sem] += step
        tok = (sem, self.cnt[sem])
        self.items[eng].append((fn, tuple(deps), sem, step))
        return tok

    def op(self, eng, fn, deps=None):
        assert eng in ("v", "s", "t", "g")
        return self._emit(eng, "gc" if eng == "g" else eng, 1, fn, deps)

    def dma(self, q, out, in_, deps=None, key=None, slow=False):
        eng = {"dS": "y", "dA": "s", "dG": "g"}[q]
        sem = f"{q}_{key if key is not None else 'misc'}"
        if sem not in self.sems:
            self._newsem(sem)
        if slow:
            return self._emit(eng, sem, 16,
                              lambda e: e.dma_start(out=out, in_=in_, allow_slow_non_contiguous=True),
                              deps)
        return self._emit(eng, sem, 16, lambda e: e.dma_start(out=out, in_=in_), deps)

    def fence(self):
        return [(s, c) for s, c in self.cnt.items() if c > 0]

    def build(self, block):
        sems = self.sems
        for eng, engname in self.ENG.items():
            items = self.items[eng]
            if not items and engname != "sync":
                continue

            def make(items, eng):
                def body(e):
                    water = {}
                    dedup = True
                    for fn, deps, sem, step in items:
                        for (ds, dv) in deps:
                            if dedup and water.get(ds, 0) >= dv:
                                continue
                            e.wait_ge(sems[ds], dv)
                            water[ds] = dv
                        fn(e).then_inc(sems[sem], step)
                    if eng == "y":
                        for s2, c in self.cnt.items():
                            if c > 0 and water.get(s2, 0) < c:
                                e.wait_ge(sems[s2], c)
                return body
            getattr(block, engname)(make(items, eng))


_BUFN = [0]


class Buf:
    __slots__ = ("ap", "w", "r", "key", "sticky", "sr")

    def __init__(self, ap, sticky=False):
        self.ap = ap
        self.w = None
        self.r = []
        self.sticky = sticky
        self.sr = {}
        _BUFN[0] += 1
        self.key = _BUFN[0]

    def wdep(self):
        base = self.r + ([self.w] if self.w else [])
        if self.sticky:
            base = base + [(s, v) for s, v in self.sr.items()]
        return base

    def rdep(self):
        return [self.w] if self.w else []

    def ww(self, tok):
        if self.sticky:
            for (s, v) in self.r:
                if self.sr.get(s, -1) < v:
                    self.sr[s] = v
            if self.w is not None:
                s, v = self.w
                if self.sr.get(s, -1) < v:
                    self.sr[s] = v
        self.w = tok
        self.r = []

    def rr(self, tok):
        self.r.append(tok)
        if len(self.r) > 24:
            self.r = self.r[-24:]


def aslist(d):
    return d if isinstance(d, list) else ([d] if d else [])


class MK:
    NS = 16
    DT = 64

    def __init__(self, T, enabled=(1, 2, 3, 4, 5, 6)):
        self.T = T
        self.DM = 1024
        self.DI = 2048
        self.DH = 4096
        self.enabled = tuple(enabled)
        self.nc = bass.Bass("TRN2", target_bir_lowering=False)
        self.ctx = ExitStack()
        self._uid = 0

    def uid(self, p="u"):
        self._uid += 1
        return f"{p}_{self._uid}"

    def dram(self, name, shape, dt=F32, kind=None):
        t = self.nc.dram_tensor(name, shape, dt, kind=kind) if kind else self.nc.dram_tensor(name, shape, dt)
        return t.ap()

    def sbuf(self, name, shape, dt=F32):
        return Buf(self.ctx.enter_context(self.nc.sbuf_tensor(name, shape, dt)))

    def pool(self, tag):
        return self._pool[tag]

    def tfm(self, name, C, T, tw, dt=F32):
        return self.dram(name, [C // 128, T // tw, 128, tw], dt)

    @staticmethod
    def tfm_tile(ap4, c, n):
        return ap4[c, n]

    @staticmethod
    def tfm_rhs(ap4, k0, k1, ntile):
        return ap4[k0:k1, ntile].rearrange("k p t -> p k t")

    def psumb(self, name, shape, dt=F32):
        return Buf(self.ctx.enter_context(self.nc.psum_tensor(name, shape, dt)), sticky=True)

    def build(self):
        nc, T, DM, DI, DH = self.nc, self.T, self.DM, self.DI, self.DH
        ein = lambda n, s, dt=F32: self.dram(n, s, dt, kind="ExternalInput")
        self.x_in = ein("x_in", [T, DM])
        self.out = self.dram("out", [T, DM], F32, kind="ExternalOutput")
        w = {}
        w["in_projT"] = ein("in_projT", [DM, 2 * DI], BF16)
        w["x_projT"] = ein("x_projT", [DI, 96], BF16)
        w["dt_projT"] = ein("dt_projT", [self.DT, DI], BF16)
        w["out_projT"] = ein("out_projT", [DI, DM], BF16)
        w["w_qkv"] = ein("w_qkv", [DM, 3 * DM], BF16)
        w["w_out"] = ein("w_out", [DM, DM], BF16)
        w["w1T"] = ein("w1T", [DM, DH])
        w["w2T"] = ein("w2T", [DH, DM])
        w["wfr"] = ein("wfr", [DM, DM], F32R)
        w["wfi"] = ein("wfi", [DM, DM], F32R)
        w["C_d"] = ein("C_d", [DM, DM], F32R)
        w["S_d"] = ein("S_d", [DM, DM], F32R)
        w["gb"] = ein("gb", [128, 4 * DM], BF16)
        w["ident"] = ein("ident", [128, 128], BF16)
        w["maskb"] = ein("maskb", [128, 512], BF16)
        w["smalls"] = ein("smalls", [128, 504])
        self.w = w
        self.xres = self.dram("xres", [T, DM])
        self.xnp_d = self.dram("xnp_d", [T, DM])
        self.xn_fm16 = self.dram("xn_fm16", [DM, T], BF16)
        self.xn_fmR = self.dram("xn_fmR", [DM, T], F32R)
        self.u2_fm = self.dram("u2_fm", [DI, T], BF16)
        self.sz_fm = self.dram("sz_fm", [DI, T], BF16)
        self.du_fm = self.dram("du_fm", [DI, T], BF16)
        self.delta_fm = self.dram("delta_fm", [DI, T])
        self.dt_d = self.dram("dt_d", [self.DT, T], BF16)
        self.BC_d = self.dram("BC_d", [32, T])
        self.yg_fm = self.dram("yg_fm", [DI, T], BF16)
        self.w1q_d = self.dram("w1q_d", [DM, DH], BF16)
        self.w2q_d = self.dram("w2q_d", [DH, DM], BF16)
        self.wsc_d = self.dram("wsc_d", [1, 2])
        self.xq_fm = self.dram("xq_fm", [DM, T], BF16)
        self.hq_fm = self.dram("hq_fm", [DH, T], BF16)
        self.qk_fm = self.dram("qk_fm", [2 * DM, T], BF16)
        self.v_tm = self.dram("v_tm", [T, DM], BF16)
        self.ao_fm = self.dram("ao_fm", [DM, T], BF16)
        self.Q1_d = self.dram("Q1_d", [DM, DM], F32R)
        self.Q2_d = self.dram("Q2_d", [DM, DM], F32R)
        self.M_d = self.dram("M_d", [DM, DM], F32R)

        self.ident = self.sbuf("sb_ident", [128, 128], BF16)
        self.identR = self.sbuf("sb_identR", [128, 128], F32R)
        self.maskb = self.sbuf("sb_maskb", [128, 512], BF16)
        self.gb = self.sbuf("sb_gb", [128, 4 * DM], BF16)
        self.smalls = self.sbuf("sb_smalls", [128, 504], F32)
        self.A_neg = self.sbuf("sb_Aneg", [128, 256], F32)
        self.invws = self.sbuf("sb_invws", [128, 4], F32)
        self.states = self.sbuf("sb_states", [128, 256], F32)
        self.tails = self.sbuf("sb_tails", [128, 128], F32)
        self.stv = self.sbuf("sb_stv", [128, 16], F32)
        self.mx = self.sbuf("sb_mx", [128, 4], F32)
        self.sms = self.sbuf("sb_sms", [128, 4], F32)
        self.sc2 = self.sbuf("sb_sc2", [128, 2], F32)
        self.ones1 = self.sbuf("sb_ones1", [1, 128], BF16)
        self.wres = self.sbuf("sb_wres", [128, 16384], BF16)
        self.wresR = self.sbuf("sb_wresR", [128, 8192], F32R)
        self.rhs16 = [self.sbuf("sb_rhsA", [128, 4096], BF16), self.sbuf("sb_rhsB", [128, 4096], BF16)]
        self.rhsR = [self.sbuf("sb_rhsRA", [128, 2048], F32R), self.sbuf("sb_rhsRB", [128, 2048], F32R)]
        self._pool = {}
        for tag, cols, dt in [
            ("F32A", 4224, F32), ("F32B", 2048, F32), ("F32C", 2048, F32),
            ("F32D", 512, F32), ("F32E", 512, F32), ("I32A", 2048, I32),
            ("B16A", 2048, BF16), ("B16B", 2048, BF16), ("B16C", 2048, BF16),
            ("B16D", 1024, BF16), ("B16E", 1024, BF16), ("B16F", 1024, BF16),
            ("B16G", 1024, BF16),
            ("BIGB", 4096, BF16), ("BIGC", 4096, BF16),
            ("R32A", 1024, F32R), ("R32B", 1024, F32R),
        ]:
            self._pool[tag] = Buf(self.ctx.enter_context(self.nc.sbuf_tensor("pool_" + tag, [128, cols], dt)))
        self.ps = {}
        for i in range(4):
            self.ps[f"mm{i}"] = self.psumb(f"ps_mm{i}", [128, 512], F32)
        self.ps["tr"] = self.psumb("ps_tr", [128, 1024], BF16)
        self.ps["trR"] = self.psumb("ps_trR", [128, 512], F32R)
        self.ps["sc"] = self.psumb("ps_sc", [128, 512], F32)
        self.ps["av"] = self.psumb("ps_av", [128, 512], F32)

        self.P = Prog(nc, self.ctx)
        self.prelude()
        subs = {1: self.sub_l0ffn, 2: self.sub_mamba, 3: self.sub_l0ffn,
                4: self.sub_l1ffn, 5: self.sub_attn, 6: self.sub_l1ffn}
        for i in self.enabled:
            subs[i]()
        self.final_out()
        with nc.Block() as block:
            self.P.build(block)
        self.ctx.close()
        return nc

    def mm_stream(self, lhsT_d, K, M, rhs_d, N, evac, rhs_dt=BF16, rhs_dep=None,
                  lhs_dep=None, mt_list=None, mt_off=0):
        P = self.P
        nK = (K + 127) // 128
        KG = 4 if rhs_dt == F32R else 8
        ngrp = (nK + KG - 1) // KG
        wbuf = self.wresR if rhs_dt == F32R else self.wres
        wcap = 8192 if rhs_dt == F32R else 16384
        resident = nK * M <= wcap
        ldep = aslist(lhs_dep)
        full_chunks = (K % 128 == 0)
        wtok = None
        if resident:
            deps = wbuf.wdep() + ldep
            if full_chunks:
                sview = lhsT_d.rearrange("(k p) m -> p k m", p=128)[:, :, :M]
                dview = wbuf.ap[:, :nK * M].rearrange("p (k m) -> p k m", m=M)
                wtok = P.dma("dS", dview, sview, deps, key=wbuf.key)
            else:
                tt = None
                for k in range(nK):
                    pk = min(128, K - k * 128)
                    tt = P.dma("dS", wbuf.ap[:pk, k * M:(k + 1) * M], lhsT_d[k * 128:k * 128 + pk, :M],
                               deps if k == 0 else [tt], key=wbuf.key)
                wtok = tt
            wbuf.ww(wtok)
        rbufs = self.rhsR if rhs_dt == F32R else self.rhs16
        rdeps = aslist(rhs_dep)
        mts = mt_list if mt_list is not None else list(range((M + 127) // 128))
        nt_cols = 512
        mt_groups = [mts[i:i + 4] for i in range(0, len(mts), 4)]
        nNT = N // nt_cols

        def load_rhs(b, k0, k1, ntile, extra):
            deps = b.wdep() + rdeps + extra
            if full_chunks:
                sview = rhs_d.rearrange("(k p) n -> p k n", p=128)[:, k0:k1,
                                                                   ntile * nt_cols:(ntile + 1) * nt_cols]
                dview = b.ap[:, :(k1 - k0) * nt_cols].rearrange("p (k n) -> p k n", n=nt_cols)
                return P.dma("dA", dview, sview, deps, key=b.key)
            tt = None
            for k in range(k0, k1):
                pk = min(128, K - k * 128)
                tt = P.dma("dA", b.ap[:pk, (k - k0) * nt_cols:(k - k0 + 1) * nt_cols],
                           rhs_d[k * 128:k * 128 + pk, ntile * nt_cols:(ntile + 1) * nt_cols],
                           deps if k == k0 else [tt], key=b.key)
            return tt

        if ngrp == 1:
            rtoks = {}
            rtoks[0] = load_rhs(rbufs[0], 0, nK, 0, [])
            rbufs[0].ww(rtoks[0])
            for ntile in range(nNT):
                b = rbufs[ntile % 2]
                tt = rtoks[ntile]
                if ntile + 1 < nNT:
                    nb = rbufs[(ntile + 1) % 2]
                    rtoks[ntile + 1] = load_rhs(nb, 0, nK, ntile + 1, [])
                    nb.ww(rtoks[ntile + 1])
                for mts_g in mt_groups:
                    mm_last = {}
                    for j, mt in enumerate(mts_g):
                        ms = min(128, M - mt * 128)
                        psb = self.ps[f"mm{j % 4}"]
                        mm_tok = None
                        for k in range(nK):
                            pk = min(128, K - k * 128)
                            wap = wbuf.ap[:pk, k * M + mt * 128:k * M + mt * 128 + ms]
                            d = [wtok, tt] + (psb.wdep() if k == 0 else [mm_tok])
                            mm_tok = P.op("t", lambda e, wap=wap, psb=psb, b=b, ms=ms, pk=pk, k=k,
                                          st=(k == 0), sp=(k == nK - 1):
                                          e.matmul(psb.ap[:ms, :nt_cols], wap,
                                                   b.ap[:pk, k * nt_cols:(k + 1) * nt_cols],
                                                   start=st, stop=sp), d)
                            psb.ww(mm_tok)
                            b.rr(mm_tok)
                            wbuf.rr(mm_tok)
                        mm_last[mt] = mm_tok
                    for j, mt in enumerate(mts_g):
                        psb = self.ps[f"mm{j % 4}"]
                        et = evac(mt + mt_off, ntile, psb, [mm_last[mt]])
                        psb.rr(et)
                        b.rr(et)
        else:
            for ntile in range(nNT):
                for mts_g in mt_groups:
                    mm_tok = {mt: None for mt in mts_g}
                    for kg in range(ngrp):
                        b = rbufs[kg % 2]
                        k0 = kg * KG
                        k1 = min(nK, k0 + KG)
                        tt = load_rhs(b, k0, k1, ntile, [])
                        b.ww(tt)
                        if not resident:
                            base = (kg % 2) * 8192
                            dw = self.wres.wdep() + ldep
                            if full_chunks:
                                sview = lhsT_d.rearrange("(k p) m -> p k m", p=128)[:, k0:k1, :M]
                                dview = self.wres.ap[:, base:base + (k1 - k0) * M].rearrange(
                                    "p (k m) -> p k m", m=M)
                                tw = P.dma("dS", dview, sview, dw, key=self.wres.key)
                            else:
                                tw = None
                                for k in range(k0, k1):
                                    pk = min(128, K - k * 128)
                                    tw = P.dma("dS", self.wres.ap[:pk, base + (k - k0) * M:
                                                                  base + (k - k0 + 1) * M],
                                               lhsT_d[k * 128:k * 128 + pk, :M], dw if k == k0 else [tw],
                                               key=self.wres.key)
                            self.wres.ww(tw)
                            wtok = tw
                        for j, mt in enumerate(mts_g):
                            ms = min(128, M - mt * 128)
                            psb = self.ps[f"mm{j % 4}"]
                            for k in range(k0, k1):
                                pk = min(128, K - k * 128)
                                if resident:
                                    wap = wbuf.ap[:pk, k * M + mt * 128:k * M + mt * 128 + ms]
                                else:
                                    base = (kg % 2) * 8192
                                    wap = self.wres.ap[:pk, base + (k - k0) * M + mt * 128:
                                                       base + (k - k0) * M + mt * 128 + ms]
                                d = [wtok, tt]
                                if k == 0:
                                    d += psb.wdep()
                                else:
                                    d.append(mm_tok[mt])
                                mm_tok[mt] = P.op("t", lambda e, wap=wap, psb=psb, b=b, ms=ms, pk=pk,
                                                  kk=(k - k0), st=(k == 0), sp=(k == nK - 1):
                                                  e.matmul(psb.ap[:ms, :nt_cols], wap,
                                                           b.ap[:pk, kk * nt_cols:(kk + 1) * nt_cols],
                                                           start=st, stop=sp), d)
                                psb.ww(mm_tok[mt])
                                b.rr(mm_tok[mt])
                                (wbuf if resident else self.wres).rr(mm_tok[mt])
                    for j, mt in enumerate(mts_g):
                        psb = self.ps[f"mm{j % 4}"]
                        et = evac(mt + mt_off, ntile, psb, [mm_tok[mt]])
                        psb.rr(et)

    def prelude(self):
        P, w = self.P, self.w
        t = P.dma("dS", self.ident.ap[:], w["ident"][:, :], key=self.ident.key); self.ident.ww(t)
        t = P.dma("dS", self.maskb.ap[:], w["maskb"][:, :], key=self.maskb.key); self.maskb.ww(t)
        t = P.dma("dS", self.gb.ap[:], w["gb"][:, :], key=self.gb.key); self.gb.ww(t)
        ts = P.dma("dS", self.smalls.ap[:], w["smalls"][:, :], key=self.smalls.key); self.smalls.ww(ts)
        ti = P.op("v", lambda e: e.tensor_copy(self.identR.ap[:], self.ident.ap[:]),
                  self.ident.rdep() + self.identR.wdep())
        self.identR.ww(ti); self.ident.rr(ti)
        self.csts = self.sbuf("sb_csts", [128, 2], F32)
        tc1 = P.op("v", lambda e: e.memset(self.csts.ap[:, 0:1], 1e-5), self.csts.wdep())
        tc2 = P.op("v", lambda e: e.memset(self.csts.ap[:, 1:2], 1.0), [tc1])
        self.csts.ww(tc2)
        self.nc.const_aps.register(F32, 1e-5, self.csts.ap[:, 0:1])
        self.nc.const_aps.register(F32, 1.0, self.csts.ap[:, 1:2])
        self.t_x0 = P.dma("dS", self.xres[:, :], self.x_in[:, :])
        self.off = {"cw0": 0, "cb0": 64, "dsk": 80, "dtb": 96, "A": 112, "cw1": 368, "cb1": 464, "fbias": 496}
        ta = P.op("s", lambda e: e.activation(out=self.A_neg.ap[:], in_=self.smalls.ap[:, 112:368],
                                              func=AF.Exp), [ts] + self.A_neg.wdep())
        self.smalls.rr(ta)
        ta = P.op("v", lambda e: e.tensor_scalar(self.A_neg.ap[:], self.A_neg.ap[:], -1.0, None, AL.mult), [ta])
        self.A_neg.ww(ta)
        tz = P.op("v", lambda e: e.memset(self.states.ap[:], 0.0), self.states.wdep())
        self.states.ww(tz)
        tz = P.op("v", lambda e: e.memset(self.tails.ap[:], 0.0), self.tails.wdep())
        self.tails.ww(tz)
        to1 = P.op("v", lambda e: e.memset(self.ones1.ap[:], 1.0), self.ones1.wdep())
        self.ones1.ww(to1)
        self.quant_weights()
        self.build_M()

    def smalls_pp(self, key, idx):
        return self.smalls.ap[:, self.off[key] + idx:self.off[key] + idx + 1]

    def quant_weights(self):
        P = self.P
        HB = 2048
        buf = self.pool("F32A")
        g = self.pool("F32B")
        qb = self.pool("F32C")
        qi = self.pool("I32A")
        qo = self.pool("B16A")
        acc = self.sbuf("q_acc", [128, 2], F32)
        part = self.sbuf("q_part", [128, 64], F32)
        sval = self.sbuf("q_sval", [1, 1], F32)
        onec = self.sbuf("q_onec", [128, 1], F32)
        totb = self.sbuf("q_totb", [128, 1], F32)
        rws = self.sbuf("q_rws", [128, 2], F32)
        toc = P.op("v", lambda e: e.memset(onec.ap[:], 1.0), onec.wdep())
        onec.ww(toc)
        self.t_wq = None
        for (wT_d, dst, K, N, idx) in [(self.w["w1T"], self.w1q_d, self.DM, self.DH, 0),
                                       (self.w["w2T"], self.w2q_d, self.DH, self.DM, 1)]:
            nH = (N + HB - 1) // HB
            tprev = None
            ncols = 0
            for kk in range(K // 128):
                for h in range(nH):
                    hw = min(HB, N - h * HB)
                    tl = P.dma("dS", buf.ap[:, :hw], wT_d[kk * 128:(kk + 1) * 128, h * HB:h * HB + hw],
                               buf.wdep() + [tprev], key=buf.key)
                    buf.ww(tl)
                    tv = P.op("s", lambda e, hw=hw: e.activation(out=g.ap[:, :hw], in_=buf.ap[:, :hw],
                                                                 func=AF.Abs, accum_out=acc.ap[:, 0:1]),
                              [tl] + g.wdep() + acc.wdep())
                    g.ww(tv); acc.ww(tv); buf.rr(tv)
                    tc = P.op("v", lambda e, c=ncols: e.tensor_copy(part.ap[:, c:c + 1], acc.ap[:, 0:1]),
                              [tv] + (part.wdep() if ncols == 0 else []))
                    part.ww(tc); acc.rr(tc)
                    tprev = tc
                    ncols += 1
            t1 = P.op("v", lambda e, nc_=ncols: e.tensor_reduce(totb.ap[:], part.ap[:, :nc_],
                                                                mybir.AxisListType.X, AL.add),
                      [tprev] + totb.wdep())
            totb.ww(t1); part.rr(t1)
            psb = self.ps["sc"]
            tmm = P.op("t", lambda e: e.matmul(psb.ap[:1, :1], totb.ap[:], onec.ap[:],
                                               start=True, stop=True),
                       [t1, toc] + psb.wdep())
            psb.ww(tmm); totb.rr(tmm); onec.rr(tmm)
            tsv = P.op("v", lambda e, d=1.0 / (K * N): e.tensor_scalar(sval.ap[:], psb.ap[0:1, 0:1], d, 1e-5,
                                                                      AL.mult, AL.max),
                       [tmm] + sval.wdep())
            sval.ww(tsv); psb.rr(tsv)
            tst = P.dma("dS", self.wsc_d[0:1, idx:idx + 1], sval.ap[:], [tsv], key=sval.key)
            sval.rr(tst)
            tbc = P.dma("dG", self.invws.ap[:, idx:idx + 1], self.wsc_d[0, idx:idx + 1].partition_broadcast(128),
                        [tst] + self.invws.wdep(), key=self.invws.key)
            self.invws.ww(tbc)
            trw = P.op("v", lambda e, idx=idx: e.reciprocal(rws.ap[:, idx:idx + 1],
                                                            self.invws.ap[:, idx:idx + 1]),
                       [tbc] + (rws.wdep() if idx == 0 else []))
            rws.ww(trw); self.invws.rr(trw)
            tpr = None
            for kk in range(K // 128):
                for h in range(nH):
                    hw = min(HB, N - h * HB)
                    tl = P.dma("dS", buf.ap[:, :hw], wT_d[kk * 128:(kk + 1) * 128, h * HB:h * HB + hw],
                               buf.wdep() + [tpr, trw], key=buf.key)
                    buf.ww(tl)
                    tg = P.op("v", lambda e, idx=idx, hw=hw: e.tensor_scalar(g.ap[:, :hw], buf.ap[:, :hw],
                                                                             rws.ap[:, idx:idx + 1], None,
                                                                             AL.mult), [tl, trw] + g.wdep())
                    g.ww(tg); buf.rr(tg); rws.rr(tg)
                    tg = P.op("v", lambda e, hw=hw: e.tensor_scalar(g.ap[:, :hw], g.ap[:, :hw], -1.5, 1.5,
                                                                    AL.max, AL.min), [tg])
                    g.ww(tg)
                    tq = P.op("v", lambda e, hw=hw: e.tensor_scalar(qi.ap[:, :hw], g.ap[:, :hw], 128.0, None,
                                                                    AL.add), [tg] + qi.wdep())
                    qi.ww(tq)
                    tq2 = P.op("v", lambda e, hw=hw: e.tensor_copy(qb.ap[:, :hw], qi.ap[:, :hw]),
                               [tq] + qb.wdep())
                    qb.ww(tq2); qi.rr(tq2)
                    tg2 = P.op("v", lambda e, hw=hw: e.scalar_tensor_tensor(g.ap[:, :hw], g.ap[:, :hw], 128.0,
                                                                            qb.ap[:, :hw], AL.add,
                                                                            AL.subtract), [tq2, tg])
                    g.ww(tg2); qb.rr(tg2)
                    tq3 = P.op("v", lambda e, hw=hw: e.scalar_tensor_tensor(qb.ap[:, :hw], g.ap[:, :hw], 0.5,
                                                                            qb.ap[:, :hw], AL.is_ge,
                                                                            AL.add), [tg2])
                    qb.ww(tq3); g.rr(tq3)
                    tq4 = P.op("v", lambda e, hw=hw: e.tensor_scalar(qb.ap[:, :hw], qb.ap[:, :hw], -128.0,
                                                                     None, AL.add), [tq3])
                    qb.ww(tq4)
                    tq5 = P.op("v", lambda e, hw=hw: e.tensor_scalar(qo.ap[:, :hw], qb.ap[:, :hw], -1.0, 1.0,
                                                                     AL.max, AL.min), [tq4] + qo.wdep())
                    qo.ww(tq5); qb.rr(tq5)
                    tpr = P.dma("dG", dst[kk * 128:(kk + 1) * 128, h * HB:h * HB + hw], qo.ap[:, :hw],
                                [tq5], key=qo.key)
                    qo.rr(tpr)
            self.t_wq = tpr if self.t_wq is None else tpr

    def build_M(self):
        P, w, DM = self.P, self.w, self.DM
        ev = self.pool("R32A")
        l1 = self.sbuf("ml1", [128, 128], F32R)
        l2 = self.sbuf("ml2", [128, 128], F32R)
        l2n = self.sbuf("ml2n", [128, 128], F32R)
        r1 = Buf(self.rhsR[0].ap[:, 0:512])
        r2 = Buf(self.rhsR[1].ap[:, 0:512])
        r1.w = self.rhsR[0].w; r2.w = self.rhsR[1].w

        def pair(A1, B1, A2, B2, neg2, out_d, scale):
            td = None
            for mt in range(DM // 128):
                for ntile in range(DM // 512):
                    psb = self.ps["mm0"]
                    mm_tok = None
                    for kk in range(DM // 128):
                        tl1 = P.dma("dS", l1.ap[:], A1[kk * 128:(kk + 1) * 128, mt * 128:(mt + 1) * 128],
                                    l1.wdep(), key=l1.key)
                        l1.ww(tl1)
                        tr1 = P.dma("dA", r1.ap[:], B1[kk * 128:(kk + 1) * 128, ntile * 512:(ntile + 1) * 512],
                                    r1.wdep(), key=r1.key)
                        r1.ww(tr1)
                        tl2 = P.dma("dS", l2.ap[:], A2[kk * 128:(kk + 1) * 128, mt * 128:(mt + 1) * 128],
                                    l2.wdep(), key=l2.key)
                        l2.ww(tl2)
                        tr2 = P.dma("dA", r2.ap[:], B2[kk * 128:(kk + 1) * 128, ntile * 512:(ntile + 1) * 512],
                                    r2.wdep(), key=r2.key)
                        r2.ww(tr2)
                        d = [tl1, tr1] + (psb.wdep() if kk == 0 else [mm_tok])
                        mm_tok = P.op("t", lambda e, f=(kk == 0): e.matmul(psb.ap[:, :], l1.ap[:], r1.ap[:],
                                                                          start=f, stop=False), d)
                        psb.ww(mm_tok); l1.rr(mm_tok); r1.rr(mm_tok)
                        lu, dl = l2, [tl2]
                        if neg2:
                            tn = P.op("v", lambda e: e.tensor_scalar(l2n.ap[:], l2.ap[:], -1.0, None,
                                                                    AL.mult), [tl2] + l2n.wdep())
                            l2n.ww(tn); l2.rr(tn)
                            lu, dl = l2n, [tn]
                        mm_tok = P.op("t", lambda e, lu=lu, lk=(kk == DM // 128 - 1): e.matmul(
                            psb.ap[:, :], lu.ap[:], r2.ap[:], start=False, stop=lk), dl + [tr2, mm_tok])
                        psb.ww(mm_tok); lu.rr(mm_tok); r2.rr(mm_tok)
                    te = P.op("s", lambda e, sc=scale: e.activation(out=ev.ap[:, :512], in_=psb.ap[:, :],
                                                                   func=AF.Identity, scale=sc),
                              [mm_tok] + ev.wdep())
                    ev.ww(te); psb.rr(te)
                    td = P.dma("dS", out_d[mt * 128:(mt + 1) * 128, ntile * 512:(ntile + 1) * 512],
                               ev.ap[:, :512], [te], key=ev.key)
                    ev.rr(td)
            return td

        pair(w["wfr"], w["C_d"], w["wfi"], w["S_d"], False, self.Q1_d, 1.0)
        pair(w["wfi"], w["C_d"], w["wfr"], w["S_d"], True, self.Q2_d, 1.0)
        self.t_M = pair(self.Q1_d, w["C_d"], self.Q2_d, w["S_d"], True, self.M_d, 1.0 / DM)

    def ln(self, src_d, dst_tm=None, dst_fm=None, fm_dt=BF16, gbi=None, dep=None):
        dep = aslist(dep)
        P, T, DM = self.P, self.T, self.DM
        bxt, xts = self.slots("F32A", 1024, 2)
        sq = self.pool("F32B")
        bxo, xos = self.slots("F32C", 1024, 2)
        bxb, xbs = self.slots("B16A", 1024, 2)
        xb2 = self.pool("B16B")
        xr = self.pool("R32A")
        evb = self.pool("B16C")
        evr = self.pool("R32B")
        stv0 = self.stv
        stvs = [Buf(stv0.ap[:, 0:8]), Buf(stv0.ap[:, 8:16])]
        for b in stvs:
            b.w = stv0.w; b.r = list(stv0.r)
        last = None
        for it in range(T // 128):
            sl = it % 2
            xt, xo, xb, stv = xts[sl], xos[sl], xbs[sl], stvs[sl]
            tl = P.dma("dS", xt.ap[:], src_d[it * 128:(it + 1) * 128, :], xt.wdep() + dep + [last],
                       key=xt.key)
            xt.ww(tl)
            t1 = P.op("s", lambda e, xt=xt, stv=stv: e.activation(out=sq.ap[:, :1024], in_=xt.ap[:],
                                                                  func=AF.Identity,
                                                                  accum_out=stv.ap[:, 0:1]),
                      [tl] + sq.wdep() + stv.wdep())
            t2 = P.op("s", lambda e, xt=xt, stv=stv: e.activation(out=sq.ap[:, :1024], in_=xt.ap[:],
                                                                  func=AF.Square,
                                                                  accum_out=stv.ap[:, 1:2]), [t1])
            sq.ww(t2); stv.ww(t2); xt.rr(t2)
            t3 = P.op("v", lambda e, d=1.0 / DM, stv=stv: e.tensor_scalar(stv.ap[:, 2:3], stv.ap[:, 0:1],
                                                                         d, None, AL.mult), [t2])
            t4 = P.op("v", lambda e, d=1.0 / DM, stv=stv: e.tensor_scalar(stv.ap[:, 3:4], stv.ap[:, 1:2],
                                                                         d, None, AL.mult), [t2])
            t5 = P.op("v", lambda e, stv=stv: e.tensor_tensor(stv.ap[:, 6:7], stv.ap[:, 2:3],
                                                              stv.ap[:, 2:3], AL.mult), [t3])
            t6 = P.op("v", lambda e, stv=stv: e.tensor_tensor(stv.ap[:, 3:4], stv.ap[:, 3:4],
                                                              stv.ap[:, 6:7], AL.subtract), [t4, t5])
            t7 = P.op("s", lambda e, stv=stv: e.activation(out=stv.ap[:, 6:7], in_=stv.ap[:, 3:4],
                                                           func=AF.Ln, bias=1e-5), [t6])
            t8 = P.op("s", lambda e, stv=stv: e.activation(out=stv.ap[:, 4:5], in_=stv.ap[:, 6:7],
                                                           func=AF.Exp, scale=-0.5), [t7])
            t9 = P.op("v", lambda e, stv=stv: e.tensor_tensor(stv.ap[:, 5:6], stv.ap[:, 2:3],
                                                              stv.ap[:, 4:5], AL.mult), [t8, t3])
            t9 = P.op("v", lambda e, stv=stv: e.tensor_scalar(stv.ap[:, 5:6], stv.ap[:, 5:6], -1.0, None,
                                                              AL.mult), [t9])
            stv.ww(t9)
            if dst_tm is not None:
                tn = P.op("s", lambda e, xt=xt, xo=xo, stv=stv: e.activation(
                    out=xo.ap[:], in_=xt.ap[:], func=AF.Identity, bias=stv.ap[:, 5:6],
                    scale=stv.ap[:, 4:5]), [t9] + xo.wdep())
                xo.ww(tn); xt.rr(tn); stv.rr(tn)
                last = P.dma("dG", dst_tm[it * 128:(it + 1) * 128, :], xo.ap[:], [tn], key=xo.key)
                xo.rr(last)
            if dst_fm is not None:
                tgt = xr if fm_dt == F32R else xb
                tn = P.op("s", lambda e, gg=tgt, xt=xt, stv=stv: e.activation(
                    out=gg.ap[:, :1024], in_=xt.ap[:], func=AF.Identity, bias=stv.ap[:, 5:6],
                    scale=stv.ap[:, 4:5]), [t9] + tgt.wdep())
                tgt.ww(tn); xt.rr(tn); stv.rr(tn)
                if gbi is not None:
                    o = gbi * 2 * DM
                    tn = P.op("v", lambda e, o=o, xb=xb: e.tensor_tensor(xb2.ap[:, :1024], xb.ap[:, :1024],
                                                                        self.gb.ap[:, o:o + DM], AL.mult),
                              [tn] + xb2.wdep() + self.gb.rdep())
                    xb2.ww(tn); xb.rr(tn); self.gb.rr(tn)
                    tn = P.op("v", lambda e, o=o, xb=xb: e.tensor_tensor(xb.ap[:, :1024], xb2.ap[:, :1024],
                                                                        self.gb.ap[:, o + DM:o + 2 * DM],
                                                                        AL.add), [tn] + xb.wdep())
                    xb.ww(tn); xb2.rr(tn); self.gb.rr(tn)
                    tgt = xb
                last = self.transpose_out(tgt, DM, dst_fm, it, fm_dt, evb if fm_dt == BF16 else evr, [tn])
        self.slots_done(bxt, xts)
        self.slots_done(bxo, xos)
        self.slots_done(bxb, xbs)
        self.slots_done(stv0, stvs)
        return last

    def transpose_out(self, srcbuf, D, dst_fm, it, dt, evbuf, deps, src_off=0, dst_row=0):
        P = self.P
        pst = self.ps["tr"] if dt == BF16 else self.ps["trR"]
        idn = self.ident if dt == BF16 else self.identR
        grp_sz = 8 if dt == BF16 else 4
        ts = None
        for grp in range((D // 128 + grp_sz - 1) // grp_sz):
            toks = []
            nb = min(grp_sz, D // 128 - grp * grp_sz)
            for b2 in range(nb):
                blk = grp * grp_sz + b2
                d = deps + idn.rdep() + (pst.wdep() if b2 == 0 else [toks[-1]])
                tt = P.op("t", lambda e, blk=blk, b2=b2: e.transpose(
                    pst.ap[:, b2 * 128:(b2 + 1) * 128],
                    srcbuf.ap[:, src_off + blk * 128:src_off + (blk + 1) * 128],
                    idn.ap[:]), d)
                toks.append(tt)
                srcbuf.rr(tt); idn.rr(tt)
            pst.ww(toks[-1])
            te = P.op("v", lambda e, nb=nb: e.tensor_copy(
                evbuf.ap[:, :nb * 128], pst.ap[:, :nb * 128]),
                [toks[-1]] + evbuf.wdep())
            evbuf.ww(te); pst.rr(te)
            r0 = dst_row + grp * grp_sz * 128
            dview = dst_fm[r0:r0 + nb * 128, :].rearrange(
                "(a p) t -> p a t", p=128)[:, :, it * 128:(it + 1) * 128]
            sview = evbuf.ap[:, :nb * 128].rearrange("p (a t) -> p a t", t=128)
            ts = P.dma("dG", dview, sview, [te], key=evbuf.key)
            evbuf.rr(ts)
        return ts

    def add_contrib_fm(self, contrib_fm_d, extra_tm_d=None, dep=None):
        dep = aslist(dep)
        P, T, DM = self.P, self.T, self.DM
        cb = self.pool("R32A")
        ct = self.pool("F32C")
        bxt, xts = self.slots("F32A", 1024, 2)
        et = self.pool("F32B")
        pst = self.ps["trR"]
        last = None
        for it in range(T // 128):
            xt = xts[it % 2]
            sview = contrib_fm_d.rearrange("(a p) t -> p a t", p=128)[:, :, it * 128:(it + 1) * 128]
            dview = cb.ap[:, :DM].rearrange("p (a t) -> p a t", t=128)
            tl = P.dma("dS", dview, sview, cb.wdep() + dep + [last], key=cb.key)
            cb.ww(tl)
            te = None
            for grp in range(2):
                toks = []
                for b2 in range(4):
                    blk = grp * 4 + b2
                    d = [tl] + self.identR.rdep() + (pst.wdep() if b2 == 0 else [toks[-1]])
                    tt = P.op("t", lambda e, blk=blk, b2=b2: e.transpose(
                        pst.ap[:, b2 * 128:(b2 + 1) * 128], cb.ap[:, blk * 128:(blk + 1) * 128],
                        self.identR.ap[:]), d)
                    toks.append(tt); cb.rr(tt); self.identR.rr(tt)
                pst.ww(toks[-1])
                te = P.op("v", lambda e, grp=grp: e.tensor_copy(ct.ap[:, grp * 512:(grp + 1) * 512],
                                                               pst.ap[:, :512]),
                          [toks[-1]] + (ct.wdep() if grp == 0 else []))
                ct.ww(te); pst.rr(te)
            tx = P.dma("dA", xt.ap[:], self.xres[it * 128:(it + 1) * 128, :],
                       xt.wdep() + dep + [self.t_x0], key=xt.key)
            xt.ww(tx)
            d2 = [te, tx]
            if extra_tm_d is not None:
                tex = P.dma("dA", et.ap[:, :DM], extra_tm_d[it * 128:(it + 1) * 128, :],
                            et.wdep() + dep, key=et.key)
                et.ww(tex)
                ta = P.op("v", lambda e, xt=xt: e.tensor_tensor(xt.ap[:], xt.ap[:], et.ap[:, 0:1024],
                                                               AL.add), [tex] + d2)
                xt.ww(ta); et.rr(ta)
                d2 = [ta]
            ta = P.op("v", lambda e, xt=xt: e.tensor_tensor(xt.ap[:], xt.ap[:], ct.ap[:, 0:1024],
                                                           AL.add), d2)
            xt.ww(ta); ct.rr(ta)
            last = P.dma("dG", self.xres[it * 128:(it + 1) * 128, :], xt.ap[:], [ta], key=xt.key)
            xt.rr(last)
        self.slots_done(bxt, xts)
        self.t_x0 = last
        return last

    def sub_l0ffn(self):
        P, T, DM = self.P, self.T, self.DM
        fence = P.fence()
        tln = self.ln(self.xres, dst_fm=self.xn_fmR, fm_dt=F32R, dep=fence)
        contrib = self.dram(self.uid("c0"), [DM, T], F32R)
        ev = self.pool("R32B")

        def evac(mt, ntile, psb, deps):
            te = P.op("s", lambda e, mt=mt: e.activation(out=ev.ap[:, :512], in_=psb.ap[:, :],
                                                         func=AF.Identity, scale=0.5,
                                                         bias=self.smalls_pp("fbias", mt)),
                      deps + ev.wdep() + self.smalls.rdep())
            ev.ww(te); self.smalls.rr(te); psb.rr(te)
            ts = P.dma("dG", contrib[mt * 128:(mt + 1) * 128, ntile * 512:(ntile + 1) * 512],
                       ev.ap[:, :512], [te], key=ev.key)
            ev.rr(ts)
            return ts

        self.mm_stream(self.M_d, DM, DM, self.xn_fmR, T, evac, rhs_dt=F32R,
                       rhs_dep=[tln], lhs_dep=[self.t_M])
        self.add_contrib_fm(contrib, dep=P.fence())

    def sub_mamba(self):
        P, T, DM, DI = self.P, self.T, self.DM, self.DI
        fence = P.fence()
        t1 = self.ln(self.xres, dst_tm=self.xnp_d, dep=fence)
        t2 = self.ln(self.xnp_d, dst_fm=self.xn_fm16, fm_dt=BF16, gbi=0, dep=[t1])
        ub = self.pool("F32A")
        acc = self.pool("F32C")
        u2b = self.pool("B16D")
        szb = self.pool("B16E")
        nDIt = DI // 128
        tz = P.op("v", lambda e: e.memset(self.tails.ap[:], 0.0), self.tails.wdep() + fence)
        self.tails.ww(tz)

        def evac_u(mt, ntile, psb, deps):
            te = P.op("s", lambda e: e.activation(out=ub.ap[:, 3:515], in_=psb.ap[:, :], func=AF.Identity),
                      deps + ub.wdep())
            ub.ww(te); psb.rr(te)
            tc = P.op("v", lambda e, mt=mt: e.tensor_copy(ub.ap[:, 0:3], self.tails.ap[:, 3 * mt:3 * mt + 3]),
                      [te] + self.tails.rdep())
            ub.ww(tc); self.tails.rr(tc)
            tt = P.op("v", lambda e, mt=mt: e.tensor_copy(self.tails.ap[:, 3 * mt:3 * mt + 3],
                                                          ub.ap[:, 512:515]), [tc] + self.tails.wdep())
            self.tails.ww(tt)
            t0 = P.op("v", lambda e, mt=mt: e.tensor_scalar(acc.ap[:, :512], ub.ap[:, 0:512],
                                                            self.smalls_pp("cw0", 4 * mt), None, AL.mult),
                      [tc] + acc.wdep() + self.smalls.rdep())
            acc.ww(t0); self.smalls.rr(t0)
            for k2 in (1, 2, 3):
                t0 = P.op("v", lambda e, mt=mt, k2=k2: e.scalar_tensor_tensor(
                    acc.ap[:, :512], ub.ap[:, k2:512 + k2], self.smalls_pp("cw0", 4 * mt + k2),
                    acc.ap[:, :512], AL.mult, AL.add), [t0])
                acc.ww(t0)
            ub.rr(t0)
            ts = P.op("s", lambda e, mt=mt: e.activation(out=u2b.ap[:, :512], in_=acc.ap[:, :512],
                                                         func=AF.Silu, bias=self.smalls_pp("cb0", mt)),
                      [t0] + u2b.wdep() + self.smalls.rdep())
            u2b.ww(ts); acc.rr(ts); self.smalls.rr(ts)
            td = P.dma("dG", self.u2_fm[mt * 128:(mt + 1) * 128, ntile * 512:(ntile + 1) * 512],
                       u2b.ap[:, :512], [ts], key=u2b.key)
            u2b.rr(td)
            return td

        def evac_z(mt, ntile, psb, deps):
            mz = mt - nDIt
            ts = P.op("s", lambda e: e.activation(out=szb.ap[:, :512], in_=psb.ap[:, :], func=AF.Silu),
                      deps + szb.wdep())
            szb.ww(ts); psb.rr(ts)
            td = P.dma("dG", self.sz_fm[mz * 128:(mz + 1) * 128, ntile * 512:(ntile + 1) * 512],
                       szb.ap[:, :512], [ts], key=szb.key)
            szb.rr(td)
            return td

        self.mm_stream(self.w["in_projT"][:, 0:DI], DM, DI, self.xn_fm16, T, evac_u, rhs_dep=[t2, tz])
        self.mm_stream(self.w["in_projT"][:, DI:2 * DI], DM, DI, self.xn_fm16, T, evac_z,
                       rhs_dep=[t2], mt_off=nDIt)
        f2 = P.fence()
        xdb = self.pool("B16D")
        xdf = self.pool("F32D")

        def evac_xproj(mt, ntile, psb, deps):
            te = P.op("v", lambda e: e.tensor_copy(xdb.ap[:96, 0:512], psb.ap[:96, :]), deps + xdb.wdep())
            xdb.ww(te); psb.rr(te)
            tf = P.op("v", lambda e: e.tensor_copy(xdf.ap[:32, 0:512], psb.ap[64:96, :]),
                      deps + [te] + xdf.wdep())
            xdf.ww(tf); psb.rr(tf)
            ta = P.dma("dG", self.dt_d[:, ntile * 512:(ntile + 1) * 512], xdb.ap[0:64, 0:512], [te], key=xdb.key)
            xdb.rr(ta)
            tb = P.dma("dG", self.BC_d[:, ntile * 512:(ntile + 1) * 512], xdf.ap[:32, 0:512], [tf], key=xdf.key)
            xdf.rr(tb)
            return tb

        self.mm_stream(self.w["x_projT"], DI, 96, self.u2_fm, T, evac_xproj, rhs_dep=f2,
                       lhs_dep=f2, mt_list=[0])
        f3 = P.fence()
        ab = self.pool("F32B")
        eb = self.pool("F32C")
        rb = self.pool("F32D")
        db = self.pool("F32E")
        dub = self.pool("B16D")
        u2l = self.pool("B16E")

        def evac_dt(mt, ntile, psb, deps):
            ta = P.op("s", lambda e, mt=mt: e.activation(out=ab.ap[:, :512], in_=psb.ap[:, :], func=AF.Abs,
                                                         bias=self.smalls_pp("dtb", mt)),
                      deps + ab.wdep() + self.smalls.rdep())
            ab.ww(ta); self.smalls.rr(ta); psb.rr(ta)
            tb = P.op("s", lambda e: e.activation(out=eb.ap[:, :512], in_=ab.ap[:, :512], func=AF.Exp,
                                                  scale=-1.0), [ta] + eb.wdep())
            eb.ww(tb); ab.rr(tb)
            tc = P.op("s", lambda e: e.activation(out=eb.ap[:, :512], in_=eb.ap[:, :512], func=AF.Ln,
                                                  bias=1.0), [tb])
            eb.ww(tc)
            tr = P.op("s", lambda e, mt=mt: e.activation(out=rb.ap[:, :512], in_=psb.ap[:, :], func=AF.Relu,
                                                         bias=self.smalls_pp("dtb", mt)),
                      deps + [tc] + rb.wdep() + self.smalls.rdep())
            rb.ww(tr); self.smalls.rr(tr); psb.rr(tr)
            td = P.op("v", lambda e: e.tensor_tensor(db.ap[:, :512], rb.ap[:, :512], eb.ap[:, :512],
                                                     AL.add), [tr, tc] + db.wdep())
            db.ww(td); rb.rr(td); eb.rr(td)
            tw = P.dma("dG", self.delta_fm[mt * 128:(mt + 1) * 128, ntile * 512:(ntile + 1) * 512],
                       db.ap[:, :512], [td], key=db.key)
            db.rr(tw)
            tu = P.dma("dA", u2l.ap[:, :512],
                       self.u2_fm[mt * 128:(mt + 1) * 128, ntile * 512:(ntile + 1) * 512], u2l.wdep(),
                       key=u2l.key)
            u2l.ww(tu)
            tm = P.op("v", lambda e: e.tensor_tensor(dub.ap[:, :512], db.ap[:, :512], u2l.ap[:, :512],
                                                     AL.mult), [td, tu] + dub.wdep())
            dub.ww(tm); db.rr(tm); u2l.rr(tm)
            t2w = P.dma("dG", self.du_fm[mt * 128:(mt + 1) * 128, ntile * 512:(ntile + 1) * 512],
                        dub.ap[:, :512], [tm], key=dub.key)
            dub.rr(t2w)
            return t2w

        self.mm_stream(self.w["dt_projT"], 64, DI, self.dt_d, T, evac_dt, rhs_dep=f3, lhs_dep=f3)
        self.scan_phase(P.fence())
        f4 = P.fence()
        contrib = self.dram(self.uid("cm"), [DM, T], F32R)
        ev4 = self.pool("R32B")

        def evac_out(mt, ntile, psb, deps):
            te = P.op("v", lambda e: e.tensor_copy(ev4.ap[:, :512], psb.ap[:, :]), deps + ev4.wdep())
            ev4.ww(te); psb.rr(te)
            ts = P.dma("dG", contrib[mt * 128:(mt + 1) * 128, ntile * 512:(ntile + 1) * 512],
                       ev4.ap[:, :512], [te], key=ev4.key)
            ev4.rr(ts)
            return ts

        self.mm_stream(self.w["out_projT"], DI, DM, self.yg_fm, T, evac_out, rhs_dep=f4, lhs_dep=f4)
        self.add_contrib_fm(contrib, extra_tm_d=self.xnp_d, dep=P.fence())

    def slots(self, tag, ncols, n):
        base = self.pool(tag)
        out = []
        for i in range(n):
            b = Buf(base.ap[:, i * ncols:(i + 1) * ncols])
            b.w = base.w
            b.r = list(base.r)
            out.append(b)
        return base, out

    def slots_done(self, base, sl):
        for b in sl:
            if b.w is not None:
                base.rr(b.w)
            for t in b.r:
                base.rr(t)

    def scan_phase(self, fence):
        P, T, DI = self.P, self.T, self.DI
        Tc = min(512, T)
        nCh = DI // 128
        RepB = [self.rhs16[0], self.rhs16[1]]
        RepC = [self.pool("BIGB"), self.pool("BIGC")]
        npb = 4096 // Tc
        bdlt, dlts = self.slots("F32A", Tc, 2)
        bdul, duls = self.slots("B16A", Tc, 2)
        bu2l, u2ls = self.slots("B16B", Tc, 2)
        bszl, szls = self.slots("B16C", Tc, 2)
        bdA, dAs = self.slots("F32B", Tc, 2)
        bbb, bbs = self.slots("B16D", Tc, 2)
        bhn, hns = self.slots("B16E", Tc, 2)
        bhc, hcs = self.slots("B16F", Tc, 2)
        yb = self.pool("F32C")
        ygb = self.pool("B16G")
        nb = (Tc + 511) // 512
        nTch = T // Tc

        def load_ct(tch, ct):
            s = ct % 2
            t1 = P.dma("dS", dlts[s].ap[:], self.delta_fm[ct * 128:(ct + 1) * 128, tch * Tc:(tch + 1) * Tc],
                       dlts[s].wdep() + fence, key=dlts[s].key)
            dlts[s].ww(t1)
            t2 = P.dma("dS", duls[s].ap[:], self.du_fm[ct * 128:(ct + 1) * 128, tch * Tc:(tch + 1) * Tc],
                       duls[s].wdep() + fence, key=duls[s].key)
            duls[s].ww(t2)
            t3 = P.dma("dA", u2ls[s].ap[:], self.u2_fm[ct * 128:(ct + 1) * 128, tch * Tc:(tch + 1) * Tc],
                       u2ls[s].wdep() + fence, key=u2ls[s].key)
            u2ls[s].ww(t3)
            t4 = P.dma("dA", szls[s].ap[:], self.sz_fm[ct * 128:(ct + 1) * 128, tch * Tc:(tch + 1) * Tc],
                       szls[s].wdep() + fence, key=szls[s].key)
            szls[s].ww(t4)

        ones1f = self.sbuf(self.uid("o1f"), [1, 128], F32)
        to1f = P.op("v", lambda e: e.memset(ones1f.ap[:], 1.0), ones1f.wdep() + fence)
        ones1f.ww(to1f)
        bc2s = [self.sbuf(self.uid(f"bc2{i}"), [1, 512], F32) for i in range(2)]
        for tch in range(nTch):
            for rep_list, base_row in ((RepB, 0), (RepC, 16)):
                for n in range(16):
                    rep = rep_list[n // npb]
                    off = (n % npb) * Tc
                    bc2 = bc2s[n % 2]
                    psb = self.ps["sc"] if n % 2 == 0 else self.ps["av"]
                    sv = self.BC_d[base_row + n:base_row + n + 1,
                                   tch * Tc:(tch + 1) * Tc].rearrange("(o r) t -> o r t", o=1)
                    dv = bc2.ap[:, :Tc].rearrange("p (r t) -> p r t", t=Tc)
                    tld = P.dma("dA", dv, sv, bc2.wdep() + fence, key=bc2.key)
                    bc2.ww(tld)
                    tmm = P.op("t", lambda e, psb=psb, bc2=bc2: e.matmul(
                        psb.ap[:, :Tc], ones1f.ap[:], bc2.ap[:, :Tc],
                        start=True, stop=True), [tld, to1f] + psb.wdep() + ones1f.rdep())
                    psb.ww(tmm); bc2.rr(tmm); ones1f.rr(tmm)
                    tev = P.op("v", lambda e, rep=rep, off=off, psb=psb: e.tensor_copy(
                        rep.ap[:, off:off + Tc], psb.ap[:, :Tc]),
                        [tmm] + rep.wdep())
                    rep.ww(tev); psb.rr(tev)
            if tch == 0:
                load_ct(0, 0)
            for ct in range(nCh):
                s = ct % 2
                dlt, dul, u2l, szl = dlts[s], duls[s], u2ls[s], szls[s]
                tl1, tl2, tl3, tl4 = dlt.w, dul.w, u2l.w, szl.w
                if ct + 1 < nCh:
                    load_ct(tch, ct + 1)
                elif tch + 1 < nTch:
                    load_ct(tch + 1, 0)
                banks = [self.ps["mm0"], self.ps["mm1"]][:nb]
                ytoks = []
                for n in range(16):
                    nr = n
                    sn = n % 2
                    dA, bb, hn, hc = dAs[sn], bbs[sn], hns[sn], hcs[sn]
                    ta = P.op("s", lambda e, ct=ct, n=n, dA=dA, dlt=dlt: e.activation(
                        out=dA.ap[:], in_=dlt.ap[:], func=AF.Exp,
                        scale=self.A_neg.ap[:, ct * 16 + n:ct * 16 + n + 1]),
                        [tl1] + dA.wdep() + self.A_neg.rdep())
                    dA.ww(ta); dlt.rr(ta); self.A_neg.rr(ta)
                    Bb = RepB[n // npb]
                    bo = (n % npb) * Tc
                    tb = P.op("v", lambda e, Bb=Bb, bo=bo, bb=bb, dul=dul: e.tensor_tensor(
                        bb.ap[:], dul.ap[:], Bb.ap[:, bo:bo + Tc], AL.mult),
                        [tl2, Bb.w] + bb.wdep())
                    bb.ww(tb); dul.rr(tb); Bb.rr(tb)
                    tsc = P.op("v", lambda e, ct=ct, n=n, hn=hn, dA=dA, bb=bb: e.tensor_tensor_scan(
                        hn.ap[:], dA.ap[:], bb.ap[:],
                        self.states.ap[:, ct * 16 + n:ct * 16 + n + 1], AL.mult, AL.add),
                        [ta, tb] + hn.wdep() + self.states.rdep() + self.states.wdep())
                    hn.ww(tsc); dA.rr(tsc); bb.rr(tsc); self.states.rr(tsc)
                    tst = P.op("v", lambda e, ct=ct, n=n, hn=hn: e.tensor_copy(
                        self.states.ap[:, ct * 16 + n:ct * 16 + n + 1], hn.ap[:, Tc - 1:Tc]), [tsc])
                    self.states.ww(tst); hn.rr(tst)
                    Cb = RepC[n // npb]
                    co = (n % npb) * Tc
                    tm = P.op("v", lambda e, Cb=Cb, co=co, hc=hc, hn=hn: e.tensor_tensor(
                        hc.ap[:], hn.ap[:], Cb.ap[:, co:co + Tc], AL.mult),
                        [tsc, Cb.w] + hc.wdep())
                    hc.ww(tm); hn.rr(tm); Cb.rr(tm)
                    for sb2 in range(nb):
                        w2 = min(512, Tc - sb2 * 512)
                        psb = banks[sb2]
                        d = [tm] + self.ident.rdep() + (psb.wdep() if n == 0 else [ytoks[-1]])
                        ty = P.op("t", lambda e, sb2=sb2, psb=psb, n=n, w2=w2, hc=hc: e.matmul(
                            psb.ap[:, :w2], self.ident.ap[:], hc.ap[:, sb2 * 512:sb2 * 512 + w2],
                            start=(n == 0), stop=(n == 15)), d)
                        ytoks.append(ty)
                        if n == 0:
                            psb.ww(ty)
                        hc.rr(ty); self.ident.rr(ty)
                for sb2 in range(nb):
                    banks[sb2].ww(ytoks[-(nb - sb2)])
                t1s = []
                for sb2 in range(nb):
                    w2 = min(512, Tc - sb2 * 512)
                    psb = banks[sb2]
                    t1 = P.op("v", lambda e, ct=ct, sb2=sb2, psb=psb, w2=w2, u2l=u2l: e.scalar_tensor_tensor(
                        yb.ap[:, sb2 * 512:sb2 * 512 + w2], u2l.ap[:, sb2 * 512:sb2 * 512 + w2],
                        self.smalls_pp("dsk", ct), psb.ap[:, :w2], AL.mult, AL.add),
                        [tl3, ytoks[-(nb - sb2)]] + (yb.wdep() if sb2 == 0 else []) + self.smalls.rdep())
                    yb.ww(t1); u2l.rr(t1); psb.rr(t1); self.smalls.rr(t1)
                    t1s.append(t1)
                t2 = P.op("v", lambda e, szl=szl: e.tensor_tensor(ygb.ap[:, :Tc], yb.ap[:, :Tc],
                                                                 szl.ap[:, :Tc], AL.mult),
                          t1s + [tl4] + ygb.wdep())
                ygb.ww(t2); yb.rr(t2); szl.rr(t2)
                ts = P.dma("dG", self.yg_fm[ct * 128:(ct + 1) * 128, tch * Tc:(tch + 1) * Tc],
                           ygb.ap[:, :Tc], [t2], key=ygb.key)
                ygb.rr(ts)
        for base, sl in [(bdlt, dlts), (bdul, duls), (bu2l, u2ls), (bszl, szls),
                         (bdA, dAs), (bbb, bbs), (bhn, hns), (bhc, hcs)]:
            self.slots_done(base, sl)

    def sub_l1ffn(self):
        P, T, DM, DH = self.P, self.T, self.DM, self.DH
        fence = P.fence()
        t1 = self.ln(self.xres, dst_tm=self.xnp_d, dep=fence)
        tq = self.quant_tm(self.xnp_d, self.xq_fm, T, DM, dep=[t1])
        hs_fm = self.dram(self.uid("hs"), [DH, T], BF16)
        ub = self.pool("F32A")
        acc = self.pool("F32C")
        hsb = self.pool("B16D")
        nT = T // 512
        nDHt = DH // 128

        def evac_mm1(mt, ntile, psb, deps):
            te = P.op("s", lambda e: e.activation(out=ub.ap[:, 2:514], in_=psb.ap[:, :], func=AF.Identity,
                                                  scale=self.invws.ap[:, 0:1]),
                      deps + ub.wdep() + self.invws.rdep())
            ub.ww(te); self.invws.rr(te); psb.rr(te)
            tc = P.op("v", lambda e, mt=mt: e.tensor_copy(ub.ap[:, 0:2],
                                                          self.tails.ap[:, 2 * mt:2 * mt + 2]),
                      [te] + self.tails.rdep())
            ub.ww(tc); self.tails.rr(tc)
            tt = P.op("v", lambda e, mt=mt: e.tensor_copy(self.tails.ap[:, 2 * mt:2 * mt + 2],
                                                          ub.ap[:, 512:514]), [tc] + self.tails.wdep())
            self.tails.ww(tt)
            t0 = P.op("v", lambda e, mt=mt: e.tensor_scalar(acc.ap[:, :512], ub.ap[:, 0:512],
                                                            self.smalls_pp("cw1", 3 * mt), None, AL.mult),
                      [tc] + acc.wdep() + self.smalls.rdep())
            acc.ww(t0); self.smalls.rr(t0)
            for k2 in (1, 2):
                t0 = P.op("v", lambda e, mt=mt, k2=k2: e.scalar_tensor_tensor(
                    acc.ap[:, :512], ub.ap[:, k2:512 + k2], self.smalls_pp("cw1", 3 * mt + k2),
                    acc.ap[:, :512], AL.mult, AL.add), [t0])
                acc.ww(t0)
            ub.rr(t0)
            ts = P.op("s", lambda e, mt=mt: e.activation(out=hsb.ap[:, :512], in_=acc.ap[:, :512],
                                                         func=AF.Silu, bias=self.smalls_pp("cb1", mt)),
                      [t0] + hsb.wdep() + self.smalls.rdep())
            hsb.ww(ts); acc.rr(ts); self.smalls.rr(ts)
            lo = ntile * 512 - 1
            if lo < 0:
                td = P.dma("dG", hs_fm[mt * 128:(mt + 1) * 128, 0:511], hsb.ap[:, 1:512], [ts], key=hsb.key)
            else:
                td = P.dma("dG", hs_fm[mt * 128:(mt + 1) * 128, lo:lo + 512], hsb.ap[:, :512], [ts], key=hsb.key)
            hsb.rr(td)
            if ntile == nT - 1:
                tf = P.op("v", lambda e, mt=mt: e.tensor_scalar(acc.ap[:, 0:1],
                                                                self.tails.ap[:, 2 * mt:2 * mt + 1],
                                                                self.smalls_pp("cw1", 3 * mt), None,
                                                                AL.mult),
                          [tt, td] + acc.wdep() + self.smalls.rdep())
                acc.ww(tf); self.tails.rr(tf); self.smalls.rr(tf)
                tf = P.op("v", lambda e, mt=mt: e.scalar_tensor_tensor(
                    acc.ap[:, 0:1], self.tails.ap[:, 2 * mt + 1:2 * mt + 2],
                    self.smalls_pp("cw1", 3 * mt + 1), acc.ap[:, 0:1], AL.mult, AL.add), [tf])
                acc.ww(tf); self.tails.rr(tf)
                tf2 = P.op("s", lambda e, mt=mt: e.activation(out=hsb.ap[:, 0:1], in_=acc.ap[:, 0:1],
                                                              func=AF.Silu, bias=self.smalls_pp("cb1", mt)),
                           [tf] + hsb.wdep())
                hsb.ww(tf2); acc.rr(tf2)
                td = P.dma("dG", hs_fm[mt * 128:(mt + 1) * 128, T - 1:T], hsb.ap[:, 0:1], [tf2, td],
                           key=hsb.key, slow=True)
                hsb.rr(td)
            return td

        tz = P.op("v", lambda e: e.memset(self.tails.ap[:], 0.0), self.tails.wdep() + fence)
        self.tails.ww(tz)
        self.mm_stream(self.w1q_d[:, 0:DH // 2], DM, DH // 2, self.xq_fm, T, evac_mm1,
                       rhs_dep=[tq, tz], lhs_dep=[self.t_wq])
        self.mm_stream(self.w1q_d[:, DH // 2:DH], DM, DH // 2, self.xq_fm, T, evac_mm1,
                       rhs_dep=[tq, tz], lhs_dep=[self.t_wq], mt_off=nDHt // 2)
        tq2 = self.quant_fm_roundtrip(hs_fm, self.hq_fm, T, DH, dep=P.fence())
        contrib = self.dram(self.uid("c1"), [DM, T], F32R)
        ev = self.pool("R32B")
        th = P.op("v", lambda e: e.tensor_scalar(self.sc2.ap[:, 0:1], self.invws.ap[:, 1:2], 0.5, None,
                                                 AL.mult), self.invws.rdep() + self.sc2.wdep())
        self.sc2.ww(th); self.invws.rr(th)

        def evac_mm2(mt, ntile, psb, deps):
            te = P.op("s", lambda e: e.activation(out=ev.ap[:, :512], in_=psb.ap[:, :],
                                                  func=AF.Identity, scale=self.sc2.ap[:, 0:1]),
                      deps + [th] + ev.wdep())
            ev.ww(te); self.sc2.rr(te); psb.rr(te)
            ts = P.dma("dG", contrib[mt * 128:(mt + 1) * 128, ntile * 512:(ntile + 1) * 512],
                       ev.ap[:, :512], [te], key=ev.key)
            ev.rr(ts)
            return ts

        self.mm_stream(self.w2q_d[:, 0:DM // 2], DH, DM // 2, self.hq_fm, T,
                       evac_mm2, rhs_dep=[tq2], lhs_dep=[tq2])
        self.mm_stream(self.w2q_d[:, DM // 2:DM], DH, DM // 2, self.hq_fm, T,
                       evac_mm2, rhs_dep=[tq2], lhs_dep=[tq2], mt_off=DM // 256)
        self.add_contrib_fm(contrib, dep=P.fence())

    def quant_tm(self, src_tm_d, dst_fm_d, T, D, dep=None):
        dep = aslist(dep)
        P = self.P
        xt = self.pool("F32A")
        g = self.pool("F32B")
        qb = self.pool("F32C")
        qi = self.pool("I32A")
        xq = self.pool("B16B")
        ev = self.pool("B16C")
        mx = self.mx
        last = None
        for it in range(T // 128):
            tl = P.dma("dS", xt.ap[:, :D], src_tm_d[it * 128:(it + 1) * 128, :], xt.wdep() + dep + [last],
                       key=xt.key)
            xt.ww(tl)
            tm = P.op("v", lambda e, D=D: e.tensor_reduce(mx.ap[:, 0:1], xt.ap[:, :D], mybir.AxisListType.X,
                                                          AL.max, apply_absolute_value=True),
                      [tl] + mx.wdep())
            tm = P.op("v", lambda e: e.tensor_scalar(mx.ap[:, 1:2], mx.ap[:, 0:1], 1e-5, None, AL.max), [tm])
            tm = P.op("v", lambda e: e.reciprocal(mx.ap[:, 2:3], mx.ap[:, 1:2]), [tm])
            tm = P.op("v", lambda e: e.tensor_scalar(mx.ap[:, 2:3], mx.ap[:, 2:3], 127.0, None, AL.mult),
                      [tm])
            tm = P.op("v", lambda e: e.tensor_scalar(mx.ap[:, 3:4], mx.ap[:, 1:2], 1.0 / 127.0, None,
                                                     AL.mult), [tm])
            mx.ww(tm)
            tqq = self._quant_block(xt, 0, D, mx, g, qb, qi, xq, [tm, tl])
            last = self.transpose_out(xq, D, dst_fm_d, it, BF16, ev, [tqq])
        return last

    def _quant_block(self, srcbuf, off, width, mx, g, qb, qi, xq, deps, out_off0=False):
        P = self.P
        HB = 2048
        tq = None
        for o in range(off, off + width, HB):
            wd = min(HB, off + width - o)
            xo = 0 if out_off0 else o
            t1 = P.op("v", lambda e, o=o, wd=wd: e.tensor_scalar(qi.ap[:, :wd], srcbuf.ap[:, o:o + wd],
                                                                 mx.ap[:, 2:3], 128.0, AL.mult, AL.add),
                      deps + qi.wdep() + ([tq] if tq else []))
            qi.ww(t1); srcbuf.rr(t1); mx.rr(t1)
            t2 = P.op("v", lambda e, wd=wd: e.tensor_copy(qb.ap[:, :wd], qi.ap[:, :wd]), [t1] + qb.wdep())
            qb.ww(t2); qi.rr(t2)
            t3 = P.op("v", lambda e, o=o, wd=wd: e.scalar_tensor_tensor(
                g.ap[:, :wd], srcbuf.ap[:, o:o + wd], mx.ap[:, 2:3], qb.ap[:, :wd], AL.mult, AL.subtract),
                [t2] + deps + g.wdep())
            g.ww(t3); srcbuf.rr(t3); qb.rr(t3); mx.rr(t3)
            t4 = P.op("v", lambda e, wd=wd: e.scalar_tensor_tensor(qb.ap[:, :wd], g.ap[:, :wd], -127.5,
                                                                  qb.ap[:, :wd], AL.is_ge, AL.add), [t3])
            qb.ww(t4); g.rr(t4)
            tq = P.op("v", lambda e, xo=xo, wd=wd: e.tensor_scalar(xq.ap[:, xo:xo + wd], qb.ap[:, :wd],
                                                                   -128.0, mx.ap[:, 3:4], AL.add, AL.mult),
                      [t4] + xq.wdep())
            xq.ww(tq); qb.rr(tq); mx.rr(tq)
        return tq

    def quant_fm_roundtrip(self, src_fm_d, dst_fm_d, T, D, dep=None):
        dep = aslist(dep)
        P = self.P
        tmv = self.pool("F32A")
        g = self.pool("F32B")
        qb = self.pool("F32C")
        qi = self.pool("I32A")
        cb = self.pool("B16A")
        xq = self.pool("B16B")
        ev = self.pool("B16C")
        mx = self.mx
        pst = self.ps["tr"]
        last = None
        for it in range(T // 128):
            te = None
            for half in range(D // 2048):
                sview = src_fm_d.rearrange("(a p) t -> p a t", p=128)[:, half * 16:(half + 1) * 16,
                                                                       it * 128:(it + 1) * 128]
                dview = cb.ap[:, :2048].rearrange("p (a t) -> p a t", t=128)
                tl = P.dma("dS", dview, sview, cb.wdep() + dep + [last, te], key=cb.key)
                cb.ww(tl)
                for grp in range(2):
                    toks = []
                    for b2 in range(8):
                        blk = grp * 8 + b2
                        d = [tl] + self.ident.rdep() + (pst.wdep() if b2 == 0 else [toks[-1]])
                        tt = P.op("t", lambda e, blk=blk, b2=b2: e.transpose(
                            pst.ap[:, b2 * 128:(b2 + 1) * 128], cb.ap[:, blk * 128:(blk + 1) * 128],
                            self.ident.ap[:]), d)
                        toks.append(tt); cb.rr(tt); self.ident.rr(tt)
                    pst.ww(toks[-1])
                    te = P.op("v", lambda e, half=half, grp=grp: e.tensor_copy(
                        tmv.ap[:, half * 2048 + grp * 1024:half * 2048 + (grp + 1) * 1024], pst.ap[:]),
                        [toks[-1]] + (tmv.wdep() if half == 0 and grp == 0 else []))
                    tmv.ww(te); pst.rr(te)
            tm = P.op("v", lambda e, D=D: e.tensor_reduce(mx.ap[:, 0:1], tmv.ap[:, :D],
                                                          mybir.AxisListType.X, AL.max,
                                                          apply_absolute_value=True), [te] + mx.wdep())
            tm = P.op("v", lambda e: e.tensor_scalar(mx.ap[:, 1:2], mx.ap[:, 0:1], 1e-5, None, AL.max), [tm])
            tm = P.op("v", lambda e: e.reciprocal(mx.ap[:, 2:3], mx.ap[:, 1:2]), [tm])
            tm = P.op("v", lambda e: e.tensor_scalar(mx.ap[:, 2:3], mx.ap[:, 2:3], 127.0, None, AL.mult),
                      [tm])
            tm = P.op("v", lambda e: e.tensor_scalar(mx.ap[:, 3:4], mx.ap[:, 1:2], 1.0 / 127.0, None,
                                                     AL.mult), [tm])
            mx.ww(tm)
            for half in range(D // 2048):
                tqq = self._quant_block(tmv, half * 2048, 2048, mx, g, qb, qi, xq, [tm], out_off0=True)
                last = self.transpose_out(xq, 2048, dst_fm_d, it, BF16, ev, [tqq],
                                          dst_row=half * 2048)
        return last

    def sub_attn(self):
        P, T, DM = self.P, self.T, self.DM
        fence = P.fence()
        t1 = self.ln(self.xres, dst_tm=self.xnp_d, dep=fence)
        t2 = self.ln(self.xnp_d, dst_fm=self.xn_fm16, fm_dt=BF16, gbi=1, dep=[t1])
        evq = self.pool("B16D")
        vtb = self.pool("B16E")
        pst = self.ps["tr"]

        def evac_qk(mt, ntile, psb, deps):
            te = P.op("v", lambda e: e.tensor_copy(evq.ap[:, :512], psb.ap[:, :]), deps + evq.wdep())
            evq.ww(te); psb.rr(te)
            ts = P.dma("dG", self.qk_fm[mt * 128:(mt + 1) * 128, ntile * 512:(ntile + 1) * 512],
                       evq.ap[:, :512], [te], key=evq.key)
            evq.rr(ts)
            return ts

        def evac_v(mt, ntile, psb, deps):
            te = P.op("v", lambda e: e.tensor_copy(vtb.ap[:, :512], psb.ap[:, :]), deps + vtb.wdep())
            vtb.ww(te); psb.rr(te)
            toks = []
            for b2 in range(4):
                d = [te] + self.ident.rdep() + (pst.wdep() if b2 == 0 else [toks[-1]])
                tt = P.op("t", lambda e, b2=b2: e.transpose(pst.ap[:, b2 * 128:(b2 + 1) * 128],
                                                            vtb.ap[:, b2 * 128:(b2 + 1) * 128],
                                                            self.ident.ap[:]), d)
                toks.append(tt); vtb.rr(tt); self.ident.rr(tt)
            pst.ww(toks[-1])
            te2 = P.op("v", lambda e: e.tensor_copy(evq.ap[:, :512], pst.ap[:, 0:512]),
                       [toks[-1]] + evq.wdep())
            evq.ww(te2); pst.rr(te2)
            dview = self.v_tm[ntile * 512:(ntile + 1) * 512, mt * 128:(mt + 1) * 128].rearrange(
                "(a p) c -> p a c", p=128)
            sview = evq.ap[:, :512].rearrange("p (a c) -> p a c", c=128)
            ts = P.dma("dG", dview, sview, [te2], key=evq.key)
            evq.rr(ts)
            return ts

        self.mm_stream(self.w["w_qkv"][:, 0:2 * DM], DM, 2 * DM, self.xn_fm16, T, evac_qk, rhs_dep=[t2])
        self.mm_stream(self.w["w_qkv"][:, 2 * DM:3 * DM], DM, DM, self.xn_fm16, T, evac_v, rhs_dep=[t2])
        fatt = P.fence()
        qsb = self.pool("B16A")
        ksb = self.pool("B16B")
        vsb = self.pool("B16C")
        ea = self.pool("B16D")
        eaT = self.pool("B16E")
        aost = self.pool("B16F")
        sms = self.sms
        W = 128
        nw = T // W
        pssc, psav = self.ps["sc"], self.ps["av"]
        last = None
        for wdx in range(nw):
            qv = self.qk_fm.rearrange("(a p) t -> p a t", p=128)[:, 0:8, wdx * W:(wdx + 1) * W]
            tq = P.dma("dS", qsb.ap[:, :8 * W].rearrange("p (a t) -> p a t", t=W), qv,
                       qsb.wdep() + fatt + [last], key=qsb.key)
            qsb.ww(tq)
            lo = max(wdx - 1, 0) * W
            kv = self.qk_fm.rearrange("(a p) t -> p a t", p=128)[:, 8:16, lo:lo + 2 * W]
            tk = P.dma("dS", ksb.ap[:, :8 * 2 * W].rearrange("p (a t) -> p a t", t=2 * W), kv,
                       ksb.wdep() + fatt, key=ksb.key)
            ksb.ww(tk)
            vv = self.v_tm[lo:lo + 2 * W, :].rearrange("(a p) c -> p a c", p=128)
            tv = P.dma("dA", vsb.ap[:, :2 * DM].rearrange("p (a c) -> p a c", c=DM), vv,
                       vsb.wdep() + fatt, key=vsb.key)
            vsb.ww(tv)
            tae = None
            for h in range(16):
                pb = 64 * (h % 2)
                tile = h // 2
                d0 = [tq, tk] + pssc.wdep()
                tmm = P.op("t", lambda e, pb=pb, tile=tile: e.matmul(
                    pssc.ap[:, 0:256], qsb.ap[pb:pb + 64, tile * W:(tile + 1) * W],
                    ksb.ap[pb:pb + 64, tile * 2 * W:(tile + 1) * 2 * W], start=True, stop=False), d0)
                qsb.rr(tmm); ksb.rr(tmm)
                moff = 0 if wdx > 0 else 256
                tmm = P.op("t", lambda e, moff=moff: e.matmul(pssc.ap[:, 0:256], self.ident.ap[:],
                                                              self.maskb.ap[:, moff:moff + 256],
                                                              start=False, stop=True),
                           [tmm] + self.maskb.rdep() + self.ident.rdep())
                pssc.ww(tmm); self.maskb.rr(tmm); self.ident.rr(tmm)
                tr = P.op("v", lambda e: e.tensor_reduce(sms.ap[:, 0:1], pssc.ap[:, 0:256],
                                                         mybir.AxisListType.X, AL.max),
                          [tmm] + sms.wdep())
                tr = P.op("v", lambda e: e.tensor_scalar(sms.ap[:, 1:2], sms.ap[:, 0:1], -0.125, None,
                                                         AL.mult), [tr])
                sms.ww(tr)
                texp = P.op("s", lambda e: e.activation(out=ea.ap[:, :256], in_=pssc.ap[:, 0:256],
                                                        func=AF.Exp, scale=0.125, bias=sms.ap[:, 1:2],
                                                        accum_out=sms.ap[:, 2:3]), [tr] + ea.wdep())
                ea.ww(texp); pssc.rr(texp); sms.ww(texp)
                trc = P.op("v", lambda e: e.reciprocal(sms.ap[:, 3:4], sms.ap[:, 2:3]), [texp])
                sms.ww(trc)
                tnm = P.op("v", lambda e: e.tensor_scalar(ea.ap[:, :256], ea.ap[:, :256], sms.ap[:, 3:4],
                                                          None, AL.mult), [trc])
                ea.ww(tnm); sms.rr(tnm)
                toks = []
                for b2 in range(2):
                    d = [tnm] + self.ident.rdep() + (pst.wdep() if b2 == 0 else [toks[-1]])
                    tt = P.op("t", lambda e, b2=b2: e.transpose(pst.ap[:, b2 * 128:(b2 + 1) * 128],
                                                                ea.ap[:, b2 * 128:(b2 + 1) * 128],
                                                                self.ident.ap[:]), d)
                    toks.append(tt); ea.rr(tt); self.ident.rr(tt)
                pst.ww(toks[-1])
                tev = P.op("v", lambda e: e.tensor_copy(eaT.ap[:, :256], pst.ap[:, 0:256]),
                           [toks[-1]] + eaT.wdep())
                eaT.ww(tev); pst.rr(tev)
                tavm = None
                for k2 in range(2):
                    d = [tev, tv] + (psav.wdep() if k2 == 0 else [tavm])
                    tavm = P.op("t", lambda e, k2=k2, h=h: e.matmul(
                        psav.ap[:64, 0:128], vsb.ap[:, k2 * DM + h * 64:k2 * DM + (h + 1) * 64],
                        eaT.ap[:, k2 * 128:(k2 + 1) * 128], start=(k2 == 0), stop=(k2 == 1)), d)
                    eaT.rr(tavm); vsb.rr(tavm)
                psav.ww(tavm)
                tae = P.op("v", lambda e, pb=pb, tile=tile: e.tensor_copy(
                    aost.ap[pb:pb + 64, tile * W:(tile + 1) * W], psav.ap[:64, 0:128]),
                    [tavm] + (aost.wdep() if h == 0 else [tae]))
                aost.ww(tae); psav.rr(tae)
            dview = self.ao_fm.rearrange("(a p) t -> p a t", p=128)[:, :, wdx * W:(wdx + 1) * W]
            sview = aost.ap[:, :8 * W].rearrange("p (a t) -> p a t", t=W)
            last = P.dma("dG", dview, sview, [tae], key=aost.key)
            aost.rr(last)
        fout = P.fence()
        contrib = self.dram(self.uid("ca"), [DM, T], F32R)
        ev2 = self.pool("R32B")

        def evac_out(mt, ntile, psb, deps):
            te = P.op("v", lambda e: e.tensor_copy(ev2.ap[:, :512], psb.ap[:, :]), deps + ev2.wdep())
            ev2.ww(te); psb.rr(te)
            ts = P.dma("dG", contrib[mt * 128:(mt + 1) * 128, ntile * 512:(ntile + 1) * 512],
                       ev2.ap[:, :512], [te], key=ev2.key)
            ev2.rr(ts)
            return ts

        self.mm_stream(self.w["w_out"], DM, DM, self.ao_fm, T, evac_out, rhs_dep=fout)
        self.add_contrib_fm(contrib, dep=P.fence())

    def final_out(self):
        self.P.dma("dS", self.out[:, :], self.xres[:, :], [self.t_x0])


def make_inputs(ins, x_seq, T):
    DM, DI, DH = 1024, 2048, 4096
    f32 = np.float32
    import ml_dtypes
    bf16 = ml_dtypes.bfloat16
    g = lambda k: np.asarray(ins[k], dtype=f32)
    d = {}
    d["x_in"] = np.ascontiguousarray(x_seq, dtype=f32)
    d["in_projT"] = np.ascontiguousarray(g("L0_attn_in_proj_w").T).astype(bf16)
    d["x_projT"] = np.ascontiguousarray(g("L0_attn_x_proj_w").T).astype(bf16)
    d["dt_projT"] = np.ascontiguousarray(g("L0_attn_dt_proj_w").T).astype(bf16)
    d["out_projT"] = np.ascontiguousarray(g("L0_attn_out_proj_w").T).astype(bf16)
    d["w_qkv"] = np.ascontiguousarray(g("L1_attn_w_qkv")).astype(bf16)
    d["w_out"] = np.ascontiguousarray(g("L1_attn_w_out")).astype(bf16)
    d["w1T"] = np.ascontiguousarray(g("L1_ffn_w1").T, dtype=f32)
    d["w2T"] = np.ascontiguousarray(g("L1_ffn_w2").T, dtype=f32)
    d["wfr"] = np.ascontiguousarray(g("L0_ffn_w_freq_real"), dtype=f32)
    d["wfi"] = np.ascontiguousarray(g("L0_ffn_w_freq_imag"), dtype=f32)
    k = np.arange(DM)
    ang = 2.0 * np.pi * ((k[:, None] * k[None, :]) % DM) / DM
    d["C_d"] = np.cos(ang).astype(f32)
    d["S_d"] = np.sin(ang).astype(f32)
    gb = np.zeros((128, 4 * DM), f32)
    gb[:, 0:DM] = g("L0_attn_ln_gamma")[None, :]
    gb[:, DM:2 * DM] = g("L0_attn_ln_beta")[None, :]
    gb[:, 2 * DM:3 * DM] = g("L1_attn_ln_gamma")[None, :]
    gb[:, 3 * DM:4 * DM] = g("L1_attn_ln_beta")[None, :]
    d["gb"] = gb.astype(bf16)
    d["ident"] = np.eye(128, dtype=bf16)
    qi = np.arange(128)
    maskp = qi[:, None] > qi[None, :]
    masko = qi[:, None] < qi[None, :]
    mb = np.zeros((128, 512), f32)
    mb[:, 0:256][np.concatenate([maskp, masko], axis=1)] = -1e38
    mb[:, 256:512][np.concatenate([masko, np.ones((128, 128), bool)], axis=1)] = -1e38
    d["maskb"] = mb.astype(bf16)
    sm = np.zeros((128, 504), f32)
    cw = g("L0_attn_conv_w")[:, 0, :].reshape(DI // 128, 128, 4)
    for c in range(DI // 128):
        sm[:, 4 * c:4 * c + 4] = cw[c]
    sm[:, 64:80] = g("L0_attn_conv_b").reshape(DI // 128, 128).T
    sm[:, 80:96] = g("L0_attn_D_skip").reshape(DI // 128, 128).T
    sm[:, 96:112] = g("L0_attn_dt_proj_b").reshape(DI // 128, 128).T
    al = g("L0_attn_A_log").reshape(DI // 128, 128, 16)
    for c in range(DI // 128):
        sm[:, 112 + 16 * c:112 + 16 * (c + 1)] = al[c]
    cw1 = g("L1_ffn_conv_w")[:, 0, :].reshape(DH // 128, 128, 3)
    for c in range(DH // 128):
        sm[:, 368 + 3 * c:368 + 3 * c + 3] = cw1[c]
    sm[:, 464:496] = g("L1_ffn_conv_b").reshape(DH // 128, 128).T
    sm[:, 496:504] = 0.5 * g("L0_ffn_bias").reshape(DM // 128, 128).T
    d["smalls"] = sm
    return d

def kernel(**inputs):
    import numpy as np
    from concourse.bass_utils import run_bass_kernel_spmd
    T = 8192
    x = np.asarray(inputs["L0_attn_x"], dtype=np.float32)
    B = x.shape[0]
    mk = MK(T)
    nc = mk.build()
    base = make_inputs(inputs, x[0], T)
    in_maps = []
    for core in range(8):
        m = dict(base)
        m["x_in"] = np.ascontiguousarray(x[core % B], dtype=np.float32)
        in_maps.append(m)
    res = run_bass_kernel_spmd(nc, in_maps, core_ids=list(range(8)))
    out = np.stack([np.asarray(res.results[b]["out"]) for b in range(B)], axis=0)
    return out.astype(np.float32)


# version 08
# speedup vs baseline: 1.2891x; 1.0009x over previous
import numpy as np
from contextlib import ExitStack
import concourse.bass as bass
import concourse.mybir as mybir

F32 = mybir.dt.float32
F32R = mybir.dt.float32r
BF16 = mybir.dt.bfloat16
I32 = mybir.dt.int32
AL = mybir.AluOpType
AF = mybir.ActivationFunctionType


class Prog:
    ENG = {"v": "vector", "s": "scalar", "t": "tensor", "g": "gpsimd", "y": "sync"}

    def __init__(self, nc, ctx):
        self.nc = nc
        self.ctx = ctx
        self.sems = {}
        self.items = {e: [] for e in self.ENG}
        self.cnt = {}
        for q in ["v", "s", "t", "gc"]:
            self._newsem(q)

    def _newsem(self, name):
        self.sems[name] = self.ctx.enter_context(self.nc.semaphore("sem_" + name))
        self.cnt[name] = 0

    def _emit(self, eng, sem, step, fn, deps):
        deps = [d for d in (deps or []) if d is not None]
        self.cnt[sem] += step
        tok = (sem, self.cnt[sem])
        self.items[eng].append((fn, tuple(deps), sem, step))
        return tok

    def op(self, eng, fn, deps=None):
        assert eng in ("v", "s", "t", "g")
        return self._emit(eng, "gc" if eng == "g" else eng, 1, fn, deps)

    def dma(self, q, out, in_, deps=None, key=None, slow=False):
        eng = {"dS": "y", "dA": "s", "dG": "g"}[q]
        sem = f"{q}_{key if key is not None else 'misc'}"
        if sem not in self.sems:
            self._newsem(sem)
        if slow:
            return self._emit(eng, sem, 16,
                              lambda e: e.dma_start(out=out, in_=in_, allow_slow_non_contiguous=True),
                              deps)
        return self._emit(eng, sem, 16, lambda e: e.dma_start(out=out, in_=in_), deps)

    def fence(self):
        return [(s, c) for s, c in self.cnt.items() if c > 0]

    def build(self, block):
        sems = self.sems
        for eng, engname in self.ENG.items():
            items = self.items[eng]
            if not items and engname != "sync":
                continue

            def make(items, eng):
                def body(e):
                    water = {}
                    dedup = True
                    for fn, deps, sem, step in items:
                        for (ds, dv) in deps:
                            if dedup and water.get(ds, 0) >= dv:
                                continue
                            e.wait_ge(sems[ds], dv)
                            water[ds] = dv
                        fn(e).then_inc(sems[sem], step)
                    if eng == "y":
                        for s2, c in self.cnt.items():
                            if c > 0 and water.get(s2, 0) < c:
                                e.wait_ge(sems[s2], c)
                return body
            getattr(block, engname)(make(items, eng))


_BUFN = [0]


class Buf:
    __slots__ = ("ap", "w", "r", "key", "sticky", "sr")

    def __init__(self, ap, sticky=False):
        self.ap = ap
        self.w = None
        self.r = []
        self.sticky = sticky
        self.sr = {}
        _BUFN[0] += 1
        self.key = _BUFN[0]

    def wdep(self):
        base = self.r + ([self.w] if self.w else [])
        if self.sticky:
            base = base + [(s, v) for s, v in self.sr.items()]
        return base

    def rdep(self):
        return [self.w] if self.w else []

    def ww(self, tok):
        if self.sticky:
            for (s, v) in self.r:
                if self.sr.get(s, -1) < v:
                    self.sr[s] = v
            if self.w is not None:
                s, v = self.w
                if self.sr.get(s, -1) < v:
                    self.sr[s] = v
        self.w = tok
        self.r = []

    def rr(self, tok):
        self.r.append(tok)
        if len(self.r) > 24:
            self.r = self.r[-24:]


def aslist(d):
    return d if isinstance(d, list) else ([d] if d else [])


class MK:
    NS = 16
    DT = 64

    def __init__(self, T, enabled=(1, 2, 3, 4, 5, 6)):
        self.T = T
        self.DM = 1024
        self.DI = 2048
        self.DH = 4096
        self.enabled = tuple(enabled)
        self.nc = bass.Bass("TRN2", target_bir_lowering=False)
        self.ctx = ExitStack()
        self._uid = 0

    def uid(self, p="u"):
        self._uid += 1
        return f"{p}_{self._uid}"

    def dram(self, name, shape, dt=F32, kind=None):
        t = self.nc.dram_tensor(name, shape, dt, kind=kind) if kind else self.nc.dram_tensor(name, shape, dt)
        return t.ap()

    def sbuf(self, name, shape, dt=F32):
        return Buf(self.ctx.enter_context(self.nc.sbuf_tensor(name, shape, dt)))

    def pool(self, tag):
        return self._pool[tag]

    def tfm(self, name, C, T, tw, dt=F32):
        return self.dram(name, [C // 128, T // tw, 128, tw], dt)

    @staticmethod
    def tfm_tile(ap4, c, n):
        return ap4[c, n]

    @staticmethod
    def tfm_rhs(ap4, k0, k1, ntile):
        return ap4[k0:k1, ntile].rearrange("k p t -> p k t")

    def psumb(self, name, shape, dt=F32):
        return Buf(self.ctx.enter_context(self.nc.psum_tensor(name, shape, dt)), sticky=True)

    def build(self):
        nc, T, DM, DI, DH = self.nc, self.T, self.DM, self.DI, self.DH
        ein = lambda n, s, dt=F32: self.dram(n, s, dt, kind="ExternalInput")
        self.x_in = ein("x_in", [T, DM])
        self.out = self.dram("out", [T, DM], F32, kind="ExternalOutput")
        w = {}
        w["in_projT"] = ein("in_projT", [DM, 2 * DI], BF16)
        w["x_projT"] = ein("x_projT", [DI, 96], BF16)
        w["dt_projT"] = ein("dt_projT", [self.DT, DI], BF16)
        w["out_projT"] = ein("out_projT", [DI, DM], BF16)
        w["w_qkv"] = ein("w_qkv", [DM, 3 * DM], BF16)
        w["w_out"] = ein("w_out", [DM, DM], BF16)
        w["w1T"] = ein("w1T", [DM, DH])
        w["w2T"] = ein("w2T", [DH, DM])
        w["wfr"] = ein("wfr", [DM, DM], F32R)
        w["wfi"] = ein("wfi", [DM, DM], F32R)
        w["C_d"] = ein("C_d", [DM, DM], F32R)
        w["S_d"] = ein("S_d", [DM, DM], F32R)
        w["gb"] = ein("gb", [128, 4 * DM], BF16)
        w["ident"] = ein("ident", [128, 128], BF16)
        w["maskb"] = ein("maskb", [128, 512], BF16)
        w["smalls"] = ein("smalls", [128, 504])
        self.w = w
        self.xres = self.dram("xres", [T, DM])
        self.xnp_d = self.dram("xnp_d", [T, DM])
        self.xn_fm16 = self.dram("xn_fm16", [DM, T], BF16)
        self.xn_fmR = self.dram("xn_fmR", [DM, T], F32R)
        self.u2_fm = self.dram("u2_fm", [DI, T], BF16)
        self.sz_fm = self.dram("sz_fm", [DI, T], BF16)
        self.du_fm = self.dram("du_fm", [DI, T], BF16)
        self.delta_fm = self.dram("delta_fm", [DI, T])
        self.dt_d = self.dram("dt_d", [self.DT, T], BF16)
        self.BC_d = self.dram("BC_d", [32, T])
        self.yg_fm = self.dram("yg_fm", [DI, T], BF16)
        self.w1q_d = self.dram("w1q_d", [DM, DH], BF16)
        self.w2q_d = self.dram("w2q_d", [DH, DM], BF16)
        self.wsc_d = self.dram("wsc_d", [1, 2])
        self.xq_fm = self.dram("xq_fm", [DM, T], BF16)
        self.hq_fm = self.dram("hq_fm", [DH, T], BF16)
        self.qk_fm = self.dram("qk_fm", [2 * DM, T], BF16)
        self.v_tm = self.dram("v_tm", [T, DM], BF16)
        self.ao_fm = self.dram("ao_fm", [DM, T], BF16)
        self.Q1_d = self.dram("Q1_d", [DM, DM], F32R)
        self.Q2_d = self.dram("Q2_d", [DM, DM], F32R)
        self.M_d = self.dram("M_d", [DM, DM], F32R)

        self.ident = self.sbuf("sb_ident", [128, 128], BF16)
        self.identR = self.sbuf("sb_identR", [128, 128], F32R)
        self.maskb = self.sbuf("sb_maskb", [128, 512], BF16)
        self.gb = self.sbuf("sb_gb", [128, 4 * DM], BF16)
        self.smalls = self.sbuf("sb_smalls", [128, 504], F32)
        self.A_neg = self.sbuf("sb_Aneg", [128, 256], F32)
        self.invws = self.sbuf("sb_invws", [128, 4], F32)
        self.states = self.sbuf("sb_states", [128, 256], F32)
        self.tails = self.sbuf("sb_tails", [128, 128], F32)
        self.stv = self.sbuf("sb_stv", [128, 16], F32)
        self.mx = self.sbuf("sb_mx", [128, 4], F32)
        self.sms = self.sbuf("sb_sms", [128, 4], F32)
        self.sc2 = self.sbuf("sb_sc2", [128, 2], F32)
        self.ones1 = self.sbuf("sb_ones1", [1, 128], BF16)
        self.wres = self.sbuf("sb_wres", [128, 16384], BF16)
        self.wresR = self.sbuf("sb_wresR", [128, 8192], F32R)
        self.rhs16 = [self.sbuf("sb_rhsA", [128, 4096], BF16), self.sbuf("sb_rhsB", [128, 4096], BF16)]
        self.rhsR = [self.sbuf("sb_rhsRA", [128, 2048], F32R), self.sbuf("sb_rhsRB", [128, 2048], F32R)]
        self._pool = {}
        for tag, cols, dt in [
            ("F32A", 4224, F32), ("F32B", 2048, F32), ("F32C", 2048, F32),
            ("F32D", 512, F32), ("F32E", 512, F32), ("I32A", 2048, I32),
            ("B16A", 2048, BF16), ("B16B", 2048, BF16), ("B16C", 2048, BF16),
            ("B16D", 1024, BF16), ("B16E", 1024, BF16), ("B16F", 1024, BF16),
            ("B16G", 1024, BF16),
            ("BIGB", 4096, BF16), ("BIGC", 4096, BF16),
            ("R32A", 1024, F32R), ("R32B", 1024, F32R),
        ]:
            self._pool[tag] = Buf(self.ctx.enter_context(self.nc.sbuf_tensor("pool_" + tag, [128, cols], dt)))
        self.ps = {}
        for i in range(4):
            self.ps[f"mm{i}"] = self.psumb(f"ps_mm{i}", [128, 512], F32)
        self.ps["tr"] = self.psumb("ps_tr", [128, 1024], BF16)
        self.ps["trR"] = self.psumb("ps_trR", [128, 512], F32R)
        self.ps["sc"] = self.psumb("ps_sc", [128, 512], F32)
        self.ps["av"] = self.psumb("ps_av", [128, 512], F32)

        self.P = Prog(nc, self.ctx)
        self.tt = {}
        self.prelude()
        subs = {1: self.sub_l0ffn, 2: self.sub_mamba, 3: self.sub_l0ffn,
                4: self.sub_l1ffn, 5: self.sub_attn, 6: self.sub_l1ffn}
        for i in self.enabled:
            subs[i]()
        self.final_out()
        with nc.Block() as block:
            self.P.build(block)
        self.ctx.close()
        return nc

    def mm_stream(self, lhsT_d, K, M, rhs_d, N, evac, rhs_dt=BF16, rhs_dep=None,
                  lhs_dep=None, mt_list=None, mt_off=0, rhs_dep_fn=None):
        P = self.P
        nK = (K + 127) // 128
        KG = 4 if rhs_dt == F32R else 8
        ngrp = (nK + KG - 1) // KG
        wbuf = self.wresR if rhs_dt == F32R else self.wres
        wcap = 8192 if rhs_dt == F32R else 16384
        resident = nK * M <= wcap
        ldep = aslist(lhs_dep)
        full_chunks = (K % 128 == 0)
        wtok = None
        if resident:
            deps = wbuf.wdep() + ldep
            if full_chunks:
                sview = lhsT_d.rearrange("(k p) m -> p k m", p=128)[:, :, :M]
                dview = wbuf.ap[:, :nK * M].rearrange("p (k m) -> p k m", m=M)
                wtok = P.dma("dS", dview, sview, deps, key=wbuf.key)
            else:
                tt = None
                for k in range(nK):
                    pk = min(128, K - k * 128)
                    tt = P.dma("dS", wbuf.ap[:pk, k * M:(k + 1) * M], lhsT_d[k * 128:k * 128 + pk, :M],
                               deps if k == 0 else [tt], key=wbuf.key)
                wtok = tt
            wbuf.ww(wtok)
        rbufs = self.rhsR if rhs_dt == F32R else self.rhs16
        rdeps = aslist(rhs_dep)
        mts = mt_list if mt_list is not None else list(range((M + 127) // 128))
        nt_cols = 512
        mt_groups = [mts[i:i + 4] for i in range(0, len(mts), 4)]
        nNT = N // nt_cols

        def load_rhs(b, k0, k1, ntile, extra):
            deps = b.wdep() + rdeps + extra
            if rhs_dep_fn is not None:
                deps = deps + rhs_dep_fn(ntile)
            if full_chunks:
                sview = rhs_d.rearrange("(k p) n -> p k n", p=128)[:, k0:k1,
                                                                   ntile * nt_cols:(ntile + 1) * nt_cols]
                dview = b.ap[:, :(k1 - k0) * nt_cols].rearrange("p (k n) -> p k n", n=nt_cols)
                return P.dma("dA", dview, sview, deps, key=b.key)
            tt = None
            for k in range(k0, k1):
                pk = min(128, K - k * 128)
                tt = P.dma("dA", b.ap[:pk, (k - k0) * nt_cols:(k - k0 + 1) * nt_cols],
                           rhs_d[k * 128:k * 128 + pk, ntile * nt_cols:(ntile + 1) * nt_cols],
                           deps if k == k0 else [tt], key=b.key)
            return tt

        if ngrp == 1:
            rtoks = {}
            rtoks[0] = load_rhs(rbufs[0], 0, nK, 0, [])
            rbufs[0].ww(rtoks[0])
            for ntile in range(nNT):
                b = rbufs[ntile % 2]
                tt = rtoks[ntile]
                if ntile + 1 < nNT:
                    nb = rbufs[(ntile + 1) % 2]
                    rtoks[ntile + 1] = load_rhs(nb, 0, nK, ntile + 1, [])
                    nb.ww(rtoks[ntile + 1])
                for mts_g in mt_groups:
                    mm_last = {}
                    for j, mt in enumerate(mts_g):
                        ms = min(128, M - mt * 128)
                        psb = self.ps[f"mm{j % 4}"]
                        mm_tok = None
                        for k in range(nK):
                            pk = min(128, K - k * 128)
                            wap = wbuf.ap[:pk, k * M + mt * 128:k * M + mt * 128 + ms]
                            d = [wtok, tt] + (psb.wdep() if k == 0 else [mm_tok])
                            mm_tok = P.op("t", lambda e, wap=wap, psb=psb, b=b, ms=ms, pk=pk, k=k,
                                          st=(k == 0), sp=(k == nK - 1):
                                          e.matmul(psb.ap[:ms, :nt_cols], wap,
                                                   b.ap[:pk, k * nt_cols:(k + 1) * nt_cols],
                                                   start=st, stop=sp), d)
                            psb.ww(mm_tok)
                            b.rr(mm_tok)
                            wbuf.rr(mm_tok)
                        mm_last[mt] = mm_tok
                    for j, mt in enumerate(mts_g):
                        psb = self.ps[f"mm{j % 4}"]
                        et = evac(mt + mt_off, ntile, psb, [mm_last[mt]])
                        psb.rr(et)
                        b.rr(et)
        else:
            for ntile in range(nNT):
                for mts_g in mt_groups:
                    mm_tok = {mt: None for mt in mts_g}
                    for kg in range(ngrp):
                        b = rbufs[kg % 2]
                        k0 = kg * KG
                        k1 = min(nK, k0 + KG)
                        tt = load_rhs(b, k0, k1, ntile, [])
                        b.ww(tt)
                        if not resident:
                            base = (kg % 2) * 8192
                            dw = self.wres.wdep() + ldep
                            if full_chunks:
                                sview = lhsT_d.rearrange("(k p) m -> p k m", p=128)[:, k0:k1, :M]
                                dview = self.wres.ap[:, base:base + (k1 - k0) * M].rearrange(
                                    "p (k m) -> p k m", m=M)
                                tw = P.dma("dS", dview, sview, dw, key=self.wres.key)
                            else:
                                tw = None
                                for k in range(k0, k1):
                                    pk = min(128, K - k * 128)
                                    tw = P.dma("dS", self.wres.ap[:pk, base + (k - k0) * M:
                                                                  base + (k - k0 + 1) * M],
                                               lhsT_d[k * 128:k * 128 + pk, :M], dw if k == k0 else [tw],
                                               key=self.wres.key)
                            self.wres.ww(tw)
                            wtok = tw
                        for j, mt in enumerate(mts_g):
                            ms = min(128, M - mt * 128)
                            psb = self.ps[f"mm{j % 4}"]
                            for k in range(k0, k1):
                                pk = min(128, K - k * 128)
                                if resident:
                                    wap = wbuf.ap[:pk, k * M + mt * 128:k * M + mt * 128 + ms]
                                else:
                                    base = (kg % 2) * 8192
                                    wap = self.wres.ap[:pk, base + (k - k0) * M + mt * 128:
                                                       base + (k - k0) * M + mt * 128 + ms]
                                d = [wtok, tt]
                                if k == 0:
                                    d += psb.wdep()
                                else:
                                    d.append(mm_tok[mt])
                                mm_tok[mt] = P.op("t", lambda e, wap=wap, psb=psb, b=b, ms=ms, pk=pk,
                                                  kk=(k - k0), st=(k == 0), sp=(k == nK - 1):
                                                  e.matmul(psb.ap[:ms, :nt_cols], wap,
                                                           b.ap[:pk, kk * nt_cols:(kk + 1) * nt_cols],
                                                           start=st, stop=sp), d)
                                psb.ww(mm_tok[mt])
                                b.rr(mm_tok[mt])
                                (wbuf if resident else self.wres).rr(mm_tok[mt])
                    for j, mt in enumerate(mts_g):
                        psb = self.ps[f"mm{j % 4}"]
                        et = evac(mt + mt_off, ntile, psb, [mm_tok[mt]])
                        psb.rr(et)

    def prelude(self):
        P, w = self.P, self.w
        t = P.dma("dS", self.ident.ap[:], w["ident"][:, :], key=self.ident.key); self.ident.ww(t)
        t = P.dma("dS", self.maskb.ap[:], w["maskb"][:, :], key=self.maskb.key); self.maskb.ww(t)
        t = P.dma("dS", self.gb.ap[:], w["gb"][:, :], key=self.gb.key); self.gb.ww(t)
        ts = P.dma("dS", self.smalls.ap[:], w["smalls"][:, :], key=self.smalls.key); self.smalls.ww(ts)
        ti = P.op("v", lambda e: e.tensor_copy(self.identR.ap[:], self.ident.ap[:]),
                  self.ident.rdep() + self.identR.wdep())
        self.identR.ww(ti); self.ident.rr(ti)
        self.csts = self.sbuf("sb_csts", [128, 2], F32)
        tc1 = P.op("v", lambda e: e.memset(self.csts.ap[:, 0:1], 1e-5), self.csts.wdep())
        tc2 = P.op("v", lambda e: e.memset(self.csts.ap[:, 1:2], 1.0), [tc1])
        self.csts.ww(tc2)
        self.nc.const_aps.register(F32, 1e-5, self.csts.ap[:, 0:1])
        self.nc.const_aps.register(F32, 1.0, self.csts.ap[:, 1:2])
        self.t_x0 = P.dma("dS", self.xres[:, :], self.x_in[:, :])
        self.off = {"cw0": 0, "cb0": 64, "dsk": 80, "dtb": 96, "A": 112, "cw1": 368, "cb1": 464, "fbias": 496}
        ta = P.op("s", lambda e: e.activation(out=self.A_neg.ap[:], in_=self.smalls.ap[:, 112:368],
                                              func=AF.Exp), [ts] + self.A_neg.wdep())
        self.smalls.rr(ta)
        ta = P.op("v", lambda e: e.tensor_scalar(self.A_neg.ap[:], self.A_neg.ap[:], -1.0, None, AL.mult), [ta])
        self.A_neg.ww(ta)
        tz = P.op("v", lambda e: e.memset(self.states.ap[:], 0.0), self.states.wdep())
        self.states.ww(tz)
        tz = P.op("v", lambda e: e.memset(self.tails.ap[:], 0.0), self.tails.wdep())
        self.tails.ww(tz)
        to1 = P.op("v", lambda e: e.memset(self.ones1.ap[:], 1.0), self.ones1.wdep())
        self.ones1.ww(to1)
        self.quant_weights()
        self.build_M()

    def smalls_pp(self, key, idx):
        return self.smalls.ap[:, self.off[key] + idx:self.off[key] + idx + 1]

    def quant_weights(self):
        P = self.P
        HB = 2048
        buf = self.pool("F32A")
        g = self.pool("F32B")
        qb = self.pool("F32C")
        qi = self.pool("I32A")
        qo = self.pool("B16A")
        acc = self.sbuf("q_acc", [128, 2], F32)
        part = self.sbuf("q_part", [128, 64], F32)
        sval = self.sbuf("q_sval", [1, 1], F32)
        onec = self.sbuf("q_onec", [128, 1], F32)
        totb = self.sbuf("q_totb", [128, 1], F32)
        rws = self.sbuf("q_rws", [128, 2], F32)
        toc = P.op("v", lambda e: e.memset(onec.ap[:], 1.0), onec.wdep())
        onec.ww(toc)
        self.t_wq = None
        for (wT_d, dst, K, N, idx) in [(self.w["w1T"], self.w1q_d, self.DM, self.DH, 0),
                                       (self.w["w2T"], self.w2q_d, self.DH, self.DM, 1)]:
            nH = (N + HB - 1) // HB
            tprev = None
            ncols = 0
            for kk in range(K // 128):
                for h in range(nH):
                    hw = min(HB, N - h * HB)
                    tl = P.dma("dS", buf.ap[:, :hw], wT_d[kk * 128:(kk + 1) * 128, h * HB:h * HB + hw],
                               buf.wdep() + [tprev], key=buf.key)
                    buf.ww(tl)
                    tv = P.op("s", lambda e, hw=hw: e.activation(out=g.ap[:, :hw], in_=buf.ap[:, :hw],
                                                                 func=AF.Abs, accum_out=acc.ap[:, 0:1]),
                              [tl] + g.wdep() + acc.wdep())
                    g.ww(tv); acc.ww(tv); buf.rr(tv)
                    tc = P.op("v", lambda e, c=ncols: e.tensor_copy(part.ap[:, c:c + 1], acc.ap[:, 0:1]),
                              [tv] + (part.wdep() if ncols == 0 else []))
                    part.ww(tc); acc.rr(tc)
                    tprev = tc
                    ncols += 1
            t1 = P.op("v", lambda e, nc_=ncols: e.tensor_reduce(totb.ap[:], part.ap[:, :nc_],
                                                                mybir.AxisListType.X, AL.add),
                      [tprev] + totb.wdep())
            totb.ww(t1); part.rr(t1)
            psb = self.ps["sc"]
            tmm = P.op("t", lambda e: e.matmul(psb.ap[:1, :1], totb.ap[:], onec.ap[:],
                                               start=True, stop=True),
                       [t1, toc] + psb.wdep())
            psb.ww(tmm); totb.rr(tmm); onec.rr(tmm)
            tsv = P.op("v", lambda e, d=1.0 / (K * N): e.tensor_scalar(sval.ap[:], psb.ap[0:1, 0:1], d, 1e-5,
                                                                      AL.mult, AL.max),
                       [tmm] + sval.wdep())
            sval.ww(tsv); psb.rr(tsv)
            tst = P.dma("dS", self.wsc_d[0:1, idx:idx + 1], sval.ap[:], [tsv], key=sval.key)
            sval.rr(tst)
            tbc = P.dma("dG", self.invws.ap[:, idx:idx + 1], self.wsc_d[0, idx:idx + 1].partition_broadcast(128),
                        [tst] + self.invws.wdep(), key=self.invws.key)
            self.invws.ww(tbc)
            trw = P.op("v", lambda e, idx=idx: e.reciprocal(rws.ap[:, idx:idx + 1],
                                                            self.invws.ap[:, idx:idx + 1]),
                       [tbc] + (rws.wdep() if idx == 0 else []))
            rws.ww(trw); self.invws.rr(trw)
            tpr = None
            for kk in range(K // 128):
                for h in range(nH):
                    hw = min(HB, N - h * HB)
                    tl = P.dma("dS", buf.ap[:, :hw], wT_d[kk * 128:(kk + 1) * 128, h * HB:h * HB + hw],
                               buf.wdep() + [tpr, trw], key=buf.key)
                    buf.ww(tl)
                    tg = P.op("v", lambda e, idx=idx, hw=hw: e.tensor_scalar(g.ap[:, :hw], buf.ap[:, :hw],
                                                                             rws.ap[:, idx:idx + 1], None,
                                                                             AL.mult), [tl, trw] + g.wdep())
                    g.ww(tg); buf.rr(tg); rws.rr(tg)
                    tg = P.op("v", lambda e, hw=hw: e.tensor_scalar(g.ap[:, :hw], g.ap[:, :hw], -1.5, 1.5,
                                                                    AL.max, AL.min), [tg])
                    g.ww(tg)
                    tq = P.op("v", lambda e, hw=hw: e.tensor_scalar(qi.ap[:, :hw], g.ap[:, :hw], 128.0, None,
                                                                    AL.add), [tg] + qi.wdep())
                    qi.ww(tq)
                    tq2 = P.op("v", lambda e, hw=hw: e.tensor_copy(qb.ap[:, :hw], qi.ap[:, :hw]),
                               [tq] + qb.wdep())
                    qb.ww(tq2); qi.rr(tq2)
                    tg2 = P.op("v", lambda e, hw=hw: e.scalar_tensor_tensor(g.ap[:, :hw], g.ap[:, :hw], 128.0,
                                                                            qb.ap[:, :hw], AL.add,
                                                                            AL.subtract), [tq2, tg])
                    g.ww(tg2); qb.rr(tg2)
                    tq3 = P.op("v", lambda e, hw=hw: e.scalar_tensor_tensor(qb.ap[:, :hw], g.ap[:, :hw], 0.5,
                                                                            qb.ap[:, :hw], AL.is_ge,
                                                                            AL.add), [tg2])
                    qb.ww(tq3); g.rr(tq3)
                    tq4 = P.op("v", lambda e, hw=hw: e.tensor_scalar(qb.ap[:, :hw], qb.ap[:, :hw], -128.0,
                                                                     None, AL.add), [tq3])
                    qb.ww(tq4)
                    tq5 = P.op("v", lambda e, hw=hw: e.tensor_scalar(qo.ap[:, :hw], qb.ap[:, :hw], -1.0, 1.0,
                                                                     AL.max, AL.min), [tq4] + qo.wdep())
                    qo.ww(tq5); qb.rr(tq5)
                    tpr = P.dma("dG", dst[kk * 128:(kk + 1) * 128, h * HB:h * HB + hw], qo.ap[:, :hw],
                                [tq5], key=qo.key)
                    qo.rr(tpr)
            self.t_wq = tpr if self.t_wq is None else tpr

    def build_M(self):
        P, w, DM = self.P, self.w, self.DM
        ev = self.pool("R32A")
        l1 = self.sbuf("ml1", [128, 128], F32R)
        l2 = self.sbuf("ml2", [128, 128], F32R)
        l2n = self.sbuf("ml2n", [128, 128], F32R)
        r1 = Buf(self.rhsR[0].ap[:, 0:512])
        r2 = Buf(self.rhsR[1].ap[:, 0:512])
        r1.w = self.rhsR[0].w; r2.w = self.rhsR[1].w

        def pair(A1, B1, A2, B2, neg2, out_d, scale):
            td = None
            for mt in range(DM // 128):
                for ntile in range(DM // 512):
                    psb = self.ps["mm0"]
                    mm_tok = None
                    for kk in range(DM // 128):
                        tl1 = P.dma("dS", l1.ap[:], A1[kk * 128:(kk + 1) * 128, mt * 128:(mt + 1) * 128],
                                    l1.wdep(), key=l1.key)
                        l1.ww(tl1)
                        tr1 = P.dma("dA", r1.ap[:], B1[kk * 128:(kk + 1) * 128, ntile * 512:(ntile + 1) * 512],
                                    r1.wdep(), key=r1.key)
                        r1.ww(tr1)
                        tl2 = P.dma("dS", l2.ap[:], A2[kk * 128:(kk + 1) * 128, mt * 128:(mt + 1) * 128],
                                    l2.wdep(), key=l2.key)
                        l2.ww(tl2)
                        tr2 = P.dma("dA", r2.ap[:], B2[kk * 128:(kk + 1) * 128, ntile * 512:(ntile + 1) * 512],
                                    r2.wdep(), key=r2.key)
                        r2.ww(tr2)
                        d = [tl1, tr1] + (psb.wdep() if kk == 0 else [mm_tok])
                        mm_tok = P.op("t", lambda e, f=(kk == 0): e.matmul(psb.ap[:, :], l1.ap[:], r1.ap[:],
                                                                          start=f, stop=False), d)
                        psb.ww(mm_tok); l1.rr(mm_tok); r1.rr(mm_tok)
                        lu, dl = l2, [tl2]
                        if neg2:
                            tn = P.op("v", lambda e: e.tensor_scalar(l2n.ap[:], l2.ap[:], -1.0, None,
                                                                    AL.mult), [tl2] + l2n.wdep())
                            l2n.ww(tn); l2.rr(tn)
                            lu, dl = l2n, [tn]
                        mm_tok = P.op("t", lambda e, lu=lu, lk=(kk == DM // 128 - 1): e.matmul(
                            psb.ap[:, :], lu.ap[:], r2.ap[:], start=False, stop=lk), dl + [tr2, mm_tok])
                        psb.ww(mm_tok); lu.rr(mm_tok); r2.rr(mm_tok)
                    te = P.op("s", lambda e, sc=scale: e.activation(out=ev.ap[:, :512], in_=psb.ap[:, :],
                                                                   func=AF.Identity, scale=sc),
                              [mm_tok] + ev.wdep())
                    ev.ww(te); psb.rr(te)
                    td = P.dma("dS", out_d[mt * 128:(mt + 1) * 128, ntile * 512:(ntile + 1) * 512],
                               ev.ap[:, :512], [te], key=ev.key)
                    ev.rr(td)
            return td

        pair(w["wfr"], w["C_d"], w["wfi"], w["S_d"], False, self.Q1_d, 1.0)
        pair(w["wfi"], w["C_d"], w["wfr"], w["S_d"], True, self.Q2_d, 1.0)
        self.t_M = pair(self.Q1_d, w["C_d"], self.Q2_d, w["S_d"], True, self.M_d, 1.0 / DM)

    def ln(self, src_d, dst_tm=None, dst_fm=None, fm_dt=BF16, gbi=None, dep=None, tkey=None):
        dep = aslist(dep)
        if tkey is not None:
            self.tt[tkey] = []
        P, T, DM = self.P, self.T, self.DM
        bxt, xts = self.slots("F32A", 1024, 2)
        sq = self.pool("F32B")
        bxo, xos = self.slots("F32C", 1024, 2)
        bxb, xbs = self.slots("B16A", 1024, 2)
        xb2 = self.pool("B16B")
        xr = self.pool("R32A")
        evb = self.pool("B16C")
        evr = self.pool("R32B")
        stv0 = self.stv
        stvs = [Buf(stv0.ap[:, 0:8]), Buf(stv0.ap[:, 8:16])]
        for b in stvs:
            b.w = stv0.w; b.r = list(stv0.r)
        last = None
        for it in range(T // 128):
            sl = it % 2
            xt, xo, xb, stv = xts[sl], xos[sl], xbs[sl], stvs[sl]
            tl = P.dma("dS", xt.ap[:], src_d[it * 128:(it + 1) * 128, :], xt.wdep() + dep + [last],
                       key=xt.key)
            xt.ww(tl)
            t1 = P.op("s", lambda e, xt=xt, stv=stv: e.activation(out=sq.ap[:, :1024], in_=xt.ap[:],
                                                                  func=AF.Identity,
                                                                  accum_out=stv.ap[:, 0:1]),
                      [tl] + sq.wdep() + stv.wdep())
            t2 = P.op("s", lambda e, xt=xt, stv=stv: e.activation(out=sq.ap[:, :1024], in_=xt.ap[:],
                                                                  func=AF.Square,
                                                                  accum_out=stv.ap[:, 1:2]), [t1])
            sq.ww(t2); stv.ww(t2); xt.rr(t2)
            t3 = P.op("v", lambda e, d=1.0 / DM, stv=stv: e.tensor_scalar(stv.ap[:, 2:3], stv.ap[:, 0:1],
                                                                         d, None, AL.mult), [t2])
            t4 = P.op("v", lambda e, d=1.0 / DM, stv=stv: e.tensor_scalar(stv.ap[:, 3:4], stv.ap[:, 1:2],
                                                                         d, None, AL.mult), [t2])
            t5 = P.op("v", lambda e, stv=stv: e.tensor_tensor(stv.ap[:, 6:7], stv.ap[:, 2:3],
                                                              stv.ap[:, 2:3], AL.mult), [t3])
            t6 = P.op("v", lambda e, stv=stv: e.tensor_tensor(stv.ap[:, 3:4], stv.ap[:, 3:4],
                                                              stv.ap[:, 6:7], AL.subtract), [t4, t5])
            t7 = P.op("s", lambda e, stv=stv: e.activation(out=stv.ap[:, 6:7], in_=stv.ap[:, 3:4],
                                                           func=AF.Ln, bias=1e-5), [t6])
            t8 = P.op("s", lambda e, stv=stv: e.activation(out=stv.ap[:, 4:5], in_=stv.ap[:, 6:7],
                                                           func=AF.Exp, scale=-0.5), [t7])
            t9 = P.op("v", lambda e, stv=stv: e.tensor_tensor(stv.ap[:, 5:6], stv.ap[:, 2:3],
                                                              stv.ap[:, 4:5], AL.mult), [t8, t3])
            t9 = P.op("v", lambda e, stv=stv: e.tensor_scalar(stv.ap[:, 5:6], stv.ap[:, 5:6], -1.0, None,
                                                              AL.mult), [t9])
            stv.ww(t9)
            if dst_tm is not None:
                tn = P.op("s", lambda e, xt=xt, xo=xo, stv=stv: e.activation(
                    out=xo.ap[:], in_=xt.ap[:], func=AF.Identity, bias=stv.ap[:, 5:6],
                    scale=stv.ap[:, 4:5]), [t9] + xo.wdep())
                xo.ww(tn); xt.rr(tn); stv.rr(tn)
                last = P.dma("dG", dst_tm[it * 128:(it + 1) * 128, :], xo.ap[:], [tn], key=xo.key)
                xo.rr(last)
            if dst_fm is not None:
                tgt = xr if fm_dt == F32R else xb
                tn = P.op("s", lambda e, gg=tgt, xt=xt, stv=stv: e.activation(
                    out=gg.ap[:, :1024], in_=xt.ap[:], func=AF.Identity, bias=stv.ap[:, 5:6],
                    scale=stv.ap[:, 4:5]), [t9] + tgt.wdep())
                tgt.ww(tn); xt.rr(tn); stv.rr(tn)
                if gbi is not None:
                    o = gbi * 2 * DM
                    tn = P.op("v", lambda e, o=o, xb=xb: e.tensor_tensor(xb2.ap[:, :1024], xb.ap[:, :1024],
                                                                        self.gb.ap[:, o:o + DM], AL.mult),
                              [tn] + xb2.wdep() + self.gb.rdep())
                    xb2.ww(tn); xb.rr(tn); self.gb.rr(tn)
                    tn = P.op("v", lambda e, o=o, xb=xb: e.tensor_tensor(xb.ap[:, :1024], xb2.ap[:, :1024],
                                                                        self.gb.ap[:, o + DM:o + 2 * DM],
                                                                        AL.add), [tn] + xb.wdep())
                    xb.ww(tn); xb2.rr(tn); self.gb.rr(tn)
                    tgt = xb
                last = self.transpose_out(tgt, DM, dst_fm, it, fm_dt, evb if fm_dt == BF16 else evr, [tn])
                if tkey is not None:
                    self.tt[tkey].append(last)
        self.slots_done(bxt, xts)
        self.slots_done(bxo, xos)
        self.slots_done(bxb, xbs)
        self.slots_done(stv0, stvs)
        return last

    def transpose_out(self, srcbuf, D, dst_fm, it, dt, evbuf, deps, src_off=0, dst_row=0):
        P = self.P
        pst = self.ps["tr"] if dt == BF16 else self.ps["trR"]
        idn = self.ident if dt == BF16 else self.identR
        grp_sz = 8 if dt == BF16 else 4
        ts = None
        for grp in range((D // 128 + grp_sz - 1) // grp_sz):
            toks = []
            nb = min(grp_sz, D // 128 - grp * grp_sz)
            for b2 in range(nb):
                blk = grp * grp_sz + b2
                d = deps + idn.rdep() + (pst.wdep() if b2 == 0 else [toks[-1]])
                tt = P.op("t", lambda e, blk=blk, b2=b2: e.transpose(
                    pst.ap[:, b2 * 128:(b2 + 1) * 128],
                    srcbuf.ap[:, src_off + blk * 128:src_off + (blk + 1) * 128],
                    idn.ap[:]), d)
                toks.append(tt)
                srcbuf.rr(tt); idn.rr(tt)
            pst.ww(toks[-1])
            te = P.op("v", lambda e, nb=nb: e.tensor_copy(
                evbuf.ap[:, :nb * 128], pst.ap[:, :nb * 128]),
                [toks[-1]] + evbuf.wdep())
            evbuf.ww(te); pst.rr(te)
            r0 = dst_row + grp * grp_sz * 128
            dview = dst_fm[r0:r0 + nb * 128, :].rearrange(
                "(a p) t -> p a t", p=128)[:, :, it * 128:(it + 1) * 128]
            sview = evbuf.ap[:, :nb * 128].rearrange("p (a t) -> p a t", t=128)
            ts = P.dma("dG", dview, sview, [te], key=evbuf.key)
            evbuf.rr(ts)
        return ts

    def add_contrib_fm(self, contrib_fm_d, extra_tm_d=None, dep=None):
        dep = aslist(dep)
        P, T, DM = self.P, self.T, self.DM
        cb = self.pool("R32A")
        ct = self.pool("F32C")
        bxt, xts = self.slots("F32A", 1024, 2)
        et = self.pool("F32B")
        pst = self.ps["trR"]
        last = None
        for it in range(T // 128):
            xt = xts[it % 2]
            sview = contrib_fm_d.rearrange("(a p) t -> p a t", p=128)[:, :, it * 128:(it + 1) * 128]
            dview = cb.ap[:, :DM].rearrange("p (a t) -> p a t", t=128)
            tl = P.dma("dS", dview, sview, cb.wdep() + dep + [last], key=cb.key)
            cb.ww(tl)
            te = None
            for grp in range(2):
                toks = []
                for b2 in range(4):
                    blk = grp * 4 + b2
                    d = [tl] + self.identR.rdep() + (pst.wdep() if b2 == 0 else [toks[-1]])
                    tt = P.op("t", lambda e, blk=blk, b2=b2: e.transpose(
                        pst.ap[:, b2 * 128:(b2 + 1) * 128], cb.ap[:, blk * 128:(blk + 1) * 128],
                        self.identR.ap[:]), d)
                    toks.append(tt); cb.rr(tt); self.identR.rr(tt)
                pst.ww(toks[-1])
                te = P.op("v", lambda e, grp=grp: e.tensor_copy(ct.ap[:, grp * 512:(grp + 1) * 512],
                                                               pst.ap[:, :512]),
                          [toks[-1]] + (ct.wdep() if grp == 0 else []))
                ct.ww(te); pst.rr(te)
            tx = P.dma("dA", xt.ap[:], self.xres[it * 128:(it + 1) * 128, :],
                       xt.wdep() + dep + [self.t_x0], key=xt.key)
            xt.ww(tx)
            d2 = [te, tx]
            if extra_tm_d is not None:
                tex = P.dma("dA", et.ap[:, :DM], extra_tm_d[it * 128:(it + 1) * 128, :],
                            et.wdep() + dep, key=et.key)
                et.ww(tex)
                ta = P.op("v", lambda e, xt=xt: e.tensor_tensor(xt.ap[:], xt.ap[:], et.ap[:, 0:1024],
                                                               AL.add), [tex] + d2)
                xt.ww(ta); et.rr(ta)
                d2 = [ta]
            ta = P.op("v", lambda e, xt=xt: e.tensor_tensor(xt.ap[:], xt.ap[:], ct.ap[:, 0:1024],
                                                           AL.add), d2)
            xt.ww(ta); ct.rr(ta)
            last = P.dma("dG", self.xres[it * 128:(it + 1) * 128, :], xt.ap[:], [ta], key=xt.key)
            xt.rr(last)
        self.slots_done(bxt, xts)
        self.t_x0 = last
        return last

    def sub_l0ffn(self):
        P, T, DM = self.P, self.T, self.DM
        fence = P.fence()
        tk0 = self.uid("xnR")
        tln = self.ln(self.xres, dst_fm=self.xn_fmR, fm_dt=F32R, dep=fence, tkey=tk0)
        contrib = self.dram(self.uid("c0"), [DM, T], F32R)
        ev = self.pool("R32B")

        def evac(mt, ntile, psb, deps):
            te = P.op("s", lambda e, mt=mt: e.activation(out=ev.ap[:, :512], in_=psb.ap[:, :],
                                                         func=AF.Identity, scale=0.5,
                                                         bias=self.smalls_pp("fbias", mt)),
                      deps + ev.wdep() + self.smalls.rdep())
            ev.ww(te); self.smalls.rr(te); psb.rr(te)
            ts = P.dma("dG", contrib[mt * 128:(mt + 1) * 128, ntile * 512:(ntile + 1) * 512],
                       ev.ap[:, :512], [te], key=ev.key)
            ev.rr(ts)
            return ts

        self.mm_stream(self.M_d, DM, DM, self.xn_fmR, T, evac, rhs_dt=F32R,
                       lhs_dep=[self.t_M], rhs_dep_fn=self.tfn(tk0))
        self.add_contrib_fm(contrib, dep=P.fence())

    def sub_mamba(self):
        P, T, DM, DI = self.P, self.T, self.DM, self.DI
        fence = P.fence()
        t1 = self.ln(self.xres, dst_tm=self.xnp_d, dep=fence)
        tk1 = self.uid("xnM")
        t2 = self.ln(self.xnp_d, dst_fm=self.xn_fm16, fm_dt=BF16, gbi=0, dep=[t1], tkey=tk1)
        ub = self.pool("F32A")
        acc = self.pool("F32C")
        u2b = self.pool("B16D")
        szb = self.pool("B16E")
        nDIt = DI // 128
        tz = P.op("v", lambda e: e.memset(self.tails.ap[:], 0.0), self.tails.wdep() + fence)
        self.tails.ww(tz)

        def evac_u(mt, ntile, psb, deps):
            te = P.op("s", lambda e: e.activation(out=ub.ap[:, 3:515], in_=psb.ap[:, :], func=AF.Identity),
                      deps + ub.wdep())
            ub.ww(te); psb.rr(te)
            tc = P.op("v", lambda e, mt=mt: e.tensor_copy(ub.ap[:, 0:3], self.tails.ap[:, 3 * mt:3 * mt + 3]),
                      [te] + self.tails.rdep())
            ub.ww(tc); self.tails.rr(tc)
            tt = P.op("v", lambda e, mt=mt: e.tensor_copy(self.tails.ap[:, 3 * mt:3 * mt + 3],
                                                          ub.ap[:, 512:515]), [tc] + self.tails.wdep())
            self.tails.ww(tt)
            t0 = P.op("v", lambda e, mt=mt: e.tensor_scalar(acc.ap[:, :512], ub.ap[:, 0:512],
                                                            self.smalls_pp("cw0", 4 * mt), None, AL.mult),
                      [tc] + acc.wdep() + self.smalls.rdep())
            acc.ww(t0); self.smalls.rr(t0)
            for k2 in (1, 2, 3):
                t0 = P.op("v", lambda e, mt=mt, k2=k2: e.scalar_tensor_tensor(
                    acc.ap[:, :512], ub.ap[:, k2:512 + k2], self.smalls_pp("cw0", 4 * mt + k2),
                    acc.ap[:, :512], AL.mult, AL.add), [t0])
                acc.ww(t0)
            ub.rr(t0)
            ts = P.op("s", lambda e, mt=mt: e.activation(out=u2b.ap[:, :512], in_=acc.ap[:, :512],
                                                         func=AF.Silu, bias=self.smalls_pp("cb0", mt)),
                      [t0] + u2b.wdep() + self.smalls.rdep())
            u2b.ww(ts); acc.rr(ts); self.smalls.rr(ts)
            td = P.dma("dG", self.u2_fm[mt * 128:(mt + 1) * 128, ntile * 512:(ntile + 1) * 512],
                       u2b.ap[:, :512], [ts], key=u2b.key)
            u2b.rr(td)
            return td

        def evac_z(mt, ntile, psb, deps):
            mz = mt - nDIt
            ts = P.op("s", lambda e: e.activation(out=szb.ap[:, :512], in_=psb.ap[:, :], func=AF.Silu),
                      deps + szb.wdep())
            szb.ww(ts); psb.rr(ts)
            td = P.dma("dG", self.sz_fm[mz * 128:(mz + 1) * 128, ntile * 512:(ntile + 1) * 512],
                       szb.ap[:, :512], [ts], key=szb.key)
            szb.rr(td)
            return td

        self.mm_stream(self.w["in_projT"][:, 0:DI], DM, DI, self.xn_fm16, T, evac_u, rhs_dep=[tz],
                       rhs_dep_fn=self.tfn(tk1))
        self.mm_stream(self.w["in_projT"][:, DI:2 * DI], DM, DI, self.xn_fm16, T, evac_z,
                       mt_off=nDIt, rhs_dep_fn=self.tfn(tk1))
        f2 = P.fence()
        xdb = self.pool("B16D")
        xdf = self.pool("F32D")

        def evac_xproj(mt, ntile, psb, deps):
            te = P.op("v", lambda e: e.tensor_copy(xdb.ap[:96, 0:512], psb.ap[:96, :]), deps + xdb.wdep())
            xdb.ww(te); psb.rr(te)
            tf = P.op("v", lambda e: e.tensor_copy(xdf.ap[:32, 0:512], psb.ap[64:96, :]),
                      deps + [te] + xdf.wdep())
            xdf.ww(tf); psb.rr(tf)
            ta = P.dma("dG", self.dt_d[:, ntile * 512:(ntile + 1) * 512], xdb.ap[0:64, 0:512], [te], key=xdb.key)
            xdb.rr(ta)
            tb = P.dma("dG", self.BC_d[:, ntile * 512:(ntile + 1) * 512], xdf.ap[:32, 0:512], [tf], key=xdf.key)
            xdf.rr(tb)
            return tb

        self.mm_stream(self.w["x_projT"], DI, 96, self.u2_fm, T, evac_xproj, rhs_dep=f2,
                       lhs_dep=f2, mt_list=[0])
        f3 = P.fence()
        ab = self.pool("F32B")
        eb = self.pool("F32C")
        rb = self.pool("F32D")
        db = self.pool("F32E")
        dub = self.pool("B16D")
        u2l = self.pool("B16E")

        def evac_dt(mt, ntile, psb, deps):
            ta = P.op("s", lambda e, mt=mt: e.activation(out=ab.ap[:, :512], in_=psb.ap[:, :], func=AF.Abs,
                                                         bias=self.smalls_pp("dtb", mt)),
                      deps + ab.wdep() + self.smalls.rdep())
            ab.ww(ta); self.smalls.rr(ta); psb.rr(ta)
            tb = P.op("s", lambda e: e.activation(out=eb.ap[:, :512], in_=ab.ap[:, :512], func=AF.Exp,
                                                  scale=-1.0), [ta] + eb.wdep())
            eb.ww(tb); ab.rr(tb)
            tc = P.op("s", lambda e: e.activation(out=eb.ap[:, :512], in_=eb.ap[:, :512], func=AF.Ln,
                                                  bias=1.0), [tb])
            eb.ww(tc)
            tr = P.op("s", lambda e, mt=mt: e.activation(out=rb.ap[:, :512], in_=psb.ap[:, :], func=AF.Relu,
                                                         bias=self.smalls_pp("dtb", mt)),
                      deps + [tc] + rb.wdep() + self.smalls.rdep())
            rb.ww(tr); self.smalls.rr(tr); psb.rr(tr)
            td = P.op("v", lambda e: e.tensor_tensor(db.ap[:, :512], rb.ap[:, :512], eb.ap[:, :512],
                                                     AL.add), [tr, tc] + db.wdep())
            db.ww(td); rb.rr(td); eb.rr(td)
            tw = P.dma("dG", self.delta_fm[mt * 128:(mt + 1) * 128, ntile * 512:(ntile + 1) * 512],
                       db.ap[:, :512], [td], key=db.key)
            db.rr(tw)
            tu = P.dma("dA", u2l.ap[:, :512],
                       self.u2_fm[mt * 128:(mt + 1) * 128, ntile * 512:(ntile + 1) * 512], u2l.wdep(),
                       key=u2l.key)
            u2l.ww(tu)
            tm = P.op("v", lambda e: e.tensor_tensor(dub.ap[:, :512], db.ap[:, :512], u2l.ap[:, :512],
                                                     AL.mult), [td, tu] + dub.wdep())
            dub.ww(tm); db.rr(tm); u2l.rr(tm)
            t2w = P.dma("dG", self.du_fm[mt * 128:(mt + 1) * 128, ntile * 512:(ntile + 1) * 512],
                        dub.ap[:, :512], [tm], key=dub.key)
            dub.rr(t2w)
            return t2w

        self.mm_stream(self.w["dt_projT"], 64, DI, self.dt_d, T, evac_dt, rhs_dep=f3, lhs_dep=f3)
        self.scan_phase(P.fence())
        f4 = P.fence()
        contrib = self.dram(self.uid("cm"), [DM, T], F32R)
        ev4 = self.pool("R32B")

        def evac_out(mt, ntile, psb, deps):
            te = P.op("v", lambda e: e.tensor_copy(ev4.ap[:, :512], psb.ap[:, :]), deps + ev4.wdep())
            ev4.ww(te); psb.rr(te)
            ts = P.dma("dG", contrib[mt * 128:(mt + 1) * 128, ntile * 512:(ntile + 1) * 512],
                       ev4.ap[:, :512], [te], key=ev4.key)
            ev4.rr(ts)
            return ts

        self.mm_stream(self.w["out_projT"], DI, DM, self.yg_fm, T, evac_out, rhs_dep=f4, lhs_dep=f4)
        self.add_contrib_fm(contrib, extra_tm_d=self.xnp_d, dep=P.fence())

    def slots(self, tag, ncols, n):
        base = self.pool(tag)
        out = []
        for i in range(n):
            b = Buf(base.ap[:, i * ncols:(i + 1) * ncols])
            b.w = base.w
            b.r = list(base.r)
            out.append(b)
        return base, out

    def tfn(self, tkey, per=4):
        tl = self.tt[tkey]
        return lambda nt: [tl[min((nt + 1) * per, len(tl)) - 1]]

    def slots_done(self, base, sl):
        for b in sl:
            if b.w is not None:
                base.rr(b.w)
            for t in b.r:
                base.rr(t)

    def scan_phase(self, fence):
        P, T, DI = self.P, self.T, self.DI
        Tc = min(512, T)
        nCh = DI // 128
        RepB = [self.rhs16[0], self.rhs16[1]]
        RepC = [self.pool("BIGB"), self.pool("BIGC")]
        npb = 4096 // Tc
        bdlt, dlts = self.slots("F32A", Tc, 2)
        bdul, duls = self.slots("B16A", Tc, 2)
        bu2l, u2ls = self.slots("B16B", Tc, 2)
        bszl, szls = self.slots("B16C", Tc, 2)
        bdA, dAs = self.slots("F32B", Tc, 2)
        bbb, bbs = self.slots("B16D", Tc, 2)
        bhn, hns = self.slots("B16E", Tc, 2)
        bhc, hcs = self.slots("B16F", Tc, 2)
        yb = self.pool("F32C")
        ygb = self.pool("B16G")
        nb = (Tc + 511) // 512
        nTch = T // Tc

        def load_ct(tch, ct):
            s = ct % 2
            t1 = P.dma("dS", dlts[s].ap[:], self.delta_fm[ct * 128:(ct + 1) * 128, tch * Tc:(tch + 1) * Tc],
                       dlts[s].wdep() + fence, key=dlts[s].key)
            dlts[s].ww(t1)
            t2 = P.dma("dS", duls[s].ap[:], self.du_fm[ct * 128:(ct + 1) * 128, tch * Tc:(tch + 1) * Tc],
                       duls[s].wdep() + fence, key=duls[s].key)
            duls[s].ww(t2)
            t3 = P.dma("dA", u2ls[s].ap[:], self.u2_fm[ct * 128:(ct + 1) * 128, tch * Tc:(tch + 1) * Tc],
                       u2ls[s].wdep() + fence, key=u2ls[s].key)
            u2ls[s].ww(t3)
            t4 = P.dma("dA", szls[s].ap[:], self.sz_fm[ct * 128:(ct + 1) * 128, tch * Tc:(tch + 1) * Tc],
                       szls[s].wdep() + fence, key=szls[s].key)
            szls[s].ww(t4)

        ones1f = self.sbuf(self.uid("o1f"), [1, 128], F32)
        to1f = P.op("v", lambda e: e.memset(ones1f.ap[:], 1.0), ones1f.wdep() + fence)
        ones1f.ww(to1f)
        bc2s = [self.sbuf(self.uid(f"bc2{i}"), [1, 512], F32) for i in range(2)]
        for tch in range(nTch):
            for rep_list, base_row in ((RepB, 0), (RepC, 16)):
                for n in range(16):
                    rep = rep_list[n // npb]
                    off = (n % npb) * Tc
                    bc2 = bc2s[n % 2]
                    psb = self.ps["sc"] if n % 2 == 0 else self.ps["av"]
                    sv = self.BC_d[base_row + n:base_row + n + 1,
                                   tch * Tc:(tch + 1) * Tc].rearrange("(o r) t -> o r t", o=1)
                    dv = bc2.ap[:, :Tc].rearrange("p (r t) -> p r t", t=Tc)
                    tld = P.dma("dA", dv, sv, bc2.wdep() + fence, key=bc2.key)
                    bc2.ww(tld)
                    tmm = P.op("t", lambda e, psb=psb, bc2=bc2: e.matmul(
                        psb.ap[:, :Tc], ones1f.ap[:], bc2.ap[:, :Tc],
                        start=True, stop=True), [tld, to1f] + psb.wdep() + ones1f.rdep())
                    psb.ww(tmm); bc2.rr(tmm); ones1f.rr(tmm)
                    tev = P.op("v", lambda e, rep=rep, off=off, psb=psb: e.tensor_copy(
                        rep.ap[:, off:off + Tc], psb.ap[:, :Tc]),
                        [tmm] + rep.wdep())
                    rep.ww(tev); psb.rr(tev)
            if tch == 0:
                load_ct(0, 0)
            for ct in range(nCh):
                s = ct % 2
                dlt, dul, u2l, szl = dlts[s], duls[s], u2ls[s], szls[s]
                tl1, tl2, tl3, tl4 = dlt.w, dul.w, u2l.w, szl.w
                if ct + 1 < nCh:
                    load_ct(tch, ct + 1)
                elif tch + 1 < nTch:
                    load_ct(tch + 1, 0)
                banks = [self.ps["mm0"], self.ps["mm1"]][:nb]
                ytoks = []
                for n in range(16):
                    nr = n
                    sn = n % 2
                    dA, bb, hn, hc = dAs[sn], bbs[sn], hns[sn], hcs[sn]
                    ta = P.op("s", lambda e, ct=ct, n=n, dA=dA, dlt=dlt: e.activation(
                        out=dA.ap[:], in_=dlt.ap[:], func=AF.Exp,
                        scale=self.A_neg.ap[:, ct * 16 + n:ct * 16 + n + 1]),
                        [tl1] + dA.wdep() + self.A_neg.rdep())
                    dA.ww(ta); dlt.rr(ta); self.A_neg.rr(ta)
                    Bb = RepB[n // npb]
                    bo = (n % npb) * Tc
                    tb = P.op("v", lambda e, Bb=Bb, bo=bo, bb=bb, dul=dul: e.tensor_tensor(
                        bb.ap[:], dul.ap[:], Bb.ap[:, bo:bo + Tc], AL.mult),
                        [tl2, Bb.w] + bb.wdep())
                    bb.ww(tb); dul.rr(tb); Bb.rr(tb)
                    tsc = P.op("v", lambda e, ct=ct, n=n, hn=hn, dA=dA, bb=bb: e.tensor_tensor_scan(
                        hn.ap[:], dA.ap[:], bb.ap[:],
                        self.states.ap[:, ct * 16 + n:ct * 16 + n + 1], AL.mult, AL.add),
                        [ta, tb] + hn.wdep() + self.states.rdep() + self.states.wdep())
                    hn.ww(tsc); dA.rr(tsc); bb.rr(tsc); self.states.rr(tsc)
                    tst = P.op("v", lambda e, ct=ct, n=n, hn=hn: e.tensor_copy(
                        self.states.ap[:, ct * 16 + n:ct * 16 + n + 1], hn.ap[:, Tc - 1:Tc]), [tsc])
                    self.states.ww(tst); hn.rr(tst)
                    Cb = RepC[n // npb]
                    co = (n % npb) * Tc
                    tm = P.op("v", lambda e, Cb=Cb, co=co, hc=hc, hn=hn: e.tensor_tensor(
                        hc.ap[:], hn.ap[:], Cb.ap[:, co:co + Tc], AL.mult),
                        [tsc, Cb.w] + hc.wdep())
                    hc.ww(tm); hn.rr(tm); Cb.rr(tm)
                    for sb2 in range(nb):
                        w2 = min(512, Tc - sb2 * 512)
                        psb = banks[sb2]
                        d = [tm] + self.ident.rdep() + (psb.wdep() if n == 0 else [ytoks[-1]])
                        ty = P.op("t", lambda e, sb2=sb2, psb=psb, n=n, w2=w2, hc=hc: e.matmul(
                            psb.ap[:, :w2], self.ident.ap[:], hc.ap[:, sb2 * 512:sb2 * 512 + w2],
                            start=(n == 0), stop=(n == 15)), d)
                        ytoks.append(ty)
                        if n == 0:
                            psb.ww(ty)
                        hc.rr(ty); self.ident.rr(ty)
                for sb2 in range(nb):
                    banks[sb2].ww(ytoks[-(nb - sb2)])
                t1s = []
                for sb2 in range(nb):
                    w2 = min(512, Tc - sb2 * 512)
                    psb = banks[sb2]
                    t1 = P.op("v", lambda e, ct=ct, sb2=sb2, psb=psb, w2=w2, u2l=u2l: e.scalar_tensor_tensor(
                        yb.ap[:, sb2 * 512:sb2 * 512 + w2], u2l.ap[:, sb2 * 512:sb2 * 512 + w2],
                        self.smalls_pp("dsk", ct), psb.ap[:, :w2], AL.mult, AL.add),
                        [tl3, ytoks[-(nb - sb2)]] + (yb.wdep() if sb2 == 0 else []) + self.smalls.rdep())
                    yb.ww(t1); u2l.rr(t1); psb.rr(t1); self.smalls.rr(t1)
                    t1s.append(t1)
                t2 = P.op("v", lambda e, szl=szl: e.tensor_tensor(ygb.ap[:, :Tc], yb.ap[:, :Tc],
                                                                 szl.ap[:, :Tc], AL.mult),
                          t1s + [tl4] + ygb.wdep())
                ygb.ww(t2); yb.rr(t2); szl.rr(t2)
                ts = P.dma("dG", self.yg_fm[ct * 128:(ct + 1) * 128, tch * Tc:(tch + 1) * Tc],
                           ygb.ap[:, :Tc], [t2], key=ygb.key)
                ygb.rr(ts)
        for base, sl in [(bdlt, dlts), (bdul, duls), (bu2l, u2ls), (bszl, szls),
                         (bdA, dAs), (bbb, bbs), (bhn, hns), (bhc, hcs)]:
            self.slots_done(base, sl)

    def sub_l1ffn(self):
        P, T, DM, DH = self.P, self.T, self.DM, self.DH
        fence = P.fence()
        t1 = self.ln(self.xres, dst_tm=self.xnp_d, dep=fence)
        tk2 = self.uid("xq")
        tq = self.quant_tm(self.xnp_d, self.xq_fm, T, DM, dep=[t1], tkey=tk2)
        hs_fm = self.dram(self.uid("hs"), [DH, T], BF16)
        ub = self.pool("F32A")
        acc = self.pool("F32C")
        hsb = self.pool("B16D")
        nT = T // 512
        nDHt = DH // 128

        def evac_mm1(mt, ntile, psb, deps):
            te = P.op("s", lambda e: e.activation(out=ub.ap[:, 2:514], in_=psb.ap[:, :], func=AF.Identity,
                                                  scale=self.invws.ap[:, 0:1]),
                      deps + ub.wdep() + self.invws.rdep())
            ub.ww(te); self.invws.rr(te); psb.rr(te)
            tc = P.op("v", lambda e, mt=mt: e.tensor_copy(ub.ap[:, 0:2],
                                                          self.tails.ap[:, 2 * mt:2 * mt + 2]),
                      [te] + self.tails.rdep())
            ub.ww(tc); self.tails.rr(tc)
            tt = P.op("v", lambda e, mt=mt: e.tensor_copy(self.tails.ap[:, 2 * mt:2 * mt + 2],
                                                          ub.ap[:, 512:514]), [tc] + self.tails.wdep())
            self.tails.ww(tt)
            t0 = P.op("v", lambda e, mt=mt: e.tensor_scalar(acc.ap[:, :512], ub.ap[:, 0:512],
                                                            self.smalls_pp("cw1", 3 * mt), None, AL.mult),
                      [tc] + acc.wdep() + self.smalls.rdep())
            acc.ww(t0); self.smalls.rr(t0)
            for k2 in (1, 2):
                t0 = P.op("v", lambda e, mt=mt, k2=k2: e.scalar_tensor_tensor(
                    acc.ap[:, :512], ub.ap[:, k2:512 + k2], self.smalls_pp("cw1", 3 * mt + k2),
                    acc.ap[:, :512], AL.mult, AL.add), [t0])
                acc.ww(t0)
            ub.rr(t0)
            ts = P.op("s", lambda e, mt=mt: e.activation(out=hsb.ap[:, :512], in_=acc.ap[:, :512],
                                                         func=AF.Silu, bias=self.smalls_pp("cb1", mt)),
                      [t0] + hsb.wdep() + self.smalls.rdep())
            hsb.ww(ts); acc.rr(ts); self.smalls.rr(ts)
            lo = ntile * 512 - 1
            if lo < 0:
                td = P.dma("dG", hs_fm[mt * 128:(mt + 1) * 128, 0:511], hsb.ap[:, 1:512], [ts], key=hsb.key)
            else:
                td = P.dma("dG", hs_fm[mt * 128:(mt + 1) * 128, lo:lo + 512], hsb.ap[:, :512], [ts], key=hsb.key)
            hsb.rr(td)
            if ntile == nT - 1:
                tf = P.op("v", lambda e, mt=mt: e.tensor_scalar(acc.ap[:, 0:1],
                                                                self.tails.ap[:, 2 * mt:2 * mt + 1],
                                                                self.smalls_pp("cw1", 3 * mt), None,
                                                                AL.mult),
                          [tt, td] + acc.wdep() + self.smalls.rdep())
                acc.ww(tf); self.tails.rr(tf); self.smalls.rr(tf)
                tf = P.op("v", lambda e, mt=mt: e.scalar_tensor_tensor(
                    acc.ap[:, 0:1], self.tails.ap[:, 2 * mt + 1:2 * mt + 2],
                    self.smalls_pp("cw1", 3 * mt + 1), acc.ap[:, 0:1], AL.mult, AL.add), [tf])
                acc.ww(tf); self.tails.rr(tf)
                tf2 = P.op("s", lambda e, mt=mt: e.activation(out=hsb.ap[:, 0:1], in_=acc.ap[:, 0:1],
                                                              func=AF.Silu, bias=self.smalls_pp("cb1", mt)),
                           [tf] + hsb.wdep())
                hsb.ww(tf2); acc.rr(tf2)
                td = P.dma("dG", hs_fm[mt * 128:(mt + 1) * 128, T - 1:T], hsb.ap[:, 0:1], [tf2, td],
                           key=hsb.key, slow=True)
                hsb.rr(td)
            return td

        tz = P.op("v", lambda e: e.memset(self.tails.ap[:], 0.0), self.tails.wdep() + fence)
        self.tails.ww(tz)
        self.mm_stream(self.w1q_d[:, 0:DH // 2], DM, DH // 2, self.xq_fm, T, evac_mm1,
                       rhs_dep=[tz], lhs_dep=[self.t_wq], rhs_dep_fn=self.tfn(tk2))
        self.mm_stream(self.w1q_d[:, DH // 2:DH], DM, DH // 2, self.xq_fm, T, evac_mm1,
                       rhs_dep=[tz], lhs_dep=[self.t_wq], mt_off=nDHt // 2, rhs_dep_fn=self.tfn(tk2))
        tk3 = self.uid("hq")
        tq2 = self.quant_fm_roundtrip(hs_fm, self.hq_fm, T, DH, dep=P.fence(), tkey=tk3)
        contrib = self.dram(self.uid("c1"), [DM, T], F32R)
        ev = self.pool("R32B")
        th = P.op("v", lambda e: e.tensor_scalar(self.sc2.ap[:, 0:1], self.invws.ap[:, 1:2], 0.5, None,
                                                 AL.mult), self.invws.rdep() + self.sc2.wdep())
        self.sc2.ww(th); self.invws.rr(th)

        def evac_mm2(mt, ntile, psb, deps):
            te = P.op("s", lambda e: e.activation(out=ev.ap[:, :512], in_=psb.ap[:, :],
                                                  func=AF.Identity, scale=self.sc2.ap[:, 0:1]),
                      deps + [th] + ev.wdep())
            ev.ww(te); self.sc2.rr(te); psb.rr(te)
            ts = P.dma("dG", contrib[mt * 128:(mt + 1) * 128, ntile * 512:(ntile + 1) * 512],
                       ev.ap[:, :512], [te], key=ev.key)
            ev.rr(ts)
            return ts

        self.mm_stream(self.w2q_d[:, 0:DM // 2], DH, DM // 2, self.hq_fm, T,
                       evac_mm2, lhs_dep=[self.t_wq], rhs_dep_fn=self.tfn(tk3))
        self.mm_stream(self.w2q_d[:, DM // 2:DM], DH, DM // 2, self.hq_fm, T,
                       evac_mm2, lhs_dep=[self.t_wq], mt_off=DM // 256, rhs_dep_fn=self.tfn(tk3))
        self.add_contrib_fm(contrib, dep=P.fence())

    def quant_tm(self, src_tm_d, dst_fm_d, T, D, dep=None, tkey=None):
        dep = aslist(dep)
        if tkey is not None:
            self.tt[tkey] = []
        P = self.P
        xt = self.pool("F32A")
        g = self.pool("F32B")
        qb = self.pool("F32C")
        qi = self.pool("I32A")
        xq = self.pool("B16B")
        ev = self.pool("B16C")
        mx = self.mx
        last = None
        for it in range(T // 128):
            tl = P.dma("dS", xt.ap[:, :D], src_tm_d[it * 128:(it + 1) * 128, :], xt.wdep() + dep + [last],
                       key=xt.key)
            xt.ww(tl)
            tm = P.op("v", lambda e, D=D: e.tensor_reduce(mx.ap[:, 0:1], xt.ap[:, :D], mybir.AxisListType.X,
                                                          AL.max, apply_absolute_value=True),
                      [tl] + mx.wdep())
            tm = P.op("v", lambda e: e.tensor_scalar(mx.ap[:, 1:2], mx.ap[:, 0:1], 1e-5, None, AL.max), [tm])
            tm = P.op("v", lambda e: e.reciprocal(mx.ap[:, 2:3], mx.ap[:, 1:2]), [tm])
            tm = P.op("v", lambda e: e.tensor_scalar(mx.ap[:, 2:3], mx.ap[:, 2:3], 127.0, None, AL.mult),
                      [tm])
            tm = P.op("v", lambda e: e.tensor_scalar(mx.ap[:, 3:4], mx.ap[:, 1:2], 1.0 / 127.0, None,
                                                     AL.mult), [tm])
            mx.ww(tm)
            tqq = self._quant_block(xt, 0, D, mx, g, qb, qi, xq, [tm, tl])
            last = self.transpose_out(xq, D, dst_fm_d, it, BF16, ev, [tqq])
            if tkey is not None:
                self.tt[tkey].append(last)
        return last

    def _quant_block(self, srcbuf, off, width, mx, g, qb, qi, xq, deps, out_off0=False):
        P = self.P
        HB = 2048
        tq = None
        for o in range(off, off + width, HB):
            wd = min(HB, off + width - o)
            xo = 0 if out_off0 else o
            t1 = P.op("v", lambda e, o=o, wd=wd: e.tensor_scalar(qi.ap[:, :wd], srcbuf.ap[:, o:o + wd],
                                                                 mx.ap[:, 2:3], 128.0, AL.mult, AL.add),
                      deps + qi.wdep() + ([tq] if tq else []))
            qi.ww(t1); srcbuf.rr(t1); mx.rr(t1)
            t2 = P.op("v", lambda e, wd=wd: e.tensor_copy(qb.ap[:, :wd], qi.ap[:, :wd]), [t1] + qb.wdep())
            qb.ww(t2); qi.rr(t2)
            t3 = P.op("v", lambda e, o=o, wd=wd: e.scalar_tensor_tensor(
                g.ap[:, :wd], srcbuf.ap[:, o:o + wd], mx.ap[:, 2:3], qb.ap[:, :wd], AL.mult, AL.subtract),
                [t2] + deps + g.wdep())
            g.ww(t3); srcbuf.rr(t3); qb.rr(t3); mx.rr(t3)
            t4 = P.op("v", lambda e, wd=wd: e.scalar_tensor_tensor(qb.ap[:, :wd], g.ap[:, :wd], -127.5,
                                                                  qb.ap[:, :wd], AL.is_ge, AL.add), [t3])
            qb.ww(t4); g.rr(t4)
            tq = P.op("v", lambda e, xo=xo, wd=wd: e.tensor_scalar(xq.ap[:, xo:xo + wd], qb.ap[:, :wd],
                                                                   -128.0, mx.ap[:, 3:4], AL.add, AL.mult),
                      [t4] + xq.wdep())
            xq.ww(tq); qb.rr(tq); mx.rr(tq)
        return tq

    def quant_fm_roundtrip(self, src_fm_d, dst_fm_d, T, D, dep=None, tkey=None):
        dep = aslist(dep)
        if tkey is not None:
            self.tt[tkey] = []
        P = self.P
        tmv = self.pool("F32A")
        g = self.pool("F32B")
        qb = self.pool("F32C")
        qi = self.pool("I32A")
        cb = self.pool("B16A")
        xq = self.pool("B16B")
        ev = self.pool("B16C")
        mx = self.mx
        pst = self.ps["tr"]
        last = None
        for it in range(T // 128):
            te = None
            for half in range(D // 2048):
                sview = src_fm_d.rearrange("(a p) t -> p a t", p=128)[:, half * 16:(half + 1) * 16,
                                                                       it * 128:(it + 1) * 128]
                dview = cb.ap[:, :2048].rearrange("p (a t) -> p a t", t=128)
                tl = P.dma("dS", dview, sview, cb.wdep() + dep + [last, te], key=cb.key)
                cb.ww(tl)
                for grp in range(2):
                    toks = []
                    for b2 in range(8):
                        blk = grp * 8 + b2
                        d = [tl] + self.ident.rdep() + (pst.wdep() if b2 == 0 else [toks[-1]])
                        tt = P.op("t", lambda e, blk=blk, b2=b2: e.transpose(
                            pst.ap[:, b2 * 128:(b2 + 1) * 128], cb.ap[:, blk * 128:(blk + 1) * 128],
                            self.ident.ap[:]), d)
                        toks.append(tt); cb.rr(tt); self.ident.rr(tt)
                    pst.ww(toks[-1])
                    te = P.op("v", lambda e, half=half, grp=grp: e.tensor_copy(
                        tmv.ap[:, half * 2048 + grp * 1024:half * 2048 + (grp + 1) * 1024], pst.ap[:]),
                        [toks[-1]] + (tmv.wdep() if half == 0 and grp == 0 else []))
                    tmv.ww(te); pst.rr(te)
            tm = P.op("v", lambda e, D=D: e.tensor_reduce(mx.ap[:, 0:1], tmv.ap[:, :D],
                                                          mybir.AxisListType.X, AL.max,
                                                          apply_absolute_value=True), [te] + mx.wdep())
            tm = P.op("v", lambda e: e.tensor_scalar(mx.ap[:, 1:2], mx.ap[:, 0:1], 1e-5, None, AL.max), [tm])
            tm = P.op("v", lambda e: e.reciprocal(mx.ap[:, 2:3], mx.ap[:, 1:2]), [tm])
            tm = P.op("v", lambda e: e.tensor_scalar(mx.ap[:, 2:3], mx.ap[:, 2:3], 127.0, None, AL.mult),
                      [tm])
            tm = P.op("v", lambda e: e.tensor_scalar(mx.ap[:, 3:4], mx.ap[:, 1:2], 1.0 / 127.0, None,
                                                     AL.mult), [tm])
            mx.ww(tm)
            for half in range(D // 2048):
                tqq = self._quant_block(tmv, half * 2048, 2048, mx, g, qb, qi, xq, [tm], out_off0=True)
                last = self.transpose_out(xq, 2048, dst_fm_d, it, BF16, ev, [tqq],
                                          dst_row=half * 2048)
            if tkey is not None:
                self.tt[tkey].append(last)
        return last

    def sub_attn(self):
        P, T, DM = self.P, self.T, self.DM
        fence = P.fence()
        t1 = self.ln(self.xres, dst_tm=self.xnp_d, dep=fence)
        tk4 = self.uid("xnA")
        t2 = self.ln(self.xnp_d, dst_fm=self.xn_fm16, fm_dt=BF16, gbi=1, dep=[t1], tkey=tk4)
        evq = self.pool("B16D")
        vtb = self.pool("B16E")
        pst = self.ps["tr"]

        def evac_qk(mt, ntile, psb, deps):
            te = P.op("v", lambda e: e.tensor_copy(evq.ap[:, :512], psb.ap[:, :]), deps + evq.wdep())
            evq.ww(te); psb.rr(te)
            ts = P.dma("dG", self.qk_fm[mt * 128:(mt + 1) * 128, ntile * 512:(ntile + 1) * 512],
                       evq.ap[:, :512], [te], key=evq.key)
            evq.rr(ts)
            return ts

        def evac_v(mt, ntile, psb, deps):
            te = P.op("v", lambda e: e.tensor_copy(vtb.ap[:, :512], psb.ap[:, :]), deps + vtb.wdep())
            vtb.ww(te); psb.rr(te)
            toks = []
            for b2 in range(4):
                d = [te] + self.ident.rdep() + (pst.wdep() if b2 == 0 else [toks[-1]])
                tt = P.op("t", lambda e, b2=b2: e.transpose(pst.ap[:, b2 * 128:(b2 + 1) * 128],
                                                            vtb.ap[:, b2 * 128:(b2 + 1) * 128],
                                                            self.ident.ap[:]), d)
                toks.append(tt); vtb.rr(tt); self.ident.rr(tt)
            pst.ww(toks[-1])
            te2 = P.op("v", lambda e: e.tensor_copy(evq.ap[:, :512], pst.ap[:, 0:512]),
                       [toks[-1]] + evq.wdep())
            evq.ww(te2); pst.rr(te2)
            dview = self.v_tm[ntile * 512:(ntile + 1) * 512, mt * 128:(mt + 1) * 128].rearrange(
                "(a p) c -> p a c", p=128)
            sview = evq.ap[:, :512].rearrange("p (a c) -> p a c", c=128)
            ts = P.dma("dG", dview, sview, [te2], key=evq.key)
            evq.rr(ts)
            return ts

        self.mm_stream(self.w["w_qkv"][:, 0:2 * DM], DM, 2 * DM, self.xn_fm16, T, evac_qk,
                       rhs_dep_fn=self.tfn(tk4))
        self.mm_stream(self.w["w_qkv"][:, 2 * DM:3 * DM], DM, DM, self.xn_fm16, T, evac_v,
                       rhs_dep_fn=self.tfn(tk4))
        fatt = P.fence()
        qsb = self.pool("B16A")
        ksb = self.pool("B16B")
        vsb = self.pool("B16C")
        ea = self.pool("B16D")
        eaT = self.pool("B16E")
        aost = self.pool("B16F")
        sms = self.sms
        W = 128
        nw = T // W
        pssc, psav = self.ps["sc"], self.ps["av"]
        last = None
        for wdx in range(nw):
            qv = self.qk_fm.rearrange("(a p) t -> p a t", p=128)[:, 0:8, wdx * W:(wdx + 1) * W]
            tq = P.dma("dS", qsb.ap[:, :8 * W].rearrange("p (a t) -> p a t", t=W), qv,
                       qsb.wdep() + fatt + [last], key=qsb.key)
            qsb.ww(tq)
            lo = max(wdx - 1, 0) * W
            kv = self.qk_fm.rearrange("(a p) t -> p a t", p=128)[:, 8:16, lo:lo + 2 * W]
            tk = P.dma("dS", ksb.ap[:, :8 * 2 * W].rearrange("p (a t) -> p a t", t=2 * W), kv,
                       ksb.wdep() + fatt, key=ksb.key)
            ksb.ww(tk)
            vv = self.v_tm[lo:lo + 2 * W, :].rearrange("(a p) c -> p a c", p=128)
            tv = P.dma("dA", vsb.ap[:, :2 * DM].rearrange("p (a c) -> p a c", c=DM), vv,
                       vsb.wdep() + fatt, key=vsb.key)
            vsb.ww(tv)
            tae = None
            for h in range(16):
                pb = 64 * (h % 2)
                tile = h // 2
                d0 = [tq, tk] + pssc.wdep()
                tmm = P.op("t", lambda e, pb=pb, tile=tile: e.matmul(
                    pssc.ap[:, 0:256], qsb.ap[pb:pb + 64, tile * W:(tile + 1) * W],
                    ksb.ap[pb:pb + 64, tile * 2 * W:(tile + 1) * 2 * W], start=True, stop=False), d0)
                qsb.rr(tmm); ksb.rr(tmm)
                moff = 0 if wdx > 0 else 256
                tmm = P.op("t", lambda e, moff=moff: e.matmul(pssc.ap[:, 0:256], self.ident.ap[:],
                                                              self.maskb.ap[:, moff:moff + 256],
                                                              start=False, stop=True),
                           [tmm] + self.maskb.rdep() + self.ident.rdep())
                pssc.ww(tmm); self.maskb.rr(tmm); self.ident.rr(tmm)
                tr = P.op("v", lambda e: e.tensor_reduce(sms.ap[:, 0:1], pssc.ap[:, 0:256],
                                                         mybir.AxisListType.X, AL.max),
                          [tmm] + sms.wdep())
                tr = P.op("v", lambda e: e.tensor_scalar(sms.ap[:, 1:2], sms.ap[:, 0:1], -0.125, None,
                                                         AL.mult), [tr])
                sms.ww(tr)
                texp = P.op("s", lambda e: e.activation(out=ea.ap[:, :256], in_=pssc.ap[:, 0:256],
                                                        func=AF.Exp, scale=0.125, bias=sms.ap[:, 1:2],
                                                        accum_out=sms.ap[:, 2:3]), [tr] + ea.wdep())
                ea.ww(texp); pssc.rr(texp); sms.ww(texp)
                trc = P.op("v", lambda e: e.reciprocal(sms.ap[:, 3:4], sms.ap[:, 2:3]), [texp])
                sms.ww(trc)
                tnm = P.op("v", lambda e: e.tensor_scalar(ea.ap[:, :256], ea.ap[:, :256], sms.ap[:, 3:4],
                                                          None, AL.mult), [trc])
                ea.ww(tnm); sms.rr(tnm)
                toks = []
                for b2 in range(2):
                    d = [tnm] + self.ident.rdep() + (pst.wdep() if b2 == 0 else [toks[-1]])
                    tt = P.op("t", lambda e, b2=b2: e.transpose(pst.ap[:, b2 * 128:(b2 + 1) * 128],
                                                                ea.ap[:, b2 * 128:(b2 + 1) * 128],
                                                                self.ident.ap[:]), d)
                    toks.append(tt); ea.rr(tt); self.ident.rr(tt)
                pst.ww(toks[-1])
                tev = P.op("v", lambda e: e.tensor_copy(eaT.ap[:, :256], pst.ap[:, 0:256]),
                           [toks[-1]] + eaT.wdep())
                eaT.ww(tev); pst.rr(tev)
                tavm = None
                for k2 in range(2):
                    d = [tev, tv] + (psav.wdep() if k2 == 0 else [tavm])
                    tavm = P.op("t", lambda e, k2=k2, h=h: e.matmul(
                        psav.ap[:64, 0:128], vsb.ap[:, k2 * DM + h * 64:k2 * DM + (h + 1) * 64],
                        eaT.ap[:, k2 * 128:(k2 + 1) * 128], start=(k2 == 0), stop=(k2 == 1)), d)
                    eaT.rr(tavm); vsb.rr(tavm)
                psav.ww(tavm)
                tae = P.op("v", lambda e, pb=pb, tile=tile: e.tensor_copy(
                    aost.ap[pb:pb + 64, tile * W:(tile + 1) * W], psav.ap[:64, 0:128]),
                    [tavm] + (aost.wdep() if h == 0 else [tae]))
                aost.ww(tae); psav.rr(tae)
            dview = self.ao_fm.rearrange("(a p) t -> p a t", p=128)[:, :, wdx * W:(wdx + 1) * W]
            sview = aost.ap[:, :8 * W].rearrange("p (a t) -> p a t", t=W)
            last = P.dma("dG", dview, sview, [tae], key=aost.key)
            aost.rr(last)
        fout = P.fence()
        contrib = self.dram(self.uid("ca"), [DM, T], F32R)
        ev2 = self.pool("R32B")

        def evac_out(mt, ntile, psb, deps):
            te = P.op("v", lambda e: e.tensor_copy(ev2.ap[:, :512], psb.ap[:, :]), deps + ev2.wdep())
            ev2.ww(te); psb.rr(te)
            ts = P.dma("dG", contrib[mt * 128:(mt + 1) * 128, ntile * 512:(ntile + 1) * 512],
                       ev2.ap[:, :512], [te], key=ev2.key)
            ev2.rr(ts)
            return ts

        self.mm_stream(self.w["w_out"], DM, DM, self.ao_fm, T, evac_out, rhs_dep=fout)
        self.add_contrib_fm(contrib, dep=P.fence())

    def final_out(self):
        self.P.dma("dS", self.out[:, :], self.xres[:, :], [self.t_x0])


def make_inputs(ins, x_seq, T):
    DM, DI, DH = 1024, 2048, 4096
    f32 = np.float32
    import ml_dtypes
    bf16 = ml_dtypes.bfloat16
    g = lambda k: np.asarray(ins[k], dtype=f32)
    d = {}
    d["x_in"] = np.ascontiguousarray(x_seq, dtype=f32)
    d["in_projT"] = np.ascontiguousarray(g("L0_attn_in_proj_w").T).astype(bf16)
    d["x_projT"] = np.ascontiguousarray(g("L0_attn_x_proj_w").T).astype(bf16)
    d["dt_projT"] = np.ascontiguousarray(g("L0_attn_dt_proj_w").T).astype(bf16)
    d["out_projT"] = np.ascontiguousarray(g("L0_attn_out_proj_w").T).astype(bf16)
    d["w_qkv"] = np.ascontiguousarray(g("L1_attn_w_qkv")).astype(bf16)
    d["w_out"] = np.ascontiguousarray(g("L1_attn_w_out")).astype(bf16)
    d["w1T"] = np.ascontiguousarray(g("L1_ffn_w1").T, dtype=f32)
    d["w2T"] = np.ascontiguousarray(g("L1_ffn_w2").T, dtype=f32)
    d["wfr"] = np.ascontiguousarray(g("L0_ffn_w_freq_real"), dtype=f32)
    d["wfi"] = np.ascontiguousarray(g("L0_ffn_w_freq_imag"), dtype=f32)
    k = np.arange(DM)
    ang = 2.0 * np.pi * ((k[:, None] * k[None, :]) % DM) / DM
    d["C_d"] = np.cos(ang).astype(f32)
    d["S_d"] = np.sin(ang).astype(f32)
    gb = np.zeros((128, 4 * DM), f32)
    gb[:, 0:DM] = g("L0_attn_ln_gamma")[None, :]
    gb[:, DM:2 * DM] = g("L0_attn_ln_beta")[None, :]
    gb[:, 2 * DM:3 * DM] = g("L1_attn_ln_gamma")[None, :]
    gb[:, 3 * DM:4 * DM] = g("L1_attn_ln_beta")[None, :]
    d["gb"] = gb.astype(bf16)
    d["ident"] = np.eye(128, dtype=bf16)
    qi = np.arange(128)
    maskp = qi[:, None] > qi[None, :]
    masko = qi[:, None] < qi[None, :]
    mb = np.zeros((128, 512), f32)
    mb[:, 0:256][np.concatenate([maskp, masko], axis=1)] = -1e38
    mb[:, 256:512][np.concatenate([masko, np.ones((128, 128), bool)], axis=1)] = -1e38
    d["maskb"] = mb.astype(bf16)
    sm = np.zeros((128, 504), f32)
    cw = g("L0_attn_conv_w")[:, 0, :].reshape(DI // 128, 128, 4)
    for c in range(DI // 128):
        sm[:, 4 * c:4 * c + 4] = cw[c]
    sm[:, 64:80] = g("L0_attn_conv_b").reshape(DI // 128, 128).T
    sm[:, 80:96] = g("L0_attn_D_skip").reshape(DI // 128, 128).T
    sm[:, 96:112] = g("L0_attn_dt_proj_b").reshape(DI // 128, 128).T
    al = g("L0_attn_A_log").reshape(DI // 128, 128, 16)
    for c in range(DI // 128):
        sm[:, 112 + 16 * c:112 + 16 * (c + 1)] = al[c]
    cw1 = g("L1_ffn_conv_w")[:, 0, :].reshape(DH // 128, 128, 3)
    for c in range(DH // 128):
        sm[:, 368 + 3 * c:368 + 3 * c + 3] = cw1[c]
    sm[:, 464:496] = g("L1_ffn_conv_b").reshape(DH // 128, 128).T
    sm[:, 496:504] = 0.5 * g("L0_ffn_bias").reshape(DM // 128, 128).T
    d["smalls"] = sm
    return d

def kernel(**inputs):
    import numpy as np
    from concourse.bass_utils import run_bass_kernel_spmd
    T = 8192
    x = np.asarray(inputs["L0_attn_x"], dtype=np.float32)
    B = x.shape[0]
    mk = MK(T)
    nc = mk.build()
    base = make_inputs(inputs, x[0], T)
    in_maps = []
    for core in range(8):
        m = dict(base)
        m["x_in"] = np.ascontiguousarray(x[core % B], dtype=np.float32)
        in_maps.append(m)
    res = run_bass_kernel_spmd(nc, in_maps, core_ids=list(range(8)))
    out = np.stack([np.asarray(res.results[b]["out"]) for b in range(B)], axis=0)
    return out.astype(np.float32)


# version 09 (final)
# speedup vs baseline: 1.3075x; 1.0142x over previous
import numpy as np
from contextlib import ExitStack
import concourse.bass as bass
import concourse.mybir as mybir

F32 = mybir.dt.float32
F32R = mybir.dt.float32r
BF16 = mybir.dt.bfloat16
I32 = mybir.dt.int32
AL = mybir.AluOpType
AF = mybir.ActivationFunctionType


class Prog:
    ENG = {"v": "vector", "s": "scalar", "t": "tensor", "g": "gpsimd", "y": "sync"}

    def __init__(self, nc, ctx):
        self.nc = nc
        self.ctx = ctx
        self.sems = {}
        self.items = {e: [] for e in self.ENG}
        self.cnt = {}
        for q in ["v", "s", "t", "gc"]:
            self._newsem(q)

    def _newsem(self, name):
        self.sems[name] = self.ctx.enter_context(self.nc.semaphore("sem_" + name))
        self.cnt[name] = 0

    def _emit(self, eng, sem, step, fn, deps):
        deps = [d for d in (deps or []) if d is not None]
        self.cnt[sem] += step
        tok = (sem, self.cnt[sem])
        self.items[eng].append((fn, tuple(deps), sem, step))
        return tok

    def op(self, eng, fn, deps=None):
        assert eng in ("v", "s", "t", "g")
        return self._emit(eng, "gc" if eng == "g" else eng, 1, fn, deps)

    def dma(self, q, out, in_, deps=None, key=None, slow=False):
        eng = {"dS": "y", "dA": "s", "dG": "g"}[q]
        sem = f"{q}_{key if key is not None else 'misc'}"
        if sem not in self.sems:
            self._newsem(sem)
        if slow:
            return self._emit(eng, sem, 16,
                              lambda e: e.dma_start(out=out, in_=in_, allow_slow_non_contiguous=True),
                              deps)
        return self._emit(eng, sem, 16, lambda e: e.dma_start(out=out, in_=in_), deps)

    def fence(self):
        return [(s, c) for s, c in self.cnt.items() if c > 0]

    def build(self, block):
        sems = self.sems
        for eng, engname in self.ENG.items():
            items = self.items[eng]
            if not items and engname != "sync":
                continue

            def make(items, eng):
                def body(e):
                    water = {}
                    dedup = True
                    for fn, deps, sem, step in items:
                        for (ds, dv) in deps:
                            if dedup and water.get(ds, 0) >= dv:
                                continue
                            e.wait_ge(sems[ds], dv)
                            water[ds] = dv
                        fn(e).then_inc(sems[sem], step)
                    if eng == "y":
                        for s2, c in self.cnt.items():
                            if c > 0 and water.get(s2, 0) < c:
                                e.wait_ge(sems[s2], c)
                return body
            getattr(block, engname)(make(items, eng))


_BUFN = [0]


class Buf:
    __slots__ = ("ap", "w", "r", "key", "sticky", "sr")

    def __init__(self, ap, sticky=False):
        self.ap = ap
        self.w = None
        self.r = []
        self.sticky = sticky
        self.sr = {}
        _BUFN[0] += 1
        self.key = _BUFN[0]

    def wdep(self):
        base = self.r + ([self.w] if self.w else [])
        if self.sticky:
            base = base + [(s, v) for s, v in self.sr.items()]
        return base

    def rdep(self):
        return [self.w] if self.w else []

    def ww(self, tok):
        if self.sticky:
            for (s, v) in self.r:
                if self.sr.get(s, -1) < v:
                    self.sr[s] = v
            if self.w is not None:
                s, v = self.w
                if self.sr.get(s, -1) < v:
                    self.sr[s] = v
        self.w = tok
        self.r = []

    def rr(self, tok):
        self.r.append(tok)
        if len(self.r) > 24:
            self.r = self.r[-24:]


def aslist(d):
    return d if isinstance(d, list) else ([d] if d else [])


class MK:
    NS = 16
    DT = 64

    def __init__(self, T, enabled=(1, 2, 3, 4, 5, 6)):
        self.T = T
        self.DM = 1024
        self.DI = 2048
        self.DH = 4096
        self.enabled = tuple(enabled)
        self.nc = bass.Bass("TRN2", target_bir_lowering=False)
        self.ctx = ExitStack()
        self._uid = 0

    def uid(self, p="u"):
        self._uid += 1
        return f"{p}_{self._uid}"

    def dram(self, name, shape, dt=F32, kind=None):
        t = self.nc.dram_tensor(name, shape, dt, kind=kind) if kind else self.nc.dram_tensor(name, shape, dt)
        return t.ap()

    def sbuf(self, name, shape, dt=F32):
        return Buf(self.ctx.enter_context(self.nc.sbuf_tensor(name, shape, dt)))

    def pool(self, tag):
        return self._pool[tag]

    def tfm(self, name, C, T, tw, dt=F32):
        return self.dram(name, [C // 128, T // tw, 128, tw], dt)

    @staticmethod
    def tfm_tile(ap4, c, n):
        return ap4[c, n]

    @staticmethod
    def tfm_rhs(ap4, k0, k1, ntile):
        return ap4[k0:k1, ntile].rearrange("k p t -> p k t")

    def psumb(self, name, shape, dt=F32):
        return Buf(self.ctx.enter_context(self.nc.psum_tensor(name, shape, dt)), sticky=True)

    def build(self):
        nc, T, DM, DI, DH = self.nc, self.T, self.DM, self.DI, self.DH
        ein = lambda n, s, dt=F32: self.dram(n, s, dt, kind="ExternalInput")
        self.x_in = ein("x_in", [T, DM])
        self.out = self.dram("out", [T, DM], F32, kind="ExternalOutput")
        w = {}
        w["in_projT"] = ein("in_projT", [DM, 2 * DI], BF16)
        w["x_projT"] = ein("x_projT", [DI, 96], BF16)
        w["dt_projT"] = ein("dt_projT", [self.DT, DI], BF16)
        w["out_projT"] = ein("out_projT", [DI, DM], BF16)
        w["w_qkv"] = ein("w_qkv", [DM, 3 * DM], BF16)
        w["w_out"] = ein("w_out", [DM, DM], BF16)
        w["w1T"] = ein("w1T", [DM, DH])
        w["w2T"] = ein("w2T", [DH, DM])
        w["wfr"] = ein("wfr", [DM, DM], F32R)
        w["wfi"] = ein("wfi", [DM, DM], F32R)
        w["C_d"] = ein("C_d", [DM, DM], F32R)
        w["S_d"] = ein("S_d", [DM, DM], F32R)
        w["gb"] = ein("gb", [128, 4 * DM], BF16)
        w["ident"] = ein("ident", [128, 128], BF16)
        w["maskb"] = ein("maskb", [128, 512], BF16)
        w["smalls"] = ein("smalls", [128, 504])
        self.w = w
        self.xres = self.dram("xres", [T, DM])
        self.xnp_d = self.dram("xnp_d", [T, DM])
        self.xn_fm16 = self.dram("xn_fm16", [DM, T], BF16)
        self.xn_fmR = self.dram("xn_fmR", [DM, T], F32R)
        self.u2_fm = self.dram("u2_fm", [DI, T], BF16)
        self.sz_fm = self.dram("sz_fm", [DI, T], BF16)
        self.du_fm = self.dram("du_fm", [DI, T], BF16)
        self.delta_fm = self.dram("delta_fm", [DI, T])
        self.dt_d = self.dram("dt_d", [self.DT, T], BF16)
        self.BC_d = self.dram("BC_d", [32, T])
        self.yg_fm = self.dram("yg_fm", [DI, T], BF16)
        self.w1q_d = self.dram("w1q_d", [DM, DH], BF16)
        self.w2q_d = self.dram("w2q_d", [DH, DM], BF16)
        self.wsc_d = self.dram("wsc_d", [1, 2])
        self.xq_fm = self.dram("xq_fm", [DM, T], BF16)
        self.hq_fm = self.dram("hq_fm", [DH, T], BF16)
        self.qk_fm = self.dram("qk_fm", [2 * DM, T], BF16)
        self.v_tm = self.dram("v_tm", [T, DM], BF16)
        self.ao_fm = self.dram("ao_fm", [DM, T], BF16)
        self.Q1_d = self.dram("Q1_d", [DM, DM], F32R)
        self.Q2_d = self.dram("Q2_d", [DM, DM], F32R)
        self.M_d = self.dram("M_d", [DM, DM], F32R)

        self.ident = self.sbuf("sb_ident", [128, 128], BF16)
        self.identR = self.sbuf("sb_identR", [128, 128], F32R)
        self.maskb = self.sbuf("sb_maskb", [128, 512], BF16)
        self.gb = self.sbuf("sb_gb", [128, 4 * DM], BF16)
        self.smalls = self.sbuf("sb_smalls", [128, 504], F32)
        self.A_neg = self.sbuf("sb_Aneg", [128, 256], F32)
        self.invws = self.sbuf("sb_invws", [128, 4], F32)
        self.states = self.sbuf("sb_states", [128, 256], F32)
        self.tails = self.sbuf("sb_tails", [128, 128], F32)
        self.stv = self.sbuf("sb_stv", [128, 16], F32)
        self.mx = self.sbuf("sb_mx", [128, 4], F32)
        self.sms = self.sbuf("sb_sms", [128, 4], F32)
        self.sc2 = self.sbuf("sb_sc2", [128, 2], F32)
        self.ones1 = self.sbuf("sb_ones1", [1, 128], BF16)
        self.wres = self.sbuf("sb_wres", [128, 16384], BF16)
        self.wresR = self.sbuf("sb_wresR", [128, 8192], F32R)
        self.rhs16 = [self.sbuf("sb_rhsA", [128, 4096], BF16), self.sbuf("sb_rhsB", [128, 4096], BF16)]
        self.rhsR = [self.sbuf("sb_rhsRA", [128, 2048], F32R), self.sbuf("sb_rhsRB", [128, 2048], F32R)]
        self._pool = {}
        for tag, cols, dt in [
            ("F32A", 4224, F32), ("F32B", 2048, F32), ("F32C", 2048, F32),
            ("F32D", 512, F32), ("F32E", 512, F32), ("I32A", 2048, I32),
            ("B16A", 2048, BF16), ("B16B", 2048, BF16), ("B16C", 2048, BF16),
            ("B16D", 1024, BF16), ("B16E", 1024, BF16), ("B16F", 1024, BF16),
            ("B16G", 1024, BF16),
            ("BIGB", 4096, BF16), ("BIGC", 4096, BF16),
            ("R32A", 1024, F32R), ("R32B", 1024, F32R),
        ]:
            self._pool[tag] = Buf(self.ctx.enter_context(self.nc.sbuf_tensor("pool_" + tag, [128, cols], dt)))
        self.ps = {}
        for i in range(4):
            self.ps[f"mm{i}"] = self.psumb(f"ps_mm{i}", [128, 512], F32)
        self.ps["tr"] = self.psumb("ps_tr", [128, 1024], BF16)
        self.ps["trR"] = self.psumb("ps_trR", [128, 512], F32R)
        self.ps["sc"] = self.psumb("ps_sc", [128, 512], F32)
        self.ps["av"] = self.psumb("ps_av", [128, 512], F32)

        self.P = Prog(nc, self.ctx)
        self.tt = {}
        self.prelude()
        subs = {1: self.sub_l0ffn, 2: self.sub_mamba, 3: self.sub_l0ffn,
                4: self.sub_l1ffn, 5: self.sub_attn, 6: self.sub_l1ffn}
        for i in self.enabled:
            subs[i]()
        self.final_out()
        with nc.Block() as block:
            self.P.build(block)
        self.ctx.close()
        return nc

    def mm_stream(self, lhsT_d, K, M, rhs_d, N, evac, rhs_dt=BF16, rhs_dep=None,
                  lhs_dep=None, mt_list=None, mt_off=0, rhs_dep_fn=None):
        P = self.P
        nK = (K + 127) // 128
        KG = 4 if rhs_dt == F32R else 8
        ngrp = (nK + KG - 1) // KG
        wbuf = self.wresR if rhs_dt == F32R else self.wres
        wcap = 8192 if rhs_dt == F32R else 16384
        resident = nK * M <= wcap
        ldep = aslist(lhs_dep)
        full_chunks = (K % 128 == 0)
        wtok = None
        if resident:
            deps = wbuf.wdep() + ldep
            if full_chunks:
                sview = lhsT_d.rearrange("(k p) m -> p k m", p=128)[:, :, :M]
                dview = wbuf.ap[:, :nK * M].rearrange("p (k m) -> p k m", m=M)
                wtok = P.dma("dS", dview, sview, deps, key=wbuf.key)
            else:
                tt = None
                for k in range(nK):
                    pk = min(128, K - k * 128)
                    tt = P.dma("dS", wbuf.ap[:pk, k * M:(k + 1) * M], lhsT_d[k * 128:k * 128 + pk, :M],
                               deps if k == 0 else [tt], key=wbuf.key)
                wtok = tt
            wbuf.ww(wtok)
        rbufs = self.rhsR if rhs_dt == F32R else self.rhs16
        rdeps = aslist(rhs_dep)
        mts = mt_list if mt_list is not None else list(range((M + 127) // 128))
        nt_cols = 512
        mt_groups = [mts[i:i + 4] for i in range(0, len(mts), 4)]
        nNT = N // nt_cols

        def load_rhs(b, k0, k1, ntile, extra):
            deps = b.wdep() + rdeps + extra
            if rhs_dep_fn is not None:
                deps = deps + rhs_dep_fn(ntile)
            if full_chunks:
                sview = rhs_d.rearrange("(k p) n -> p k n", p=128)[:, k0:k1,
                                                                   ntile * nt_cols:(ntile + 1) * nt_cols]
                dview = b.ap[:, :(k1 - k0) * nt_cols].rearrange("p (k n) -> p k n", n=nt_cols)
                return P.dma("dA", dview, sview, deps, key=b.key)
            tt = None
            for k in range(k0, k1):
                pk = min(128, K - k * 128)
                tt = P.dma("dA", b.ap[:pk, (k - k0) * nt_cols:(k - k0 + 1) * nt_cols],
                           rhs_d[k * 128:k * 128 + pk, ntile * nt_cols:(ntile + 1) * nt_cols],
                           deps if k == k0 else [tt], key=b.key)
            return tt

        if ngrp == 1:
            rtoks = {}
            rtoks[0] = load_rhs(rbufs[0], 0, nK, 0, [])
            rbufs[0].ww(rtoks[0])
            for ntile in range(nNT):
                b = rbufs[ntile % 2]
                tt = rtoks[ntile]
                if ntile + 1 < nNT:
                    nb = rbufs[(ntile + 1) % 2]
                    rtoks[ntile + 1] = load_rhs(nb, 0, nK, ntile + 1, [])
                    nb.ww(rtoks[ntile + 1])
                for mts_g in mt_groups:
                    mm_last = {}
                    for j, mt in enumerate(mts_g):
                        ms = min(128, M - mt * 128)
                        psb = self.ps[f"mm{j % 4}"]
                        mm_tok = None
                        for k in range(nK):
                            pk = min(128, K - k * 128)
                            wap = wbuf.ap[:pk, k * M + mt * 128:k * M + mt * 128 + ms]
                            d = [wtok, tt] + (psb.wdep() if k == 0 else [mm_tok])
                            mm_tok = P.op("t", lambda e, wap=wap, psb=psb, b=b, ms=ms, pk=pk, k=k,
                                          st=(k == 0), sp=(k == nK - 1):
                                          e.matmul(psb.ap[:ms, :nt_cols], wap,
                                                   b.ap[:pk, k * nt_cols:(k + 1) * nt_cols],
                                                   start=st, stop=sp), d)
                            psb.ww(mm_tok)
                            b.rr(mm_tok)
                            wbuf.rr(mm_tok)
                        mm_last[mt] = mm_tok
                    for j, mt in enumerate(mts_g):
                        psb = self.ps[f"mm{j % 4}"]
                        et = evac(mt + mt_off, ntile, psb, [mm_last[mt]])
                        psb.rr(et)
                        b.rr(et)
        else:
            for ntile in range(nNT):
                for mts_g in mt_groups:
                    mm_tok = {mt: None for mt in mts_g}
                    for kg in range(ngrp):
                        b = rbufs[kg % 2]
                        k0 = kg * KG
                        k1 = min(nK, k0 + KG)
                        tt = load_rhs(b, k0, k1, ntile, [])
                        b.ww(tt)
                        if not resident:
                            base = (kg % 2) * 8192
                            dw = self.wres.wdep() + ldep
                            if full_chunks:
                                sview = lhsT_d.rearrange("(k p) m -> p k m", p=128)[:, k0:k1, :M]
                                dview = self.wres.ap[:, base:base + (k1 - k0) * M].rearrange(
                                    "p (k m) -> p k m", m=M)
                                tw = P.dma("dS", dview, sview, dw, key=self.wres.key)
                            else:
                                tw = None
                                for k in range(k0, k1):
                                    pk = min(128, K - k * 128)
                                    tw = P.dma("dS", self.wres.ap[:pk, base + (k - k0) * M:
                                                                  base + (k - k0 + 1) * M],
                                               lhsT_d[k * 128:k * 128 + pk, :M], dw if k == k0 else [tw],
                                               key=self.wres.key)
                            self.wres.ww(tw)
                            wtok = tw
                        for j, mt in enumerate(mts_g):
                            ms = min(128, M - mt * 128)
                            psb = self.ps[f"mm{j % 4}"]
                            for k in range(k0, k1):
                                pk = min(128, K - k * 128)
                                if resident:
                                    wap = wbuf.ap[:pk, k * M + mt * 128:k * M + mt * 128 + ms]
                                else:
                                    base = (kg % 2) * 8192
                                    wap = self.wres.ap[:pk, base + (k - k0) * M + mt * 128:
                                                       base + (k - k0) * M + mt * 128 + ms]
                                d = [wtok, tt]
                                if k == 0:
                                    d += psb.wdep()
                                else:
                                    d.append(mm_tok[mt])
                                mm_tok[mt] = P.op("t", lambda e, wap=wap, psb=psb, b=b, ms=ms, pk=pk,
                                                  kk=(k - k0), st=(k == 0), sp=(k == nK - 1):
                                                  e.matmul(psb.ap[:ms, :nt_cols], wap,
                                                           b.ap[:pk, kk * nt_cols:(kk + 1) * nt_cols],
                                                           start=st, stop=sp), d)
                                psb.ww(mm_tok[mt])
                                b.rr(mm_tok[mt])
                                (wbuf if resident else self.wres).rr(mm_tok[mt])
                    for j, mt in enumerate(mts_g):
                        psb = self.ps[f"mm{j % 4}"]
                        et = evac(mt + mt_off, ntile, psb, [mm_tok[mt]])
                        psb.rr(et)

    def prelude(self):
        P, w = self.P, self.w
        t = P.dma("dS", self.ident.ap[:], w["ident"][:, :], key=self.ident.key); self.ident.ww(t)
        t = P.dma("dS", self.maskb.ap[:], w["maskb"][:, :], key=self.maskb.key); self.maskb.ww(t)
        t = P.dma("dS", self.gb.ap[:], w["gb"][:, :], key=self.gb.key); self.gb.ww(t)
        ts = P.dma("dS", self.smalls.ap[:], w["smalls"][:, :], key=self.smalls.key); self.smalls.ww(ts)
        ti = P.op("v", lambda e: e.tensor_copy(self.identR.ap[:], self.ident.ap[:]),
                  self.ident.rdep() + self.identR.wdep())
        self.identR.ww(ti); self.ident.rr(ti)
        self.csts = self.sbuf("sb_csts", [128, 2], F32)
        tc1 = P.op("v", lambda e: e.memset(self.csts.ap[:, 0:1], 1e-5), self.csts.wdep())
        tc2 = P.op("v", lambda e: e.memset(self.csts.ap[:, 1:2], 1.0), [tc1])
        self.csts.ww(tc2)
        self.nc.const_aps.register(F32, 1e-5, self.csts.ap[:, 0:1])
        self.nc.const_aps.register(F32, 1.0, self.csts.ap[:, 1:2])
        self.t_x0 = P.dma("dS", self.xres[:, :], self.x_in[:, :])
        self.off = {"cw0": 0, "cb0": 64, "dsk": 80, "dtb": 96, "A": 112, "cw1": 368, "cb1": 464, "fbias": 496}
        ta = P.op("s", lambda e: e.activation(out=self.A_neg.ap[:], in_=self.smalls.ap[:, 112:368],
                                              func=AF.Exp), [ts] + self.A_neg.wdep())
        self.smalls.rr(ta)
        ta = P.op("v", lambda e: e.tensor_scalar(self.A_neg.ap[:], self.A_neg.ap[:], -1.0, None, AL.mult), [ta])
        self.A_neg.ww(ta)
        tz = P.op("v", lambda e: e.memset(self.states.ap[:], 0.0), self.states.wdep())
        self.states.ww(tz)
        tz = P.op("v", lambda e: e.memset(self.tails.ap[:], 0.0), self.tails.wdep())
        self.tails.ww(tz)
        to1 = P.op("v", lambda e: e.memset(self.ones1.ap[:], 1.0), self.ones1.wdep())
        self.ones1.ww(to1)
        self.quant_weights()
        self.build_M()

    def smalls_pp(self, key, idx):
        return self.smalls.ap[:, self.off[key] + idx:self.off[key] + idx + 1]

    def quant_weights(self):
        P = self.P
        HB = 2048
        buf = self.pool("F32A")
        g = self.pool("F32B")
        qb = self.pool("F32C")
        qi = self.pool("I32A")
        qo = self.pool("B16A")
        acc = self.sbuf("q_acc", [128, 2], F32)
        part = self.sbuf("q_part", [128, 64], F32)
        sval = self.sbuf("q_sval", [1, 1], F32)
        onec = self.sbuf("q_onec", [128, 1], F32)
        totb = self.sbuf("q_totb", [128, 1], F32)
        rws = self.sbuf("q_rws", [128, 2], F32)
        toc = P.op("v", lambda e: e.memset(onec.ap[:], 1.0), onec.wdep())
        onec.ww(toc)
        self.t_wq = None
        for (wT_d, dst, K, N, idx) in [(self.w["w1T"], self.w1q_d, self.DM, self.DH, 0),
                                       (self.w["w2T"], self.w2q_d, self.DH, self.DM, 1)]:
            nH = (N + HB - 1) // HB
            tprev = None
            ncols = 0
            for kk in range(K // 128):
                for h in range(nH):
                    hw = min(HB, N - h * HB)
                    tl = P.dma("dS", buf.ap[:, :hw], wT_d[kk * 128:(kk + 1) * 128, h * HB:h * HB + hw],
                               buf.wdep() + [tprev], key=buf.key)
                    buf.ww(tl)
                    tv = P.op("s", lambda e, hw=hw: e.activation(out=g.ap[:, :hw], in_=buf.ap[:, :hw],
                                                                 func=AF.Abs, accum_out=acc.ap[:, 0:1]),
                              [tl] + g.wdep() + acc.wdep())
                    g.ww(tv); acc.ww(tv); buf.rr(tv)
                    tc = P.op("v", lambda e, c=ncols: e.tensor_copy(part.ap[:, c:c + 1], acc.ap[:, 0:1]),
                              [tv] + (part.wdep() if ncols == 0 else []))
                    part.ww(tc); acc.rr(tc)
                    tprev = tc
                    ncols += 1
            t1 = P.op("v", lambda e, nc_=ncols: e.tensor_reduce(totb.ap[:], part.ap[:, :nc_],
                                                                mybir.AxisListType.X, AL.add),
                      [tprev] + totb.wdep())
            totb.ww(t1); part.rr(t1)
            psb = self.ps["sc"]
            tmm = P.op("t", lambda e: e.matmul(psb.ap[:1, :1], totb.ap[:], onec.ap[:],
                                               start=True, stop=True),
                       [t1, toc] + psb.wdep())
            psb.ww(tmm); totb.rr(tmm); onec.rr(tmm)
            tsv = P.op("v", lambda e, d=1.0 / (K * N): e.tensor_scalar(sval.ap[:], psb.ap[0:1, 0:1], d, 1e-5,
                                                                      AL.mult, AL.max),
                       [tmm] + sval.wdep())
            sval.ww(tsv); psb.rr(tsv)
            tst = P.dma("dS", self.wsc_d[0:1, idx:idx + 1], sval.ap[:], [tsv], key=sval.key)
            sval.rr(tst)
            tbc = P.dma("dG", self.invws.ap[:, idx:idx + 1], self.wsc_d[0, idx:idx + 1].partition_broadcast(128),
                        [tst] + self.invws.wdep(), key=self.invws.key)
            self.invws.ww(tbc)
            trw = P.op("v", lambda e, idx=idx: e.reciprocal(rws.ap[:, idx:idx + 1],
                                                            self.invws.ap[:, idx:idx + 1]),
                       [tbc] + (rws.wdep() if idx == 0 else []))
            rws.ww(trw); self.invws.rr(trw)
            tpr = None
            for kk in range(K // 128):
                for h in range(nH):
                    hw = min(HB, N - h * HB)
                    tl = P.dma("dS", buf.ap[:, :hw], wT_d[kk * 128:(kk + 1) * 128, h * HB:h * HB + hw],
                               buf.wdep() + [tpr, trw], key=buf.key)
                    buf.ww(tl)
                    tg = P.op("v", lambda e, idx=idx, hw=hw: e.tensor_scalar(g.ap[:, :hw], buf.ap[:, :hw],
                                                                             rws.ap[:, idx:idx + 1], None,
                                                                             AL.mult), [tl, trw] + g.wdep())
                    g.ww(tg); buf.rr(tg); rws.rr(tg)
                    tg = P.op("v", lambda e, hw=hw: e.tensor_scalar(g.ap[:, :hw], g.ap[:, :hw], -1.5, 1.5,
                                                                    AL.max, AL.min), [tg])
                    g.ww(tg)
                    tq = P.op("v", lambda e, hw=hw: e.tensor_scalar(qi.ap[:, :hw], g.ap[:, :hw], 128.0, None,
                                                                    AL.add), [tg] + qi.wdep())
                    qi.ww(tq)
                    tq2 = P.op("v", lambda e, hw=hw: e.tensor_copy(qb.ap[:, :hw], qi.ap[:, :hw]),
                               [tq] + qb.wdep())
                    qb.ww(tq2); qi.rr(tq2)
                    tg2 = P.op("v", lambda e, hw=hw: e.scalar_tensor_tensor(g.ap[:, :hw], g.ap[:, :hw], 128.0,
                                                                            qb.ap[:, :hw], AL.add,
                                                                            AL.subtract), [tq2, tg])
                    g.ww(tg2); qb.rr(tg2)
                    tq3 = P.op("v", lambda e, hw=hw: e.scalar_tensor_tensor(qb.ap[:, :hw], g.ap[:, :hw], 0.5,
                                                                            qb.ap[:, :hw], AL.is_ge,
                                                                            AL.add), [tg2])
                    qb.ww(tq3); g.rr(tq3)
                    tq4 = P.op("v", lambda e, hw=hw: e.tensor_scalar(qb.ap[:, :hw], qb.ap[:, :hw], -128.0,
                                                                     None, AL.add), [tq3])
                    qb.ww(tq4)
                    tq5 = P.op("v", lambda e, hw=hw: e.tensor_scalar(qo.ap[:, :hw], qb.ap[:, :hw], -1.0, 1.0,
                                                                     AL.max, AL.min), [tq4] + qo.wdep())
                    qo.ww(tq5); qb.rr(tq5)
                    tpr = P.dma("dG", dst[kk * 128:(kk + 1) * 128, h * HB:h * HB + hw], qo.ap[:, :hw],
                                [tq5], key=qo.key)
                    qo.rr(tpr)
            self.t_wq = tpr if self.t_wq is None else tpr

    def build_M(self):
        P, w, DM = self.P, self.w, self.DM
        ev = self.pool("R32A")
        l1 = self.sbuf("ml1", [128, 128], F32R)
        l2 = self.sbuf("ml2", [128, 128], F32R)
        l2n = self.sbuf("ml2n", [128, 128], F32R)
        r1 = Buf(self.rhsR[0].ap[:, 0:512])
        r2 = Buf(self.rhsR[1].ap[:, 0:512])
        r1.w = self.rhsR[0].w; r2.w = self.rhsR[1].w

        def pair(A1, B1, A2, B2, neg2, out_d, scale):
            td = None
            for mt in range(DM // 128):
                for ntile in range(DM // 512):
                    psb = self.ps["mm0"]
                    mm_tok = None
                    for kk in range(DM // 128):
                        tl1 = P.dma("dS", l1.ap[:], A1[kk * 128:(kk + 1) * 128, mt * 128:(mt + 1) * 128],
                                    l1.wdep(), key=l1.key)
                        l1.ww(tl1)
                        tr1 = P.dma("dA", r1.ap[:], B1[kk * 128:(kk + 1) * 128, ntile * 512:(ntile + 1) * 512],
                                    r1.wdep(), key=r1.key)
                        r1.ww(tr1)
                        tl2 = P.dma("dS", l2.ap[:], A2[kk * 128:(kk + 1) * 128, mt * 128:(mt + 1) * 128],
                                    l2.wdep(), key=l2.key)
                        l2.ww(tl2)
                        tr2 = P.dma("dA", r2.ap[:], B2[kk * 128:(kk + 1) * 128, ntile * 512:(ntile + 1) * 512],
                                    r2.wdep(), key=r2.key)
                        r2.ww(tr2)
                        d = [tl1, tr1] + (psb.wdep() if kk == 0 else [mm_tok])
                        mm_tok = P.op("t", lambda e, f=(kk == 0): e.matmul(psb.ap[:, :], l1.ap[:], r1.ap[:],
                                                                          start=f, stop=False), d)
                        psb.ww(mm_tok); l1.rr(mm_tok); r1.rr(mm_tok)
                        lu, dl = l2, [tl2]
                        if neg2:
                            tn = P.op("v", lambda e: e.tensor_scalar(l2n.ap[:], l2.ap[:], -1.0, None,
                                                                    AL.mult), [tl2] + l2n.wdep())
                            l2n.ww(tn); l2.rr(tn)
                            lu, dl = l2n, [tn]
                        mm_tok = P.op("t", lambda e, lu=lu, lk=(kk == DM // 128 - 1): e.matmul(
                            psb.ap[:, :], lu.ap[:], r2.ap[:], start=False, stop=lk), dl + [tr2, mm_tok])
                        psb.ww(mm_tok); lu.rr(mm_tok); r2.rr(mm_tok)
                    te = P.op("s", lambda e, sc=scale: e.activation(out=ev.ap[:, :512], in_=psb.ap[:, :],
                                                                   func=AF.Identity, scale=sc),
                              [mm_tok] + ev.wdep())
                    ev.ww(te); psb.rr(te)
                    td = P.dma("dS", out_d[mt * 128:(mt + 1) * 128, ntile * 512:(ntile + 1) * 512],
                               ev.ap[:, :512], [te], key=ev.key)
                    ev.rr(td)
            return td

        pair(w["wfr"], w["C_d"], w["wfi"], w["S_d"], False, self.Q1_d, 1.0)
        pair(w["wfi"], w["C_d"], w["wfr"], w["S_d"], True, self.Q2_d, 1.0)
        self.t_M = pair(self.Q1_d, w["C_d"], self.Q2_d, w["S_d"], True, self.M_d, 1.0 / DM)

    def ln(self, src_d, dst_tm=None, dst_fm=None, fm_dt=BF16, gbi=None, dep=None, tkey=None):
        dep = aslist(dep)
        if tkey is not None:
            self.tt[tkey] = []
        P, T, DM = self.P, self.T, self.DM
        bxt, xts = self.slots("F32A", 1024, 2)
        sq = self.pool("F32B")
        bxo, xos = self.slots("F32C", 1024, 2)
        bxb, xbs = self.slots("B16A", 1024, 2)
        xb2 = self.pool("B16B")
        xr = self.pool("R32A")
        evb = self.pool("B16C")
        evr = self.pool("R32B")
        stv0 = self.stv
        stvs = [Buf(stv0.ap[:, 0:8]), Buf(stv0.ap[:, 8:16])]
        for b in stvs:
            b.w = stv0.w; b.r = list(stv0.r)
        last = None
        for it in range(T // 128):
            sl = it % 2
            xt, xo, xb, stv = xts[sl], xos[sl], xbs[sl], stvs[sl]
            tl = P.dma("dS", xt.ap[:], src_d[it * 128:(it + 1) * 128, :], xt.wdep() + dep + [last],
                       key=xt.key)
            xt.ww(tl)
            t1 = P.op("s", lambda e, xt=xt, stv=stv: e.activation(out=sq.ap[:, :1024], in_=xt.ap[:],
                                                                  func=AF.Identity,
                                                                  accum_out=stv.ap[:, 0:1]),
                      [tl] + sq.wdep() + stv.wdep())
            t2 = P.op("s", lambda e, xt=xt, stv=stv: e.activation(out=sq.ap[:, :1024], in_=xt.ap[:],
                                                                  func=AF.Square,
                                                                  accum_out=stv.ap[:, 1:2]), [t1])
            sq.ww(t2); stv.ww(t2); xt.rr(t2)
            t3 = P.op("v", lambda e, d=1.0 / DM, stv=stv: e.tensor_scalar(stv.ap[:, 2:3], stv.ap[:, 0:1],
                                                                         d, None, AL.mult), [t2])
            t4 = P.op("v", lambda e, d=1.0 / DM, stv=stv: e.tensor_scalar(stv.ap[:, 3:4], stv.ap[:, 1:2],
                                                                         d, None, AL.mult), [t2])
            t5 = P.op("v", lambda e, stv=stv: e.tensor_tensor(stv.ap[:, 6:7], stv.ap[:, 2:3],
                                                              stv.ap[:, 2:3], AL.mult), [t3])
            t6 = P.op("v", lambda e, stv=stv: e.tensor_tensor(stv.ap[:, 3:4], stv.ap[:, 3:4],
                                                              stv.ap[:, 6:7], AL.subtract), [t4, t5])
            t7 = P.op("s", lambda e, stv=stv: e.activation(out=stv.ap[:, 6:7], in_=stv.ap[:, 3:4],
                                                           func=AF.Ln, bias=1e-5), [t6])
            t8 = P.op("s", lambda e, stv=stv: e.activation(out=stv.ap[:, 4:5], in_=stv.ap[:, 6:7],
                                                           func=AF.Exp, scale=-0.5), [t7])
            t9 = P.op("v", lambda e, stv=stv: e.tensor_tensor(stv.ap[:, 5:6], stv.ap[:, 2:3],
                                                              stv.ap[:, 4:5], AL.mult), [t8, t3])
            t9 = P.op("v", lambda e, stv=stv: e.tensor_scalar(stv.ap[:, 5:6], stv.ap[:, 5:6], -1.0, None,
                                                              AL.mult), [t9])
            stv.ww(t9)
            if dst_tm is not None:
                tn = P.op("s", lambda e, xt=xt, xo=xo, stv=stv: e.activation(
                    out=xo.ap[:], in_=xt.ap[:], func=AF.Identity, bias=stv.ap[:, 5:6],
                    scale=stv.ap[:, 4:5]), [t9] + xo.wdep())
                xo.ww(tn); xt.rr(tn); stv.rr(tn)
                last = P.dma("dG", dst_tm[it * 128:(it + 1) * 128, :], xo.ap[:], [tn], key=xo.key)
                xo.rr(last)
            if dst_fm is not None:
                tgt = xr if fm_dt == F32R else xb
                tn = P.op("s", lambda e, gg=tgt, xt=xt, stv=stv: e.activation(
                    out=gg.ap[:, :1024], in_=xt.ap[:], func=AF.Identity, bias=stv.ap[:, 5:6],
                    scale=stv.ap[:, 4:5]), [t9] + tgt.wdep())
                tgt.ww(tn); xt.rr(tn); stv.rr(tn)
                if gbi is not None:
                    o = gbi * 2 * DM
                    tn = P.op("v", lambda e, o=o, xb=xb: e.tensor_tensor(xb2.ap[:, :1024], xb.ap[:, :1024],
                                                                        self.gb.ap[:, o:o + DM], AL.mult),
                              [tn] + xb2.wdep() + self.gb.rdep())
                    xb2.ww(tn); xb.rr(tn); self.gb.rr(tn)
                    tn = P.op("v", lambda e, o=o, xb=xb: e.tensor_tensor(xb.ap[:, :1024], xb2.ap[:, :1024],
                                                                        self.gb.ap[:, o + DM:o + 2 * DM],
                                                                        AL.add), [tn] + xb.wdep())
                    xb.ww(tn); xb2.rr(tn); self.gb.rr(tn)
                    tgt = xb
                last = self.transpose_out(tgt, DM, dst_fm, it, fm_dt, evb if fm_dt == BF16 else evr, [tn])
                if tkey is not None:
                    self.tt[tkey].append(last)
        self.slots_done(bxt, xts)
        self.slots_done(bxo, xos)
        self.slots_done(bxb, xbs)
        self.slots_done(stv0, stvs)
        return last

    def transpose_out(self, srcbuf, D, dst_fm, it, dt, evbuf, deps, src_off=0, dst_row=0):
        P = self.P
        pst = self.ps["tr"] if dt == BF16 else self.ps["trR"]
        idn = self.ident if dt == BF16 else self.identR
        grp_sz = 8 if dt == BF16 else 4
        ts = None
        for grp in range((D // 128 + grp_sz - 1) // grp_sz):
            toks = []
            nb = min(grp_sz, D // 128 - grp * grp_sz)
            for b2 in range(nb):
                blk = grp * grp_sz + b2
                d = deps + idn.rdep() + (pst.wdep() if b2 == 0 else [toks[-1]])
                tt = P.op("t", lambda e, blk=blk, b2=b2: e.transpose(
                    pst.ap[:, b2 * 128:(b2 + 1) * 128],
                    srcbuf.ap[:, src_off + blk * 128:src_off + (blk + 1) * 128],
                    idn.ap[:]), d)
                toks.append(tt)
                srcbuf.rr(tt); idn.rr(tt)
            pst.ww(toks[-1])
            te = P.op("v", lambda e, nb=nb: e.tensor_copy(
                evbuf.ap[:, :nb * 128], pst.ap[:, :nb * 128]),
                [toks[-1]] + evbuf.wdep())
            evbuf.ww(te); pst.rr(te)
            r0 = dst_row + grp * grp_sz * 128
            dview = dst_fm[r0:r0 + nb * 128, :].rearrange(
                "(a p) t -> p a t", p=128)[:, :, it * 128:(it + 1) * 128]
            sview = evbuf.ap[:, :nb * 128].rearrange("p (a t) -> p a t", t=128)
            ts = P.dma("dG", dview, sview, [te], key=evbuf.key)
            evbuf.rr(ts)
        return ts

    def add_contrib_fm(self, contrib_fm_d, extra_tm_d=None, dep=None):
        dep = aslist(dep)
        P, T, DM = self.P, self.T, self.DM
        cb = self.pool("R32A")
        ct = self.pool("F32C")
        bxt, xts = self.slots("F32A", 1024, 2)
        et = self.pool("F32B")
        pst = self.ps["trR"]
        last = None
        for it in range(T // 128):
            xt = xts[it % 2]
            sview = contrib_fm_d.rearrange("(a p) t -> p a t", p=128)[:, :, it * 128:(it + 1) * 128]
            dview = cb.ap[:, :DM].rearrange("p (a t) -> p a t", t=128)
            tl = P.dma("dS", dview, sview, cb.wdep() + dep + [last], key=cb.key)
            cb.ww(tl)
            te = None
            for grp in range(2):
                toks = []
                for b2 in range(4):
                    blk = grp * 4 + b2
                    d = [tl] + self.identR.rdep() + (pst.wdep() if b2 == 0 else [toks[-1]])
                    tt = P.op("t", lambda e, blk=blk, b2=b2: e.transpose(
                        pst.ap[:, b2 * 128:(b2 + 1) * 128], cb.ap[:, blk * 128:(blk + 1) * 128],
                        self.identR.ap[:]), d)
                    toks.append(tt); cb.rr(tt); self.identR.rr(tt)
                pst.ww(toks[-1])
                te = P.op("v", lambda e, grp=grp: e.tensor_copy(ct.ap[:, grp * 512:(grp + 1) * 512],
                                                               pst.ap[:, :512]),
                          [toks[-1]] + (ct.wdep() if grp == 0 else []))
                ct.ww(te); pst.rr(te)
            tx = P.dma("dA", xt.ap[:], self.xres[it * 128:(it + 1) * 128, :],
                       xt.wdep() + dep + [self.t_x0], key=xt.key)
            xt.ww(tx)
            d2 = [te, tx]
            if extra_tm_d is not None:
                tex = P.dma("dA", et.ap[:, :DM], extra_tm_d[it * 128:(it + 1) * 128, :],
                            et.wdep() + dep, key=et.key)
                et.ww(tex)
                ta = P.op("v", lambda e, xt=xt: e.tensor_tensor(xt.ap[:], xt.ap[:], et.ap[:, 0:1024],
                                                               AL.add), [tex] + d2)
                xt.ww(ta); et.rr(ta)
                d2 = [ta]
            ta = P.op("v", lambda e, xt=xt: e.tensor_tensor(xt.ap[:], xt.ap[:], ct.ap[:, 0:1024],
                                                           AL.add), d2)
            xt.ww(ta); ct.rr(ta)
            last = P.dma("dG", self.xres[it * 128:(it + 1) * 128, :], xt.ap[:], [ta], key=xt.key)
            xt.rr(last)
        self.slots_done(bxt, xts)
        self.t_x0 = last
        return last

    def sub_l0ffn(self):
        P, T, DM = self.P, self.T, self.DM
        fence = P.fence()
        tk0 = self.uid("xnR")
        tln = self.ln(self.xres, dst_fm=self.xn_fmR, fm_dt=F32R, dep=fence, tkey=tk0)
        contrib = self.dram(self.uid("c0"), [DM, T], F32R)
        ev = self.pool("R32B")

        def evac(mt, ntile, psb, deps):
            te = P.op("s", lambda e, mt=mt: e.activation(out=ev.ap[:, :512], in_=psb.ap[:, :],
                                                         func=AF.Identity, scale=0.5,
                                                         bias=self.smalls_pp("fbias", mt)),
                      deps + ev.wdep() + self.smalls.rdep())
            ev.ww(te); self.smalls.rr(te); psb.rr(te)
            ts = P.dma("dG", contrib[mt * 128:(mt + 1) * 128, ntile * 512:(ntile + 1) * 512],
                       ev.ap[:, :512], [te], key=ev.key)
            ev.rr(ts)
            return ts

        self.mm_stream(self.M_d, DM, DM, self.xn_fmR, T, evac, rhs_dt=F32R,
                       lhs_dep=[self.t_M], rhs_dep_fn=self.tfn(tk0))
        self.add_contrib_fm(contrib, dep=P.fence())

    def sub_mamba(self):
        P, T, DM, DI = self.P, self.T, self.DM, self.DI
        fence = P.fence()
        t1 = self.ln(self.xres, dst_tm=self.xnp_d, dep=fence)
        tk1 = self.uid("xnM")
        t2 = self.ln(self.xnp_d, dst_fm=self.xn_fm16, fm_dt=BF16, gbi=0, dep=[t1], tkey=tk1)
        ub = self.pool("F32A")
        acc = self.pool("F32C")
        u2b = self.pool("B16D")
        szb = self.pool("B16E")
        nDIt = DI // 128
        tz = P.op("v", lambda e: e.memset(self.tails.ap[:], 0.0), self.tails.wdep() + fence)
        self.tails.ww(tz)

        def evac_u(mt, ntile, psb, deps):
            te = P.op("s", lambda e: e.activation(out=ub.ap[:, 3:515], in_=psb.ap[:, :], func=AF.Identity),
                      deps + ub.wdep())
            ub.ww(te); psb.rr(te)
            tc = P.op("v", lambda e, mt=mt: e.tensor_copy(ub.ap[:, 0:3], self.tails.ap[:, 3 * mt:3 * mt + 3]),
                      [te] + self.tails.rdep())
            ub.ww(tc); self.tails.rr(tc)
            tt = P.op("v", lambda e, mt=mt: e.tensor_copy(self.tails.ap[:, 3 * mt:3 * mt + 3],
                                                          ub.ap[:, 512:515]), [tc] + self.tails.wdep())
            self.tails.ww(tt)
            t0 = P.op("v", lambda e, mt=mt: e.tensor_scalar(acc.ap[:, :512], ub.ap[:, 0:512],
                                                            self.smalls_pp("cw0", 4 * mt), None, AL.mult),
                      [tc] + acc.wdep() + self.smalls.rdep())
            acc.ww(t0); self.smalls.rr(t0)
            for k2 in (1, 2, 3):
                t0 = P.op("v", lambda e, mt=mt, k2=k2: e.scalar_tensor_tensor(
                    acc.ap[:, :512], ub.ap[:, k2:512 + k2], self.smalls_pp("cw0", 4 * mt + k2),
                    acc.ap[:, :512], AL.mult, AL.add), [t0])
                acc.ww(t0)
            ub.rr(t0)
            ts = P.op("s", lambda e, mt=mt: e.activation(out=u2b.ap[:, :512], in_=acc.ap[:, :512],
                                                         func=AF.Silu, bias=self.smalls_pp("cb0", mt)),
                      [t0] + u2b.wdep() + self.smalls.rdep())
            u2b.ww(ts); acc.rr(ts); self.smalls.rr(ts)
            td = P.dma("dG", self.u2_fm[mt * 128:(mt + 1) * 128, ntile * 512:(ntile + 1) * 512],
                       u2b.ap[:, :512], [ts], key=u2b.key)
            u2b.rr(td)
            return td

        def evac_z(mt, ntile, psb, deps):
            mz = mt - nDIt
            ts = P.op("s", lambda e: e.activation(out=szb.ap[:, :512], in_=psb.ap[:, :], func=AF.Silu),
                      deps + szb.wdep())
            szb.ww(ts); psb.rr(ts)
            td = P.dma("dG", self.sz_fm[mz * 128:(mz + 1) * 128, ntile * 512:(ntile + 1) * 512],
                       szb.ap[:, :512], [ts], key=szb.key)
            szb.rr(td)
            return td

        self.mm_stream(self.w["in_projT"][:, 0:DI], DM, DI, self.xn_fm16, T, evac_u, rhs_dep=[tz],
                       rhs_dep_fn=self.tfn(tk1))
        self.mm_stream(self.w["in_projT"][:, DI:2 * DI], DM, DI, self.xn_fm16, T, evac_z,
                       mt_off=nDIt, rhs_dep_fn=self.tfn(tk1))
        f2 = P.fence()
        xdb = self.pool("B16D")
        xdf = self.pool("F32D")

        def evac_xproj(mt, ntile, psb, deps):
            te = P.op("v", lambda e: e.tensor_copy(xdb.ap[:96, 0:512], psb.ap[:96, :]), deps + xdb.wdep())
            xdb.ww(te); psb.rr(te)
            tf = P.op("v", lambda e: e.tensor_copy(xdf.ap[:32, 0:512], psb.ap[64:96, :]),
                      deps + [te] + xdf.wdep())
            xdf.ww(tf); psb.rr(tf)
            ta = P.dma("dG", self.dt_d[:, ntile * 512:(ntile + 1) * 512], xdb.ap[0:64, 0:512], [te], key=xdb.key)
            xdb.rr(ta)
            tb = P.dma("dG", self.BC_d[:, ntile * 512:(ntile + 1) * 512], xdf.ap[:32, 0:512], [tf], key=xdf.key)
            xdf.rr(tb)
            return tb

        self.mm_stream(self.w["x_projT"], DI, 96, self.u2_fm, T, evac_xproj, rhs_dep=f2,
                       lhs_dep=f2, mt_list=[0])
        f3 = P.fence()
        ab = self.pool("F32B")
        eb = self.pool("F32C")
        rb = self.pool("F32D")
        db = self.pool("F32E")
        dub = self.pool("B16D")
        u2l = self.pool("B16E")

        def evac_dt(mt, ntile, psb, deps):
            ta = P.op("s", lambda e, mt=mt: e.activation(out=ab.ap[:, :512], in_=psb.ap[:, :], func=AF.Abs,
                                                         bias=self.smalls_pp("dtb", mt)),
                      deps + ab.wdep() + self.smalls.rdep())
            ab.ww(ta); self.smalls.rr(ta); psb.rr(ta)
            tb = P.op("s", lambda e: e.activation(out=eb.ap[:, :512], in_=ab.ap[:, :512], func=AF.Exp,
                                                  scale=-1.0), [ta] + eb.wdep())
            eb.ww(tb); ab.rr(tb)
            tc = P.op("s", lambda e: e.activation(out=eb.ap[:, :512], in_=eb.ap[:, :512], func=AF.Ln,
                                                  bias=1.0), [tb])
            eb.ww(tc)
            tr = P.op("s", lambda e, mt=mt: e.activation(out=rb.ap[:, :512], in_=psb.ap[:, :], func=AF.Relu,
                                                         bias=self.smalls_pp("dtb", mt)),
                      deps + [tc] + rb.wdep() + self.smalls.rdep())
            rb.ww(tr); self.smalls.rr(tr); psb.rr(tr)
            td = P.op("v", lambda e: e.tensor_tensor(db.ap[:, :512], rb.ap[:, :512], eb.ap[:, :512],
                                                     AL.add), [tr, tc] + db.wdep())
            db.ww(td); rb.rr(td); eb.rr(td)
            tw = P.dma("dG", self.delta_fm[mt * 128:(mt + 1) * 128, ntile * 512:(ntile + 1) * 512],
                       db.ap[:, :512], [td], key=db.key)
            db.rr(tw)
            tu = P.dma("dA", u2l.ap[:, :512],
                       self.u2_fm[mt * 128:(mt + 1) * 128, ntile * 512:(ntile + 1) * 512], u2l.wdep(),
                       key=u2l.key)
            u2l.ww(tu)
            tm = P.op("v", lambda e: e.tensor_tensor(dub.ap[:, :512], db.ap[:, :512], u2l.ap[:, :512],
                                                     AL.mult), [td, tu] + dub.wdep())
            dub.ww(tm); db.rr(tm); u2l.rr(tm)
            t2w = P.dma("dG", self.du_fm[mt * 128:(mt + 1) * 128, ntile * 512:(ntile + 1) * 512],
                        dub.ap[:, :512], [tm], key=dub.key)
            dub.rr(t2w)
            return t2w

        self.mm_stream(self.w["dt_projT"], 64, DI, self.dt_d, T, evac_dt, rhs_dep=f3, lhs_dep=f3)
        self.scan_phase(P.fence())
        f4 = P.fence()
        contrib = self.dram(self.uid("cm"), [DM, T], F32R)
        ev4 = self.pool("R32B")

        def evac_out(mt, ntile, psb, deps):
            te = P.op("v", lambda e: e.tensor_copy(ev4.ap[:, :512], psb.ap[:, :]), deps + ev4.wdep())
            ev4.ww(te); psb.rr(te)
            ts = P.dma("dG", contrib[mt * 128:(mt + 1) * 128, ntile * 512:(ntile + 1) * 512],
                       ev4.ap[:, :512], [te], key=ev4.key)
            ev4.rr(ts)
            return ts

        self.mm_stream(self.w["out_projT"], DI, DM, self.yg_fm, T, evac_out, rhs_dep=f4, lhs_dep=f4)
        self.add_contrib_fm(contrib, extra_tm_d=self.xnp_d, dep=P.fence())

    def slots(self, tag, ncols, n):
        base = self.pool(tag)
        out = []
        for i in range(n):
            b = Buf(base.ap[:, i * ncols:(i + 1) * ncols])
            b.w = base.w
            b.r = list(base.r)
            out.append(b)
        return base, out

    def tfn(self, tkey, per=4):
        tl = self.tt[tkey]
        return lambda nt: [tl[min((nt + 1) * per, len(tl)) - 1]]

    def slots_done(self, base, sl):
        for b in sl:
            if b.w is not None:
                base.rr(b.w)
            for t in b.r:
                base.rr(t)

    def scan_phase(self, fence):
        P, T, DI = self.P, self.T, self.DI
        Tc = min(512, T)
        nCh = DI // 128
        RepB = [self.rhs16[0], self.rhs16[1]]
        RepC = [self.pool("BIGB"), self.pool("BIGC")]
        npb = 4096 // Tc
        bdlt, dlts = self.slots("F32A", Tc, 2)
        bdul, duls = self.slots("B16A", Tc, 2)
        bu2l, u2ls = self.slots("B16B", Tc, 2)
        bszl, szls = self.slots("B16C", Tc, 2)
        bdA, dAs = self.slots("F32B", Tc, 2)
        bbb, bbs = self.slots("B16D", Tc, 2)
        bhn, hns = self.slots("B16E", Tc, 2)
        bhc, hcs = self.slots("B16F", Tc, 2)
        yb = self.pool("F32C")
        ygb = self.pool("B16G")
        nb = (Tc + 511) // 512
        nTch = T // Tc

        def load_ct(tch, ct):
            s = ct % 2
            t1 = P.dma("dS", dlts[s].ap[:], self.delta_fm[ct * 128:(ct + 1) * 128, tch * Tc:(tch + 1) * Tc],
                       dlts[s].wdep() + fence, key=dlts[s].key)
            dlts[s].ww(t1)
            t2 = P.dma("dS", duls[s].ap[:], self.du_fm[ct * 128:(ct + 1) * 128, tch * Tc:(tch + 1) * Tc],
                       duls[s].wdep() + fence, key=duls[s].key)
            duls[s].ww(t2)
            t3 = P.dma("dA", u2ls[s].ap[:], self.u2_fm[ct * 128:(ct + 1) * 128, tch * Tc:(tch + 1) * Tc],
                       u2ls[s].wdep() + fence, key=u2ls[s].key)
            u2ls[s].ww(t3)
            t4 = P.dma("dA", szls[s].ap[:], self.sz_fm[ct * 128:(ct + 1) * 128, tch * Tc:(tch + 1) * Tc],
                       szls[s].wdep() + fence, key=szls[s].key)
            szls[s].ww(t4)

        ones1f = self.sbuf(self.uid("o1f"), [1, 128], F32)
        to1f = P.op("v", lambda e: e.memset(ones1f.ap[:], 1.0), ones1f.wdep() + fence)
        ones1f.ww(to1f)
        bc2s = [self.sbuf(self.uid(f"bc2{i}"), [1, 512], F32) for i in range(2)]
        for tch in range(nTch):
            for rep_list, base_row in ((RepB, 0), (RepC, 16)):
                for n in range(16):
                    rep = rep_list[n // npb]
                    off = (n % npb) * Tc
                    bc2 = bc2s[n % 2]
                    psb = self.ps["sc"] if n % 2 == 0 else self.ps["av"]
                    sv = self.BC_d[base_row + n:base_row + n + 1,
                                   tch * Tc:(tch + 1) * Tc].rearrange("(o r) t -> o r t", o=1)
                    dv = bc2.ap[:, :Tc].rearrange("p (r t) -> p r t", t=Tc)
                    tld = P.dma("dA", dv, sv, bc2.wdep() + fence, key=bc2.key)
                    bc2.ww(tld)
                    tmm = P.op("t", lambda e, psb=psb, bc2=bc2: e.matmul(
                        psb.ap[:, :Tc], ones1f.ap[:], bc2.ap[:, :Tc],
                        start=True, stop=True), [tld, to1f] + psb.wdep() + ones1f.rdep())
                    psb.ww(tmm); bc2.rr(tmm); ones1f.rr(tmm)
                    tev = P.op("v", lambda e, rep=rep, off=off, psb=psb: e.tensor_copy(
                        rep.ap[:, off:off + Tc], psb.ap[:, :Tc]),
                        [tmm] + rep.wdep())
                    rep.ww(tev); psb.rr(tev)
            if tch == 0:
                load_ct(0, 0)
            for ct in range(nCh):
                s = ct % 2
                dlt, dul, u2l, szl = dlts[s], duls[s], u2ls[s], szls[s]
                tl1, tl2, tl3, tl4 = dlt.w, dul.w, u2l.w, szl.w
                if ct + 1 < nCh:
                    load_ct(tch, ct + 1)
                elif tch + 1 < nTch:
                    load_ct(tch + 1, 0)
                banks = [self.ps["mm0"], self.ps["mm1"]][:nb]
                ytoks = []
                for n in range(16):
                    nr = n
                    sn = n % 2
                    dA, bb, hn, hc = dAs[sn], bbs[sn], hns[sn], hcs[sn]
                    ta = P.op("s", lambda e, ct=ct, n=n, dA=dA, dlt=dlt: e.activation(
                        out=dA.ap[:], in_=dlt.ap[:], func=AF.Exp,
                        scale=self.A_neg.ap[:, ct * 16 + n:ct * 16 + n + 1]),
                        [tl1] + dA.wdep() + self.A_neg.rdep())
                    dA.ww(ta); dlt.rr(ta); self.A_neg.rr(ta)
                    Bb = RepB[n // npb]
                    bo = (n % npb) * Tc
                    tb = P.op("v", lambda e, Bb=Bb, bo=bo, bb=bb, dul=dul: e.tensor_tensor(
                        bb.ap[:], dul.ap[:], Bb.ap[:, bo:bo + Tc], AL.mult),
                        [tl2, Bb.w] + bb.wdep())
                    bb.ww(tb); dul.rr(tb); Bb.rr(tb)
                    tsc = P.op("v", lambda e, ct=ct, n=n, hn=hn, dA=dA, bb=bb: e.tensor_tensor_scan(
                        hn.ap[:], dA.ap[:], bb.ap[:],
                        self.states.ap[:, ct * 16 + n:ct * 16 + n + 1], AL.mult, AL.add),
                        [ta, tb] + hn.wdep() + self.states.rdep() + self.states.wdep())
                    hn.ww(tsc); dA.rr(tsc); bb.rr(tsc); self.states.rr(tsc)
                    tst = P.op("v", lambda e, ct=ct, n=n, hn=hn: e.tensor_copy(
                        self.states.ap[:, ct * 16 + n:ct * 16 + n + 1], hn.ap[:, Tc - 1:Tc]), [tsc])
                    self.states.ww(tst); hn.rr(tst)
                    Cb = RepC[n // npb]
                    co = (n % npb) * Tc
                    tm = P.op("v", lambda e, Cb=Cb, co=co, hc=hc, hn=hn: e.tensor_tensor(
                        hc.ap[:], hn.ap[:], Cb.ap[:, co:co + Tc], AL.mult),
                        [tsc, Cb.w] + hc.wdep())
                    hc.ww(tm); hn.rr(tm); Cb.rr(tm)
                    for sb2 in range(nb):
                        w2 = min(512, Tc - sb2 * 512)
                        psb = banks[sb2]
                        d = [tm] + self.ident.rdep() + (psb.wdep() if n == 0 else [ytoks[-1]])
                        ty = P.op("t", lambda e, sb2=sb2, psb=psb, n=n, w2=w2, hc=hc: e.matmul(
                            psb.ap[:, :w2], self.ident.ap[:], hc.ap[:, sb2 * 512:sb2 * 512 + w2],
                            start=(n == 0), stop=(n == 15)), d)
                        ytoks.append(ty)
                        if n == 0:
                            psb.ww(ty)
                        hc.rr(ty); self.ident.rr(ty)
                for sb2 in range(nb):
                    banks[sb2].ww(ytoks[-(nb - sb2)])
                t1s = []
                for sb2 in range(nb):
                    w2 = min(512, Tc - sb2 * 512)
                    psb = banks[sb2]
                    t1 = P.op("v", lambda e, ct=ct, sb2=sb2, psb=psb, w2=w2, u2l=u2l: e.scalar_tensor_tensor(
                        yb.ap[:, sb2 * 512:sb2 * 512 + w2], u2l.ap[:, sb2 * 512:sb2 * 512 + w2],
                        self.smalls_pp("dsk", ct), psb.ap[:, :w2], AL.mult, AL.add),
                        [tl3, ytoks[-(nb - sb2)]] + (yb.wdep() if sb2 == 0 else []) + self.smalls.rdep())
                    yb.ww(t1); u2l.rr(t1); psb.rr(t1); self.smalls.rr(t1)
                    t1s.append(t1)
                t2 = P.op("v", lambda e, szl=szl: e.tensor_tensor(ygb.ap[:, :Tc], yb.ap[:, :Tc],
                                                                 szl.ap[:, :Tc], AL.mult),
                          t1s + [tl4] + ygb.wdep())
                ygb.ww(t2); yb.rr(t2); szl.rr(t2)
                ts = P.dma("dG", self.yg_fm[ct * 128:(ct + 1) * 128, tch * Tc:(tch + 1) * Tc],
                           ygb.ap[:, :Tc], [t2], key=ygb.key)
                ygb.rr(ts)
        for base, sl in [(bdlt, dlts), (bdul, duls), (bu2l, u2ls), (bszl, szls),
                         (bdA, dAs), (bbb, bbs), (bhn, hns), (bhc, hcs)]:
            self.slots_done(base, sl)

    def sub_l1ffn(self):
        P, T, DM, DH = self.P, self.T, self.DM, self.DH
        fence = P.fence()
        t1 = self.ln(self.xres, dst_tm=self.xnp_d, dep=fence)
        tk2 = self.uid("xq")
        tq = self.quant_tm(self.xnp_d, self.xq_fm, T, DM, dep=[t1], tkey=tk2)
        hs_fm = self.dram(self.uid("hs"), [DH, T], BF16)
        ub = self.pool("F32A")
        acc = self.pool("F32C")
        hsb = self.pool("B16D")
        nT = T // 512
        nDHt = DH // 128

        def evac_mm1(mt, ntile, psb, deps):
            te = P.op("s", lambda e: e.activation(out=ub.ap[:, 2:514], in_=psb.ap[:, :], func=AF.Identity,
                                                  scale=self.invws.ap[:, 0:1]),
                      deps + ub.wdep() + self.invws.rdep())
            ub.ww(te); self.invws.rr(te); psb.rr(te)
            tc = P.op("v", lambda e, mt=mt: e.tensor_copy(ub.ap[:, 0:2],
                                                          self.tails.ap[:, 2 * mt:2 * mt + 2]),
                      [te] + self.tails.rdep())
            ub.ww(tc); self.tails.rr(tc)
            tt = P.op("v", lambda e, mt=mt: e.tensor_copy(self.tails.ap[:, 2 * mt:2 * mt + 2],
                                                          ub.ap[:, 512:514]), [tc] + self.tails.wdep())
            self.tails.ww(tt)
            t0 = P.op("v", lambda e, mt=mt: e.tensor_scalar(acc.ap[:, :512], ub.ap[:, 0:512],
                                                            self.smalls_pp("cw1", 3 * mt), None, AL.mult),
                      [tc] + acc.wdep() + self.smalls.rdep())
            acc.ww(t0); self.smalls.rr(t0)
            for k2 in (1, 2):
                t0 = P.op("v", lambda e, mt=mt, k2=k2: e.scalar_tensor_tensor(
                    acc.ap[:, :512], ub.ap[:, k2:512 + k2], self.smalls_pp("cw1", 3 * mt + k2),
                    acc.ap[:, :512], AL.mult, AL.add), [t0])
                acc.ww(t0)
            ub.rr(t0)
            ts = P.op("s", lambda e, mt=mt: e.activation(out=hsb.ap[:, :512], in_=acc.ap[:, :512],
                                                         func=AF.Silu, bias=self.smalls_pp("cb1", mt)),
                      [t0] + hsb.wdep() + self.smalls.rdep())
            hsb.ww(ts); acc.rr(ts); self.smalls.rr(ts)
            lo = ntile * 512 - 1
            if lo < 0:
                td = P.dma("dG", hs_fm[mt * 128:(mt + 1) * 128, 0:511], hsb.ap[:, 1:512], [ts], key=hsb.key)
            else:
                td = P.dma("dG", hs_fm[mt * 128:(mt + 1) * 128, lo:lo + 512], hsb.ap[:, :512], [ts], key=hsb.key)
            hsb.rr(td)
            if ntile == nT - 1:
                tf = P.op("v", lambda e, mt=mt: e.tensor_scalar(acc.ap[:, 0:1],
                                                                self.tails.ap[:, 2 * mt:2 * mt + 1],
                                                                self.smalls_pp("cw1", 3 * mt), None,
                                                                AL.mult),
                          [tt, td] + acc.wdep() + self.smalls.rdep())
                acc.ww(tf); self.tails.rr(tf); self.smalls.rr(tf)
                tf = P.op("v", lambda e, mt=mt: e.scalar_tensor_tensor(
                    acc.ap[:, 0:1], self.tails.ap[:, 2 * mt + 1:2 * mt + 2],
                    self.smalls_pp("cw1", 3 * mt + 1), acc.ap[:, 0:1], AL.mult, AL.add), [tf])
                acc.ww(tf); self.tails.rr(tf)
                tf2 = P.op("s", lambda e, mt=mt: e.activation(out=hsb.ap[:, 0:1], in_=acc.ap[:, 0:1],
                                                              func=AF.Silu, bias=self.smalls_pp("cb1", mt)),
                           [tf] + hsb.wdep())
                hsb.ww(tf2); acc.rr(tf2)
                td = P.dma("dG", hs_fm[mt * 128:(mt + 1) * 128, T - 1:T], hsb.ap[:, 0:1], [tf2, td],
                           key=hsb.key, slow=True)
                hsb.rr(td)
            return td

        tz = P.op("v", lambda e: e.memset(self.tails.ap[:], 0.0), self.tails.wdep() + fence)
        self.tails.ww(tz)
        self.mm_stream(self.w1q_d[:, 0:DH // 2], DM, DH // 2, self.xq_fm, T, evac_mm1,
                       rhs_dep=[tz], lhs_dep=[self.t_wq], rhs_dep_fn=self.tfn(tk2))
        self.mm_stream(self.w1q_d[:, DH // 2:DH], DM, DH // 2, self.xq_fm, T, evac_mm1,
                       rhs_dep=[tz], lhs_dep=[self.t_wq], mt_off=nDHt // 2, rhs_dep_fn=self.tfn(tk2))
        tk3 = self.uid("hq")
        tq2 = self.quant_fm_roundtrip(hs_fm, self.hq_fm, T, DH, dep=P.fence(), tkey=tk3)
        contrib = self.dram(self.uid("c1"), [DM, T], F32R)
        ev = self.pool("R32B")
        th = P.op("v", lambda e: e.tensor_scalar(self.sc2.ap[:, 0:1], self.invws.ap[:, 1:2], 0.5, None,
                                                 AL.mult), self.invws.rdep() + self.sc2.wdep())
        self.sc2.ww(th); self.invws.rr(th)

        def evac_mm2(mt, ntile, psb, deps):
            te = P.op("s", lambda e: e.activation(out=ev.ap[:, :512], in_=psb.ap[:, :],
                                                  func=AF.Identity, scale=self.sc2.ap[:, 0:1]),
                      deps + [th] + ev.wdep())
            ev.ww(te); self.sc2.rr(te); psb.rr(te)
            ts = P.dma("dG", contrib[mt * 128:(mt + 1) * 128, ntile * 512:(ntile + 1) * 512],
                       ev.ap[:, :512], [te], key=ev.key)
            ev.rr(ts)
            return ts

        self.mm_stream(self.w2q_d[:, 0:DM // 2], DH, DM // 2, self.hq_fm, T,
                       evac_mm2, lhs_dep=[self.t_wq], rhs_dep_fn=self.tfn(tk3))
        self.mm_stream(self.w2q_d[:, DM // 2:DM], DH, DM // 2, self.hq_fm, T,
                       evac_mm2, lhs_dep=[self.t_wq], mt_off=DM // 256, rhs_dep_fn=self.tfn(tk3))
        self.add_contrib_fm(contrib, dep=P.fence())

    def quant_tm(self, src_tm_d, dst_fm_d, T, D, dep=None, tkey=None):
        dep = aslist(dep)
        if tkey is not None:
            self.tt[tkey] = []
        P = self.P
        xt = self.pool("F32A")
        g = self.pool("F32B")
        qb = self.pool("F32C")
        qi = self.pool("I32A")
        xq = self.pool("B16B")
        ev = self.pool("B16C")
        mx = self.mx
        last = None
        for it in range(T // 128):
            tl = P.dma("dS", xt.ap[:, :D], src_tm_d[it * 128:(it + 1) * 128, :], xt.wdep() + dep + [last],
                       key=xt.key)
            xt.ww(tl)
            tm = P.op("v", lambda e, D=D: e.tensor_reduce(mx.ap[:, 0:1], xt.ap[:, :D], mybir.AxisListType.X,
                                                          AL.max, apply_absolute_value=True),
                      [tl] + mx.wdep())
            tm = P.op("v", lambda e: e.tensor_scalar(mx.ap[:, 1:2], mx.ap[:, 0:1], 1e-5, None, AL.max), [tm])
            tm = P.op("v", lambda e: e.reciprocal(mx.ap[:, 2:3], mx.ap[:, 1:2]), [tm])
            tm = P.op("v", lambda e: e.tensor_scalar(mx.ap[:, 2:3], mx.ap[:, 2:3], 127.0, None, AL.mult),
                      [tm])
            tm = P.op("v", lambda e: e.tensor_scalar(mx.ap[:, 3:4], mx.ap[:, 1:2], 1.0 / 127.0, None,
                                                     AL.mult), [tm])
            mx.ww(tm)
            tqq = self._quant_block(xt, 0, D, mx, g, qb, qi, xq, [tm, tl])
            last = self.transpose_out(xq, D, dst_fm_d, it, BF16, ev, [tqq])
            if tkey is not None:
                self.tt[tkey].append(last)
        return last

    def _quant_block(self, srcbuf, off, width, mx, g, qb, qi, xq, deps, out_off0=False):
        P = self.P
        HB = 2048
        tq = None
        for o in range(off, off + width, HB):
            wd = min(HB, off + width - o)
            xo = 0 if out_off0 else o
            t1 = P.op("v", lambda e, o=o, wd=wd: e.tensor_scalar(qi.ap[:, :wd], srcbuf.ap[:, o:o + wd],
                                                                 mx.ap[:, 2:3], 128.0, AL.mult, AL.add),
                      deps + qi.wdep() + ([tq] if tq else []))
            qi.ww(t1); srcbuf.rr(t1); mx.rr(t1)
            t2 = P.op("v", lambda e, wd=wd: e.tensor_copy(qb.ap[:, :wd], qi.ap[:, :wd]), [t1] + qb.wdep())
            qb.ww(t2); qi.rr(t2)
            t3 = P.op("v", lambda e, o=o, wd=wd: e.scalar_tensor_tensor(
                g.ap[:, :wd], srcbuf.ap[:, o:o + wd], mx.ap[:, 2:3], qb.ap[:, :wd], AL.mult, AL.subtract),
                [t2] + deps + g.wdep())
            g.ww(t3); srcbuf.rr(t3); qb.rr(t3); mx.rr(t3)
            t4 = P.op("v", lambda e, wd=wd: e.scalar_tensor_tensor(qb.ap[:, :wd], g.ap[:, :wd], -127.5,
                                                                  qb.ap[:, :wd], AL.is_ge, AL.add), [t3])
            qb.ww(t4); g.rr(t4)
            tq = P.op("v", lambda e, xo=xo, wd=wd: e.tensor_scalar(xq.ap[:, xo:xo + wd], qb.ap[:, :wd],
                                                                   -128.0, mx.ap[:, 3:4], AL.add, AL.mult),
                      [t4] + xq.wdep())
            xq.ww(tq); qb.rr(tq); mx.rr(tq)
        return tq

    def quant_fm_roundtrip(self, src_fm_d, dst_fm_d, T, D, dep=None, tkey=None):
        dep = aslist(dep)
        if tkey is not None:
            self.tt[tkey] = []
        P = self.P
        tmv = self.pool("F32A")
        g = self.pool("F32B")
        qb = self.pool("F32C")
        qi = self.pool("I32A")
        cb = self.pool("B16A")
        xq = self.pool("B16B")
        ev = self.pool("B16C")
        mx = self.mx
        pst = self.ps["tr"]
        last = None
        for it in range(T // 128):
            te = None
            for half in range(D // 2048):
                sview = src_fm_d.rearrange("(a p) t -> p a t", p=128)[:, half * 16:(half + 1) * 16,
                                                                       it * 128:(it + 1) * 128]
                dview = cb.ap[:, :2048].rearrange("p (a t) -> p a t", t=128)
                tl = P.dma("dS", dview, sview, cb.wdep() + dep + [last, te], key=cb.key)
                cb.ww(tl)
                for grp in range(2):
                    toks = []
                    for b2 in range(8):
                        blk = grp * 8 + b2
                        d = [tl] + self.ident.rdep() + (pst.wdep() if b2 == 0 else [toks[-1]])
                        tt = P.op("t", lambda e, blk=blk, b2=b2: e.transpose(
                            pst.ap[:, b2 * 128:(b2 + 1) * 128], cb.ap[:, blk * 128:(blk + 1) * 128],
                            self.ident.ap[:]), d)
                        toks.append(tt); cb.rr(tt); self.ident.rr(tt)
                    pst.ww(toks[-1])
                    te = P.op("v", lambda e, half=half, grp=grp: e.tensor_copy(
                        tmv.ap[:, half * 2048 + grp * 1024:half * 2048 + (grp + 1) * 1024], pst.ap[:]),
                        [toks[-1]] + (tmv.wdep() if half == 0 and grp == 0 else []))
                    tmv.ww(te); pst.rr(te)
            tm = P.op("v", lambda e, D=D: e.tensor_reduce(mx.ap[:, 0:1], tmv.ap[:, :D],
                                                          mybir.AxisListType.X, AL.max,
                                                          apply_absolute_value=True), [te] + mx.wdep())
            tm = P.op("v", lambda e: e.tensor_scalar(mx.ap[:, 1:2], mx.ap[:, 0:1], 1e-5, None, AL.max), [tm])
            tm = P.op("v", lambda e: e.reciprocal(mx.ap[:, 2:3], mx.ap[:, 1:2]), [tm])
            tm = P.op("v", lambda e: e.tensor_scalar(mx.ap[:, 2:3], mx.ap[:, 2:3], 127.0, None, AL.mult),
                      [tm])
            tm = P.op("v", lambda e: e.tensor_scalar(mx.ap[:, 3:4], mx.ap[:, 1:2], 1.0 / 127.0, None,
                                                     AL.mult), [tm])
            mx.ww(tm)
            for half in range(D // 2048):
                tqq = self._quant_block(tmv, half * 2048, 2048, mx, g, qb, qi, xq, [tm], out_off0=True)
                last = self.transpose_out(xq, 2048, dst_fm_d, it, BF16, ev, [tqq],
                                          dst_row=half * 2048)
            if tkey is not None:
                self.tt[tkey].append(last)
        return last

    def sub_attn(self):
        P, T, DM = self.P, self.T, self.DM
        fence = P.fence()
        t1 = self.ln(self.xres, dst_tm=self.xnp_d, dep=fence)
        tk4 = self.uid("xnA")
        t2 = self.ln(self.xnp_d, dst_fm=self.xn_fm16, fm_dt=BF16, gbi=1, dep=[t1], tkey=tk4)
        evq = self.pool("B16D")
        vtb = self.pool("B16E")
        pst = self.ps["tr"]

        def evac_qk(mt, ntile, psb, deps):
            te = P.op("v", lambda e: e.tensor_copy(evq.ap[:, :512], psb.ap[:, :]), deps + evq.wdep())
            evq.ww(te); psb.rr(te)
            ts = P.dma("dG", self.qk_fm[mt * 128:(mt + 1) * 128, ntile * 512:(ntile + 1) * 512],
                       evq.ap[:, :512], [te], key=evq.key)
            evq.rr(ts)
            return ts

        def evac_v(mt, ntile, psb, deps):
            te = P.op("v", lambda e: e.tensor_copy(vtb.ap[:, :512], psb.ap[:, :]), deps + vtb.wdep())
            vtb.ww(te); psb.rr(te)
            toks = []
            for b2 in range(4):
                d = [te] + self.ident.rdep() + (pst.wdep() if b2 == 0 else [toks[-1]])
                tt = P.op("t", lambda e, b2=b2: e.transpose(pst.ap[:, b2 * 128:(b2 + 1) * 128],
                                                            vtb.ap[:, b2 * 128:(b2 + 1) * 128],
                                                            self.ident.ap[:]), d)
                toks.append(tt); vtb.rr(tt); self.ident.rr(tt)
            pst.ww(toks[-1])
            te2 = P.op("v", lambda e: e.tensor_copy(evq.ap[:, :512], pst.ap[:, 0:512]),
                       [toks[-1]] + evq.wdep())
            evq.ww(te2); pst.rr(te2)
            dview = self.v_tm[ntile * 512:(ntile + 1) * 512, mt * 128:(mt + 1) * 128].rearrange(
                "(a p) c -> p a c", p=128)
            sview = evq.ap[:, :512].rearrange("p (a c) -> p a c", c=128)
            ts = P.dma("dG", dview, sview, [te2], key=evq.key)
            evq.rr(ts)
            return ts

        self.mm_stream(self.w["w_qkv"][:, 0:2 * DM], DM, 2 * DM, self.xn_fm16, T, evac_qk,
                       rhs_dep_fn=self.tfn(tk4))
        self.mm_stream(self.w["w_qkv"][:, 2 * DM:3 * DM], DM, DM, self.xn_fm16, T, evac_v,
                       rhs_dep_fn=self.tfn(tk4))
        fatt = P.fence()
        bqs, qsbs = self.slots("B16A", 1024, 2)
        bks, ksbs = self.slots("BIGB", 2048, 2)
        bvs, vsbs = self.slots("BIGC", 2048, 2)
        bao, aosts = self.slots("B16B", 1024, 2)
        bea, eas = self.slots("B16D", 256, 2)
        bet, eaTs = self.slots("B16E", 256, 2)
        smss = [self.sms, self.mx]
        W = 128
        nw = T // W
        psscs = [self.ps["sc"], self.ps["mm0"]]
        psavs = [self.ps["av"], self.ps["mm1"]]
        pst = self.ps["tr"]
        last = None
        for wdx in range(nw):
            ws = wdx % 2
            qsb, ksb, vsb, aost = qsbs[ws], ksbs[ws], vsbs[ws], aosts[ws]
            qv = self.qk_fm.rearrange("(a p) t -> p a t", p=128)[:, 0:8, wdx * W:(wdx + 1) * W]
            tq = P.dma("dS", qsb.ap[:, :8 * W].rearrange("p (a t) -> p a t", t=W), qv,
                       qsb.wdep() + fatt, key=qsb.key)
            qsb.ww(tq)
            lo = max(wdx - 1, 0) * W
            kv = self.qk_fm.rearrange("(a p) t -> p a t", p=128)[:, 8:16, lo:lo + 2 * W]
            tk = P.dma("dS", ksb.ap[:, :8 * 2 * W].rearrange("p (a t) -> p a t", t=2 * W), kv,
                       ksb.wdep() + fatt, key=ksb.key)
            ksb.ww(tk)
            vv = self.v_tm[lo:lo + 2 * W, :].rearrange("(a p) c -> p a c", p=128)
            tv = P.dma("dA", vsb.ap[:, :2 * DM].rearrange("p (a c) -> p a c", c=DM), vv,
                       vsb.wdep() + fatt, key=vsb.key)
            vsb.ww(tv)
            tae = None
            for h in range(16):
                hs = h % 2
                pssc, psav, ea, eaT, sms = psscs[hs], psavs[hs], eas[hs], eaTs[hs], smss[hs]
                po = hs * 256
                pb = 64 * (h % 2)
                tile = h // 2
                d0 = [tq, tk] + pssc.wdep()
                tmm = P.op("t", lambda e, pssc=pssc, pb=pb, tile=tile, qsb=qsb, ksb=ksb: e.matmul(
                    pssc.ap[:, 0:256], qsb.ap[pb:pb + 64, tile * W:(tile + 1) * W],
                    ksb.ap[pb:pb + 64, tile * 2 * W:(tile + 1) * 2 * W], start=True, stop=False), d0)
                qsb.rr(tmm); ksb.rr(tmm)
                moff = 0 if wdx > 0 else 256
                tmm = P.op("t", lambda e, pssc=pssc, moff=moff: e.matmul(
                    pssc.ap[:, 0:256], self.ident.ap[:], self.maskb.ap[:, moff:moff + 256],
                    start=False, stop=True), [tmm] + self.maskb.rdep() + self.ident.rdep())
                pssc.ww(tmm); self.maskb.rr(tmm); self.ident.rr(tmm)
                tr = P.op("v", lambda e, pssc=pssc, sms=sms: e.tensor_reduce(
                    sms.ap[:, 0:1], pssc.ap[:, 0:256], mybir.AxisListType.X, AL.max),
                    [tmm] + sms.wdep())
                tr = P.op("v", lambda e, sms=sms: e.tensor_scalar(sms.ap[:, 1:2], sms.ap[:, 0:1],
                                                                  -0.125, None, AL.mult), [tr])
                sms.ww(tr)
                texp = P.op("s", lambda e, pssc=pssc, ea=ea, sms=sms: e.activation(
                    out=ea.ap[:, :256], in_=pssc.ap[:, 0:256], func=AF.Exp, scale=0.125,
                    bias=sms.ap[:, 1:2], accum_out=sms.ap[:, 2:3]), [tr] + ea.wdep())
                ea.ww(texp); pssc.rr(texp); sms.ww(texp)
                trc = P.op("v", lambda e, sms=sms: e.reciprocal(sms.ap[:, 3:4], sms.ap[:, 2:3]), [texp])
                sms.ww(trc)
                tnm = P.op("v", lambda e, ea=ea, sms=sms: e.tensor_scalar(
                    ea.ap[:, :256], ea.ap[:, :256], sms.ap[:, 3:4], None, AL.mult), [trc])
                ea.ww(tnm); sms.rr(tnm)
                toks = []
                for b2 in range(2):
                    d = [tnm] + self.ident.rdep() + (pst.wdep() if b2 == 0 else [toks[-1]])
                    tt = P.op("t", lambda e, po=po, b2=b2, ea=ea: e.transpose(
                        pst.ap[:, po + b2 * 128:po + (b2 + 1) * 128],
                        ea.ap[:, b2 * 128:(b2 + 1) * 128], self.ident.ap[:]), d)
                    toks.append(tt); ea.rr(tt); self.ident.rr(tt)
                pst.ww(toks[-1])
                tev = P.op("v", lambda e, po=po, eaT=eaT: e.tensor_copy(
                    eaT.ap[:, :256], pst.ap[:, po:po + 256]), [toks[-1]] + eaT.wdep())
                eaT.ww(tev); pst.rr(tev)
                tavm = None
                for k2 in range(2):
                    d = [tev, tv] + (psav.wdep() if k2 == 0 else [tavm])
                    tavm = P.op("t", lambda e, psav=psav, eaT=eaT, vsb=vsb, k2=k2, h=h: e.matmul(
                        psav.ap[:64, 0:128], vsb.ap[:, k2 * DM + h * 64:k2 * DM + (h + 1) * 64],
                        eaT.ap[:, k2 * 128:(k2 + 1) * 128], start=(k2 == 0), stop=(k2 == 1)), d)
                    eaT.rr(tavm); vsb.rr(tavm)
                psav.ww(tavm)
                tae = P.op("v", lambda e, psav=psav, aost=aost, pb=pb, tile=tile: e.tensor_copy(
                    aost.ap[pb:pb + 64, tile * W:(tile + 1) * W], psav.ap[:64, 0:128]),
                    [tavm] + (aost.wdep() if h == 0 else [tae]))
                aost.ww(tae); psav.rr(tae)
            dview = self.ao_fm.rearrange("(a p) t -> p a t", p=128)[:, :, wdx * W:(wdx + 1) * W]
            sview = aost.ap[:, :8 * W].rearrange("p (a t) -> p a t", t=W)
            last = P.dma("dG", dview, sview, [tae], key=aost.key)
            aost.rr(last)
        for base, sl in [(bqs, qsbs), (bks, ksbs), (bvs, vsbs), (bao, aosts),
                         (bea, eas), (bet, eaTs)]:
            self.slots_done(base, sl)
        fout = P.fence()
        contrib = self.dram(self.uid("ca"), [DM, T], F32R)
        ev2 = self.pool("R32B")

        def evac_out(mt, ntile, psb, deps):
            te = P.op("v", lambda e: e.tensor_copy(ev2.ap[:, :512], psb.ap[:, :]), deps + ev2.wdep())
            ev2.ww(te); psb.rr(te)
            ts = P.dma("dG", contrib[mt * 128:(mt + 1) * 128, ntile * 512:(ntile + 1) * 512],
                       ev2.ap[:, :512], [te], key=ev2.key)
            ev2.rr(ts)
            return ts

        self.mm_stream(self.w["w_out"], DM, DM, self.ao_fm, T, evac_out, rhs_dep=fout)
        self.add_contrib_fm(contrib, dep=P.fence())

    def final_out(self):
        self.P.dma("dS", self.out[:, :], self.xres[:, :], [self.t_x0])


def make_inputs(ins, x_seq, T):
    DM, DI, DH = 1024, 2048, 4096
    f32 = np.float32
    import ml_dtypes
    bf16 = ml_dtypes.bfloat16
    g = lambda k: np.asarray(ins[k], dtype=f32)
    d = {}
    d["x_in"] = np.ascontiguousarray(x_seq, dtype=f32)
    d["in_projT"] = np.ascontiguousarray(g("L0_attn_in_proj_w").T).astype(bf16)
    d["x_projT"] = np.ascontiguousarray(g("L0_attn_x_proj_w").T).astype(bf16)
    d["dt_projT"] = np.ascontiguousarray(g("L0_attn_dt_proj_w").T).astype(bf16)
    d["out_projT"] = np.ascontiguousarray(g("L0_attn_out_proj_w").T).astype(bf16)
    d["w_qkv"] = np.ascontiguousarray(g("L1_attn_w_qkv")).astype(bf16)
    d["w_out"] = np.ascontiguousarray(g("L1_attn_w_out")).astype(bf16)
    d["w1T"] = np.ascontiguousarray(g("L1_ffn_w1").T, dtype=f32)
    d["w2T"] = np.ascontiguousarray(g("L1_ffn_w2").T, dtype=f32)
    d["wfr"] = np.ascontiguousarray(g("L0_ffn_w_freq_real"), dtype=f32)
    d["wfi"] = np.ascontiguousarray(g("L0_ffn_w_freq_imag"), dtype=f32)
    k = np.arange(DM)
    ang = 2.0 * np.pi * ((k[:, None] * k[None, :]) % DM) / DM
    d["C_d"] = np.cos(ang).astype(f32)
    d["S_d"] = np.sin(ang).astype(f32)
    gb = np.zeros((128, 4 * DM), f32)
    gb[:, 0:DM] = g("L0_attn_ln_gamma")[None, :]
    gb[:, DM:2 * DM] = g("L0_attn_ln_beta")[None, :]
    gb[:, 2 * DM:3 * DM] = g("L1_attn_ln_gamma")[None, :]
    gb[:, 3 * DM:4 * DM] = g("L1_attn_ln_beta")[None, :]
    d["gb"] = gb.astype(bf16)
    d["ident"] = np.eye(128, dtype=bf16)
    qi = np.arange(128)
    maskp = qi[:, None] > qi[None, :]
    masko = qi[:, None] < qi[None, :]
    mb = np.zeros((128, 512), f32)
    mb[:, 0:256][np.concatenate([maskp, masko], axis=1)] = -1e38
    mb[:, 256:512][np.concatenate([masko, np.ones((128, 128), bool)], axis=1)] = -1e38
    d["maskb"] = mb.astype(bf16)
    sm = np.zeros((128, 504), f32)
    cw = g("L0_attn_conv_w")[:, 0, :].reshape(DI // 128, 128, 4)
    for c in range(DI // 128):
        sm[:, 4 * c:4 * c + 4] = cw[c]
    sm[:, 64:80] = g("L0_attn_conv_b").reshape(DI // 128, 128).T
    sm[:, 80:96] = g("L0_attn_D_skip").reshape(DI // 128, 128).T
    sm[:, 96:112] = g("L0_attn_dt_proj_b").reshape(DI // 128, 128).T
    al = g("L0_attn_A_log").reshape(DI // 128, 128, 16)
    for c in range(DI // 128):
        sm[:, 112 + 16 * c:112 + 16 * (c + 1)] = al[c]
    cw1 = g("L1_ffn_conv_w")[:, 0, :].reshape(DH // 128, 128, 3)
    for c in range(DH // 128):
        sm[:, 368 + 3 * c:368 + 3 * c + 3] = cw1[c]
    sm[:, 464:496] = g("L1_ffn_conv_b").reshape(DH // 128, 128).T
    sm[:, 496:504] = 0.5 * g("L0_ffn_bias").reshape(DM // 128, 128).T
    d["smalls"] = sm
    return d

def kernel(**inputs):
    import numpy as np
    from concourse.bass_utils import run_bass_kernel_spmd
    T = 8192
    x = np.asarray(inputs["L0_attn_x"], dtype=np.float32)
    B = x.shape[0]
    mk = MK(T)
    nc = mk.build()
    base = make_inputs(inputs, x[0], T)
    in_maps = []
    for core in range(8):
        m = dict(base)
        m["x_in"] = np.ascontiguousarray(x[core % B], dtype=np.float32)
        in_maps.append(m)
    res = run_bass_kernel_spmd(nc, in_maps, core_ids=list(range(8)))
    out = np.stack([np.asarray(res.results[b]["out"]) for b in range(B)], axis=0)
    return out.astype(np.float32)
